# version 02
# baseline (speedup 1.0000x reference)
import numpy as np
import ml_dtypes
from contextlib import ExitStack

import concourse.bass as bass
import concourse.mybir as mybir
from concourse.bass_utils import run_bass_kernel_spmd

F32 = mybir.dt.float32
BF16 = mybir.dt.bfloat16
I32 = mybir.dt.int32
AF = mybir.ActivationFunctionType
ALU = mybir.AluOpType
AX = mybir.AxisListType.X

NCORES = 8
T_CORE = 2048
D = 1024
NPBF16 = ml_dtypes.bfloat16

J_ONE = 0x7F0000
J_HALF = 0x7E0000
J_B025 = 0x3E800000 >> 7
J_B075 = 0x3F400000 >> 7
J_GMASK = 0x00FF8000
SIXTH = float(np.float32(1.0 / 6.0))


class Prog:

    def __init__(self, name="k"):
        self.nc = bass.Bass("TRN2", target_bir_lowering=False)
        self.ctx = ExitStack()
        self.sem = {}
        self.cnt = {}
        for s in ("t", "v", "s", "g", "y", "dh", "dg", "cc",
                  "dx", "dwa", "dwb", "dso", "dm"):
            self.sem[s] = self.ctx.enter_context(self.nc.semaphore("sem_" + s))
            self.cnt[s] = 0
        self.block = self.ctx.enter_context(self.nc.Block())
        self.watermark = {e: {} for e in ("t", "v", "s", "g", "y")}
        self._cur = None

    def sbuf(self, name, shape, dt=F32):
        return self.ctx.enter_context(self.nc.sbuf_tensor(name, shape, dt))

    def psum(self, name, shape, dt=F32):
        return self.ctx.enter_context(self.nc.psum_tensor(name, shape, dt))

    def dram(self, name, shape, dt=F32, kind=None):
        if kind:
            return self.nc.dram_tensor(name, shape, dt, kind=kind)
        return self.nc.dram_tensor(name, shape, dt)

    def section(self, engname, fn):
        attr = {"t": "tensor", "v": "vector", "s": "scalar", "g": "gpsimd",
                "y": "sync"}[engname]

        def run(e):
            self._cur = (engname, e)
            fn(e)
            self._cur = None
        getattr(self.block, attr)(run)

    def wait(self, tok):
        engname, e = self._cur
        s, v = tok
        if v <= 0:
            return
        wm = self.watermark[engname]
        if wm.get(s, 0) >= v:
            return
        e.wait_ge(self.sem[s], v)
        wm[s] = v

    def op(self, fn, waits=(), self_wait=True):
        engname, e = self._cur
        for w in waits:
            self.wait(w)
        if self_wait and engname != "t":
            self.wait((engname, self.cnt[engname]))
        inst = fn()
        self.cnt[engname] += 1
        inst.then_inc(self.sem[engname], 1)
        return (engname, self.cnt[engname])

    def dma(self, out, in_, waits=(), sem=None):
        engname, e = self._cur
        if sem is None:
            sem = "dg" if engname == "g" else "dh"
        for w in waits:
            self.wait(w)
        self.wait((sem, self.cnt[sem]))
        if engname != "t":
            self.wait((engname, self.cnt[engname]))
        self.cnt[sem] += 16
        e.dma_start(out=out, in_=in_).then_inc(self.sem[sem], 16)
        return (sem, self.cnt[sem])

    def tok(self, engname):
        return (engname, self.cnt[engname])

    def barrier(self):
        snap = {e: self.cnt[e] for e in ("t", "v", "s", "g")}
        dsnap = {d: self.cnt[d] for d in ("dh", "dx", "dwa", "dwb", "dso", "dm", "dg")}

        def mk(engname):
            def fn(e):
                for oe, c in snap.items():
                    self.wait((oe, c))
                for d, c in dsnap.items():
                    self.wait((d, c))
            return fn
        for e in ("t", "v", "s", "g", "y"):
            self.section(e, mk(e))

    def init_arena(self, fp32_elems=46000):
        self.ar_sb = self.sbuf("ARENA", [128, fp32_elems], F32)
        self.ar_ps = self.psum("PSARENA", [128, 4096], F32)
        self.ar_off = 0
        self.ar_ps_off = 0
        self.ar_elems = fp32_elems

    def phase_reset(self):
        self.ar_off = 0
        self.ar_ps_off = 0
        self.barrier()

    def sb(self, name, shape, dt=F32):
        p, n = shape[0], int(np.prod(shape[1:]))
        if dt == F32 or dt == I32:
            elems = n
        elif dt == BF16:
            elems = (n + 1) // 2
        else:
            raise ValueError(dt)
        off = self.ar_off
        self.ar_off += elems
        assert self.ar_off <= self.ar_elems, f"SBUF arena overflow at {name}: {self.ar_off}"
        view = self.ar_sb[0:p, off:off + elems]
        if dt == BF16:
            view = self.ar_sb[0:p, off:off + elems].bitcast(BF16)
            if view.shape[1] != n:
                view = view[:, 0:n]
        elif dt == I32:
            view = view.bitcast(I32)
        return view

    def ps(self, name, shape, dt=F32):
        p, n = shape[0], int(np.prod(shape[1:]))
        elems = n if dt in (F32, I32) else (n + 1) // 2
        banks = -(-elems // 512)
        off = self.ar_ps_off
        self.ar_ps_off += banks * 512
        assert self.ar_ps_off <= 4096, f"PSUM arena overflow at {name}"
        view = self.ar_ps[0:p, off:off + elems]
        if dt == BF16:
            view = self.ar_ps[0:p, off:off + (n + 1) // 2].bitcast(BF16)
            if view.shape[1] != n:
                view = view[:, 0:n]
        return view

    def done(self):
        self.ctx.close()
        return self.nc


def bcast16(scale_ap, p, nblk):
    return scale_ap.unsqueeze(2).broadcast_to([p, nblk, 16])


def blk16(ap):
    return ap.rearrange("p (b k) -> p b k", k=16)


def make_quant_scratch(prog, tag, p, Fmax):
    nblk = Fmax // 16
    alloc = prog.sb if hasattr(prog, "ar_sb") else prog.sbuf
    return {
        "smax": alloc(f"q_smax_{tag}", [p, nblk]),
        "rs6": alloc(f"q_rs6_{tag}", [p, nblk]),
        "scale": alloc(f"q_scale_{tag}", [p, nblk]),
        "t2": alloc(f"q_t2_{tag}", [p, Fmax]),
        "sgn": alloc(f"q_sgn_{tag}", [p, Fmax], BF16),
    }


def emit_quant(prog, src_ap, out_bf_ap, scr, p, F, in_tok, scr_free_tok=None, out_free_tok=None):
    nblk = F // 16
    smax, rs6, scale = scr["smax"], scr["rs6"], scr["scale"]
    t2, sgn = scr["t2"], scr["sgn"]
    xn = src_ap
    toks = {}

    def vsec(v):
        waits = [in_tok]
        if scr_free_tok is not None:
            waits.append(scr_free_tok)
        prog.op(lambda: v.tensor_reduce(out=smax[:p, :nblk], in_=blk16(src_ap),
                                        axis=AX, op=ALU.max,
                                        apply_absolute_value=True), waits=waits)
        prog.op(lambda: v.tensor_scalar(out=smax[:p, :nblk], in0=smax[:p, :nblk],
                                        scalar1=6e-12, scalar2=None, op0=ALU.max))
        prog.op(lambda: v.reciprocal(out=rs6[:p, :nblk], in_=smax[:p, :nblk]))
        prog.op(lambda: v.tensor_scalar(out=rs6[:p, :nblk], in0=rs6[:p, :nblk],
                                        scalar1=6.0, scalar2=None, op0=ALU.mult))
        prog.op(lambda: v.tensor_scalar(out=scale[:p, :nblk], in0=smax[:p, :nblk],
                                        scalar1=SIXTH, scalar2=None, op0=ALU.mult))
        toks["scales"] = prog.tok("v")
    prog.section("v", vsec)

    def gsec2(g):
        prog.op(lambda: g.tensor_tensor(out=blk16(src_ap), in0=blk16(src_ap),
                                        in1=bcast16(rs6[:p, :nblk], p, nblk),
                                        op=ALU.mult), waits=[toks["scales"]])
        toks["xn"] = prog.tok("g")
    prog.section("g", gsec2)

    def vsec2(v):
        prog.op(lambda: v.tensor_scalar(out=t2[:p, :F].bitcast(I32),
                                        in0=xn.bitcast(I32),
                                        scalar1=7, scalar2=0x00FFFFFF,
                                        op0=ALU.logical_shift_right, op1=ALU.bitwise_and),
                waits=[toks["xn"]])
        prog.op(lambda: v.tensor_scalar(out=sgn[:p, :F], in0=xn,
                                        scalar1=0.0, scalar2=None,
                                        op0=ALU.is_ge))
        prog.op(lambda: v.tensor_scalar(out=sgn[:p, :F], in0=sgn[:p, :F],
                                        scalar1=2.0, scalar2=-1.0,
                                        op0=ALU.mult, op1=ALU.add))
        prog.op(lambda: v.tensor_scalar(out=xn[:p, :F].bitcast(I32),
                                        in0=t2[:p, :F].bitcast(I32),
                                        scalar1=0x4000, scalar2=J_ONE,
                                        op0=ALU.add, op1=ALU.max))
        prog.op(lambda: v.tensor_scalar(out=xn[:p, :F].bitcast(I32),
                                        in0=xn[:p, :F].bitcast(I32),
                                        scalar1=J_GMASK, scalar2=None,
                                        op0=ALU.bitwise_and))
        prog.op(lambda: v.tensor_scalar(out=xn[:p, :F].bitcast(I32),
                                        in0=xn[:p, :F].bitcast(I32),
                                        scalar1=J_HALF, scalar2=None,
                                        op0=ALU.subtract))
        prog.op(lambda: v.scalar_tensor_tensor(out=xn[:p, :F].bitcast(I32),
                                               in0=t2[:p, :F].bitcast(I32),
                                               scalar=J_B075,
                                               in1=xn[:p, :F].bitcast(I32),
                                               op0=ALU.is_gt, op1=ALU.mult))
        prog.op(lambda: v.tensor_scalar(out=t2[:p, :F].bitcast(I32),
                                        in0=t2[:p, :F].bitcast(I32),
                                        scalar1=J_B025, scalar2=J_HALF,
                                        op0=ALU.is_gt, op1=ALU.mult))
        prog.op(lambda: v.tensor_tensor(out=t2[:p, :F].bitcast(I32),
                                        in0=t2[:p, :F].bitcast(I32),
                                        in1=xn[:p, :F].bitcast(I32), op=ALU.add))
        prog.op(lambda: v.tensor_scalar(out=t2[:p, :F].bitcast(I32),
                                        in0=t2[:p, :F].bitcast(I32),
                                        scalar1=7, scalar2=None,
                                        op0=ALU.logical_shift_left))
        prog.op(lambda: v.tensor_tensor(out=t2[:p, :F], in0=t2[:p, :F],
                                        in1=sgn[:p, :F], op=ALU.mult))
        toks["qb"] = prog.tok("v")
    prog.section("v", vsec2)

    def gsec3(g):
        w3 = [toks["qb"]]
        if out_free_tok is not None:
            w3.append(out_free_tok)
        prog.op(lambda: g.tensor_tensor(out=blk16(out_bf_ap),
                                        in0=blk16(t2[:p, :F]),
                                        in1=bcast16(scale[:p, :nblk], p, nblk),
                                        op=ALU.mult), waits=w3)
        toks["out"] = prog.tok("g")
    prog.section("g", gsec3)
    return toks["out"], toks["out"]


def emit_identity(prog, ident_bf, io1, io2):
    toks = {}

    def gsec(g):
        prog.op(lambda: g.iota(io1[:, :], [[1, 128]], channel_multiplier=0,
                               allow_small_or_imprecise_dtypes=True))
        prog.op(lambda: g.iota(io2[:, :], [[0, 128]], channel_multiplier=1,
                               allow_small_or_imprecise_dtypes=True))
        toks["io"] = prog.tok("g")
    prog.section("g", gsec)

    def vsec(v):
        prog.op(lambda: v.tensor_tensor(out=ident_bf[:, :], in0=io1[:, :],
                                        in1=io2[:, :], op=ALU.is_equal),
                waits=[toks["io"]])
        toks["id"] = prog.tok("v")
    prog.section("v", vsec)
    return toks["id"]


def build_wprep(rows_g, rows_d, ncast_cols):
    prog = Prog("wprep")
    wg = prog.dram("wg_s", [rows_g, 1024], F32, kind="ExternalInput")
    wu = prog.dram("wu_s", [rows_g, 1024], F32, kind="ExternalInput")
    wd = prog.dram("wd_s", [rows_d, 2816], F32, kind="ExternalInput")
    o_wg = prog.dram("o_wgqT", [1024, rows_g], BF16, kind="ExternalOutput")
    o_wu = prog.dram("o_wuqT", [1024, rows_g], BF16, kind="ExternalOutput")
    o_wd = prog.dram("o_wdqT", [2816, rows_d], BF16, kind="ExternalOutput")
    cin = prog.dram("cast_in", [128, ncast_cols], F32, kind="ExternalInput")
    cout = prog.dram("o_cast", [128, ncast_cols], BF16, kind="ExternalOutput")

    ident = prog.sbuf("ident", [128, 128], BF16)
    io1 = prog.sbuf("id_io1", [128, 128], I32)
    io2 = prog.sbuf("id_io2", [128, 128], I32)
    id_tok = emit_identity(prog, ident, io1, io2)

    scr = make_quant_scratch(prog, "w", 128, 2816)
    src = prog.sbuf("w_src", [128, 2816], F32)
    qbf = prog.sbuf("w_qbf", [128, 2816], BF16)
    tps = prog.psum("w_tp", [128, 128], BF16)
    tsb = prog.sbuf("w_tsb", [128, 128], BF16)

    jobs = [(wg, o_wg, rows_g, 1024), (wu, o_wu, rows_g, 1024),
            (wd, o_wd, rows_d, 2816)]

    prev_q = (None)
    src_free = ("v", 0)
    qbf_free = ("v", 0)
    store_done = ("dh", 0)
    for jn, (src_d, out_d, nrows, K) in enumerate(jobs):
        r0 = 0
        while r0 < nrows:
            p = min(128, nrows - r0)
            ld_tok = [None]

            def ysec(y, src_d=src_d, r0=r0, p=p, K=K, sf=src_free):
                ld_tok[0] = prog.dma(src[:p, :K], src_d.ap()[r0:r0 + p, :K],
                                     waits=(sf,))
            prog.section("y", ysec)

            q_tok, _ = emit_quant(prog, src[:p, :K], qbf[:p, :K], scr, p, K,
                                  ld_tok[0], scr_free_tok=qbf_free)
            src_free = q_tok

            nkc = K // 128
            last_cp = None
            for kc in range(nkc):
                def tsec(t, kc=kc, sd=store_done):
                    prog.op(lambda: t.transpose(tps[:, :],
                                                qbf[:, kc * 128:(kc + 1) * 128],
                                                ident[:, :]),
                            waits=[q_tok, id_tok, prog.tok("v")])
                prog.section("t", tsec)
                tr_tok = prog.tok("t")

                def vsec(v, sd=store_done):
                    prog.op(lambda: v.tensor_copy(out=tsb[:, :], in_=tps[:, :]),
                            waits=[tr_tok, sd])
                prog.section("v", vsec)
                cp_tok = prog.tok("v")
                last_cp = cp_tok

                def ysec2(y, kc=kc, out_d=out_d, r0=r0, p=p):
                    prog.dma(out_d.ap()[kc * 128:(kc + 1) * 128, r0:r0 + p],
                             tsb[:, 0:p], waits=(cp_tok,))
                prog.section("y", ysec2)
                store_done = ("dh", prog.cnt["dh"])
            qbf_free = last_cp
            qbf_free = tr_tok
            r0 += 128

    cast_sb = prog.sbuf("cast_sb", [128, 4096], F32)
    cast_ob = prog.sbuf("cast_ob", [128, 4096], BF16)
    c0 = 0
    cast_free = ("v", 0)
    cast_store = ("dh", 0)
    while c0 < ncast_cols:
        w = min(4096, ncast_cols - c0)
        ld = [None]

        def ysec3(y, c0=c0, w=w, cf=cast_free):
            ld[0] = prog.dma(cast_sb[:, :w], cin.ap()[:, c0:c0 + w], waits=(cf,))
        prog.section("y", ysec3)

        def vsec2(v, w=w, cs=cast_store):
            prog.op(lambda: v.tensor_copy(out=cast_ob[:, :w], in_=cast_sb[:, :w]),
                    waits=[ld[0], cs])
        prog.section("v", vsec2)
        cast_tok = prog.tok("v")
        cast_free = cast_tok

        def ysec4(y, c0=c0, w=w):
            prog.dma(cout.ap()[:, c0:c0 + w], cast_ob[:, :w], waits=(cast_tok,))
        prog.section("y", ysec4)
        cast_store = ("dh", prog.cnt["dh"])
        c0 += 4096

    def fin(y):
        y.wait_ge(prog.sem["dh"], prog.cnt["dh"])
        if prog.cnt["dg"]:
            y.wait_ge(prog.sem["dg"], prog.cnt["dg"])
    prog.section("y", fin)
    return prog.done()


def _collect_cast_weights(inp):
    names = ["L0_attn_in_proj_w", "L0_attn_dt_proj_w", "L0_attn_x_proj_w",
             "L0_attn_out_proj_w", "L1_attn_Wq", "L1_attn_Wk", "L1_attn_Wv",
             "L1_attn_Wo", "L1_ffn_w_gate", "L1_ffn_Wd", "L1_ffn_Wu",
             "L1_ffn_bu", "L1_attn_projection"]
    return [(n, np.ascontiguousarray(inp[n], dtype=np.float32)) for n in names]


def run_wprep(inp):
    arrs = _collect_cast_weights(inp)
    total = sum(a.size for _, a in arrs)
    per_core = -(-total // NCORES)
    per_core = -(-per_core // 128) * 128
    ncast_cols = per_core // 128
    rows_g = 2816 // NCORES
    rows_d = 1024 // NCORES
    nc0 = build_wprep(rows_g, rows_d, ncast_cols)

    flat = np.concatenate([a.reshape(-1) for _, a in arrs]).astype(np.float32)
    flat = np.concatenate([flat, np.zeros(per_core * NCORES - total, np.float32)])
    in_maps = []
    for i in range(NCORES):
        in_maps.append({
            "wg_s": np.ascontiguousarray(inp["L0_ffn_w_gate"][i * rows_g:(i + 1) * rows_g]),
            "wu_s": np.ascontiguousarray(inp["L0_ffn_w_up"][i * rows_g:(i + 1) * rows_g]),
            "wd_s": np.ascontiguousarray(inp["L0_ffn_w_down"][i * rows_d:(i + 1) * rows_d]),
            "cast_in": np.ascontiguousarray(
                flat[i * per_core:(i + 1) * per_core].reshape(128, ncast_cols)),
        })
    res = run_bass_kernel_spmd(nc0, in_maps, core_ids=list(range(NCORES)))
    wgqT = np.concatenate([res.results[i]["o_wgqT"] for i in range(NCORES)], axis=1)
    wuqT = np.concatenate([res.results[i]["o_wuqT"] for i in range(NCORES)], axis=1)
    wdqT = np.concatenate([res.results[i]["o_wdqT"] for i in range(NCORES)], axis=1)
    cflat = np.concatenate([res.results[i]["o_cast"].reshape(-1) for i in range(NCORES)])[:total]
    casts = {}
    off = 0
    for n, a in arrs:
        casts[n] = cflat[off:off + a.size].reshape(a.shape)
        off += a.size
    return wgqT, wuqT, wdqT, casts


_NC_CACHE = {}


def _get_nc(name, builder):
    if name not in _NC_CACHE:
        _NC_CACHE[name] = builder()
    return _NC_CACHE[name]


def kernel(**inp):
    return kernel_fused(**inp)


def emit_ffn(prog, io, halo=False):
    NBW = [512] * 5 + [256]
    x_d = io["x"]
    wg_d = io["wgqT"]
    wu_d = io["wuqT"]
    wd_d = io["wdqT"]
    o_d = io["o"]
    ident = io["_ident"]
    id_tok = io["_id_tok"]
    if halo:
        chunks = [(128, HALO + i * 128) for i in range(16)] + [(HALO, 0)]
    else:
        chunks = [(128, i * 128) for i in range(16)]
    TCn = len(chunks)

    xqT = prog.sb("xqT", [128, 8 * TCn * 128], BF16)
    hq = prog.sb("hq", [128, TCn * 2816], BF16)
    wd_sb = prog.sb("wd_sb", [128, 22 * 1024], BF16)
    xc = prog.sb("xc", [128, 1024])
    xln = prog.sb("xln", [128, 1024])
    sqs = xln
    xq = prog.sb("xq", [128, 1024], BF16)
    stat = {k: prog.sb("st_" + k, [128, 1]) for k in
            ("sum", "ssq", "mean", "ex2", "m2", "var", "rstd")}
    scr = make_quant_scratch(prog, "f", 128, 1024)
    scrh = scr
    wgc = prog.sb("wgc", [128, 8 * 512], BF16)
    wuc = prog.sb("wuc", [128, 8 * 512], BF16)
    hs = None
    hf = None
    hqT = None
    oc = xln
    ps_tp = prog.ps("ps_tp", [128, 128], BF16)
    ps_g = prog.ps("ps_g", [128, 512])
    ps_u = prog.ps("ps_u", [128, 512])
    ps_dn = prog.ps("ps_dn", [128, 1024])

    hs = xc[:, 0:512]
    hf = xc[:, 512:1024]
    hqT = wgc[:, 0:2816]

    def vini(v):
        prog.op(lambda: v.memset(xc[:, :], 0.0))
        prog.op(lambda: v.memset(xq[:, :], 0.0))
    prog.section("v", vini)

    def ldwd(y):
        prog.dma(wd_sb[:, :].rearrange("p (kc n) -> p kc n", n=1024),
                 wd_d.ap().rearrange("(kc p) n -> p kc n", p=128), sem="dwa")
    prog.section("y", ldwd)
    wd_tok = ("dwa", prog.cnt["dwa"])

    xc_free = ("v", 0)
    xq_free = ("t", 0)
    q_prev = ("g", 0)
    prev_aff2 = ("t", 0)
    for tc in range(TCn):
        p, r0 = chunks[tc]
        ld = [None]

        def yld(y, tc=tc, xf=xc_free, p=p, r0=r0):
            ld[0] = prog.dma(xc[:p, :], x_d.ap()[r0:r0 + p, :],
                             waits=(xf,), sem="dx")
        prog.section("y", yld)

        def ssec(s, tc=tc, pa=prev_aff2, p=p):
            prog.op(lambda: s.activation(out=sqs[:p, :], in_=xc[:p, :], func=AF.Square,
                                         accum_out=stat["ssq"][:p, :]),
                    waits=[ld[0], pa])
        prog.section("s", ssec)
        sq_tok = prog.tok("s")

        def vsec(v, tc=tc):
            prog.op(lambda: v.tensor_reduce(out=stat["sum"][:, :], in_=xc[:, :],
                                            axis=AX, op=ALU.add), waits=[ld[0]])
            prog.op(lambda: v.tensor_scalar(out=stat["mean"][:, :], in0=stat["sum"][:, :],
                                            scalar1=1.0 / D, scalar2=None, op0=ALU.mult))
            prog.op(lambda: v.tensor_scalar(out=stat["ex2"][:, :], in0=stat["ssq"][:, :],
                                            scalar1=1.0 / D, scalar2=None, op0=ALU.mult),
                    waits=[sq_tok])
            prog.op(lambda: v.tensor_tensor(out=stat["m2"][:, :], in0=stat["mean"][:, :],
                                            in1=stat["mean"][:, :], op=ALU.mult))
            prog.op(lambda: v.tensor_tensor(out=stat["var"][:, :], in0=stat["ex2"][:, :],
                                            in1=stat["m2"][:, :], op=ALU.subtract))
            prog.op(lambda: v.tensor_scalar(out=stat["var"][:, :], in0=stat["var"][:, :],
                                            scalar1=1e-5, scalar2=None, op0=ALU.add))
        prog.section("v", vsec)
        var_tok = prog.tok("v")

        def ssec2(s, tc=tc):
            prog.op(lambda: s.activation(out=stat["rstd"][:, :], in_=stat["var"][:, :],
                                         func=AF.Ln), waits=[var_tok])
            prog.op(lambda: s.activation(out=stat["rstd"][:, :], in_=stat["rstd"][:, :],
                                         func=AF.Exp, scale=-0.5))
        prog.section("s", ssec2)
        rstd_tok = prog.tok("s")

        def vsec2(v, tc=tc):
            prog.op(lambda: v.tensor_scalar(out=xln[:, :], in0=xc[:, :],
                                            scalar1=stat["mean"][:, :],
                                            scalar2=stat["rstd"][:, :],
                                            op0=ALU.subtract, op1=ALU.mult),
                    waits=[rstd_tok])
        prog.section("v", vsec2)
        xln_tok = prog.tok("v")
        xc_free = xln_tok

        q_tok, _ = emit_quant(prog, xln[:p, :], xq[:p, :], scr, p, 1024, xln_tok,
                              scr_free_tok=q_prev, out_free_tok=xq_free)
        q_prev = q_tok

        for kc in range(8):
            def tsec(t, kc=kc, tc=tc):
                prog.op(lambda: t.transpose(ps_tp[:, :],
                                            xq[:, kc * 128:(kc + 1) * 128],
                                            ident[:, :]),
                        waits=[q_tok, id_tok, prog.tok("v")])
            prog.section("t", tsec)
            tr_tok = prog.tok("t")

            def vsec3(v, kc=kc, tc=tc, p=p):
                prog.op(lambda: v.tensor_copy(
                    out=xqT[:, (kc * TCn + tc) * 128:(kc * TCn + tc) * 128 + p],
                    in_=ps_tp[:, 0:p]), waits=[tr_tok])
            prog.section("v", vsec3)
        xq_free = prog.tok("t")
        prev_aff2 = prog.tok("t")
    xqT_tok = prog.tok("v")

    w_free = ("t", 0)
    hq_toks = {}
    ps_g_free = ("v", 0)
    ps_u_free = ("v", 0)
    n0 = 0
    for nb, NW in enumerate(NBW):
        wld = [None, None]

        def yw(y, nb=nb, NW=NW, n0=n0, wf=w_free):
            wld[0] = prog.dma(wgc[:, :8 * NW].rearrange("p (kc n) -> p kc n", n=NW),
                              wg_d.ap()[:, n0:n0 + NW].rearrange("(kc p) n -> p kc n", p=128),
                              waits=(wf,), sem="dwb")
            wld[1] = prog.dma(wuc[:, :8 * NW].rearrange("p (kc n) -> p kc n", n=NW),
                              wu_d.ap()[:, n0:n0 + NW].rearrange("(kc p) n -> p kc n", p=128),
                              sem="dwb")
        prog.section("y", yw)

        for tc in range(TCn):
            p, r0 = chunks[tc]
            def tmm(t, tc=tc, NW=NW, pg=ps_g_free, pu=ps_u_free, p=p):
                prog.wait(wld[0]); prog.wait(wld[1])
                prog.wait(xqT_tok)
                prog.wait(pg); prog.wait(pu)
                for kc in range(8):
                    lhs = xqT[:, (kc * TCn + tc) * 128:(kc * TCn + tc) * 128 + p]
                    prog.op(lambda kc=kc, lhs=lhs: t.matmul(ps_g[:p, :NW], lhs,
                                                   wgc[:, kc * NW:(kc + 1) * NW],
                                                   start=(kc == 0), stop=(kc == 7)))
                for kc in range(8):
                    lhs = xqT[:, (kc * TCn + tc) * 128:(kc * TCn + tc) * 128 + p]
                    prog.op(lambda kc=kc, lhs=lhs: t.matmul(ps_u[:p, :NW], lhs,
                                                   wuc[:, kc * NW:(kc + 1) * NW],
                                                   start=(kc == 0), stop=(kc == 7)))
            prog.section("t", tmm)
            mm_tok = prog.tok("t")

            def shsec(s, NW=NW, p=p):
                prog.op(lambda: s.activation(out=hs[:p, :NW], in_=ps_g[:p, :NW],
                                             func=AF.Silu), waits=[mm_tok])
            prog.section("s", shsec)
            hs_tok = prog.tok("s")

            def vh(v, NW=NW, p=p):
                prog.op(lambda: v.tensor_tensor(out=hf[:p, :NW], in0=hs[:p, :NW],
                                                in1=ps_u[:p, :NW], op=ALU.mult),
                        waits=[hs_tok])
            prog.section("v", vh)
            hf_tok = prog.tok("v")
            ps_g_free = hs_tok
            ps_u_free = hf_tok

            hq_out = hq[:p, tc * 2816 + n0: tc * 2816 + n0 + NW]
            q2_tok, _ = emit_quant(prog, hf[:p, :NW], hq_out, scrh, p, NW,
                                   hf_tok, scr_free_tok=hq_toks.get("prev"))
            hq_toks["prev"] = q2_tok
        w_free = prog.tok("t")
        n0 += NW
    hq_done = hq_toks["prev"]

    o_store = [("dso", 0)]
    ps_dn_free = ("v", 0)
    hqT_free = ("t", 0)
    for tc in range(TCn):
        p, r0 = chunks[tc]
        cps = []
        for kc in range(22):
            def tsec2(t, kc=kc, tc=tc):
                prog.op(lambda: t.transpose(
                    ps_tp[:, :], hq[:, tc * 2816 + kc * 128: tc * 2816 + kc * 128 + 128],
                    ident[:, :]), waits=[hq_done, prog.tok("v")])
            prog.section("t", tsec2)
            tr2 = prog.tok("t")

            def vsec4(v, kc=kc):
                prog.op(lambda: v.tensor_copy(out=hqT[:, kc * 128:(kc + 1) * 128],
                                              in_=ps_tp[:, :]), waits=[tr2])
            prog.section("v", vsec4)
        hqT_tok = prog.tok("v")

        def tdn(t, tc=tc, pf=ps_dn_free, p=p):
            prog.wait(hqT_tok)
            prog.wait(wd_tok)
            prog.wait(pf)
            for half in range(2):
                for kc in range(22):
                    prog.op(lambda kc=kc, half=half: t.matmul(
                        ps_dn[:p, half * 512:(half + 1) * 512],
                        hqT[:, kc * 128:kc * 128 + p],
                        wd_sb[:, kc * 1024 + half * 512: kc * 1024 + half * 512 + 512],
                        start=(kc == 0), stop=(kc == 21)))
        prog.section("t", tdn)
        dn_tok = prog.tok("t")

        ld2 = [None]

        def yld2(y, tc=tc, p=p, r0=r0):
            ld2[0] = prog.dma(xc[:p, :], x_d.ap()[r0:r0 + p, :],
                              waits=(prog.tok("v"),), sem="dx")
        prog.section("y", yld2)

        def vev(v, tc=tc, p=p, ost=o_store[0]):
            prog.wait(ost)
            prog.op(lambda: v.scalar_tensor_tensor(out=oc[:p, :], in0=ps_dn[:p, :1024],
                                                   scalar=0.5, in1=xc[:p, :],
                                                   op0=ALU.mult, op1=ALU.add),
                    waits=[dn_tok, ld2[0]])
        prog.section("v", vev)
        ev_tok = prog.tok("v")
        ps_dn_free = ev_tok
        xc_free = ev_tok

        def yst(y, tc=tc, p=p, r0=r0):
            o_store[0] = prog.dma(o_d.ap()[r0:r0 + p, :], oc[:p, :], waits=(ev_tok,),
                                  sem="dso")
        prog.section("y", yst)

    return


def run_ffn(nc, x_full, wgqT, wuqT, wdqT):
    in_maps = []
    for i in range(NCORES):
        in_maps.append({
            "x_s": np.ascontiguousarray(x_full[i * T_CORE:(i + 1) * T_CORE]),
            "wgqT": wgqT, "wuqT": wuqT, "wdqT": wdqT,
        })
    res = run_bass_kernel_spmd(nc, in_maps, core_ids=list(range(NCORES)))
    return np.concatenate([res.results[i]["o_x"] for i in range(NCORES)], axis=0)


HALO = 3
XCOLS = 2176
LT = 512
NT = 4


def emit_mamba_a(prog, io):
    x_d = io["x"]
    msk_d = io["halo_mask"]
    ip_d = io["inproj"]
    dt_d = io["dtw"]
    xp_d = io["xpwB"]
    cw_d = io["convw_r"]
    cb_d = io["convb_r"]
    db_d = io["dtb_r"]
    al_d = io["alog_r"]
    dp_d = io["dparam_r"]
    lg_d = io["lng_r"]
    lb_d = io["lnb_r"]
    y1_d = io["y1"]
    dc_d = io["dcum"]
    xt_d = io["xn2T"]
    he_d = io["hend"]
    pc_d = io["pcore"]
    ms_d = io["mstat"]
    rs_d = io["rstat"]
    bbounce = io["b_bounce"]
    ident = io["_ident"]
    id_tok = io["_id_tok"]

    xn2T = prog.sb("xn2T", [128, 8 * XCOLS], BF16)
    u_f = prog.sb("u_f", [128, 16 * LT])
    u_bf = prog.sb("u_bf", [128, 16 * LT], BF16)
    Brep = prog.sb("Brep", [128, 16 * LT])
    A_sb = prog.sb("A_sb", [128, 256])
    st_sb = prog.sb("st_sb", [128, 256])
    Dst = prog.sb("Dst", [128, 16])
    gam = prog.sb("gam", [128, 1024])
    bet = prog.sb("bet", [128, 1024])
    cw_sb = prog.sb("cw_sb", [128, 16 * 4])
    cb_sb = prog.sb("cb_sb", [128, 16])
    db_sb = prog.sb("db_sb", [128, 16])
    dp_sb = prog.sb("dp_sb", [128, 16])
    msk = prog.sb("msk", [128, HALO])
    xc = prog.sb("xc", [128, 1024])
    xl2 = prog.sb("xl2", [128, 1024])
    xq = prog.sb("xq", [128, 1024], BF16)
    stat = {k: prog.sb("stm_" + k, [128, 1]) for k in
            ("sum", "ssq", "mean", "ex2", "m2", "var", "rstd")}
    stat_m = prog.sb("stat_m", [128, 16])
    stat_r = prog.sb("stat_r", [128, 16])
    uraw = prog.sb("uraw", [128, LT + 8])
    wrot = prog.sb("wrot", [128, 16 * 128], BF16)
    delta = prog.sb("delta", [128, LT])
    dscr = prog.sb("dscr", [128, LT])
    dscr2 = prog.sb("dscr2", [128, LT])
    udc = prog.sb("udc", [128, LT])
    dA_t = prog.sb("dA_t", [128, LT])
    b_t = prog.sb("b_t", [128, LT])
    h_bf = prog.sb("h_bf", [128, LT], BF16)
    dcum_t = prog.sb("dcum_t", [128, LT])
    zeros = prog.sb("zeros", [128, LT])
    y1c = prog.sb("y1c", [128, LT])
    bsb = prog.sb("bsb", [16, LT])

    ps_a = prog.ps("ps_a", [128, 512])
    ps_h = prog.ps("ps_h", [128, 16])
    ps_y = prog.ps("ps_y", [128, 512])
    ps_x = prog.ps("ps_x", [16, 512])
    ps_tp = prog.ps("ps_tp", [128, 128], BF16)

    def pld(y):
        prog.dma(gam[:, :], lg_d.ap(), sem="dm")
        prog.dma(bet[:, :], lb_d.ap(), sem="dm")
        prog.dma(cw_sb[:, :], cw_d.ap(), sem="dm")
        prog.dma(cb_sb[:, :], cb_d.ap(), sem="dm")
        prog.dma(db_sb[:, :], db_d.ap(), sem="dm")
        prog.dma(dp_sb[:, :], dp_d.ap(), sem="dm")
        prog.dma(msk[:, :], msk_d.ap(), sem="dm")
        prog.dma(A_sb[:, :], al_d.ap(), sem="dm")
    prog.section("y", pld)
    pld_tok = ("dm", prog.cnt["dm"])

    def vinit(v):
        prog.op(lambda: v.memset(zeros[:, :], 0.0))
        prog.op(lambda: v.memset(st_sb[:, :], 0.0))
        prog.op(lambda: v.memset(Dst[:, :], 0.0))
        prog.op(lambda: v.memset(xn2T[:, :], 0.0))
    prog.section("v", vinit)

    def sA(s):
        prog.op(lambda: s.activation(out=A_sb[:, :], in_=A_sb[:, :], func=AF.Exp),
                waits=[pld_tok])
        prog.op(lambda: s.activation(out=A_sb[:, :], in_=A_sb[:, :], func=AF.Copy,
                                     scale=-1.0))
    prog.section("s", sA)
    A_tok = prog.tok("s")

    xc_free = ("g", 0)
    q_free = ("t", 0)
    prev_aff = ("g", 0)
    for tc in range(17):
        if tc < 16:
            p = 128
            rows = (HALO + tc * 128, HALO + tc * 128 + 128)
            colbase = 3 + tc * 128
        else:
            p = HALO
            rows = (0, HALO)
            colbase = 0
        ld = [None]

        def yld(y, rows=rows, p=p, xf=xc_free):
            ld[0] = prog.dma(xc[:p, :], x_d.ap()[rows[0]:rows[1], :], waits=(xf,),
                             sem="dx")
        prog.section("y", yld)

        def ssec(s, p=p, pa=prev_aff):
            prog.op(lambda: s.activation(out=xl2[:p, :], in_=xc[:p, :], func=AF.Square,
                                         accum_out=stat["ssq"][:p, :]),
                    waits=[ld[0], pa])
        prog.section("s", ssec)
        sq_tok = prog.tok("s")

        def vsec(v, p=p, tc=tc):
            prog.op(lambda: v.tensor_reduce(out=stat["sum"][:p, :], in_=xc[:p, :],
                                            axis=AX, op=ALU.add), waits=[ld[0]])
            prog.op(lambda: v.tensor_scalar(out=stat["mean"][:p, :], in0=stat["sum"][:p, :],
                                            scalar1=1.0 / D, scalar2=None, op0=ALU.mult))
            prog.op(lambda: v.tensor_scalar(out=stat["ex2"][:p, :], in0=stat["ssq"][:p, :],
                                            scalar1=1.0 / D, scalar2=None, op0=ALU.mult),
                    waits=[sq_tok])
            prog.op(lambda: v.tensor_tensor(out=stat["m2"][:p, :], in0=stat["mean"][:p, :],
                                            in1=stat["mean"][:p, :], op=ALU.mult))
            prog.op(lambda: v.tensor_tensor(out=stat["var"][:p, :], in0=stat["ex2"][:p, :],
                                            in1=stat["m2"][:p, :], op=ALU.subtract))
            prog.op(lambda: v.tensor_scalar(out=stat["var"][:p, :], in0=stat["var"][:p, :],
                                            scalar1=1e-5, scalar2=None, op0=ALU.add))
        prog.section("v", vsec)
        var_tok = prog.tok("v")

        def ssec2(s, p=p):
            prog.op(lambda: s.activation(out=stat["rstd"][:p, :], in_=stat["var"][:p, :],
                                         func=AF.Ln), waits=[var_tok])
            prog.op(lambda: s.activation(out=stat["rstd"][:p, :], in_=stat["rstd"][:p, :],
                                         func=AF.Exp, scale=-0.5))
        prog.section("s", ssec2)
        rstd_tok = prog.tok("s")

        def vsec2(v, p=p, tc=tc):
            prog.op(lambda: v.tensor_scalar(out=xl2[:p, :], in0=xc[:p, :],
                                            scalar1=stat["mean"][:p, :],
                                            scalar2=stat["rstd"][:p, :],
                                            op0=ALU.subtract, op1=ALU.mult),
                    waits=[rstd_tok])
            if tc < 16:
                prog.op(lambda: v.tensor_copy(out=stat_m[:, tc:tc + 1],
                                              in_=stat["mean"][:, :]))
                prog.op(lambda: v.tensor_copy(out=stat_r[:, tc:tc + 1],
                                              in_=stat["rstd"][:, :]))
        prog.section("v", vsec2)
        xl_tok = prog.tok("v")
        xc_free = xl_tok

        def ssec3(s, p=p, qf=q_free):
            prog.op(lambda: s.activation(out=xq[:p, :], in_=xl2[:p, :], func=AF.Square,
                                         accum_out=stat["ssq"][:p, :]),
                    waits=[xl_tok, qf])
        prog.section("s", ssec3)
        sq2_tok = prog.tok("s")

        def vsec2b(v, p=p):
            prog.op(lambda: v.tensor_reduce(out=stat["sum"][:p, :], in_=xl2[:p, :],
                                            axis=AX, op=ALU.add), waits=[xl_tok])
            prog.op(lambda: v.tensor_scalar(out=stat["mean"][:p, :], in0=stat["sum"][:p, :],
                                            scalar1=1.0 / D, scalar2=None, op0=ALU.mult))
            prog.op(lambda: v.tensor_scalar(out=stat["ex2"][:p, :], in0=stat["ssq"][:p, :],
                                            scalar1=1.0 / D, scalar2=None, op0=ALU.mult),
                    waits=[sq2_tok])
            prog.op(lambda: v.tensor_tensor(out=stat["m2"][:p, :], in0=stat["mean"][:p, :],
                                            in1=stat["mean"][:p, :], op=ALU.mult))
            prog.op(lambda: v.tensor_tensor(out=stat["var"][:p, :], in0=stat["ex2"][:p, :],
                                            in1=stat["m2"][:p, :], op=ALU.subtract))
            prog.op(lambda: v.tensor_scalar(out=stat["var"][:p, :], in0=stat["var"][:p, :],
                                            scalar1=1e-5, scalar2=None, op0=ALU.add))
        prog.section("v", vsec2b)
        var2_tok = prog.tok("v")

        def ssec4(s, p=p):
            prog.op(lambda: s.activation(out=stat["rstd"][:p, :], in_=stat["var"][:p, :],
                                         func=AF.Ln), waits=[var2_tok])
            prog.op(lambda: s.activation(out=stat["rstd"][:p, :], in_=stat["rstd"][:p, :],
                                         func=AF.Exp, scale=-0.5))
        prog.section("s", ssec4)
        rstd2_tok = prog.tok("s")

        def vsec2c(v, p=p):
            prog.op(lambda: v.tensor_scalar(out=xl2[:p, :], in0=xl2[:p, :],
                                            scalar1=stat["mean"][:p, :],
                                            scalar2=stat["rstd"][:p, :],
                                            op0=ALU.subtract, op1=ALU.mult),
                    waits=[rstd2_tok])
        prog.section("v", vsec2c)

        def gsec(g, p=p):
            prog.op(lambda: g.tensor_tensor(out=xl2[:p, :], in0=xl2[:p, :],
                                            in1=gam[:p, :], op=ALU.mult),
                    waits=[prog.tok("v"), pld_tok])
            prog.op(lambda: g.tensor_tensor(out=xq[:p, :], in0=xl2[:p, :],
                                            in1=bet[:p, :], op=ALU.add))
        prog.section("g", gsec)
        aff_tok = prog.tok("g")
        prev_aff = aff_tok

        for kc in range(8):
            def tsec(t, kc=kc, p=p):
                prog.op(lambda: t.transpose(ps_tp[:, :],
                                            xq[:, kc * 128:(kc + 1) * 128],
                                            ident[:, :]),
                        waits=[aff_tok, id_tok, prog.tok("v")])
            prog.section("t", tsec)
            tr_tok = prog.tok("t")

            def vsec3(v, kc=kc, colbase=colbase, p=p):
                prog.op(lambda: v.tensor_copy(
                    out=xn2T[:, kc * XCOLS + colbase: kc * XCOLS + colbase + p],
                    in_=ps_tp[:, 0:p]), waits=[tr_tok])
            prog.section("v", vsec3)
        q_free = prog.tok("t")
    xn2T_tok = prog.tok("v")

    def ydump(y):
        prog.dma(xt_d.ap(), xn2T[:, :], waits=(xn2T_tok,), sem="dso")
        prog.dma(ms_d.ap(), stat_m[:, :], sem="dso")
        prog.dma(rs_d.ap(), stat_r[:, :], sem="dso")
    prog.section("y", ydump)

    bb_store = [0]
    w_free = ("t", 0)
    uraw_free = ("s", 0)
    brep_free = ("v", 0)
    for tt in range(NT):
        for nc_ in range(16):
            wld = [None]

            def yw(y, nc_=nc_, wf=w_free):
                wld[0] = prog.dma(
                    wrot[:, 0:8 * 128].rearrange("p (kc n) -> p kc n", n=128),
                    ip_d.ap()[:, nc_ * 128:(nc_ + 1) * 128]
                        .rearrange("(kc p) n -> p kc n", p=128),
                    waits=(wf,), sem="dwa")
            prog.section("y", yw)

            def tmm(t, nc_=nc_, tt=tt, uf=uraw_free):
                prog.wait(wld[0])
                prog.wait(xn2T_tok)
                prog.wait(uf)
                for kc in range(8):
                    prog.op(lambda kc=kc: t.matmul(
                        ps_a[:, :512],
                        wrot[:, kc * 128:(kc + 1) * 128],
                        xn2T[:, kc * XCOLS + tt * LT: kc * XCOLS + tt * LT + 512],
                        start=(kc == 0), stop=(kc == 7)))
                for kc in range(8):
                    prog.op(lambda kc=kc: t.matmul(
                        ps_h[:, 0:HALO],
                        wrot[:, kc * 128:(kc + 1) * 128],
                        xn2T[:, kc * XCOLS + tt * LT + 512: kc * XCOLS + tt * LT + 512 + HALO],
                        start=(kc == 0), stop=(kc == 7)))
            prog.section("t", tmm)
            mm_tok = prog.tok("t")
            w_free = mm_tok

            def vev(v, nc_=nc_, tt=tt):
                prog.op(lambda: v.tensor_copy(out=uraw[:, 0:512], in_=ps_a[:, :512]),
                        waits=[mm_tok])
                prog.op(lambda: v.tensor_copy(out=uraw[:, 512:512 + HALO],
                                              in_=ps_h[:, 0:HALO]))
                if tt == 0:
                    prog.op(lambda: v.tensor_tensor(out=uraw[:, 0:HALO],
                                                    in0=uraw[:, 0:HALO],
                                                    in1=msk[:, :], op=ALU.mult))
            prog.section("v", vev)
            ur_tok = prog.tok("v")

            def vconv(v, nc_=nc_):
                prog.op(lambda: v.tensor_scalar(out=udc[:, 0:LT], in0=uraw[:, 0:LT],
                                                scalar1=cw_sb[:, nc_ * 4:nc_ * 4 + 1],
                                                scalar2=None, op0=ALU.mult),
                        waits=[ur_tok, pld_tok])
                for k in range(1, 4):
                    prog.op(lambda k=k: v.scalar_tensor_tensor(
                        out=udc[:, 0:LT], in0=uraw[:, k:k + LT],
                        scalar=cw_sb[:, nc_ * 4 + k:nc_ * 4 + k + 1],
                        in1=udc[:, 0:LT], op0=ALU.mult, op1=ALU.add))
            prog.section("v", vconv)
            cv_tok = prog.tok("v")

            def ssil(s, nc_=nc_):
                prog.op(lambda: s.activation(out=u_f[:, nc_ * LT:(nc_ + 1) * LT],
                                             in_=udc[:, 0:LT], func=AF.Silu,
                                             bias=cb_sb[:, nc_:nc_ + 1]),
                        waits=[cv_tok])
            prog.section("s", ssil)
            sil_tok = prog.tok("s")
            uraw_free = sil_tok

            def vcast(v, nc_=nc_):
                prog.op(lambda: v.tensor_copy(out=u_bf[:, nc_ * LT:(nc_ + 1) * LT],
                                              in_=u_f[:, nc_ * LT:(nc_ + 1) * LT]),
                        waits=[sil_tok])
            prog.section("v", vcast)
        ubf_tok = prog.tok("v")

        xpld = [None]

        def yxp(y, wf=w_free):
            xpld[0] = prog.dma(wrot[:, 0:16 * 16].rearrange("p (kc n) -> p kc n", n=16),
                               xp_d.ap().rearrange("(kc p) n -> p kc n", p=128),
                               waits=(wf,), sem="dwa")
        prog.section("y", yxp)

        def txp(t, tt=tt):
            prog.wait(xpld[0])
            prog.wait(ubf_tok)
            for kc in range(16):
                prog.op(lambda kc=kc: t.matmul(
                    ps_x[:16, :512], wrot[:, kc * 16:(kc + 1) * 16],
                    u_bf[:, kc * LT:(kc + 1) * LT],
                    start=(kc == 0), stop=(kc == 15)))
        prog.section("t", txp)
        xp_tok = prog.tok("t")
        w_free = xp_tok

        def vxp(v, bf_=brep_free, bt=None):
            prog.op(lambda: v.tensor_copy(out=bsb[:16, :], in_=ps_x[:16, :512]),
                    waits=[xp_tok, bf_, ("dwb", bb_store[0])])
        prog.section("v", vxp)
        bsb_tok = prog.tok("v")

        def ybb(y, tt=tt):
            tok = prog.dma(bbounce.ap(), bsb[:16, :], waits=(bsb_tok,), sem="dwb")
            bb_store[0] = tok[1]
            for n in range(16):
                prog.dma(Brep[:, n * LT:(n + 1) * LT],
                         bbounce.ap()[n:n + 1, :].broadcast_to([128, LT]),
                         waits=(tok,), sem="dwb")
        prog.section("y", ybb)
        brep_tok = ("dwb", prog.cnt["dwb"])

        dc_store = [("dso", 0)]
        y1_store = [("dso", 0)]
        for cc in range(16):
            dtl = [None]

            def ydt(y, cc=cc, wf=w_free):
                dtl[0] = prog.dma(
                    wrot[:, :].rearrange("p (kc n) -> p kc n", n=128),
                    dt_d.ap()[:, cc * 128:(cc + 1) * 128]
                        .rearrange("(kc p) n -> p kc n", p=128),
                    waits=(wf,), sem="dwa")
            prog.section("y", ydt)

            def tdt(t, cc=cc):
                prog.wait(dtl[0])
                prog.wait(ubf_tok)
                prog.wait(prog.tok("v"))
                for kc in range(16):
                    prog.op(lambda kc=kc: t.matmul(
                        ps_a[:, :512], wrot[:, kc * 128:(kc + 1) * 128],
                        u_bf[:, kc * LT:(kc + 1) * LT],
                        start=(kc == 0), stop=(kc == 15)))
            prog.section("t", tdt)
            dt_tok = prog.tok("t")
            w_free = dt_tok

            def vsp(v, cc=cc):
                prog.op(lambda: v.tensor_scalar(out=dscr[:, :], in0=ps_a[:, :512],
                                                scalar1=db_sb[:, cc:cc + 1],
                                                scalar2=None, op0=ALU.add),
                        waits=[dt_tok])
                prog.op(lambda: v.tensor_scalar(out=dscr2[:, :].bitcast(I32),
                                                in0=dscr[:, :].bitcast(I32),
                                                scalar1=0x7FFFFFFF, scalar2=None,
                                                op0=ALU.bitwise_and))
            prog.section("v", vsp)
            sp1 = prog.tok("v")

            def ssp(s, cc=cc):
                prog.op(lambda: s.activation(out=dscr2[:, :], in_=dscr2[:, :],
                                             func=AF.Exp, scale=-1.0), waits=[sp1])
                prog.op(lambda: s.activation(out=dscr2[:, :], in_=dscr2[:, :],
                                             func=AF.Ln, bias=1.0))
            prog.section("s", ssp)
            sp2 = prog.tok("s")

            def vsp2(v, cc=cc, dct=dc_store[0]):
                prog.wait(dct)
                prog.op(lambda: v.tensor_scalar(out=delta[:, :], in0=dscr[:, :],
                                                scalar1=0.0, scalar2=None,
                                                op0=ALU.max), waits=[sp2])
                prog.op(lambda: v.tensor_tensor(out=delta[:, :], in0=delta[:, :],
                                                in1=dscr2[:, :], op=ALU.add))
                prog.op(lambda: v.tensor_tensor(out=udc[:, 0:LT],
                                                in0=u_f[:, cc * LT:(cc + 1) * LT],
                                                in1=delta[:, :], op=ALU.mult))
                prog.op(lambda: v.tensor_tensor_scan(out=dcum_t[:, :], data0=delta[:, :],
                                                     data1=zeros[:, :],
                                                     initial=Dst[:, cc:cc + 1],
                                                     op0=ALU.add, op1=ALU.add))
                prog.op(lambda: v.tensor_copy(out=Dst[:, cc:cc + 1],
                                              in_=dcum_t[:, LT - 1:LT]))
            prog.section("v", vsp2)
            dsc_tok = prog.tok("v")

            def ydc(y, cc=cc, tt=tt):
                dc_store[0] = prog.dma(
                    dc_d.ap()[cc * 128:(cc + 1) * 128, tt * LT:(tt + 1) * LT],
                    dcum_t[:, :], waits=(dsc_tok,), sem="dso")
            prog.section("y", ydc)

            for n in range(16):
                def sdA(s, cc=cc, n=n):
                    prog.op(lambda: s.activation(out=dA_t[:, :], in_=delta[:, :],
                                                 func=AF.Exp,
                                                 scale=A_sb[:, cc * 16 + n:cc * 16 + n + 1]),
                            waits=[dsc_tok, A_tok, prog.tok("v")])
                prog.section("s", sdA)
                dA_tok = prog.tok("s")

                def gb(g, n=n):
                    prog.op(lambda: g.tensor_tensor(out=b_t[:, :], in0=udc[:, 0:LT],
                                                    in1=Brep[:, n * LT:(n + 1) * LT],
                                                    op=ALU.mult),
                            waits=[dsc_tok, brep_tok, prog.tok("v")])
                prog.section("g", gb)
                b_tok = prog.tok("g")

                def vscan(v, cc=cc, n=n):
                    prog.op(lambda: v.tensor_tensor_scan(out=h_bf[:, :], data0=dA_t[:, :],
                                                         data1=b_t[:, :],
                                                         initial=st_sb[:, cc * 16 + n:cc * 16 + n + 1],
                                                         op0=ALU.mult, op1=ALU.add),
                            waits=[dA_tok, b_tok, prog.tok("t")])
                    prog.op(lambda: v.tensor_copy(out=st_sb[:, cc * 16 + n:cc * 16 + n + 1],
                                                  in_=h_bf[:, LT - 1:LT]))
                prog.section("v", vscan)
                h_tok = prog.tok("v")

                def ty(t, n=n):
                    prog.wait(h_tok)
                    prog.op(lambda: t.matmul(ps_y[:, :512], ident[:, :], h_bf[:, :],
                                             start=(n == 0), stop=(n == 15)))
                prog.section("t", ty)
            ysum_tok = prog.tok("t")

            def vy(v, cc=cc, tt=tt, yst=y1_store[0]):
                prog.wait(yst)
                prog.op(lambda: v.scalar_tensor_tensor(
                    out=y1c[:, :], in0=u_f[:, cc * LT:(cc + 1) * LT],
                    scalar=dp_sb[:, cc:cc + 1], in1=ps_y[:, :512],
                    op0=ALU.mult, op1=ALU.add), waits=[ysum_tok])
            prog.section("v", vy)
            y1_tok = prog.tok("v")

            def yy1(y, cc=cc, tt=tt):
                y1_store[0] = prog.dma(
                    y1_d.ap()[cc * 128:(cc + 1) * 128, tt * LT:(tt + 1) * LT],
                    y1c[:, :], waits=(y1_tok,), sem="dso")
            prog.section("y", yy1)
        brep_free = prog.tok("g")

    def sfin(s):
        for cc in range(16):
            prog.op(lambda cc=cc: s.activation(out=dA_t[:, cc * 16:cc * 16 + 16],
                                               in_=A_sb[:, cc * 16:cc * 16 + 16],
                                               func=AF.Exp,
                                               scale=Dst[:, cc:cc + 1]),
                    waits=[prog.tok("v"), A_tok])
    prog.section("s", sfin)
    p_tok = prog.tok("s")

    def yfin(y):
        prog.dma(he_d.ap(), st_sb[:, :], waits=(prog.tok("v"),), sem="dso")
        prog.dma(pc_d.ap(), dA_t[:, 0:256], waits=(p_tok,), sem="dso")
    prog.section("y", yfin)
    return


def _mamba_a_inmaps(x_full, casts, inp):
    c2 = lambda a: np.ascontiguousarray(a)
    ip_b = c2(casts["L0_attn_in_proj_w"])
    dt_b = c2(casts["L0_attn_dt_proj_w"])
    xp_b = c2(np.asarray(casts["L0_attn_x_proj_w"])[:, 16:32])
    cw = inp["L0_attn_conv_w"][:, 0, :].astype(np.float32)
    cw_r = c2(cw.reshape(16, 128, 4).transpose(1, 0, 2).reshape(128, 64))
    cb_r = c2(inp["L0_attn_conv_b"].astype(np.float32).reshape(16, 128).T)
    db_r = c2(inp["L0_attn_dt_proj_b"].astype(np.float32).reshape(16, 128).T)
    al_r = c2(inp["L0_attn_A_log"].astype(np.float32)
              .reshape(16, 128, 16).transpose(1, 0, 2).reshape(128, 256))
    dp_r = c2(inp["L0_attn_D_param"].astype(np.float32).reshape(16, 128).T)
    lg_r = c2(np.broadcast_to(inp["L0_attn_ln_gamma"].astype(np.float32), (128, 1024)))
    lb_r = c2(np.broadcast_to(inp["L0_attn_ln_beta"].astype(np.float32), (128, 1024)))
    maps = []
    for i in range(NCORES):
        lo = i * T_CORE
        halo = (np.zeros((HALO, D), np.float32) if i == 0
                else x_full[lo - HALO:lo])
        xs = np.ascontiguousarray(np.concatenate([halo, x_full[lo:lo + T_CORE]], 0))
        mask = np.zeros((128, HALO), np.float32) if i == 0 else np.ones((128, HALO), np.float32)
        maps.append({"x_s": xs, "halo_mask": mask, "inproj": ip_b, "dtw": dt_b,
                     "xpwB": xp_b, "convw_r": cw_r, "convb_r": cb_r, "dtb_r": db_r,
                     "alog_r": al_r, "dparam_r": dp_r, "lng_r": lg_r, "lnb_r": lb_r})
    return maps


def emit_mamba_b(prog, io):
    x_d = io["x"]
    y1_d = io["y1"]
    dc_d = io["dcum"]
    xt_d = io["xn2T"]
    ms_d = io["mstat"]
    rs_d = io["rstat"]
    sp_d = io["slotsP"]
    sh_d = io["slotsH"]
    zw_d = io["inprojZ"]
    ow_d = io["outw"]
    al_d = io["alog_r"]
    o_d = io["o"]
    xn2T = prog.sb("xn2T_sb", [128, 8 * XCOLS], BF16)
    outw = prog.sb("outw_sb", [128, 16 * 1024], BF16)
    ygb = prog.sb("ygb", [128, 16 * 512], BF16)
    A_sb = prog.sb("A_sb", [128, 256])
    h0 = prog.sb("h0", [128, 256])
    slP = prog.sb("slP", [128, 7 * 256])
    slH = prog.sb("slH", [128, 7 * 256])
    stat_m = prog.sb("stat_m", [128, 16])
    stat_r = prog.sb("stat_r", [128, 16])
    dcum_c = prog.sb("dcum_c", [128, 512])
    e_t = prog.sb("e_t", [128, 512])
    acc = prog.sb("acc", [128, 512])
    y1c = prog.sb("y1c", [128, 512])
    sz_t = prog.sb("sz_t", [128, 512])
    zrot = prog.sb("zrot", [128, 8 * 128], BF16)
    x1c = prog.sb("x1c", [128, 1024])
    xnp = prog.sb("xnp", [128, 1024])
    oc = prog.sb("oc", [128, 1024])
    ps_z = prog.ps("ps_z", [128, 512])
    ps_o = prog.ps("ps_o", [128, 1024])

    def pld(y):
        prog.dma(xn2T[:, :], xt_d.ap(), sem="dm")
        prog.dma(outw[:, :].rearrange("p (c n) -> p c n", n=1024),
                 ow_d.ap().rearrange("(c p) n -> p c n", p=128), sem="dm")
        prog.dma(stat_m[:, :], ms_d.ap(), sem="dm")
        prog.dma(stat_r[:, :], rs_d.ap(), sem="dm")
        prog.dma(A_sb[:, :], al_d.ap(), sem="dm")
        prog.dma(slP[:, :].rearrange("p (j c) -> p j c", c=256), sp_d.ap().transpose([1, 0, 2]), sem="dm")
        prog.dma(slH[:, :].rearrange("p (j c) -> p j c", c=256), sh_d.ap().transpose([1, 0, 2]), sem="dm")
    prog.section("y", pld)
    pld_tok = ("dm", prog.cnt["dm"])

    def sA(s):
        prog.op(lambda: s.activation(out=A_sb[:, :], in_=A_sb[:, :], func=AF.Exp),
                waits=[pld_tok])
        prog.op(lambda: s.activation(out=A_sb[:, :], in_=A_sb[:, :], func=AF.Copy,
                                     scale=-1.0))
    prog.section("s", sA)
    A_tok = prog.tok("s")

    def vh0(v):
        prog.op(lambda: v.memset(h0[:, :], 0.0), waits=[pld_tok])
        for j in range(7):
            prog.op(lambda j=j: v.tensor_tensor(out=h0[:, :], in0=h0[:, :],
                                                in1=slP[:, j * 256:(j + 1) * 256],
                                                op=ALU.mult))
            prog.op(lambda j=j: v.tensor_tensor(out=h0[:, :], in0=h0[:, :],
                                                in1=slH[:, j * 256:(j + 1) * 256],
                                                op=ALU.add))
    prog.section("v", vh0)
    h0_tok = prog.tok("v")

    o_store = [("dso", 0)]
    zw_free = ("t", 0)
    for tt in range(NT):
        for cc in range(16):
            dld = [None]

            def ydc(y, cc=cc, tt=tt):
                dld[0] = prog.dma(dcum_c[:, :],
                                  dc_d.ap()[cc * 128:(cc + 1) * 128, tt * LT:(tt + 1) * LT],
                                  waits=(prog.tok("s"),), sem="dx")
                dld[0] = (dld[0][0], dld[0][1])
                dld.append(prog.dma(y1c[:, :],
                                    y1_d.ap()[cc * 128:(cc + 1) * 128, tt * LT:(tt + 1) * LT],
                                    waits=(prog.tok("v"),), sem="dx"))
            prog.section("y", ydc)

            for n in range(16):
                def sE(s, cc=cc, n=n):
                    prog.op(lambda: s.activation(out=e_t[:, :], in_=dcum_c[:, :],
                                                 func=AF.Exp,
                                                 scale=A_sb[:, cc * 16 + n:cc * 16 + n + 1]),
                            waits=[dld[0], A_tok, prog.tok("v")])
                prog.section("s", sE)
                e_tok = prog.tok("s")

                def vacc(v, cc=cc, n=n):
                    col = h0[:, cc * 16 + n:cc * 16 + n + 1]
                    if n == 0:
                        prog.op(lambda: v.tensor_scalar(out=acc[:, :], in0=e_t[:, :],
                                                        scalar1=col, scalar2=None,
                                                        op0=ALU.mult),
                                waits=[e_tok, h0_tok])
                    else:
                        prog.op(lambda: v.scalar_tensor_tensor(out=acc[:, :], in0=e_t[:, :],
                                                               scalar=col, in1=acc[:, :],
                                                               op0=ALU.mult, op1=ALU.add),
                                waits=[e_tok])
                prog.section("v", vacc)
            corr_tok = prog.tok("v")

            zld = [None]

            def yz(y, cc=cc, zf=zw_free):
                zld[0] = prog.dma(zrot[:, :].rearrange("p (kc n) -> p kc n", n=128),
                                  zw_d.ap()[:, cc * 128:(cc + 1) * 128]
                                      .rearrange("(kc p) n -> p kc n", p=128),
                                  waits=(zf,), sem="dwa")
            prog.section("y", yz)

            def tz(t, cc=cc, tt=tt):
                prog.wait(zld[0])
                prog.wait(pld_tok)
                prog.wait(prog.tok("s"))
                for kc in range(8):
                    prog.op(lambda kc=kc: t.matmul(
                        ps_z[:, :512], zrot[:, kc * 128:(kc + 1) * 128],
                        xn2T[:, kc * XCOLS + tt * LT + HALO: kc * XCOLS + tt * LT + HALO + 512],
                        start=(kc == 0), stop=(kc == 7)))
            prog.section("t", tz)
            z_tok = prog.tok("t")
            zw_free = z_tok

            def ssz(s, cc=cc):
                prog.op(lambda: s.activation(out=sz_t[:, :], in_=ps_z[:, :512],
                                             func=AF.Silu), waits=[z_tok, prog.tok("v")])
            prog.section("s", ssz)
            sz_tok = prog.tok("s")

            def vyg(v, cc=cc):
                prog.op(lambda: v.tensor_tensor(out=y1c[:, :], in0=y1c[:, :],
                                                in1=acc[:, :], op=ALU.add),
                        waits=[corr_tok, dld[1], sz_tok])
                prog.op(lambda: v.tensor_tensor(out=ygb[:, cc * 512:(cc + 1) * 512],
                                                in0=y1c[:, :], in1=sz_t[:, :],
                                                op=ALU.mult))
            prog.section("v", vyg)
        ygb_tok = prog.tok("v")

        for tq in range(4):
            def to(t, tt=tt, tq=tq):
                prog.wait(ygb_tok)
                prog.wait(prog.tok("v"))
                for half in range(2):
                    for cc in range(16):
                        prog.op(lambda cc=cc, half=half: t.matmul(
                            ps_o[:, half * 512:(half + 1) * 512],
                            ygb[:, cc * 512 + tq * 128: cc * 512 + tq * 128 + 128],
                            outw[:, cc * 1024 + half * 512: cc * 1024 + half * 512 + 512],
                            start=(cc == 0), stop=(cc == 15)))
            prog.section("t", to)
            o_tok = prog.tok("t")

            xld = [None]

            def yx(y, tt=tt, tq=tq):
                xld[0] = prog.dma(x1c[:, :],
                                  x_d.ap()[tt * 512 + tq * 128: tt * 512 + tq * 128 + 128, :],
                                  waits=(prog.tok("v"),), sem="dx")
            prog.section("y", yx)

            def vev(v, tt=tt, tq=tq, ost=o_store[0]):
                tc = tt * 4 + tq
                prog.wait(ost)
                prog.op(lambda: v.tensor_scalar(out=xnp[:, :], in0=x1c[:, :],
                                                scalar1=stat_m[:, tc:tc + 1],
                                                scalar2=stat_r[:, tc:tc + 1],
                                                op0=ALU.subtract, op1=ALU.mult),
                        waits=[xld[0], pld_tok])
                prog.op(lambda: v.tensor_tensor(out=xnp[:, :], in0=xnp[:, :],
                                                in1=x1c[:, :], op=ALU.add))
                prog.op(lambda: v.tensor_tensor(out=oc[:, :], in0=xnp[:, :],
                                                in1=ps_o[:, :1024], op=ALU.add),
                        waits=[o_tok])
            prog.section("v", vev)
            ev_tok = prog.tok("v")

            def yo(y, tt=tt, tq=tq):
                o_store[0] = prog.dma(
                    o_d.ap()[tt * 512 + tq * 128: tt * 512 + tq * 128 + 128, :],
                    oc[:, :], waits=(ev_tok,), sem="dso")
            prog.section("y", yo)

    return


def _mamba_b_inmaps(x1_full, resA, casts, inp):
    c2 = lambda a: np.ascontiguousarray(a)
    zw = c2(np.asarray(casts["L0_attn_in_proj_w"])[:, 2048:])
    ow = c2(casts["L0_attn_out_proj_w"])
    al_r = c2(inp["L0_attn_A_log"].astype(np.float32)
              .reshape(16, 128, 16).transpose(1, 0, 2).reshape(128, 256))
    ones = np.ones((128, 256), np.float32)
    zeross = np.zeros((128, 256), np.float32)
    maps = []
    for i in range(NCORES):
        sp = []
        sh = []
        for j in range(7):
            src = i - 7 + j
            if src < 0:
                sp.append(ones); sh.append(zeross)
            else:
                sp.append(np.asarray(resA[src]["o_pcore"], np.float32))
                sh.append(np.asarray(resA[src]["o_hend"], np.float32))
        maps.append({
            "x1_s": c2(x1_full[i * T_CORE:(i + 1) * T_CORE]),
            "y1": np.asarray(resA[i]["o_y1"], np.float32),
            "dcum": np.asarray(resA[i]["o_dcum"], np.float32),
            "xn2T": np.asarray(resA[i]["o_xn2T"]),
            "mstat": np.asarray(resA[i]["o_mstat"], np.float32),
            "rstat": np.asarray(resA[i]["o_rstat"], np.float32),
            "slotsP": c2(np.stack(sp)), "slotsH": c2(np.stack(sh)),
            "inprojZ": zw, "outw": ow, "alog_r": al_r,
        })
    return maps


def run_mamba(x1_full, casts, inp):
    ncA = build_mamba_a()
    mapsA = _mamba_a_inmaps(x1_full, casts, inp)
    resA = run_bass_kernel_spmd(ncA, mapsA, core_ids=list(range(NCORES))).results
    ncB = build_mamba_b()
    mapsB = _mamba_b_inmaps(x1_full, resA, casts, inp)
    resB = run_bass_kernel_spmd(ncB, mapsB, core_ids=list(range(NCORES))).results
    return np.concatenate([resB[i]["o_x"] for i in range(NCORES)], axis=0)


def emit_moe(prog, io):
    TCn = 16
    x_d = io["x"]
    wg_d = io["wgate"]
    wd_d = io["wdview"]
    wu_d = io["wuview"]
    bu_d = io["buw"]
    bd_d = io["bdrep"]
    o_d = io["o"]
    ident = io["_ident"]
    id_tok = io["_id_tok"]

    xlT = prog.sb("xlT", [128, 8 * TCn * 128], BF16)
    wg_sb = prog.sb("wg_sb", [128, 8 * 8], BF16)
    wd_sb = prog.sb("wd_sb", [128, 8 * 512], BF16)
    wu_sb = prog.sb("wu_sb", [128, 4 * 1024], BF16)
    bu_sb = prog.sb("bu_sb", [8, 1024], BF16)
    bd_sb = prog.sb("bd_sb", [128, 512])
    xc = prog.sb("xc", [128, 1024])
    xln = prog.sb("xln", [128, 1024])
    xq = prog.sb("xq", [128, 1024], BF16)
    stat = {k: prog.sb("stq_" + k, [128, 1]) for k in
            ("sum", "ssq", "mean", "ex2", "m2", "var", "rstd")}
    l_sb = prog.sb("l_sb", [128, 8])
    g_sb = prog.sb("g_sb", [128, 8])
    gb_sb = prog.sb("gb_sb", [128, 8], BF16)
    gT_sb = prog.sb("gT_sb", [8, 128], BF16)
    m1 = prog.sb("m1", [128, 1])
    m2 = prog.sb("m2", [128, 1])
    nm1 = prog.sb("nm1", [128, 1])
    dd = prog.sb("dd", [128, 1])
    msk2 = prog.sb("msk2", [128, 8])
    hgb = prog.sb("hgb", [128, 512], BF16)
    hgT = prog.sb("hgT", [128, 4 * 128], BF16)
    hsc = prog.sb("hsc", [128, 512])
    oc = prog.sb("oc", [128, 1024])
    ps_l = prog.ps("ps_l", [128, 8])
    ps_hd = prog.ps("ps_hd", [128, 512])
    ps_o = prog.ps("ps_o", [128, 1024])
    ps_tp = prog.ps("ps_tp", [128, 128], BF16)
    ps_gt = prog.ps("ps_gt", [8, 128], BF16)

    def pld(y):
        prog.dma(wg_sb[:, :].rearrange("p (kc n) -> p kc n", n=8),
                 wg_d.ap().rearrange("(kc p) n -> p kc n", p=128), sem="dwa")
        prog.dma(wd_sb[:, :].rearrange("p (kc n) -> p kc n", n=512),
                 wd_d.ap().rearrange("(kc p) n -> p kc n", p=128), sem="dwa")
        prog.dma(wu_sb[:, :].rearrange("p (kc n) -> p kc n", n=1024),
                 wu_d.ap().rearrange("(kc p) n -> p kc n", p=128), sem="dwa")
        prog.dma(bu_sb[:, :], bu_d.ap(), sem="dwa")
        prog.dma(bd_sb[:, :], bd_d.ap(), sem="dwa")
    prog.section("y", pld)
    pld_tok = ("dwa", prog.cnt["dwa"])

    xc_free = ("v", 0)
    prev_aff = ("t", 0)
    for tc in range(TCn):
        ld = [None]

        def yld(y, tc=tc, xf=xc_free):
            ld[0] = prog.dma(xc[:, :], x_d.ap()[tc * 128:(tc + 1) * 128, :],
                             waits=(xf,), sem="dx")
        prog.section("y", yld)

        def ssec(s, pa=prev_aff):
            prog.op(lambda: s.activation(out=xln[:, :], in_=xc[:, :], func=AF.Square,
                                         accum_out=stat["ssq"][:, :]),
                    waits=[ld[0], pa])
        prog.section("s", ssec)
        sq_tok = prog.tok("s")

        def vsec(v, tc=tc):
            prog.op(lambda: v.tensor_reduce(out=stat["sum"][:, :], in_=xc[:, :],
                                            axis=AX, op=ALU.add), waits=[ld[0]])
            prog.op(lambda: v.tensor_scalar(out=stat["mean"][:, :], in0=stat["sum"][:, :],
                                            scalar1=1.0 / D, scalar2=None, op0=ALU.mult))
            prog.op(lambda: v.tensor_scalar(out=stat["ex2"][:, :], in0=stat["ssq"][:, :],
                                            scalar1=1.0 / D, scalar2=None, op0=ALU.mult),
                    waits=[sq_tok])
            prog.op(lambda: v.tensor_tensor(out=stat["m2"][:, :], in0=stat["mean"][:, :],
                                            in1=stat["mean"][:, :], op=ALU.mult))
            prog.op(lambda: v.tensor_tensor(out=stat["var"][:, :], in0=stat["ex2"][:, :],
                                            in1=stat["m2"][:, :], op=ALU.subtract))
            prog.op(lambda: v.tensor_scalar(out=stat["var"][:, :], in0=stat["var"][:, :],
                                            scalar1=1e-5, scalar2=None, op0=ALU.add))
        prog.section("v", vsec)
        var_tok = prog.tok("v")

        def ssec2(s):
            prog.op(lambda: s.activation(out=stat["rstd"][:, :], in_=stat["var"][:, :],
                                         func=AF.Ln), waits=[var_tok])
            prog.op(lambda: s.activation(out=stat["rstd"][:, :], in_=stat["rstd"][:, :],
                                         func=AF.Exp, scale=-0.5))
        prog.section("s", ssec2)
        rstd_tok = prog.tok("s")

        def vsec2(v, tc=tc, qf=prev_aff):
            prog.op(lambda: v.tensor_scalar(out=xq[:, :], in0=xc[:, :],
                                            scalar1=stat["mean"][:, :],
                                            scalar2=stat["rstd"][:, :],
                                            op0=ALU.subtract, op1=ALU.mult),
                    waits=[rstd_tok, qf])
        prog.section("v", vsec2)
        aff_tok = prog.tok("v")
        xc_free = aff_tok

        for kc in range(8):
            def tsec(t, kc=kc):
                prog.op(lambda: t.transpose(ps_tp[:, :],
                                            xq[:, kc * 128:(kc + 1) * 128],
                                            ident[:, :]),
                        waits=[aff_tok, id_tok, prog.tok("v")])
            prog.section("t", tsec)
            tr_tok = prog.tok("t")

            def vsec3(v, kc=kc, tc=tc):
                prog.op(lambda: v.tensor_copy(
                    out=xlT[:, (kc * TCn + tc) * 128:(kc * TCn + tc) * 128 + 128],
                    in_=ps_tp[:, :]), waits=[tr_tok])
            prog.section("v", vsec3)
        prev_aff = prog.tok("t")
    xlT_tok = prog.tok("v")

    o_store = ("dso", 0)
    for tc in range(TCn):
        def tmm(t, tc=tc):
            prog.wait(xlT_tok)
            prog.wait(pld_tok)
            prog.wait(prog.tok("v"))
            for kc in range(8):
                prog.op(lambda kc=kc: t.matmul(
                    ps_l[:, :8],
                    xlT[:, (kc * TCn + tc) * 128:(kc * TCn + tc) * 128 + 128],
                    wg_sb[:, kc * 8:(kc + 1) * 8],
                    start=(kc == 0), stop=(kc == 7)))
            for kc in range(8):
                prog.op(lambda kc=kc: t.matmul(
                    ps_hd[:, :512],
                    xlT[:, (kc * TCn + tc) * 128:(kc * TCn + tc) * 128 + 128],
                    wd_sb[:, kc * 512:(kc + 1) * 512],
                    start=(kc == 0), stop=(kc == 7)))
        prog.section("t", tmm)
        mm_tok = prog.tok("t")

        def vg(v, tc=tc):
            prog.op(lambda: v.tensor_copy(out=l_sb[:, :], in_=ps_l[:, :8]),
                    waits=[mm_tok])
            prog.op(lambda: v.tensor_reduce(out=m1[:, :], in_=l_sb[:, :],
                                            axis=AX, op=ALU.max))
            prog.op(lambda: v.tensor_scalar(out=msk2[:, :], in0=l_sb[:, :],
                                            scalar1=m1[:, :], scalar2=-1e30,
                                            op0=ALU.is_ge, op1=ALU.mult))
            prog.op(lambda: v.tensor_tensor(out=msk2[:, :], in0=msk2[:, :],
                                            in1=l_sb[:, :], op=ALU.add))
            prog.op(lambda: v.tensor_reduce(out=m2[:, :], in_=msk2[:, :],
                                            axis=AX, op=ALU.max))
            prog.op(lambda: v.tensor_scalar(out=nm1[:, :], in0=m1[:, :],
                                            scalar1=-1.0, scalar2=None, op0=ALU.mult))
            prog.op(lambda: v.tensor_tensor(out=dd[:, :], in0=m2[:, :],
                                            in1=m1[:, :], op=ALU.subtract))
        prog.section("v", vg)
        vg_tok = prog.tok("v")

        def sg(s, tc=tc):
            prog.op(lambda: s.activation(out=g_sb[:, :], in_=l_sb[:, :], func=AF.Exp,
                                         bias=nm1[:, :]), waits=[vg_tok])
            prog.op(lambda: s.activation(out=dd[:, :], in_=dd[:, :], func=AF.Exp))
        prog.section("s", sg)
        sg_tok = prog.tok("s")

        def vg2(v, tc=tc):
            prog.op(lambda: v.tensor_scalar(out=dd[:, :], in0=dd[:, :],
                                            scalar1=1.0, scalar2=None, op0=ALU.add),
                    waits=[sg_tok])
            prog.op(lambda: v.reciprocal(out=dd[:, :], in_=dd[:, :]))
            prog.op(lambda: v.tensor_scalar(out=g_sb[:, :], in0=g_sb[:, :],
                                            scalar1=dd[:, :], scalar2=None,
                                            op0=ALU.mult))
            prog.op(lambda: v.tensor_scalar(out=msk2[:, :], in0=l_sb[:, :],
                                            scalar1=m2[:, :], scalar2=None,
                                            op0=ALU.is_ge))
            prog.op(lambda: v.tensor_tensor(out=g_sb[:, :], in0=g_sb[:, :],
                                            in1=msk2[:, :], op=ALU.mult))
            prog.op(lambda: v.tensor_copy(out=gb_sb[:, :], in_=g_sb[:, :]))
            prog.op(lambda: v.tensor_tensor(out=hsc[:, :], in0=ps_hd[:, :512],
                                            in1=bd_sb[:, :], op=ALU.add))
            prog.op(lambda: v.tensor_scalar(out=hsc[:, :], in0=hsc[:, :],
                                            scalar1=0.0, scalar2=None, op0=ALU.max))
            prog.op(lambda: v.tensor_tensor(out=hgb[:, :].rearrange("p (e b) -> p e b", b=64),
                                            in0=hsc[:, :].rearrange("p (e b) -> p e b", b=64),
                                            in1=g_sb[:, :].unsqueeze(2).broadcast_to([128, 8, 64]),
                                            op=ALU.mult))
        prog.section("v", vg2)
        hg_tok = prog.tok("v")

        for kc in range(4):
            def tsec2(t, kc=kc):
                prog.op(lambda: t.transpose(ps_tp[:, :],
                                            hgb[:, kc * 128:(kc + 1) * 128],
                                            ident[:, :]), waits=[hg_tok, prog.tok("v")])
            prog.section("t", tsec2)
            tr2 = prog.tok("t")

            def vsec4(v, kc=kc):
                prog.op(lambda: v.tensor_copy(out=hgT[:, kc * 128:(kc + 1) * 128],
                                              in_=ps_tp[:, :]), waits=[tr2])
            prog.section("v", vsec4)

        def tgt(t, tc=tc):
            prog.op(lambda: t.transpose(ps_gt[:8, :], gb_sb[:, 0:8], ident[:, :]),
                    waits=[hg_tok, prog.tok("v")])
        prog.section("t", tgt)
        gt_tok = prog.tok("t")

        def vgt(v):
            prog.op(lambda: v.tensor_copy(out=gT_sb[:8, :], in_=ps_gt[:8, :128]),
                    waits=[gt_tok])
        prog.section("v", vgt)
        hgT_tok = prog.tok("v")

        def tup(t, tc=tc):
            prog.wait(hgT_tok)
            prog.wait(prog.tok("v"))
            for half in range(2):
                for kc in range(4):
                    prog.op(lambda kc=kc, half=half: t.matmul(
                        ps_o[:, half * 512:(half + 1) * 512],
                        hgT[:, kc * 128:(kc + 1) * 128],
                        wu_sb[:, kc * 1024 + half * 512: kc * 1024 + half * 512 + 512],
                        start=(kc == 0), stop=False))
                prog.op(lambda half=half: t.matmul(
                    ps_o[:, half * 512:(half + 1) * 512],
                    gT_sb[:8, :], bu_sb[:8, half * 512:(half + 1) * 512],
                    start=False, stop=True))
        prog.section("t", tup)
        up_tok = prog.tok("t")

        ld2 = [None]

        def yld2(y, tc=tc):
            ld2[0] = prog.dma(xc[:, :], x_d.ap()[tc * 128:(tc + 1) * 128, :],
                              waits=(prog.tok("v"),), sem="dx")
        prog.section("y", yld2)

        def vev(v, tc=tc, ost=o_store):
            prog.wait(ost)
            prog.op(lambda: v.scalar_tensor_tensor(out=oc[:, :], in0=ps_o[:, :1024],
                                                   scalar=0.5, in1=xc[:, :],
                                                   op0=ALU.mult, op1=ALU.add),
                    waits=[up_tok, ld2[0]])
        prog.section("v", vev)
        ev_tok = prog.tok("v")
        xc_free = ev_tok

        def yst(y, tc=tc):
            prog.dma(o_d.ap()[tc * 128:(tc + 1) * 128, :], oc[:, :], waits=(ev_tok,),
                     sem="dso")
        prog.section("y", yst)
        o_store = ("dso", prog.cnt["dso"])

    return


def run_moe(nc, x_full, casts, inp):
    c2 = lambda a: np.ascontiguousarray(a)
    wd_view = c2(np.asarray(casts["L1_ffn_Wd"]).transpose(1, 0, 2).reshape(1024, 512))
    wu_view = c2(np.asarray(casts["L1_ffn_Wu"]).reshape(512, 1024))
    bd_rep = c2(np.broadcast_to(inp["L1_ffn_bd"].astype(np.float32).reshape(512), (128, 512)))
    bu_b = c2(np.asarray(casts["L1_ffn_bu"]))
    wg_b = c2(casts["L1_ffn_w_gate"])
    in_maps = []
    for i in range(NCORES):
        in_maps.append({
            "x_s": c2(x_full[i * T_CORE:(i + 1) * T_CORE]),
            "wgate": wg_b, "wdview": wd_view, "wuview": wu_view,
            "buw": bu_b, "bdrep": bd_rep,
        })
    res = run_bass_kernel_spmd(nc, in_maps, core_ids=list(range(NCORES)))
    return np.concatenate([res.results[i]["o_x"] for i in range(NCORES)], axis=0)


def emit_ln_xlT(prog, x_d, xlT, ident, id_tok, bufs):
    xc, xln, xq, stat = bufs["xc"], bufs["xln"], bufs["xq"], bufs["stat"]
    TCn = 16
    xc_free = ("v", 0)
    prev_aff = ("t", 0)
    for tc in range(TCn):
        ld = [None]

        def yld(y, tc=tc, xf=xc_free):
            ld[0] = prog.dma(xc[:, :], x_d.ap()[tc * 128:(tc + 1) * 128, :],
                             waits=(xf,), sem="dx")
        prog.section("y", yld)

        def ssec(s, pa=prev_aff):
            prog.op(lambda: s.activation(out=xln[:, :], in_=xc[:, :], func=AF.Square,
                                         accum_out=stat["ssq"][:, :]),
                    waits=[ld[0], pa])
        prog.section("s", ssec)
        sq_tok = prog.tok("s")

        def vsec(v, tc=tc):
            prog.op(lambda: v.tensor_reduce(out=stat["sum"][:, :], in_=xc[:, :],
                                            axis=AX, op=ALU.add), waits=[ld[0]])
            prog.op(lambda: v.tensor_scalar(out=stat["mean"][:, :], in0=stat["sum"][:, :],
                                            scalar1=1.0 / D, scalar2=None, op0=ALU.mult))
            prog.op(lambda: v.tensor_scalar(out=stat["ex2"][:, :], in0=stat["ssq"][:, :],
                                            scalar1=1.0 / D, scalar2=None, op0=ALU.mult),
                    waits=[sq_tok])
            prog.op(lambda: v.tensor_tensor(out=stat["m2"][:, :], in0=stat["mean"][:, :],
                                            in1=stat["mean"][:, :], op=ALU.mult))
            prog.op(lambda: v.tensor_tensor(out=stat["var"][:, :], in0=stat["ex2"][:, :],
                                            in1=stat["m2"][:, :], op=ALU.subtract))
            prog.op(lambda: v.tensor_scalar(out=stat["var"][:, :], in0=stat["var"][:, :],
                                            scalar1=1e-5, scalar2=None, op0=ALU.add))
        prog.section("v", vsec)
        var_tok = prog.tok("v")

        def ssec2(s):
            prog.op(lambda: s.activation(out=stat["rstd"][:, :], in_=stat["var"][:, :],
                                         func=AF.Ln), waits=[var_tok])
            prog.op(lambda: s.activation(out=stat["rstd"][:, :], in_=stat["rstd"][:, :],
                                         func=AF.Exp, scale=-0.5))
        prog.section("s", ssec2)
        rstd_tok = prog.tok("s")

        def vsec2(v, tc=tc, qf=prev_aff):
            prog.op(lambda: v.tensor_scalar(out=xq[:, :], in0=xc[:, :],
                                            scalar1=stat["mean"][:, :],
                                            scalar2=stat["rstd"][:, :],
                                            op0=ALU.subtract, op1=ALU.mult),
                    waits=[rstd_tok, qf])
        prog.section("v", vsec2)
        aff_tok = prog.tok("v")
        xc_free = aff_tok

        for kc in range(8):
            def tsec(t, kc=kc):
                prog.op(lambda: t.transpose(bufs["ps_tp"][:, :],
                                            xq[:, kc * 128:(kc + 1) * 128],
                                            ident[:, :]),
                        waits=[aff_tok, id_tok, prog.tok("v")])
            prog.section("t", tsec)
            tr_tok = prog.tok("t")

            def vsec3(v, kc=kc, tc=tc):
                prog.op(lambda: v.tensor_copy(
                    out=xlT[:, (kc * TCn + tc) * 128:(kc * TCn + tc) * 128 + 128],
                    in_=bufs["ps_tp"][:, :]), waits=[tr_tok])
            prog.section("v", vsec3)
        prev_aff = prog.tok("t")
    return prog.tok("v")


def emit_consts_att(prog):
    E2f = prog.sbuf("E2f", [128, 2])
    onesb = prog.sbuf("onesb", [128, 128], BF16)
    iop = prog.sbuf("iop", [128, 1], I32)

    def gsec(g):
        prog.op(lambda: g.iota(iop[:, :], [[0, 1]], channel_multiplier=1,
                               allow_small_or_imprecise_dtypes=True))
    prog.section("g", gsec)
    io_tok = prog.tok("g")

    def vsec(v):
        prog.op(lambda: v.tensor_scalar(out=E2f[:, 0:1], in0=iop[:, :],
                                        scalar1=64, scalar2=None, op0=ALU.is_lt),
                waits=[io_tok])
        prog.op(lambda: v.tensor_scalar(out=E2f[:, 1:2], in0=iop[:, :],
                                        scalar1=63, scalar2=None, op0=ALU.is_gt))
        prog.op(lambda: v.memset(onesb[:, :], 1.0))
    prog.section("v", vsec)
    return E2f, onesb, prog.tok("v")


def emit_att_a(prog, io):
    TCn = 16
    x_d = io["x"]
    wk_d = io["Wk_b"]
    wv_d = io["Wv_b"]
    bk_d = io["bk_r"]
    bv_d = io["bv_rep"]
    pj_d = io["projT_b"]
    kv_d = io["kv"]
    ident = io["_ident"]
    id_tok = io["_id_tok"]
    identf = io["_identf"]
    idf_tok = io["_idf_tok"]
    E2f = io["_E2f"]
    onesb = io["_onesb"]
    cst_tok = io["_cst_tok"]

    xlT = prog.sb("xlT", [128, 8 * TCn * 128], BF16)
    bufs = {"xc": prog.sb("xc", [128, 1024]),
            "xln": prog.sb("xln", [128, 1024]),
            "xq": prog.sb("xq", [128, 1024], BF16),
            "stat": {k: prog.sb("sta_" + k, [128, 1]) for k in
                     ("sum", "ssq", "mean", "ex2", "m2", "var", "rstd")},
            "ps_tp": prog.ps("ps_tp", [128, 128], BF16)}
    xlT_tok = emit_ln_xlT(prog, x_d, xlT, ident, id_tok, bufs)

    wrot = prog.sb("wrot", [128, 8 * 128], BF16)
    vrot = prog.sb("vrot", [128, 8 * 128], BF16)
    prot = prog.sb("prot", [128, 512], BF16)
    bk_sb = prog.sb("bk_sb", [128, 8])
    bv_sb = prog.sb("bv_sb", [128, 1024])
    Kf = prog.sb("Kf", [128, 2048])
    Kfb = prog.sb("Kfb", [128, 2048], BF16)
    sqs = prog.sb("sqs", [128, 2048])
    ss_sb = prog.sb("ss_sb", [2, 2048])
    rrT = prog.sb("rrT", [128, 32])
    Vf = prog.sb("Vf", [128, TCn * 128])
    RK = prog.sb("RK", [128, TCn * 256], BF16)
    Vs = prog.sb("Vs", [128, TCn * 65], BF16)
    Vu = prog.sb("Vu", [128, TCn * 65], BF16)
    kvacc = prog.sb("kvacc", [128, 2080])
    ps_f = prog.ps("ps_f", [128, 512])
    ps_ss = prog.ps("ps_ss", [2, 512])
    ps_rt = prog.ps("ps_rt", [128, 2])
    ps_kp = prog.ps("ps_kp", [128, 256])
    ps_kv = prog.ps("ps_kv", [128, 65])

    def pbias(y):
        prog.dma(bk_sb[:, :], bk_d.ap(), sem="dm")
        prog.dma(bv_sb[:, :], bv_d.ap(), sem="dm")
    prog.section("y", pbias)
    pb_tok = ("dm", prog.cnt["dm"])

    w_free = ("t", 0)
    ss_bounce_busy = ("dh", 0)
    for hp in range(8):
        wld = [None, None, None]

        def yw(y, hp=hp, wf=w_free):
            wld[0] = prog.dma(wrot[:, :].rearrange("p (kc n) -> p kc n", n=128),
                              wk_d.ap()[:, hp * 128:(hp + 1) * 128]
                                  .rearrange("(kc p) n -> p kc n", p=128),
                              waits=(wf,), sem="dwa")
            wld[1] = prog.dma(vrot[:, :].rearrange("p (kc n) -> p kc n", n=128),
                              wv_d.ap()[:, hp * 128:(hp + 1) * 128]
                                  .rearrange("(kc p) n -> p kc n", p=128), sem="dwa")
            wld[2] = prog.dma(prot[0:64, 0:256],
                              pj_d.ap()[hp * 2], sem="dwb")
            wld[2] = prog.dma(prot[64:128, 256:512],
                              pj_d.ap()[hp * 2 + 1], sem="dwb")
        prog.section("y", yw)

        for tt in range(4):
            def tkf(t, tt=tt, hp=hp):
                prog.wait(wld[0])
                prog.wait(xlT_tok)
                prog.wait(prog.tok("s"))
                for kc in range(8):
                    prog.op(lambda kc=kc: t.matmul(
                        ps_f[:, :512], wrot[:, kc * 128:(kc + 1) * 128],
                        xlT[:, kc * TCn * 128 + tt * 512: kc * TCn * 128 + tt * 512 + 512],
                        start=(kc == 0), stop=(kc == 7)))
            prog.section("t", tkf)
            kf_tok = prog.tok("t")

            def skf(s, tt=tt, hp=hp):
                prog.op(lambda: s.activation(out=Kf[:, tt * 512:(tt + 1) * 512],
                                             in_=ps_f[:, :512], func=AF.Identity,
                                             bias=bk_sb[:, hp:hp + 1]),
                        waits=[kf_tok, pb_tok, prog.tok("v")])
            prog.section("s", skf)
        kf_done = prog.tok("s")

        def skq(s, hp=hp):
            prog.op(lambda: s.activation(out=sqs[:, :], in_=Kf[:, :], func=AF.Square),
                    waits=[kf_done, prog.tok("t")])
        prog.section("s", skq)
        sq_tok = prog.tok("s")

        def vkb(v, hp=hp):
            prog.op(lambda: v.tensor_copy(out=Kfb[:, :], in_=Kf[:, :]),
                    waits=[kf_done, prog.tok("t")])
        prog.section("v", vkb)
        kfb_tok = prog.tok("v")

        for tt in range(4):
            def tss(t, tt=tt):
                prog.wait(sq_tok)
                prog.wait(cst_tok)
                prog.wait(prog.tok("v"))
                prog.op(lambda: t.matmul(ps_ss[:2, :512], E2f[:, :],
                                         sqs[:, tt * 512:(tt + 1) * 512],
                                         start=True, stop=True))
            prog.section("t", tss)
            ssm_tok = prog.tok("t")

            def vss(v, tt=tt):
                prog.op(lambda: v.tensor_copy(out=ss_sb[:2, tt * 512:(tt + 1) * 512],
                                              in_=ps_ss[:2, :512]), waits=[ssm_tok])
            prog.section("v", vss)
        ss_tok = prog.tok("v")

        def srr(s):
            prog.op(lambda: s.activation(out=ss_sb[:2, :], in_=ss_sb[:2, :], func=AF.Ln),
                    waits=[ss_tok])
            prog.op(lambda: s.activation(out=ss_sb[:2, :], in_=ss_sb[:2, :], func=AF.Exp,
                                         scale=0.5))
        prog.section("s", srr)
        nrm_tok = prog.tok("s")

        def vrr(v):
            prog.op(lambda: v.tensor_scalar(out=ss_sb[:2, :], in0=ss_sb[:2, :],
                                            scalar1=1e-8, scalar2=None, op0=ALU.add),
                    waits=[nrm_tok])
            prog.op(lambda: v.reciprocal(out=ss_sb[:2, :], in_=ss_sb[:2, :]))
        prog.section("v", vrr)
        rr_tok = prog.tok("v")

        for tc in range(TCn):
            def trt(t, tc=tc):
                prog.op(lambda: t.transpose(ps_rt[:, :2],
                                            ss_sb[0:2, tc * 128:(tc + 1) * 128],
                                            identf[0:2, 0:2]),
                        waits=[rr_tok, idf_tok, prog.tok("v")])
            prog.section("t", trt)
            rt_tok = prog.tok("t")

            def vrt(v, tc=tc):
                prog.op(lambda: v.tensor_copy(out=rrT[:, tc:tc + 1],
                                              in_=ps_rt[:, 0:1]), waits=[rt_tok])
                prog.op(lambda: v.tensor_copy(out=rrT[:, 16 + tc:16 + tc + 1],
                                              in_=ps_rt[:, 1:2]))
            prog.section("v", vrt)
        rrT_tok = prog.tok("v")

        for tc in range(TCn):
            def tv(t, tc=tc):
                prog.wait(wld[1])
                prog.wait(xlT_tok)
                prog.wait(prog.tok("v"))
                for kc in range(8):
                    prog.op(lambda kc=kc: t.matmul(
                        ps_f[:, :128],
                        xlT[:, (kc * TCn + tc) * 128:(kc * TCn + tc) * 128 + 128],
                        vrot[:, kc * 128:(kc + 1) * 128],
                        start=(kc == 0), stop=(kc == 7)))
            prog.section("t", tv)
            v_tok = prog.tok("t")

            def vv(v, tc=tc, hp=hp):
                prog.op(lambda: v.tensor_tensor(out=Vf[:, tc * 128:(tc + 1) * 128],
                                                in0=ps_f[:, :128],
                                                in1=bv_sb[:, hp * 128:(hp + 1) * 128],
                                                op=ALU.add), waits=[v_tok, pb_tok])
            prog.section("v", vv)
        vf_tok = prog.tok("v")

        for h2 in range(2):
            for tc in range(TCn):
                def vaug(v, h2=h2, tc=tc):
                    src = Vf[:, tc * 128 + h2 * 64: tc * 128 + h2 * 64 + 64]
                    rcol = rrT[:, h2 * 16 + tc: h2 * 16 + tc + 1]
                    prog.op(lambda: v.tensor_scalar(
                        out=Vs[:, tc * 65:tc * 65 + 64], in0=src,
                        scalar1=rcol, scalar2=None, op0=ALU.mult),
                        waits=[vf_tok, rrT_tok])
                    prog.op(lambda: v.tensor_copy(out=Vs[:, tc * 65 + 64:tc * 65 + 65],
                                                  in_=rcol))
                    prog.op(lambda: v.tensor_scalar(out=Vu[:, tc * 65:tc * 65 + 64],
                                                    in0=src, scalar1=0.1,
                                                    scalar2=None, op0=ALU.mult))
                    prog.op(lambda: v.memset(Vu[:, tc * 65 + 64:tc * 65 + 65], 0.1))
                prog.section("v", vaug)
            vaug_tok = prog.tok("v")

            for tc in range(TCn):
                def trk(t, h2=h2, tc=tc):
                    prog.wait(wld[2])
                    prog.wait(kfb_tok)
                    prog.wait(prog.tok("v"))
                    prog.op(lambda: t.matmul(
                        ps_kp[:, :256],
                        Kfb[h2 * 64:h2 * 64 + 64, tc * 128:(tc + 1) * 128],
                        prot[h2 * 64:h2 * 64 + 64, h2 * 256:(h2 + 1) * 256],
                        start=True, stop=True))
                prog.section("t", trk)
                rk_tok = prog.tok("t")

                def vrk(v, h2=h2, tc=tc):
                    prog.op(lambda: v.tensor_scalar(out=RK[:, tc * 256:(tc + 1) * 256],
                                                    in0=ps_kp[:, :256],
                                                    scalar1=0.0, scalar2=None,
                                                    op0=ALU.max), waits=[rk_tok])
                prog.section("v", vrk)
            rk_done = prog.tok("v")

            for fc in range(2):
                def tkv(t, h2=h2, fc=fc):
                    prog.wait(rk_done)
                    prog.wait(vaug_tok)
                    prog.wait(cst_tok)
                    prog.wait(prog.tok("v"))
                    for tc in range(TCn):
                        prog.op(lambda tc=tc: t.matmul(
                            ps_kv[:, :65],
                            RK[:, tc * 256 + fc * 128: tc * 256 + fc * 128 + 128],
                            Vs[:, tc * 65:(tc + 1) * 65],
                            start=(tc == 0), stop=False))
                    for tc in range(TCn):
                        prog.op(lambda tc=tc: t.matmul(
                            ps_kv[:, :65],
                            onesb[:, 0:128], Vu[:, tc * 65:(tc + 1) * 65],
                            start=False, stop=(tc == TCn - 1)))
                prog.section("t", tkv)
                kv_tok = prog.tok("t")

                def vkv(v, h2=h2, fc=fc, hp=hp):
                    h = hp * 2 + h2
                    col = (h * 2 + fc) * 65
                    prog.op(lambda: v.tensor_copy(out=kvacc[:, col:col + 65],
                                                  in_=ps_kv[:, :65]), waits=[kv_tok])
                prog.section("v", vkv)
        w_free = prog.tok("t")

    def yout(y):
        prog.dma(kv_d.ap(), kvacc[:, :], waits=(prog.tok("v"),), sem="dso")
    prog.section("y", yout)
    return


def emit_att_b(prog, io):
    TCn = 16
    x_d = io["x"]
    kva_d = io["kv_all"]
    wq_d = io["Wq_b"]
    bq_d = io["bq_r"]
    pj_d = io["projT_b"]
    wo_d = io["Wo_b"]
    bo_d = io["bo_rep"]
    o_d = io["o"]
    ident = io["_ident"]
    id_tok = io["_id_tok"]
    identf = io["_identf"]
    idf_tok = io["_idf_tok"]
    E2f = io["_E2f"]
    onesb = io["_onesb"]
    cst_tok = io["_cst_tok"]

    xlT = prog.sb("xlT", [128, 8 * TCn * 128], BF16)
    bufs = {"xc": prog.sb("xc", [128, 1024]),
            "xln": prog.sb("xln", [128, 1024]),
            "xq": prog.sb("xq", [128, 1024], BF16),
            "stat": {k: prog.sb("sta_" + k, [128, 1]) for k in
                     ("sum", "ssq", "mean", "ex2", "m2", "var", "rstd")},
            "ps_tp": prog.ps("ps_tp", [128, 128], BF16)}
    xlT_tok = emit_ln_xlT(prog, x_d, xlT, ident, id_tok, bufs)

    kvs = prog.sb("kvs", [128, 2080])
    kvb = prog.sb("kvb", [128, 2080], BF16)
    kvt = prog.sb("kvt", [128, 2080])
    wrot = prog.sb("wrot", [128, 8 * 128], BF16)
    prot = prog.sb("prot", [128, 512], BF16)
    bq_sb = prog.sb("bq_sb", [128, 8])
    bo_sb = prog.sb("bo_sb", [128, 1024])
    Qf = prog.sb("Qf", [128, 2048])
    Qfb = prog.sb("Qfb", [128, 2048], BF16)
    sqs = prog.sb("sqs", [128, 2048])
    ss_sb = prog.sb("ss_sb", [4, 2048])
    qq_sb = prog.sb("qq_sb", [2, 2048])
    rqT = prog.sb("rqT", [128, 64])
    Rf = prog.sb("Rf", [128, 2 * 2048], BF16)
    Srep = prog.sb("Srep", [128, 65])
    Srow = prog.sb("Srow", [1, 65], BF16)
    t1 = prog.sb("t1", [128, 65])
    outg = prog.sb("outg", [128, TCn * 1024], BF16)
    ogT = prog.sb("ogT", [128, 8 * TCn * 128], BF16)
    oc = prog.sb("oc", [128, 1024])
    xc2 = prog.sb("xc2", [128, 1024])
    ps_f = prog.ps("ps_f", [128, 512])
    ps_ss = prog.ps("ps_ss", [4, 512])
    ps_small = prog.ps("ps_small", [128, 512])
    ps_rt = ps_small[:, 0:4]
    ps_m = ps_small[:, 128:193]
    ps_s1 = ps_small[0:1, 256:321]
    ps_s2 = ps_small[:, 384:449]
    ps_o = prog.ps("ps_o", [128, 1024])

    def pbias(y):
        prog.dma(bq_sb[:, :], bq_d.ap(), sem="dm")
        prog.dma(bo_sb[:, :], bo_d.ap(), sem="dm")
    prog.section("y", pbias)

    def ykv0(y):
        prog.dma(kvs[:, :], kva_d.ap()[0], sem="dm")
    prog.section("y", ykv0)
    kv_tok = ("dm", prog.cnt["dm"])
    for j in range(1, 8):
        def ykvj(y, j=j):
            prog.dma(kvt[:, :], kva_d.ap()[j], sem="dm",
                     waits=(prog.tok("v"),))
        prog.section("y", ykvj)
        ld_tok = ("dm", prog.cnt["dm"])

        def vadd(v, j=j):
            prog.op(lambda: v.tensor_tensor(out=kvs[:, :], in0=kvs[:, :],
                                            in1=kvt[:, :], op=ALU.add),
                    waits=[ld_tok, kv_tok])
        prog.section("v", vadd)
    
    def vcast(v):
        prog.op(lambda: v.tensor_copy(out=kvb[:, :], in_=kvs[:, :]))
    prog.section("v", vcast)
    kvb_tok = prog.tok("v")
    pb_tok = ("dm", prog.cnt["dm"])

    o_store = ("dso", 0)
    w_free = ("t", 0)
    for hp in range(8):
        wld = [None, None]

        def yw(y, hp=hp, wf=w_free):
            wld[0] = prog.dma(wrot[:, :].rearrange("p (kc n) -> p kc n", n=128),
                              wq_d.ap()[:, hp * 128:(hp + 1) * 128]
                                  .rearrange("(kc p) n -> p kc n", p=128),
                              waits=(wf,), sem="dwa")
            prog.dma(prot[0:64, 0:256], pj_d.ap()[hp * 2], sem="dwb")
            wld[1] = prog.dma(prot[64:128, 256:512], pj_d.ap()[hp * 2 + 1], sem="dwb")
        prog.section("y", yw)

        for tt in range(4):
            def tqf(t, tt=tt, hp=hp):
                prog.wait(wld[0])
                prog.wait(xlT_tok)
                prog.wait(prog.tok("s"))
                for kc in range(8):
                    prog.op(lambda kc=kc: t.matmul(
                        ps_f[:, :512], wrot[:, kc * 128:(kc + 1) * 128],
                        xlT[:, kc * TCn * 128 + tt * 512: kc * TCn * 128 + tt * 512 + 512],
                        start=(kc == 0), stop=(kc == 7)))
            prog.section("t", tqf)
            qf_tok = prog.tok("t")

            def sqf(s, tt=tt, hp=hp):
                prog.op(lambda: s.activation(out=Qf[:, tt * 512:(tt + 1) * 512],
                                             in_=ps_f[:, :512], func=AF.Identity,
                                             bias=bq_sb[:, hp:hp + 1]),
                        waits=[qf_tok, pb_tok, prog.tok("v")])
            prog.section("s", sqf)
        qf_done = prog.tok("s")

        def sqq(s, hp=hp):
            prog.op(lambda: s.activation(out=sqs[:, :], in_=Qf[:, :], func=AF.Square),
                    waits=[qf_done, prog.tok("t")])
        prog.section("s", sqq)
        sq_tok = prog.tok("s")

        def vqb(v, hp=hp):
            prog.op(lambda: v.tensor_copy(out=Qfb[:, :], in_=Qf[:, :]),
                    waits=[qf_done, prog.tok("t")])
        prog.section("v", vqb)
        qfb_tok = prog.tok("v")

        for tt in range(4):
            def tss(t, tt=tt):
                prog.wait(sq_tok)
                prog.wait(cst_tok)
                prog.wait(prog.tok("v"))
                prog.op(lambda: t.matmul(ps_ss[:2, :512], E2f[:, :],
                                         sqs[:, tt * 512:(tt + 1) * 512],
                                         start=True, stop=True))
            prog.section("t", tss)
            ssm_tok = prog.tok("t")

            def vss(v, tt=tt):
                prog.op(lambda: v.tensor_copy(out=ss_sb[0:2, tt * 512:(tt + 1) * 512],
                                              in_=ps_ss[:2, :512]), waits=[ssm_tok])
            prog.section("v", vss)
        ss_tok = prog.tok("v")

        def srr(s):
            prog.op(lambda: s.activation(out=ss_sb[0:2, :], in_=ss_sb[0:2, :], func=AF.Ln),
                    waits=[ss_tok])
            prog.op(lambda: s.activation(out=ss_sb[0:2, :], in_=ss_sb[0:2, :], func=AF.Exp,
                                         scale=0.5))
        prog.section("s", srr)
        nrm_tok = prog.tok("s")

        def vrr(v):
            prog.op(lambda: v.tensor_scalar(out=ss_sb[0:2, :], in0=ss_sb[0:2, :],
                                            scalar1=1e-8, scalar2=None, op0=ALU.add),
                    waits=[nrm_tok])
            prog.op(lambda: v.tensor_scalar(out=qq_sb[0:2, :], in0=ss_sb[0:2, :],
                                            scalar1=0.1, scalar2=None, op0=ALU.mult))
            prog.op(lambda: v.reciprocal(out=ss_sb[0:2, :], in_=ss_sb[0:2, :]))
        prog.section("v", vrr)
        rr_tok = prog.tok("v")

        for tc in range(TCn):
            def trt(t, tc=tc):
                prog.op(lambda: t.transpose(ps_rt[:, 0:2],
                                            ss_sb[0:2, tc * 128:(tc + 1) * 128],
                                            identf[0:2, 0:2]),
                        waits=[rr_tok, idf_tok, prog.tok("v")])
                prog.op(lambda: t.transpose(ps_rt[:, 2:4],
                                            qq_sb[0:2, tc * 128:(tc + 1) * 128],
                                            identf[0:2, 0:2]))
            prog.section("t", trt)
            rt_tok = prog.tok("t")

            def vrt(v, tc=tc):
                prog.op(lambda: v.tensor_copy(out=rqT[:, tc:tc + 1],
                                              in_=ps_rt[:, 0:1]), waits=[rt_tok])
                prog.op(lambda: v.tensor_copy(out=rqT[:, 16 + tc:16 + tc + 1],
                                              in_=ps_rt[:, 1:2]))
                prog.op(lambda: v.tensor_copy(out=rqT[:, 32 + tc:32 + tc + 1],
                                              in_=ps_rt[:, 2:3]))
                prog.op(lambda: v.tensor_copy(out=rqT[:, 48 + tc:48 + tc + 1],
                                              in_=ps_rt[:, 3:4]))
            prog.section("v", vrt)
        rqT_tok = prog.tok("v")

        for h2 in range(2):
            h = hp * 2 + h2
            for fc in range(2):
                for tt in range(4):
                    def trf(t, h2=h2, fc=fc, tt=tt):
                        prog.wait(wld[1])
                        prog.wait(qfb_tok)
                        prog.wait(prog.tok("v"))
                        prog.op(lambda: t.matmul(
                            ps_f[:, :512],
                            prot[h2 * 64:h2 * 64 + 64,
                                 h2 * 256 + fc * 128: h2 * 256 + fc * 128 + 128],
                            Qfb[h2 * 64:h2 * 64 + 64, tt * 512:(tt + 1) * 512],
                            start=True, stop=True))
                    prog.section("t", trf)
                    rf_tok = prog.tok("t")

                    def vrf(v, fc=fc, tt=tt):
                        prog.op(lambda: v.tensor_scalar(
                            out=Rf[:, fc * 2048 + tt * 512: fc * 2048 + (tt + 1) * 512],
                            in0=ps_f[:, :512], scalar1=0.0, scalar2=None,
                            op0=ALU.max), waits=[rf_tok])
                    prog.section("v", vrf)
            rf_done = prog.tok("v")

            def tsp(t, h=h):
                prog.wait(kvb_tok)
                prog.wait(cst_tok)
                prog.wait(prog.tok("v"))
                for fc in range(2):
                    prog.op(lambda fc=fc: t.matmul(
                        ps_s1[0:1, 0:65], onesb[:, 0:1],
                        kvb[:, (h * 2 + fc) * 65:(h * 2 + fc) * 65 + 65],
                        start=(fc == 0), stop=(fc == 1)))
            prog.section("t", tsp)
            sp_tok = prog.tok("t")

            def vsp(v):
                prog.op(lambda: v.tensor_copy(out=Srow[:1, :], in_=ps_s1[0:1, 0:65]),
                        waits=[sp_tok])
            prog.section("v", vsp)
            srow_tok = prog.tok("v")

            def tsp2(t):
                prog.wait(srow_tok)
                prog.op(lambda: t.matmul(ps_s2[:, 0:65], onesb[0:1, 0:128],
                                         Srow[0:1, 0:65], start=True, stop=True))
            prog.section("t", tsp2)
            sp2_tok = prog.tok("t")

            def vsp2(v):
                prog.op(lambda: v.tensor_copy(out=Srep[:, :], in_=ps_s2[:, 0:65]),
                        waits=[sp2_tok])
            prog.section("v", vsp2)
            srep_tok = prog.tok("v")

            for tc in range(TCn):
                def tm(t, h=h, h2=h2, tc=tc):
                    prog.wait(rf_done)
                    prog.wait(kvb_tok)
                    prog.wait(prog.tok("v"))
                    for fc in range(2):
                        prog.op(lambda fc=fc: t.matmul(
                            ps_m[:, 0:65],
                            Rf[:, fc * 2048 + tc * 128: fc * 2048 + tc * 128 + 128],
                            kvb[:, (h * 2 + fc) * 65:(h * 2 + fc) * 65 + 65],
                            start=(fc == 0), stop=(fc == 1)))
                prog.section("t", tm)
                m_tok = prog.tok("t")

                def vm(v, h=h, h2=h2, tc=tc):
                    qcol = rqT[:, 32 + h2 * 16 + tc: 32 + h2 * 16 + tc + 1]
                    prog.op(lambda: v.tensor_scalar(out=t1[:, :], in0=Srep[:, :],
                                                    scalar1=qcol, scalar2=None,
                                                    op0=ALU.mult),
                            waits=[srep_tok, rqT_tok])
                    prog.op(lambda: v.tensor_tensor(out=t1[:, :], in0=t1[:, :],
                                                    in1=ps_m[:, 0:65], op=ALU.add),
                            waits=[m_tok])
                    prog.op(lambda: v.tensor_scalar(out=oc[:, 0:1], in0=qcol,
                                                    scalar1=1e-5, scalar2=None,
                                                    op0=ALU.mult))
                    prog.op(lambda: v.tensor_tensor(out=oc[:, 0:1], in0=t1[:, 64:65],
                                                    in1=oc[:, 0:1], op=ALU.max))
                    prog.op(lambda: v.reciprocal(out=oc[:, 0:1], in_=oc[:, 0:1]))
                    prog.op(lambda: v.tensor_scalar(
                        out=outg[:, tc * 1024 + h * 64: tc * 1024 + h * 64 + 64],
                        in0=t1[:, 0:64], scalar1=oc[:, 0:1], scalar2=None,
                        op0=ALU.mult))
                prog.section("v", vm)
        w_free = prog.tok("t")
    outg_tok = prog.tok("v")

    for tc in range(TCn):
        for kc in range(8):
            def tsec(t, tc=tc, kc=kc):
                prog.op(lambda: t.transpose(bufs["ps_tp"][:, :],
                                            outg[:, tc * 1024 + kc * 128: tc * 1024 + kc * 128 + 128],
                                            ident[:, :]),
                        waits=[outg_tok, prog.tok("v")])
            prog.section("t", tsec)
            tr_tok = prog.tok("t")

            def vsec3(v, tc=tc, kc=kc):
                prog.op(lambda: v.tensor_copy(
                    out=ogT[:, (kc * TCn + tc) * 128:(kc * TCn + tc) * 128 + 128],
                    in_=bufs["ps_tp"][:, :]), waits=[tr_tok])
            prog.section("v", vsec3)
    ogT_tok = prog.tok("v")

    woall = prog.sb("woall", [128, 8 * 1024], BF16)

    def ywoall(y):
        prog.dma(woall[:, :].rearrange("p (kc n) -> p kc n", n=1024),
                 wo_d.ap().rearrange("(kc p) n -> p kc n", p=128), sem="dwa")
    prog.section("y", ywoall)
    wo_tok = ("dwa", prog.cnt["dwa"])

    for tc in range(TCn):
        def tmo(t, tc=tc):
            prog.wait(ogT_tok)
            prog.wait(wo_tok)
            prog.wait(prog.tok("v"))
            for half in range(2):
                for kc in range(8):
                    prog.op(lambda kc=kc, half=half: t.matmul(
                        ps_o[:, half * 512:(half + 1) * 512],
                        ogT[:, (kc * TCn + tc) * 128:(kc * TCn + tc) * 128 + 128],
                        woall[:, kc * 1024 + half * 512: kc * 1024 + half * 512 + 512],
                        start=(kc == 0), stop=(kc == 7)))
        prog.section("t", tmo)
        mo_tok = prog.tok("t")

        ld2 = [None]

        def yx2(y, tc=tc):
            ld2[0] = prog.dma(xc2[:, :], x_d.ap()[tc * 128:(tc + 1) * 128, :],
                              waits=(prog.tok("v"),), sem="dx")
        prog.section("y", yx2)

        def vev(v, tc=tc, ost=o_store):
            prog.wait(ost)
            prog.op(lambda: v.tensor_tensor(out=oc[:, :], in0=ps_o[:, :1024],
                                            in1=bo_sb[:, :], op=ALU.add),
                    waits=[mo_tok, pb_tok])
            prog.op(lambda: v.tensor_tensor(out=oc[:, :], in0=oc[:, :],
                                            in1=xc2[:, :], op=ALU.add),
                    waits=[ld2[0]])
        prog.section("v", vev)
        ev_tok = prog.tok("v")

        def yst(y, tc=tc):
            prog.dma(o_d.ap()[tc * 128:(tc + 1) * 128, :], oc[:, :], waits=(ev_tok,),
                     sem="dso")
        prog.section("y", yst)
        o_store = ("dso", prog.cnt["dso"])

    return


def run_attention(x_full, casts, inp):
    c2 = lambda a: np.ascontiguousarray(a)
    projT = c2(np.asarray(casts["L1_attn_projection"]).transpose(0, 2, 1))
    bk_r = c2(inp["L1_attn_bk"].astype(np.float32).reshape(8, 128).T)
    bq_r = c2(inp["L1_attn_bq"].astype(np.float32).reshape(8, 128).T)
    bv_rep = c2(np.broadcast_to(inp["L1_attn_bv"].astype(np.float32), (128, 1024)))
    bo_rep = c2(np.broadcast_to(inp["L1_attn_bo"].astype(np.float32), (128, 1024)))
    ncA = build_att_a()
    mapsA = []
    for i in range(NCORES):
        mapsA.append({"x_s": c2(x_full[i * T_CORE:(i + 1) * T_CORE]),
                      "Wk_b": c2(casts["L1_attn_Wk"]), "Wv_b": c2(casts["L1_attn_Wv"]),
                      "bk_r": bk_r, "bv_rep": bv_rep, "projT_b": projT})
    resA = run_bass_kernel_spmd(ncA, mapsA, core_ids=list(range(NCORES))).results
    kv_all = np.stack([np.asarray(resA[i]["o_kv"], np.float32) for i in range(NCORES)])
    ncB = build_att_b()
    mapsB = []
    for i in range(NCORES):
        mapsB.append({"x_s": c2(x_full[i * T_CORE:(i + 1) * T_CORE]),
                      "kv_all": kv_all, "Wq_b": c2(casts["L1_attn_Wq"]),
                      "bq_r": bq_r, "projT_b": projT,
                      "Wo_b": c2(casts["L1_attn_Wo"]), "bo_rep": bo_rep})
    resB = run_bass_kernel_spmd(ncB, mapsB, core_ids=list(range(NCORES))).results
    return np.concatenate([resB[i]["o_x"] for i in range(NCORES)], axis=0)


def emit_prelude(prog):
    ident = prog.sbuf("P_ident", [128, 128], BF16)
    io1 = prog.sbuf("P_io1", [128, 128], I32)
    io2 = prog.sbuf("P_io2", [128, 128], I32)
    id_tok = emit_identity(prog, ident, io1, io2)
    E2f, onesb, cst_tok = emit_consts_att(prog)
    identf = prog.sbuf("P_identf", [128, 128])

    def vidf(v):
        prog.op(lambda: v.tensor_copy(out=identf[:, :], in_=ident[:, :]),
                waits=[id_tok])
    prog.section("v", vidf)
    idf_tok = prog.tok("v")
    return {"_ident": ident, "_id_tok": id_tok, "_identf": identf,
            "_idf_tok": idf_tok, "_E2f": E2f, "_onesb": onesb,
            "_cst_tok": cst_tok}


def _fin(prog):
    def fin(y):
        for sname in ("dh", "dx", "dwa", "dwb", "dso", "dm"):
            if prog.cnt[sname]:
                y.wait_ge(prog.sem[sname], prog.cnt[sname])
        if prog.cnt["dg"]:
            y.wait_ge(prog.sem["dg"], prog.cnt["dg"])
    prog.section("y", fin)


def build_L1():
    prog = Prog("L1")
    prog.init_arena(52400)
    pre = emit_prelude(prog)
    TH = T_CORE + HALO
    x_d = prog.dram("x_s", [TH, D], F32, kind="ExternalInput")
    wg_d = prog.dram("wgqT", [1024, 2816], BF16, kind="ExternalInput")
    wu_d = prog.dram("wuqT", [1024, 2816], BF16, kind="ExternalInput")
    wd_d = prog.dram("wdqT", [2816, 1024], BF16, kind="ExternalInput")
    x1_d = prog.dram("o_x1", [TH, D], F32, kind="ExternalOutput")
    io_m = {k: prog.dram(n, sh, dt, kind=kk) for k, n, sh, dt, kk in [
        ("y1", "o_y1", [2048, 2048], F32, "ExternalOutput"),
        ("dcum", "o_dcum", [2048, 2048], F32, "ExternalOutput"),
        ("xn2T", "o_xn2T", [128, 8 * XCOLS], BF16, "ExternalOutput"),
        ("hend", "o_hend", [128, 256], F32, "ExternalOutput"),
        ("pcore", "o_pcore", [128, 256], F32, "ExternalOutput"),
        ("mstat", "o_mstat", [128, 16], F32, "ExternalOutput"),
        ("rstat", "o_rstat", [128, 16], F32, "ExternalOutput"),
        ("halo_mask", "halo_mask", [128, HALO], F32, "ExternalInput"),
        ("inproj", "inproj", [1024, 4096], BF16, "ExternalInput"),
        ("dtw", "dtw", [2048, 2048], BF16, "ExternalInput"),
        ("xpwB", "xpwB", [2048, 16], BF16, "ExternalInput"),
        ("convw_r", "convw_r", [128, 64], F32, "ExternalInput"),
        ("convb_r", "convb_r", [128, 16], F32, "ExternalInput"),
        ("dtb_r", "dtb_r", [128, 16], F32, "ExternalInput"),
        ("alog_r", "alog_r", [128, 256], F32, "ExternalInput"),
        ("dparam_r", "dparam_r", [128, 16], F32, "ExternalInput"),
        ("lng_r", "lng_r", [128, 1024], F32, "ExternalInput"),
        ("lnb_r", "lnb_r", [128, 1024], F32, "ExternalInput"),
    ]}
    io_m["b_bounce"] = prog.dram("b_bounce", [16, LT], F32)
    io_m["x"] = x1_d
    prog.phase_reset()
    emit_ffn(prog, {"x": x_d, "o": x1_d, "wgqT": wg_d, "wuqT": wu_d,
                    "wdqT": wd_d, **pre}, halo=True)
    prog.phase_reset()
    emit_mamba_a(prog, {**io_m, **pre})
    _fin(prog)
    return prog.done()


def build_L2():
    prog = Prog("L2")
    prog.init_arena(52400)
    pre = emit_prelude(prog)
    x1_d = prog.dram("x1_s", [T_CORE, D], F32, kind="ExternalInput")
    x2_d = prog.dram("i_x2", [T_CORE, D], F32)
    x3_d = prog.dram("i_x3", [T_CORE, D], F32)
    x4_d = prog.dram("o_x4", [T_CORE, D], F32, kind="ExternalOutput")
    kv_d = prog.dram("o_kv", [128, 2080], F32, kind="ExternalOutput")
    dd = lambda n, sh, dt, kk=None: prog.dram(n, sh, dt, kind=kk)
    io_b = {
        "x": x1_d, "o": x2_d,
        "y1": dd("y1", [2048, 2048], F32, "ExternalInput"),
        "dcum": dd("dcum", [2048, 2048], F32, "ExternalInput"),
        "xn2T": dd("xn2T", [128, 8 * XCOLS], BF16, "ExternalInput"),
        "mstat": dd("mstat", [128, 16], F32, "ExternalInput"),
        "rstat": dd("rstat", [128, 16], F32, "ExternalInput"),
        "slotsP": dd("slotsP", [7, 128, 256], F32, "ExternalInput"),
        "slotsH": dd("slotsH", [7, 128, 256], F32, "ExternalInput"),
        "inprojZ": dd("inprojZ", [1024, 2048], BF16, "ExternalInput"),
        "outw": dd("outw", [2048, 1024], BF16, "ExternalInput"),
        "alog_r": dd("alog_r", [128, 256], F32, "ExternalInput"),
    }
    io_f = {"x": x2_d, "o": x3_d,
            "wgqT": dd("wgqT", [1024, 2816], BF16, "ExternalInput"),
            "wuqT": dd("wuqT", [1024, 2816], BF16, "ExternalInput"),
            "wdqT": dd("wdqT", [2816, 1024], BF16, "ExternalInput")}
    io_moe = {"x": x3_d, "o": x4_d,
              "wgate": dd("wgate", [1024, 8], BF16, "ExternalInput"),
              "wdview": dd("wdview", [1024, 512], BF16, "ExternalInput"),
              "wuview": dd("wuview", [512, 1024], BF16, "ExternalInput"),
              "buw": dd("buw", [8, 1024], BF16, "ExternalInput"),
              "bdrep": dd("bdrep", [128, 512], F32, "ExternalInput")}
    io_a = {"x": x4_d, "kv": kv_d,
            "Wk_b": dd("Wk_b", [1024, 1024], BF16, "ExternalInput"),
            "Wv_b": dd("Wv_b", [1024, 1024], BF16, "ExternalInput"),
            "bk_r": dd("bk_r", [128, 8], F32, "ExternalInput"),
            "bv_rep": dd("bv_rep", [128, 1024], F32, "ExternalInput"),
            "projT_b": dd("projT_b", [16, 64, 256], BF16, "ExternalInput")}
    prog.phase_reset()
    emit_mamba_b(prog, {**io_b, **pre})
    prog.phase_reset()
    emit_ffn(prog, {**io_f, **pre}, halo=False)
    prog.phase_reset()
    emit_moe(prog, {**io_moe, **pre})
    prog.phase_reset()
    emit_att_a(prog, {**io_a, **pre})
    _fin(prog)
    return prog.done()


def build_L3():
    prog = Prog("L3")
    prog.init_arena(52400)
    pre = emit_prelude(prog)
    x4_d = prog.dram("x4_s", [T_CORE, D], F32, kind="ExternalInput")
    x5_d = prog.dram("i_x5", [T_CORE, D], F32)
    o_d = prog.dram("o_x", [T_CORE, D], F32, kind="ExternalOutput")
    dd = lambda n, sh, dt, kk=None: prog.dram(n, sh, dt, kind=kk)
    io_ab = {"x": x4_d, "o": x5_d,
             "kv_all": dd("kv_all", [8, 128, 2080], F32, "ExternalInput"),
             "Wq_b": dd("Wq_b", [1024, 1024], BF16, "ExternalInput"),
             "bq_r": dd("bq_r", [128, 8], F32, "ExternalInput"),
             "projT_b": dd("projT_b", [16, 64, 256], BF16, "ExternalInput"),
             "Wo_b": dd("Wo_b", [1024, 1024], BF16, "ExternalInput"),
             "bo_rep": dd("bo_rep", [128, 1024], F32, "ExternalInput"),
             "ss_bounce": dd("ss_bounce", [4, 2048], F32)}
    io_moe = {"x": x5_d, "o": o_d,
              "wgate": dd("wgate", [1024, 8], BF16, "ExternalInput"),
              "wdview": dd("wdview", [1024, 512], BF16, "ExternalInput"),
              "wuview": dd("wuview", [512, 1024], BF16, "ExternalInput"),
              "buw": dd("buw", [8, 1024], BF16, "ExternalInput"),
              "bdrep": dd("bdrep", [128, 512], F32, "ExternalInput")}
    prog.phase_reset()
    emit_att_b(prog, {**io_ab, **pre})
    prog.phase_reset()
    emit_moe(prog, {**io_moe, **pre})
    _fin(prog)
    return prog.done()


def kernel_fused(**inp):
    wgqT, wuqT, wdqT, casts = run_wprep(inp)
    c2 = lambda a: np.ascontiguousarray(a)
    x = c2(inp["L0_attn_x"][0]).astype(np.float32, copy=False)

    nc1 = _get_nc("L1", build_L1)
    mapsA = _mamba_a_inmaps(x, casts, inp)
    maps1 = []
    for i in range(NCORES):
        m = dict(mapsA[i])
        m["wgqT"] = wgqT; m["wuqT"] = wuqT; m["wdqT"] = wdqT
        maps1.append(m)
    res1 = run_bass_kernel_spmd(nc1, maps1, core_ids=list(range(NCORES))).results

    x1_full = np.concatenate([np.asarray(res1[i]["o_x1"], np.float32)[HALO:]
                              for i in range(NCORES)], axis=0)

    nc2 = _get_nc("L2", build_L2)
    mapsB = _mamba_b_inmaps(x1_full, res1, casts, inp)
    wd_view = c2(np.asarray(casts["L1_ffn_Wd"]).transpose(1, 0, 2).reshape(1024, 512))
    wu_view = c2(np.asarray(casts["L1_ffn_Wu"]).reshape(512, 1024))
    bd_rep = c2(np.broadcast_to(inp["L1_ffn_bd"].astype(np.float32).reshape(512), (128, 512)))
    projT = c2(np.asarray(casts["L1_attn_projection"]).transpose(0, 2, 1))
    bk_r = c2(inp["L1_attn_bk"].astype(np.float32).reshape(8, 128).T)
    bq_r = c2(inp["L1_attn_bq"].astype(np.float32).reshape(8, 128).T)
    bv_rep = c2(np.broadcast_to(inp["L1_attn_bv"].astype(np.float32), (128, 1024)))
    bo_rep = c2(np.broadcast_to(inp["L1_attn_bo"].astype(np.float32), (128, 1024)))
    maps2 = []
    for i in range(NCORES):
        m = dict(mapsB[i])
        m.pop("x1_s", None)
        m["x1_s"] = c2(x1_full[i * T_CORE:(i + 1) * T_CORE])
        m["wgqT"] = wgqT; m["wuqT"] = wuqT; m["wdqT"] = wdqT
        m["wgate"] = c2(casts["L1_ffn_w_gate"]); m["wdview"] = wd_view
        m["wuview"] = wu_view; m["buw"] = c2(np.asarray(casts["L1_ffn_bu"]))
        m["bdrep"] = bd_rep
        m["Wk_b"] = c2(casts["L1_attn_Wk"]); m["Wv_b"] = c2(casts["L1_attn_Wv"])
        m["bk_r"] = bk_r; m["bv_rep"] = bv_rep; m["projT_b"] = projT
        maps2.append(m)
    res2 = run_bass_kernel_spmd(nc2, maps2, core_ids=list(range(NCORES))).results

    x4_full = np.concatenate([np.asarray(res2[i]["o_x4"], np.float32)
                              for i in range(NCORES)], axis=0)
    kv_all = np.stack([np.asarray(res2[i]["o_kv"], np.float32) for i in range(NCORES)])

    nc3 = _get_nc("L3", build_L3)
    maps3 = []
    for i in range(NCORES):
        maps3.append({"x4_s": c2(x4_full[i * T_CORE:(i + 1) * T_CORE]),
                      "kv_all": kv_all, "Wq_b": c2(casts["L1_attn_Wq"]),
                      "bq_r": bq_r, "projT_b": projT,
                      "Wo_b": c2(casts["L1_attn_Wo"]), "bo_rep": bo_rep,
                      "wgate": c2(casts["L1_ffn_w_gate"]), "wdview": wd_view,
                      "wuview": wu_view, "buw": c2(np.asarray(casts["L1_ffn_bu"])),
                      "bdrep": bd_rep})
    res3 = run_bass_kernel_spmd(nc3, maps3, core_ids=list(range(NCORES))).results
    out = np.concatenate([res3[i]["o_x"] for i in range(NCORES)], axis=0)
    return out[None].astype(np.float32)


# version 05
# speedup vs baseline: 1.2507x; 1.2507x over previous
import numpy as np
import ml_dtypes
from contextlib import ExitStack

import concourse.bass as bass
import concourse.mybir as mybir
from concourse.bass_utils import run_bass_kernel_spmd

F32 = mybir.dt.float32
BF16 = mybir.dt.bfloat16
I32 = mybir.dt.int32
AF = mybir.ActivationFunctionType
ALU = mybir.AluOpType
AX = mybir.AxisListType.X

NCORES = 8
T_CORE = 2048
D = 1024
NPBF16 = ml_dtypes.bfloat16

J_ONE = 0x7F0000
J_HALF = 0x7E0000
J_B025 = 0x3E800000 >> 7
J_B075 = 0x3F400000 >> 7
J_GMASK = 0x00FF8000
SIXTH = float(np.float32(1.0 / 6.0))


class Prog:

    def __init__(self, name="k"):
        self.nc = bass.Bass("TRN2", target_bir_lowering=False)
        self.ctx = ExitStack()
        self.sem = {}
        self.cnt = {}
        for s in ("t", "v", "s", "g", "y", "dh", "dg", "cc",
                  "dx", "dwa", "dwb", "dso", "dm"):
            self.sem[s] = self.ctx.enter_context(self.nc.semaphore("sem_" + s))
            self.cnt[s] = 0
        self.block = self.ctx.enter_context(self.nc.Block())
        self.watermark = {e: {} for e in ("t", "v", "s", "g", "y")}
        self.oplog = []
        Prog.last = self
        self._cur = None

    def sbuf(self, name, shape, dt=F32):
        return self.ctx.enter_context(self.nc.sbuf_tensor(name, shape, dt))

    def psum(self, name, shape, dt=F32):
        return self.ctx.enter_context(self.nc.psum_tensor(name, shape, dt))

    def dram(self, name, shape, dt=F32, kind=None):
        if kind:
            return self.nc.dram_tensor(name, shape, dt, kind=kind)
        return self.nc.dram_tensor(name, shape, dt)

    def section(self, engname, fn):
        attr = {"t": "tensor", "v": "vector", "s": "scalar", "g": "gpsimd",
                "y": "sync"}[engname]

        def run(e):
            self._cur = (engname, e)
            fn(e)
            self._cur = None
        getattr(self.block, attr)(run)

    def wait(self, tok):
        engname, e = self._cur
        s, v = tok
        if v <= 0:
            return
        wm = self.watermark[engname]
        if wm.get(s, 0) >= v:
            return
        e.wait_ge(self.sem[s], v)
        wm[s] = v

    def op(self, fn, waits=(), self_wait=True):
        engname, e = self._cur
        for w in waits:
            self.wait(w)
        if self_wait and engname != "t":
            self.wait((engname, self.cnt[engname]))
        inst = fn()
        self.cnt[engname] += 1
        inst.then_inc(self.sem[engname], 1)
        import sys as _sys
        self.oplog.append((engname, _sys._getframe(1).f_lineno))
        return (engname, self.cnt[engname])

    def dma(self, out, in_, waits=(), sem=None):
        engname, e = self._cur
        if sem is None:
            sem = "dg" if engname == "g" else "dh"
        for w in waits:
            self.wait(w)
        self.wait((sem, self.cnt[sem]))
        if engname != "t":
            self.wait((engname, self.cnt[engname]))
        self.cnt[sem] += 16
        e.dma_start(out=out, in_=in_).then_inc(self.sem[sem], 16)
        return (sem, self.cnt[sem])

    def tok(self, engname):
        return (engname, self.cnt[engname])

    def barrier(self):
        snap = {e: self.cnt[e] for e in ("t", "v", "s", "g")}
        dsnap = {d: self.cnt[d] for d in ("dh", "dx", "dwa", "dwb", "dso", "dm", "dg")}

        def mk(engname):
            def fn(e):
                for oe, c in snap.items():
                    self.wait((oe, c))
                for d, c in dsnap.items():
                    self.wait((d, c))
            return fn
        for e in ("t", "v", "s", "g", "y"):
            self.section(e, mk(e))

    def init_arena(self, fp32_elems=46000):
        self.ar_sb = self.sbuf("ARENA", [128, fp32_elems], F32)
        self.ar_ps = self.psum("PSARENA", [128, 4096], F32)
        self.ar_off = 0
        self.ar_ps_off = 0
        self.ar_elems = fp32_elems

    def phase_reset(self):
        self.ar_off = 0
        self.ar_ps_off = 0
        self.barrier()

    def sb(self, name, shape, dt=F32):
        p, n = shape[0], int(np.prod(shape[1:]))
        if dt == F32 or dt == I32:
            elems = n
        elif dt == BF16:
            elems = (n + 1) // 2
        else:
            raise ValueError(dt)
        off = self.ar_off
        self.ar_off += elems
        assert self.ar_off <= self.ar_elems, f"SBUF arena overflow at {name}: {self.ar_off}"
        view = self.ar_sb[0:p, off:off + elems]
        if dt == BF16:
            view = self.ar_sb[0:p, off:off + elems].bitcast(BF16)
            if view.shape[1] != n:
                view = view[:, 0:n]
        elif dt == I32:
            view = view.bitcast(I32)
        return view

    def ps(self, name, shape, dt=F32):
        p, n = shape[0], int(np.prod(shape[1:]))
        elems = n if dt in (F32, I32) else (n + 1) // 2
        banks = -(-elems // 512)
        off = self.ar_ps_off
        self.ar_ps_off += banks * 512
        assert self.ar_ps_off <= 4096, f"PSUM arena overflow at {name}"
        view = self.ar_ps[0:p, off:off + elems]
        if dt == BF16:
            view = self.ar_ps[0:p, off:off + (n + 1) // 2].bitcast(BF16)
            if view.shape[1] != n:
                view = view[:, 0:n]
        return view

    def done(self):
        self.ctx.close()
        return self.nc


def bcast16(scale_ap, p, nblk):
    return scale_ap.unsqueeze(2).broadcast_to([p, nblk, 16])


def blk16(ap):
    return ap.rearrange("p (b k) -> p b k", k=16)


def make_quant_scratch(prog, tag, p, Fmax):
    nblk = Fmax // 16
    alloc = prog.sb if hasattr(prog, "ar_sb") else prog.sbuf
    return {
        "smax": alloc(f"q_smax_{tag}", [p, nblk]),
        "rs6": alloc(f"q_rs6_{tag}", [p, nblk]),
        "scale": alloc(f"q_scale_{tag}", [p, nblk]),
        "t2": alloc(f"q_t2_{tag}", [p, Fmax]),
        "sgn": alloc(f"q_sgn_{tag}", [p, Fmax], BF16),
    }


def emit_quant(prog, src_ap, out_bf_ap, scr, p, F, in_tok, scr_free_tok=None, out_free_tok=None):
    nblk = F // 16
    smax, rs6, scale = scr["smax"], scr["rs6"], scr["scale"]
    t2, sgn = scr["t2"], scr["sgn"]
    xn = src_ap
    toks = {}

    def vsec(v):
        waits = [in_tok]
        if scr_free_tok is not None:
            waits.append(scr_free_tok)
        prog.op(lambda: v.tensor_reduce(out=smax[:p, :nblk], in_=blk16(src_ap),
                                        axis=AX, op=ALU.max,
                                        apply_absolute_value=True), waits=waits)
        prog.op(lambda: v.tensor_scalar(out=smax[:p, :nblk], in0=smax[:p, :nblk],
                                        scalar1=6e-12, scalar2=None, op0=ALU.max))
        prog.op(lambda: v.reciprocal(out=rs6[:p, :nblk], in_=smax[:p, :nblk]))
        prog.op(lambda: v.tensor_scalar(out=rs6[:p, :nblk], in0=rs6[:p, :nblk],
                                        scalar1=6.0, scalar2=None, op0=ALU.mult))
        prog.op(lambda: v.tensor_scalar(out=scale[:p, :nblk], in0=smax[:p, :nblk],
                                        scalar1=SIXTH, scalar2=None, op0=ALU.mult))
        toks["scales"] = prog.tok("v")
    prog.section("v", vsec)

    def gsec2(g):
        prog.op(lambda: g.tensor_tensor(out=blk16(src_ap), in0=blk16(src_ap),
                                        in1=bcast16(rs6[:p, :nblk], p, nblk),
                                        op=ALU.mult), waits=[toks["scales"]])
        toks["xn"] = prog.tok("g")
    prog.section("g", gsec2)

    def vsec2(v):
        prog.op(lambda: v.tensor_scalar(out=t2[:p, :F].bitcast(I32),
                                        in0=xn.bitcast(I32),
                                        scalar1=7, scalar2=0x00FFFFFF,
                                        op0=ALU.logical_shift_right, op1=ALU.bitwise_and),
                waits=[toks["xn"]])
        prog.op(lambda: v.tensor_scalar(out=sgn[:p, :F], in0=xn,
                                        scalar1=0.0, scalar2=None,
                                        op0=ALU.is_ge))
        prog.op(lambda: v.tensor_scalar(out=sgn[:p, :F], in0=sgn[:p, :F],
                                        scalar1=2.0, scalar2=-1.0,
                                        op0=ALU.mult, op1=ALU.add))
        prog.op(lambda: v.tensor_scalar(out=xn[:p, :F].bitcast(I32),
                                        in0=t2[:p, :F].bitcast(I32),
                                        scalar1=0x4000, scalar2=J_ONE,
                                        op0=ALU.add, op1=ALU.max))
        prog.op(lambda: v.tensor_scalar(out=xn[:p, :F].bitcast(I32),
                                        in0=xn[:p, :F].bitcast(I32),
                                        scalar1=J_GMASK, scalar2=None,
                                        op0=ALU.bitwise_and))
        prog.op(lambda: v.tensor_scalar(out=xn[:p, :F].bitcast(I32),
                                        in0=xn[:p, :F].bitcast(I32),
                                        scalar1=J_HALF, scalar2=None,
                                        op0=ALU.subtract))
        prog.op(lambda: v.scalar_tensor_tensor(out=xn[:p, :F].bitcast(I32),
                                               in0=t2[:p, :F].bitcast(I32),
                                               scalar=J_B075,
                                               in1=xn[:p, :F].bitcast(I32),
                                               op0=ALU.is_gt, op1=ALU.mult))
        prog.op(lambda: v.tensor_scalar(out=t2[:p, :F].bitcast(I32),
                                        in0=t2[:p, :F].bitcast(I32),
                                        scalar1=J_B025, scalar2=J_HALF,
                                        op0=ALU.is_gt, op1=ALU.mult))
        prog.op(lambda: v.tensor_tensor(out=t2[:p, :F].bitcast(I32),
                                        in0=t2[:p, :F].bitcast(I32),
                                        in1=xn[:p, :F].bitcast(I32), op=ALU.add))
        prog.op(lambda: v.tensor_scalar(out=t2[:p, :F].bitcast(I32),
                                        in0=t2[:p, :F].bitcast(I32),
                                        scalar1=7, scalar2=None,
                                        op0=ALU.logical_shift_left))
        prog.op(lambda: v.tensor_tensor(out=t2[:p, :F], in0=t2[:p, :F],
                                        in1=sgn[:p, :F], op=ALU.mult))
        toks["qb"] = prog.tok("v")
    prog.section("v", vsec2)

    def gsec3(g):
        w3 = [toks["qb"]]
        if out_free_tok is not None:
            w3.append(out_free_tok)
        prog.op(lambda: g.tensor_tensor(out=blk16(out_bf_ap),
                                        in0=blk16(t2[:p, :F]),
                                        in1=bcast16(scale[:p, :nblk], p, nblk),
                                        op=ALU.mult), waits=w3)
        toks["out"] = prog.tok("g")
    prog.section("g", gsec3)
    return toks["out"], toks["out"]


def emit_identity(prog, ident_bf, io1, io2):
    toks = {}

    def gsec(g):
        prog.op(lambda: g.iota(io1[:, :], [[1, 128]], channel_multiplier=0,
                               allow_small_or_imprecise_dtypes=True))
        prog.op(lambda: g.iota(io2[:, :], [[0, 128]], channel_multiplier=1,
                               allow_small_or_imprecise_dtypes=True))
        toks["io"] = prog.tok("g")
    prog.section("g", gsec)

    def vsec(v):
        prog.op(lambda: v.tensor_tensor(out=ident_bf[:, :], in0=io1[:, :],
                                        in1=io2[:, :], op=ALU.is_equal),
                waits=[toks["io"]])
        toks["id"] = prog.tok("v")
    prog.section("v", vsec)
    return toks["id"]


def build_wprep(rows_g, rows_d, ncast_cols):
    prog = Prog("wprep")
    wg = prog.dram("wg_s", [rows_g, 1024], F32, kind="ExternalInput")
    wu = prog.dram("wu_s", [rows_g, 1024], F32, kind="ExternalInput")
    wd = prog.dram("wd_s", [rows_d, 2816], F32, kind="ExternalInput")
    o_wg = prog.dram("o_wgqT", [1024, rows_g], BF16, kind="ExternalOutput")
    o_wu = prog.dram("o_wuqT", [1024, rows_g], BF16, kind="ExternalOutput")
    o_wd = prog.dram("o_wdqT", [2816, rows_d], BF16, kind="ExternalOutput")
    cin = prog.dram("cast_in", [128, ncast_cols], F32, kind="ExternalInput")
    cout = prog.dram("o_cast", [128, ncast_cols], BF16, kind="ExternalOutput")

    ident = prog.sbuf("ident", [128, 128], BF16)
    io1 = prog.sbuf("id_io1", [128, 128], I32)
    io2 = prog.sbuf("id_io2", [128, 128], I32)
    id_tok = emit_identity(prog, ident, io1, io2)

    scr = make_quant_scratch(prog, "w", 128, 2816)
    src = prog.sbuf("w_src", [128, 2816], F32)
    qbf = prog.sbuf("w_qbf", [128, 2816], BF16)
    tps = prog.psum("w_tp", [128, 128], BF16)
    tsb = prog.sbuf("w_tsb", [128, 128], BF16)

    jobs = [(wg, o_wg, rows_g, 1024), (wu, o_wu, rows_g, 1024),
            (wd, o_wd, rows_d, 2816)]

    prev_q = (None)
    src_free = ("v", 0)
    qbf_free = ("v", 0)
    store_done = ("dh", 0)
    for jn, (src_d, out_d, nrows, K) in enumerate(jobs):
        r0 = 0
        while r0 < nrows:
            p = min(128, nrows - r0)
            ld_tok = [None]

            def ysec(y, src_d=src_d, r0=r0, p=p, K=K, sf=src_free):
                ld_tok[0] = prog.dma(src[:p, :K], src_d.ap()[r0:r0 + p, :K],
                                     waits=(sf,))
            prog.section("y", ysec)

            q_tok, _ = emit_quant(prog, src[:p, :K], qbf[:p, :K], scr, p, K,
                                  ld_tok[0], scr_free_tok=qbf_free)
            src_free = q_tok

            nkc = K // 128
            last_cp = None
            for kc in range(nkc):
                def tsec(t, kc=kc, sd=store_done):
                    prog.op(lambda: t.transpose(tps[:, :],
                                                qbf[:, kc * 128:(kc + 1) * 128],
                                                ident[:, :]),
                            waits=[q_tok, id_tok, prog.tok("v")])
                prog.section("t", tsec)
                tr_tok = prog.tok("t")

                def vsec(v, sd=store_done):
                    prog.op(lambda: v.tensor_copy(out=tsb[:, :], in_=tps[:, :]),
                            waits=[tr_tok, sd])
                prog.section("v", vsec)
                cp_tok = prog.tok("v")
                last_cp = cp_tok

                def ysec2(y, kc=kc, out_d=out_d, r0=r0, p=p):
                    prog.dma(out_d.ap()[kc * 128:(kc + 1) * 128, r0:r0 + p],
                             tsb[:, 0:p], waits=(cp_tok,))
                prog.section("y", ysec2)
                store_done = ("dh", prog.cnt["dh"])
            qbf_free = last_cp
            qbf_free = tr_tok
            r0 += 128

    cast_sb = prog.sbuf("cast_sb", [128, 4096], F32)
    cast_ob = prog.sbuf("cast_ob", [128, 4096], BF16)
    c0 = 0
    cast_free = ("v", 0)
    cast_store = ("dh", 0)
    while c0 < ncast_cols:
        w = min(4096, ncast_cols - c0)
        ld = [None]

        def ysec3(y, c0=c0, w=w, cf=cast_free):
            ld[0] = prog.dma(cast_sb[:, :w], cin.ap()[:, c0:c0 + w], waits=(cf,))
        prog.section("y", ysec3)

        def vsec2(v, w=w, cs=cast_store):
            prog.op(lambda: v.tensor_copy(out=cast_ob[:, :w], in_=cast_sb[:, :w]),
                    waits=[ld[0], cs])
        prog.section("v", vsec2)
        cast_tok = prog.tok("v")
        cast_free = cast_tok

        def ysec4(y, c0=c0, w=w):
            prog.dma(cout.ap()[:, c0:c0 + w], cast_ob[:, :w], waits=(cast_tok,))
        prog.section("y", ysec4)
        cast_store = ("dh", prog.cnt["dh"])
        c0 += 4096

    def fin(y):
        y.wait_ge(prog.sem["dh"], prog.cnt["dh"])
        if prog.cnt["dg"]:
            y.wait_ge(prog.sem["dg"], prog.cnt["dg"])
    prog.section("y", fin)
    return prog.done()


def _collect_cast_weights(inp):
    names = ["L0_attn_in_proj_w", "L0_attn_dt_proj_w", "L0_attn_x_proj_w",
             "L0_attn_out_proj_w", "L1_attn_Wq", "L1_attn_Wk", "L1_attn_Wv",
             "L1_attn_Wo", "L1_ffn_w_gate", "L1_ffn_Wd", "L1_ffn_Wu",
             "L1_ffn_bu", "L1_attn_projection"]
    return [(n, np.ascontiguousarray(inp[n], dtype=np.float32)) for n in names]


def run_wprep(inp):
    arrs = _collect_cast_weights(inp)
    total = sum(a.size for _, a in arrs)
    per_core = -(-total // NCORES)
    per_core = -(-per_core // 128) * 128
    ncast_cols = per_core // 128
    rows_g = 2816 // NCORES
    rows_d = 1024 // NCORES
    nc0 = build_wprep(rows_g, rows_d, ncast_cols)

    flat = np.concatenate([a.reshape(-1) for _, a in arrs]).astype(np.float32)
    flat = np.concatenate([flat, np.zeros(per_core * NCORES - total, np.float32)])
    in_maps = []
    for i in range(NCORES):
        in_maps.append({
            "wg_s": np.ascontiguousarray(inp["L0_ffn_w_gate"][i * rows_g:(i + 1) * rows_g]),
            "wu_s": np.ascontiguousarray(inp["L0_ffn_w_up"][i * rows_g:(i + 1) * rows_g]),
            "wd_s": np.ascontiguousarray(inp["L0_ffn_w_down"][i * rows_d:(i + 1) * rows_d]),
            "cast_in": np.ascontiguousarray(
                flat[i * per_core:(i + 1) * per_core].reshape(128, ncast_cols)),
        })
    res = run_bass_kernel_spmd(nc0, in_maps, core_ids=list(range(NCORES)))
    wgqT = np.concatenate([res.results[i]["o_wgqT"] for i in range(NCORES)], axis=1)
    wuqT = np.concatenate([res.results[i]["o_wuqT"] for i in range(NCORES)], axis=1)
    wdqT = np.concatenate([res.results[i]["o_wdqT"] for i in range(NCORES)], axis=1)
    cflat = np.concatenate([res.results[i]["o_cast"].reshape(-1) for i in range(NCORES)])[:total]
    casts = {}
    off = 0
    for n, a in arrs:
        casts[n] = cflat[off:off + a.size].reshape(a.shape)
        off += a.size
    return wgqT, wuqT, wdqT, casts


_NC_CACHE = {}


def _get_nc(name, builder):
    if name not in _NC_CACHE:
        _NC_CACHE[name] = builder()
    return _NC_CACHE[name]


def kernel(**inp):
    return kernel_fused(**inp)


def emit_ffn(prog, io, halo=False):
    NBW = [512] * 5 + [256]
    x_d = io["x"]
    wg_d = io["wgqT"]
    wu_d = io["wuqT"]
    wd_d = io["wdqT"]
    o_d = io["o"]
    ident = io["_ident"]
    id_tok = io["_id_tok"]
    if halo:
        chunks = [(128, HALO + i * 128) for i in range(16)] + [(HALO, 0)]
    else:
        chunks = [(128, i * 128) for i in range(16)]
    TCn = len(chunks)

    xqT = prog.sb("xqT", [128, 8 * TCn * 128], BF16)
    hq = prog.sb("hq", [128, TCn * 2816], BF16)
    wd_sb = prog.sb("wd_sb", [128, 22 * 1024], BF16)
    xc = prog.sb("xc", [128, 1024])
    xln = prog.sb("xln", [128, 1024])
    sqs = xln
    xq = prog.sb("xq", [128, 1024], BF16)
    stat = {k: prog.sb("st_" + k, [128, 1]) for k in
            ("sum", "ssq", "mean", "ex2", "m2", "var", "rstd")}
    scr = make_quant_scratch(prog, "f", 128, 1024)
    scrh = scr
    wgc = prog.sb("wgc", [128, 8 * 512], BF16)
    wuc = prog.sb("wuc", [128, 8 * 512], BF16)
    hs = None
    hf = None
    hqT = None
    oc = xln
    ps_tp = prog.ps("ps_tp", [128, 128], BF16)
    ps_g = prog.ps("ps_g", [128, 512])
    ps_u = prog.ps("ps_u", [128, 512])
    ps_dn = prog.ps("ps_dn", [128, 1024])

    hs = xc[:, 0:512]
    hf = xc[:, 512:1024]
    hqT = wgc[:, 0:2816]

    def vini(v):
        prog.op(lambda: v.memset(xc[:, :], 0.0))
        prog.op(lambda: v.memset(xq[:, :], 0.0))
    prog.section("v", vini)

    def ldwd(y):
        prog.dma(wd_sb[:, :].rearrange("p (kc n) -> p kc n", n=1024),
                 wd_d.ap().rearrange("(kc p) n -> p kc n", p=128), sem="dwa")
    prog.section("y", ldwd)
    wd_tok = ("dwa", prog.cnt["dwa"])

    xc_free = ("v", 0)
    xq_free = ("t", 0)
    q_prev = ("g", 0)
    prev_aff2 = ("t", 0)
    for tc in range(TCn):
        p, r0 = chunks[tc]
        ld = [None]

        def yld(y, tc=tc, xf=xc_free, p=p, r0=r0):
            ld[0] = prog.dma(xc[:p, :], x_d.ap()[r0:r0 + p, :],
                             waits=(xf,), sem="dx")
        prog.section("y", yld)

        def ssec(s, tc=tc, pa=prev_aff2, p=p):
            prog.op(lambda: s.activation(out=sqs[:p, :], in_=xc[:p, :], func=AF.Square,
                                         accum_out=stat["ssq"][:p, :]),
                    waits=[ld[0], pa])
        prog.section("s", ssec)
        sq_tok = prog.tok("s")

        def vsec(v, tc=tc):
            prog.op(lambda: v.tensor_reduce(out=stat["sum"][:, :], in_=xc[:, :],
                                            axis=AX, op=ALU.add), waits=[ld[0]])
            prog.op(lambda: v.tensor_scalar(out=stat["mean"][:, :], in0=stat["sum"][:, :],
                                            scalar1=1.0 / D, scalar2=None, op0=ALU.mult))
            prog.op(lambda: v.tensor_scalar(out=stat["ex2"][:, :], in0=stat["ssq"][:, :],
                                            scalar1=1.0 / D, scalar2=None, op0=ALU.mult),
                    waits=[sq_tok])
            prog.op(lambda: v.tensor_tensor(out=stat["m2"][:, :], in0=stat["mean"][:, :],
                                            in1=stat["mean"][:, :], op=ALU.mult))
            prog.op(lambda: v.tensor_tensor(out=stat["var"][:, :], in0=stat["ex2"][:, :],
                                            in1=stat["m2"][:, :], op=ALU.subtract))
            prog.op(lambda: v.tensor_scalar(out=stat["var"][:, :], in0=stat["var"][:, :],
                                            scalar1=1e-5, scalar2=None, op0=ALU.add))
        prog.section("v", vsec)
        var_tok = prog.tok("v")

        def ssec2(s, tc=tc):
            prog.op(lambda: s.activation(out=stat["rstd"][:, :], in_=stat["var"][:, :],
                                         func=AF.Ln), waits=[var_tok])
            prog.op(lambda: s.activation(out=stat["rstd"][:, :], in_=stat["rstd"][:, :],
                                         func=AF.Exp, scale=-0.5))
        prog.section("s", ssec2)
        rstd_tok = prog.tok("s")

        def vsec2(v, tc=tc):
            prog.op(lambda: v.tensor_scalar(out=xln[:, :], in0=xc[:, :],
                                            scalar1=stat["mean"][:, :],
                                            scalar2=stat["rstd"][:, :],
                                            op0=ALU.subtract, op1=ALU.mult),
                    waits=[rstd_tok])
        prog.section("v", vsec2)
        xln_tok = prog.tok("v")
        xc_free = xln_tok

        q_tok, _ = emit_quant(prog, xln[:p, :], xq[:p, :], scr, p, 1024, xln_tok,
                              scr_free_tok=q_prev, out_free_tok=xq_free)
        q_prev = q_tok

        for kc in range(8):
            def tsec(t, kc=kc, tc=tc):
                prog.op(lambda: t.transpose(ps_tp[:, :],
                                            xq[:, kc * 128:(kc + 1) * 128],
                                            ident[:, :]),
                        waits=[q_tok, id_tok, prog.tok("v")])
            prog.section("t", tsec)
            tr_tok = prog.tok("t")

            def vsec3(v, kc=kc, tc=tc, p=p):
                prog.op(lambda: v.tensor_copy(
                    out=xqT[:, (kc * TCn + tc) * 128:(kc * TCn + tc) * 128 + p],
                    in_=ps_tp[:, 0:p]), waits=[tr_tok])
            prog.section("v", vsec3)
        xq_free = prog.tok("t")
        prev_aff2 = prog.tok("t")
    xqT_tok = prog.tok("v")

    w_free = ("t", 0)
    hq_toks = {}
    ps_g_free = ("v", 0)
    ps_u_free = ("v", 0)
    n0 = 0
    for nb, NW in enumerate(NBW):
        wld = [None, None]

        def yw(y, nb=nb, NW=NW, n0=n0, wf=w_free):
            wld[0] = prog.dma(wgc[:, :8 * NW].rearrange("p (kc n) -> p kc n", n=NW),
                              wg_d.ap()[:, n0:n0 + NW].rearrange("(kc p) n -> p kc n", p=128),
                              waits=(wf,), sem="dwb")
            wld[1] = prog.dma(wuc[:, :8 * NW].rearrange("p (kc n) -> p kc n", n=NW),
                              wu_d.ap()[:, n0:n0 + NW].rearrange("(kc p) n -> p kc n", p=128),
                              sem="dwb")
        prog.section("y", yw)

        for tc in range(TCn):
            p, r0 = chunks[tc]
            def tmm(t, tc=tc, NW=NW, pg=ps_g_free, pu=ps_u_free, p=p):
                prog.wait(wld[0]); prog.wait(wld[1])
                prog.wait(xqT_tok)
                prog.wait(pg); prog.wait(pu)
                for kc in range(8):
                    lhs = xqT[:, (kc * TCn + tc) * 128:(kc * TCn + tc) * 128 + p]
                    prog.op(lambda kc=kc, lhs=lhs: t.matmul(ps_g[:p, :NW], lhs,
                                                   wgc[:, kc * NW:(kc + 1) * NW],
                                                   start=(kc == 0), stop=(kc == 7)))
                for kc in range(8):
                    lhs = xqT[:, (kc * TCn + tc) * 128:(kc * TCn + tc) * 128 + p]
                    prog.op(lambda kc=kc, lhs=lhs: t.matmul(ps_u[:p, :NW], lhs,
                                                   wuc[:, kc * NW:(kc + 1) * NW],
                                                   start=(kc == 0), stop=(kc == 7)))
            prog.section("t", tmm)
            mm_tok = prog.tok("t")

            def shsec(s, NW=NW, p=p):
                prog.op(lambda: s.activation(out=hs[:p, :NW], in_=ps_g[:p, :NW],
                                             func=AF.Silu), waits=[mm_tok])
            prog.section("s", shsec)
            hs_tok = prog.tok("s")

            def vh(v, NW=NW, p=p):
                prog.op(lambda: v.tensor_tensor(out=hf[:p, :NW], in0=hs[:p, :NW],
                                                in1=ps_u[:p, :NW], op=ALU.mult),
                        waits=[hs_tok])
            prog.section("v", vh)
            hf_tok = prog.tok("v")
            ps_g_free = hs_tok
            ps_u_free = hf_tok

            hq_out = hq[:p, tc * 2816 + n0: tc * 2816 + n0 + NW]
            q2_tok, _ = emit_quant(prog, hf[:p, :NW], hq_out, scrh, p, NW,
                                   hf_tok, scr_free_tok=hq_toks.get("prev"))
            hq_toks["prev"] = q2_tok
        w_free = prog.tok("t")
        n0 += NW
    hq_done = hq_toks["prev"]

    o_store = [("dso", 0)]
    ps_dn_free = ("v", 0)
    hqT_free = ("t", 0)
    for tc in range(TCn):
        p, r0 = chunks[tc]
        cps = []
        for kc in range(22):
            def tsec2(t, kc=kc, tc=tc):
                prog.op(lambda: t.transpose(
                    ps_tp[:, :], hq[:, tc * 2816 + kc * 128: tc * 2816 + kc * 128 + 128],
                    ident[:, :]), waits=[hq_done, prog.tok("v")])
            prog.section("t", tsec2)
            tr2 = prog.tok("t")

            def vsec4(v, kc=kc):
                prog.op(lambda: v.tensor_copy(out=hqT[:, kc * 128:(kc + 1) * 128],
                                              in_=ps_tp[:, :]), waits=[tr2])
            prog.section("v", vsec4)
        hqT_tok = prog.tok("v")

        def tdn(t, tc=tc, pf=ps_dn_free, p=p):
            prog.wait(hqT_tok)
            prog.wait(wd_tok)
            prog.wait(pf)
            for half in range(2):
                for kc in range(22):
                    prog.op(lambda kc=kc, half=half: t.matmul(
                        ps_dn[:p, half * 512:(half + 1) * 512],
                        hqT[:, kc * 128:kc * 128 + p],
                        wd_sb[:, kc * 1024 + half * 512: kc * 1024 + half * 512 + 512],
                        start=(kc == 0), stop=(kc == 21)))
        prog.section("t", tdn)
        dn_tok = prog.tok("t")

        ld2 = [None]

        def yld2(y, tc=tc, p=p, r0=r0):
            ld2[0] = prog.dma(xc[:p, :], x_d.ap()[r0:r0 + p, :],
                              waits=(prog.tok("v"),), sem="dx")
        prog.section("y", yld2)

        def vev(v, tc=tc, p=p, ost=o_store[0]):
            prog.wait(ost)
            prog.op(lambda: v.scalar_tensor_tensor(out=oc[:p, :], in0=ps_dn[:p, :1024],
                                                   scalar=0.5, in1=xc[:p, :],
                                                   op0=ALU.mult, op1=ALU.add),
                    waits=[dn_tok, ld2[0]])
        prog.section("v", vev)
        ev_tok = prog.tok("v")
        ps_dn_free = ev_tok
        xc_free = ev_tok

        def yst(y, tc=tc, p=p, r0=r0):
            o_store[0] = prog.dma(o_d.ap()[r0:r0 + p, :], oc[:p, :], waits=(ev_tok,),
                                  sem="dso")
        prog.section("y", yst)

    return


def run_ffn(nc, x_full, wgqT, wuqT, wdqT):
    in_maps = []
    for i in range(NCORES):
        in_maps.append({
            "x_s": np.ascontiguousarray(x_full[i * T_CORE:(i + 1) * T_CORE]),
            "wgqT": wgqT, "wuqT": wuqT, "wdqT": wdqT,
        })
    res = run_bass_kernel_spmd(nc, in_maps, core_ids=list(range(NCORES)))
    return np.concatenate([res.results[i]["o_x"] for i in range(NCORES)], axis=0)


HALO = 3
XCOLS = 2176
LT = 512
NT = 4


def emit_mamba_a(prog, io):
    x_d = io["x"]
    msk_d = io["halo_mask"]
    ip_d = io["inproj"]
    dt_d = io["dtw"]
    xp_d = io["xpwB"]
    cw_d = io["convw_r"]
    cb_d = io["convb_r"]
    db_d = io["dtb_r"]
    al_d = io["alog_r"]
    dp_d = io["dparam_r"]
    lg_d = io["lng_r"]
    lb_d = io["lnb_r"]
    y1_d = io["y1"]
    dc_d = io["dcum"]
    xt_d = io["xn2T"]
    he_d = io["hend"]
    pc_d = io["pcore"]
    ms_d = io["mstat"]
    rs_d = io["rstat"]
    bbounce = io["b_bounce"]
    ident = io["_ident"]
    id_tok = io["_id_tok"]

    xn2T = prog.sb("xn2T", [128, 8 * XCOLS], BF16)
    u_f = prog.sb("u_f", [128, 16 * LT])
    u_bf = prog.sb("u_bf", [128, 16 * LT], BF16)
    Brep = prog.sb("Brep", [128, 16 * LT])
    A_sb = prog.sb("A_sb", [128, 256])
    st_sb = prog.sb("st_sb", [128, 256])
    Dst = prog.sb("Dst", [128, 16])
    gam = prog.sb("gam", [128, 1024])
    bet = prog.sb("bet", [128, 1024])
    cw_sb = prog.sb("cw_sb", [128, 16 * 4])
    cb_sb = prog.sb("cb_sb", [128, 16])
    db_sb = prog.sb("db_sb", [128, 16])
    dp_sb = prog.sb("dp_sb", [128, 16])
    msk = prog.sb("msk", [128, HALO])
    xc = prog.sb("xc", [128, 1024])
    xl2 = prog.sb("xl2", [128, 1024])
    xq = prog.sb("xq", [128, 1024], BF16)
    stat = {k: prog.sb("stm_" + k, [128, 1]) for k in
            ("sum", "ssq", "mean", "ex2", "m2", "var", "rstd")}
    stat_m = prog.sb("stat_m", [128, 16])
    stat_r = prog.sb("stat_r", [128, 16])
    uraw = prog.sb("uraw", [128, LT + 4])
    wrot = prog.sb("wrot", [128, 16 * 128], BF16)
    wrot2 = prog.sb("wrot2", [128, 16 * 128], BF16)
    delta = prog.sb("delta", [128, LT])
    delta2 = prog.sb("delta2", [128, LT])
    dscr2 = prog.sb("dscr2", [128, LT])
    udc = prog.sb("udc", [128, LT])
    dA_b = prog.sb("dA_b", [128, 7 * LT])
    b_b = prog.sb("b_b", [128, 7 * LT])
    h_b = prog.sb("h_b", [128, 16 * LT], BF16)
    dcum_t = prog.sb("dcum_t", [128, LT])
    zeros = prog.sb("zeros", [128, LT])
    y1c = prog.sb("y1c", [128, LT])
    bsb = prog.sb("bsb", [16, LT])

    ps_a = prog.ps("ps_a", [128, 512])
    ps_a2 = prog.ps("ps_a2", [128, 512])
    ps_h = prog.ps("ps_h", [128, 16])
    ps_y = prog.ps("ps_y", [128, 512])
    ps_x = prog.ps("ps_x", [16, 512])
    ps_tp = prog.ps("ps_tp", [128, 128], BF16)

    def pld(y):
        prog.dma(gam[:, :], lg_d.ap(), sem="dm")
        prog.dma(bet[:, :], lb_d.ap(), sem="dm")
        prog.dma(cw_sb[:, :], cw_d.ap(), sem="dm")
        prog.dma(cb_sb[:, :], cb_d.ap(), sem="dm")
        prog.dma(db_sb[:, :], db_d.ap(), sem="dm")
        prog.dma(dp_sb[:, :], dp_d.ap(), sem="dm")
        prog.dma(msk[:, :], msk_d.ap(), sem="dm")
        prog.dma(A_sb[:, :], al_d.ap(), sem="dm")
    prog.section("y", pld)
    pld_tok = ("dm", prog.cnt["dm"])

    def vinit(v):
        prog.op(lambda: v.memset(zeros[:, :], 0.0))
        prog.op(lambda: v.memset(st_sb[:, :], 0.0))
        prog.op(lambda: v.memset(Dst[:, :], 0.0))
        prog.op(lambda: v.memset(xn2T[:, :], 0.0))
    prog.section("v", vinit)

    def sA(s):
        prog.op(lambda: s.activation(out=A_sb[:, :], in_=A_sb[:, :], func=AF.Exp),
                waits=[pld_tok])
        prog.op(lambda: s.activation(out=A_sb[:, :], in_=A_sb[:, :], func=AF.Copy,
                                     scale=-1.0))
    prog.section("s", sA)
    A_tok = prog.tok("s")

    xc_free = ("g", 0)
    q_free = ("t", 0)
    prev_aff = ("g", 0)
    for tc in range(17):
        if tc < 16:
            p = 128
            rows = (HALO + tc * 128, HALO + tc * 128 + 128)
            colbase = 3 + tc * 128
        else:
            p = HALO
            rows = (0, HALO)
            colbase = 0
        ld = [None]

        def yld(y, rows=rows, p=p, xf=xc_free):
            ld[0] = prog.dma(xc[:p, :], x_d.ap()[rows[0]:rows[1], :], waits=(xf,),
                             sem="dx")
        prog.section("y", yld)

        def ssec(s, p=p, pa=prev_aff):
            prog.op(lambda: s.activation(out=xl2[:p, :], in_=xc[:p, :], func=AF.Square,
                                         accum_out=stat["ssq"][:p, :]),
                    waits=[ld[0], pa])
        prog.section("s", ssec)
        sq_tok = prog.tok("s")

        def vsec(v, p=p, tc=tc):
            prog.op(lambda: v.tensor_reduce(out=stat["sum"][:p, :], in_=xc[:p, :],
                                            axis=AX, op=ALU.add), waits=[ld[0]])
            prog.op(lambda: v.tensor_scalar(out=stat["mean"][:p, :], in0=stat["sum"][:p, :],
                                            scalar1=1.0 / D, scalar2=None, op0=ALU.mult))
            prog.op(lambda: v.tensor_scalar(out=stat["ex2"][:p, :], in0=stat["ssq"][:p, :],
                                            scalar1=1.0 / D, scalar2=None, op0=ALU.mult),
                    waits=[sq_tok])
            prog.op(lambda: v.tensor_tensor(out=stat["m2"][:p, :], in0=stat["mean"][:p, :],
                                            in1=stat["mean"][:p, :], op=ALU.mult))
            prog.op(lambda: v.tensor_tensor(out=stat["var"][:p, :], in0=stat["ex2"][:p, :],
                                            in1=stat["m2"][:p, :], op=ALU.subtract))
            prog.op(lambda: v.tensor_scalar(out=stat["var"][:p, :], in0=stat["var"][:p, :],
                                            scalar1=1e-5, scalar2=None, op0=ALU.add))
        prog.section("v", vsec)
        var_tok = prog.tok("v")

        def ssec2(s, p=p):
            prog.op(lambda: s.activation(out=stat["rstd"][:p, :], in_=stat["var"][:p, :],
                                         func=AF.Ln), waits=[var_tok])
            prog.op(lambda: s.activation(out=stat["rstd"][:p, :], in_=stat["rstd"][:p, :],
                                         func=AF.Exp, scale=-0.5))
        prog.section("s", ssec2)
        rstd_tok = prog.tok("s")

        def vsec2(v, p=p, tc=tc):
            prog.op(lambda: v.tensor_scalar(out=xl2[:p, :], in0=xc[:p, :],
                                            scalar1=stat["mean"][:p, :],
                                            scalar2=stat["rstd"][:p, :],
                                            op0=ALU.subtract, op1=ALU.mult),
                    waits=[rstd_tok])
            if tc < 16:
                prog.op(lambda: v.tensor_copy(out=stat_m[:, tc:tc + 1],
                                              in_=stat["mean"][:, :]))
                prog.op(lambda: v.tensor_copy(out=stat_r[:, tc:tc + 1],
                                              in_=stat["rstd"][:, :]))
        prog.section("v", vsec2)
        xl_tok = prog.tok("v")
        xc_free = xl_tok

        def ssec3(s, p=p, qf=q_free):
            prog.op(lambda: s.activation(out=xq[:p, :], in_=xl2[:p, :], func=AF.Square,
                                         accum_out=stat["ssq"][:p, :]),
                    waits=[xl_tok, qf])
        prog.section("s", ssec3)
        sq2_tok = prog.tok("s")

        def vsec2b(v, p=p):
            prog.op(lambda: v.tensor_reduce(out=stat["sum"][:p, :], in_=xl2[:p, :],
                                            axis=AX, op=ALU.add), waits=[xl_tok])
            prog.op(lambda: v.tensor_scalar(out=stat["mean"][:p, :], in0=stat["sum"][:p, :],
                                            scalar1=1.0 / D, scalar2=None, op0=ALU.mult))
            prog.op(lambda: v.tensor_scalar(out=stat["ex2"][:p, :], in0=stat["ssq"][:p, :],
                                            scalar1=1.0 / D, scalar2=None, op0=ALU.mult),
                    waits=[sq2_tok])
            prog.op(lambda: v.tensor_tensor(out=stat["m2"][:p, :], in0=stat["mean"][:p, :],
                                            in1=stat["mean"][:p, :], op=ALU.mult))
            prog.op(lambda: v.tensor_tensor(out=stat["var"][:p, :], in0=stat["ex2"][:p, :],
                                            in1=stat["m2"][:p, :], op=ALU.subtract))
            prog.op(lambda: v.tensor_scalar(out=stat["var"][:p, :], in0=stat["var"][:p, :],
                                            scalar1=1e-5, scalar2=None, op0=ALU.add))
        prog.section("v", vsec2b)
        var2_tok = prog.tok("v")

        def ssec4(s, p=p):
            prog.op(lambda: s.activation(out=stat["rstd"][:p, :], in_=stat["var"][:p, :],
                                         func=AF.Ln), waits=[var2_tok])
            prog.op(lambda: s.activation(out=stat["rstd"][:p, :], in_=stat["rstd"][:p, :],
                                         func=AF.Exp, scale=-0.5))
        prog.section("s", ssec4)
        rstd2_tok = prog.tok("s")

        def vsec2c(v, p=p):
            prog.op(lambda: v.tensor_scalar(out=xl2[:p, :], in0=xl2[:p, :],
                                            scalar1=stat["mean"][:p, :],
                                            scalar2=stat["rstd"][:p, :],
                                            op0=ALU.subtract, op1=ALU.mult),
                    waits=[rstd2_tok])
        prog.section("v", vsec2c)

        def gsec(g, p=p):
            prog.op(lambda: g.tensor_tensor(out=xl2[:p, :], in0=xl2[:p, :],
                                            in1=gam[:p, :], op=ALU.mult),
                    waits=[prog.tok("v"), pld_tok])
            prog.op(lambda: g.tensor_tensor(out=xq[:p, :], in0=xl2[:p, :],
                                            in1=bet[:p, :], op=ALU.add))
        prog.section("g", gsec)
        aff_tok = prog.tok("g")
        prev_aff = aff_tok

        for kc in range(8):
            def tsec(t, kc=kc, p=p):
                prog.op(lambda: t.transpose(ps_tp[:, :],
                                            xq[:, kc * 128:(kc + 1) * 128],
                                            ident[:, :]),
                        waits=[aff_tok, id_tok, prog.tok("v")])
            prog.section("t", tsec)
            tr_tok = prog.tok("t")

            def vsec3(v, kc=kc, colbase=colbase, p=p):
                prog.op(lambda: v.tensor_copy(
                    out=xn2T[:, kc * XCOLS + colbase: kc * XCOLS + colbase + p],
                    in_=ps_tp[:, 0:p]), waits=[tr_tok])
            prog.section("v", vsec3)
        q_free = prog.tok("t")
    xn2T_tok = prog.tok("v")

    def ydump(y):
        prog.dma(xt_d.ap(), xn2T[:, :], waits=(xn2T_tok,), sem="dso")
        prog.dma(ms_d.ap(), stat_m[:, :], sem="dso")
        prog.dma(rs_d.ap(), stat_r[:, :], sem="dso")
    prog.section("y", ydump)

    bb_store = [0]
    w_free = ("t", 0)
    uraw_free = ("s", 0)
    brep_free = ("v", 0)
    dc_store = [("dso", 0)]
    y1_store = [("dh", 0)]
    wrot_free = {0: ("t", 0), 1: ("t", 0)}
    ps_free = {0: [], 1: []}
    dscr2_box = [("v", 0)]
    delta_war = [("s", 0), ("s", 0)]
    prev_b_tok = ("g", 0)
    prev_b_box = [("g", 0)]
    scan_war_s = [("v", 0)] * 7
    mm_war = [("t", 0)] * 16
    st_war = [("g", 0)] * 16
    stc_war = [("g", 0)]
    vy_war = [("v", 0)]
    for tt in range(NT):
        for nc_ in range(16):
            wld = [None]
            wsel_f = wrot if (nc_ % 2 == 0) else wrot2

            def yw(y, nc_=nc_, wsel_f=wsel_f, wf=wrot_free[nc_ % 2]):
                wld[0] = prog.dma(
                    wsel_f[:, 0:8 * 128].rearrange("p (kc n) -> p kc n", n=128),
                    ip_d.ap()[:, nc_ * 128:(nc_ + 1) * 128]
                        .rearrange("(kc p) n -> p kc n", p=128),
                    waits=(wf,), sem="dwa")
            prog.section("y", yw)

            def tmm(t, nc_=nc_, tt=tt, wsel_f=wsel_f, uf=uraw_free):
                prog.wait(wld[0])
                prog.wait(xn2T_tok)
                prog.wait(uf)
                for kc in range(8):
                    prog.op(lambda kc=kc: t.matmul(
                        ps_a[:, :512],
                        wsel_f[:, kc * 128:(kc + 1) * 128],
                        xn2T[:, kc * XCOLS + tt * LT: kc * XCOLS + tt * LT + 512],
                        start=(kc == 0), stop=(kc == 7)))
                for kc in range(8):
                    prog.op(lambda kc=kc: t.matmul(
                        ps_h[:, 0:HALO],
                        wsel_f[:, kc * 128:(kc + 1) * 128],
                        xn2T[:, kc * XCOLS + tt * LT + 512: kc * XCOLS + tt * LT + 512 + HALO],
                        start=(kc == 0), stop=(kc == 7)))
            prog.section("t", tmm)
            mm_tok = prog.tok("t")
            wrot_free[nc_ % 2] = mm_tok
            w_free = mm_tok

            def vev(v, nc_=nc_, tt=tt):
                prog.op(lambda: v.tensor_copy(out=uraw[:, 0:512], in_=ps_a[:, :512]),
                        waits=[mm_tok])
                prog.op(lambda: v.tensor_copy(out=uraw[:, 512:512 + HALO],
                                              in_=ps_h[:, 0:HALO]))
                if tt == 0:
                    prog.op(lambda: v.tensor_tensor(out=uraw[:, 0:HALO],
                                                    in0=uraw[:, 0:HALO],
                                                    in1=msk[:, :], op=ALU.mult))
            prog.section("v", vev)
            ur_tok = prog.tok("v")

            def vconv(v, nc_=nc_, pb=prev_b_tok):
                prog.op(lambda: v.tensor_scalar(out=udc[:, 0:LT], in0=uraw[:, 0:LT],
                                                scalar1=cw_sb[:, nc_ * 4:nc_ * 4 + 1],
                                                scalar2=None, op0=ALU.mult),
                        waits=[ur_tok, pld_tok, pb])
                for k in range(1, 4):
                    prog.op(lambda k=k: v.scalar_tensor_tensor(
                        out=udc[:, 0:LT], in0=uraw[:, k:k + LT],
                        scalar=cw_sb[:, nc_ * 4 + k:nc_ * 4 + k + 1],
                        in1=udc[:, 0:LT], op0=ALU.mult, op1=ALU.add))
            prog.section("v", vconv)
            cv_tok = prog.tok("v")

            def ssil(s, nc_=nc_):
                prog.op(lambda: s.activation(out=u_f[:, nc_ * LT:(nc_ + 1) * LT],
                                             in_=udc[:, 0:LT], func=AF.Silu,
                                             bias=cb_sb[:, nc_:nc_ + 1]),
                        waits=[cv_tok])
            prog.section("s", ssil)
            sil_tok = prog.tok("s")
            uraw_free = sil_tok

            def vcast(v, nc_=nc_):
                prog.op(lambda: v.tensor_copy(out=u_bf[:, nc_ * LT:(nc_ + 1) * LT],
                                              in_=u_f[:, nc_ * LT:(nc_ + 1) * LT]),
                        waits=[sil_tok])
            prog.section("v", vcast)
        ubf_tok = prog.tok("v")

        xpld = [None]

        def yxp(y, wf=w_free):
            xpld[0] = prog.dma(wrot[:, 0:16 * 16].rearrange("p (kc n) -> p kc n", n=16),
                               xp_d.ap().rearrange("(kc p) n -> p kc n", p=128),
                               waits=(wf,), sem="dwa")
        prog.section("y", yxp)

        def txp(t, tt=tt):
            prog.wait(xpld[0])
            prog.wait(ubf_tok)
            for kc in range(16):
                prog.op(lambda kc=kc: t.matmul(
                    ps_x[:16, :512], wrot[:, kc * 16:(kc + 1) * 16],
                    u_bf[:, kc * LT:(kc + 1) * LT],
                    start=(kc == 0), stop=(kc == 15)))
        prog.section("t", txp)
        xp_tok = prog.tok("t")
        w_free = xp_tok
        wrot_free[0] = xp_tok

        def vxp(v, bf_=brep_free, bt=None):
            prog.op(lambda: v.tensor_copy(out=bsb[:16, :], in_=ps_x[:16, :512]),
                    waits=[xp_tok, bf_, ("dwb", bb_store[0])])
        prog.section("v", vxp)
        bsb_tok = prog.tok("v")

        def ybb(y, tt=tt):
            tok = prog.dma(bbounce.ap(), bsb[:16, :], waits=(bsb_tok,), sem="dwb")
            bb_store[0] = tok[1]
            for n in range(16):
                prog.dma(Brep[:, n * LT:(n + 1) * LT],
                         bbounce.ap()[n:n + 1, :].broadcast_to([128, LT]),
                         waits=(tok,), sem="dwb")
        prog.section("y", ybb)
        brep_tok = ("dwb", prog.cnt["dwb"])
        wrot_free[0] = w_free

        P1 = {}

        def phase1a(k):
            wsel = wrot if (k % 2 == 0) else wrot2
            pa = ps_a if (k % 2 == 0) else ps_a2
            dtl = [None]

            def ydt(y, k=k, wsel=wsel, wf=wrot_free[k % 2]):
                dtl[0] = prog.dma(
                    wsel[:, :].rearrange("p (kc n) -> p kc n", n=128),
                    dt_d.ap()[:, k * 128:(k + 1) * 128]
                        .rearrange("(kc p) n -> p kc n", p=128),
                    waits=(wf,), sem="dwa")
            prog.section("y", ydt)

            def tdt(t, k=k, wsel=wsel, pa=pa, paf=ps_free[k % 2]):
                prog.wait(dtl[0])
                prog.wait(ubf_tok)
                for w in paf:
                    prog.wait(w)
                for kc in range(16):
                    prog.op(lambda kc=kc: t.matmul(
                        pa[:, :512], wsel[:, kc * 128:(kc + 1) * 128],
                        u_bf[:, kc * LT:(kc + 1) * LT],
                        start=(kc == 0), stop=(kc == 15)))
            prog.section("t", tdt)
            P1[k] = {"dt_tok": prog.tok("t"), "pa": pa,
                     "delta": delta if (k % 2 == 0) else delta2}
            wrot_free[k % 2] = P1[k]["dt_tok"]

        def phase1b(k):
            st = P1[k]
            pa = st["pa"]
            dl = st["delta"]

            def sdelta(s, k=k, df=dscr2_box[0]):
                prog.op(lambda: s.activation(out=dscr2[:, :], in_=pa[:, :512],
                                             func=AF.Abs,
                                             bias=db_sb[:, k:k + 1]),
                        waits=[st["dt_tok"], pld_tok, df], self_wait=False)
                prog.op(lambda: s.activation(out=dscr2[:, :], in_=dscr2[:, :],
                                             func=AF.Exp, scale=-1.0))
                prog.op(lambda: s.activation(out=dscr2[:, :], in_=dscr2[:, :],
                                             func=AF.Ln, bias=1.0))
            prog.section("s", sdelta)
            del_tok = prog.tok("s")
            abs_tok = ("s", del_tok[1] - 2)

            def vd(v, k=k, dw=delta_war[k % 2]):
                prog.op(lambda: v.tensor_scalar(out=dl[:, :], in0=pa[:, :512],
                                                scalar1=db_sb[:, k:k + 1],
                                                scalar2=0.0, op0=ALU.add,
                                                op1=ALU.max),
                        waits=[st["dt_tok"], pld_tok, dw], self_wait=False)
                prog.op(lambda: v.tensor_tensor(out=dl[:, :], in0=dl[:, :],
                                                in1=dscr2[:, :], op=ALU.add),
                        waits=[del_tok])
            prog.section("v", vd)
            st["d2_tok"] = prog.tok("v")
            ps_free[k % 2] = [abs_tok, st["d2_tok"]]
            dscr2_box[0] = st["d2_tok"]
            return st["d2_tok"]

        def phase2(cc, tt):
            st = P1[cc]
            dl = st["delta"]

            def vsp2(v, cc=cc, dct=dc_store[0], pb=prev_b_tok):
                prog.op(lambda: v.tensor_tensor(out=udc[:, 0:LT],
                                                in0=u_f[:, cc * LT:(cc + 1) * LT],
                                                in1=dl[:, :], op=ALU.mult),
                        waits=[pb], self_wait=True)
                prog.op(lambda: v.tensor_tensor_scan(out=dcum_t[:, :], data0=dl[:, :],
                                                     data1=zeros[:, :],
                                                     initial=Dst[:, cc:cc + 1],
                                                     op0=ALU.add, op1=ALU.add),
                        waits=[dct], self_wait=False)
                prog.op(lambda: v.tensor_copy(out=Dst[:, cc:cc + 1],
                                              in_=dcum_t[:, LT - 1:LT]))
            prog.section("v", vsp2)
            ud_tok = prog.tok("v")
            dsc_tok = ud_tok

            def ydc(y, cc=cc, tt=tt):
                dc_store[0] = prog.dma(
                    dc_d.ap()[cc * 128:(cc + 1) * 128, tt * LT:(tt + 1) * LT],
                    dcum_t[:, :], waits=(dsc_tok,), sem="dso")
            prog.section("y", ydc)

            scan_toks = [None] * 16
            dA_last = None
            for n in range(16):
                slot = n % 7

                def sdA(s, cc=cc, n=n, slot=slot, ws=scan_war_s[slot],
                        dlt=st["d2_tok"]):
                    prog.op(lambda: s.activation(
                        out=dA_b[:, slot * LT:(slot + 1) * LT],
                        in_=dl[:, :], func=AF.Exp,
                        scale=A_sb[:, cc * 16 + n:cc * 16 + n + 1]),
                        waits=[A_tok, ws, dlt], self_wait=False)
                prog.section("s", sdA)
                dA_tok = prog.tok("s")
                dA_last = dA_tok

                def gb(g, n=n, slot=slot, ws=scan_war_s[slot], ut=ud_tok):
                    prog.op(lambda: g.tensor_tensor(
                        out=b_b[:, slot * LT:(slot + 1) * LT],
                        in0=udc[:, 0:LT], in1=Brep[:, n * LT:(n + 1) * LT],
                        op=ALU.mult),
                        waits=[ut, brep_tok, ws], self_wait=False)
                prog.section("g", gb)
                b_tok = prog.tok("g")

                def vscan(v, cc=cc, n=n, slot=slot, dAt=dA_tok, bt=b_tok,
                          wm=mm_war[n], wst=st_war[cc], wg=stc_war[0]):
                    prog.op(lambda: v.tensor_tensor_scan(
                        out=h_b[:, n * LT:(n + 1) * LT],
                        data0=dA_b[:, slot * LT:(slot + 1) * LT],
                        data1=b_b[:, slot * LT:(slot + 1) * LT],
                        initial=st_sb[:, cc * 16 + n:cc * 16 + n + 1],
                        op0=ALU.mult, op1=ALU.add),
                        waits=[dAt, bt, wm, wst, wg], self_wait=False)
                prog.section("v", vscan)
                scan_toks[n] = prog.tok("v")
                scan_war_s[slot] = scan_toks[n]

                def ty(t, n=n, st2=scan_toks[n], vyw=vy_war[0]):
                    prog.wait(st2)
                    if n == 0:
                        prog.wait(vyw)
                    prog.op(lambda: t.matmul(ps_y[:, :512], ident[:, :],
                                             h_b[:, n * LT:(n + 1) * LT],
                                             start=(n == 0), stop=(n == 15)))
                prog.section("t", ty)
                mm_war[n] = prog.tok("t")
            ysum_tok = prog.tok("t")
            prev_b_box[0] = b_tok
            delta_war[cc % 2] = dA_last

            def gstc(g, cc=cc, st15=scan_toks[15]):
                src = h_b.rearrange("p (n l) -> p n l", l=LT)[:, :, LT - 1:LT]
                prog.op(lambda: g.tensor_copy(
                    out=st_sb[:, cc * 16:(cc + 1) * 16].unsqueeze(2),
                    in_=src), waits=[st15], self_wait=False)
            prog.section("g", gstc)
            stc_war[0] = prog.tok("g")
            st_war[cc] = stc_war[0]

            def vy(v, cc=cc, tt=tt, yst=y1_store[0]):
                prog.wait(yst)
                prog.op(lambda: v.scalar_tensor_tensor(
                    out=y1c[:, :], in0=u_f[:, cc * LT:(cc + 1) * LT],
                    scalar=dp_sb[:, cc:cc + 1], in1=ps_y[:, :512],
                    op0=ALU.mult, op1=ALU.add), waits=[ysum_tok])
            prog.section("v", vy)
            y1_tok = prog.tok("v")
            vy_war[0] = y1_tok

            def yy1(y, cc=cc, tt=tt):
                y1_store[0] = prog.dma(
                    y1_d.ap()[cc * 128:(cc + 1) * 128, tt * LT:(tt + 1) * LT],
                    y1c[:, :], waits=(y1_tok,), sem="dh")
            prog.section("y", yy1)

        for k in range(17):
            if k < 16:
                phase1a(k)
            if k >= 1:
                phase2(k - 1, tt)
            if k < 16:
                phase1b(k)
            prev_b_tok = prev_b_box[0]
        w_free = wrot_free[0]
        brep_free = prog.tok("g")

    def sfin(s):
        for cc in range(16):
            prog.op(lambda cc=cc: s.activation(out=dA_b[:, cc * 16:cc * 16 + 16],
                                               in_=A_sb[:, cc * 16:cc * 16 + 16],
                                               func=AF.Exp,
                                               scale=Dst[:, cc:cc + 1]),
                    waits=[prog.tok("v"), A_tok])
    prog.section("s", sfin)
    p_tok = prog.tok("s")

    def yfin(y):
        prog.dma(he_d.ap(), st_sb[:, :], waits=(prog.tok("v"),), sem="dso")
        prog.dma(pc_d.ap(), dA_b[:, 0:256], waits=(p_tok,), sem="dso")
    prog.section("y", yfin)
    return


def _mamba_a_inmaps(x_full, casts, inp):
    c2 = lambda a: np.ascontiguousarray(a)
    ip_b = c2(casts["L0_attn_in_proj_w"])
    dt_b = c2(casts["L0_attn_dt_proj_w"])
    xp_b = c2(np.asarray(casts["L0_attn_x_proj_w"])[:, 16:32])
    cw = inp["L0_attn_conv_w"][:, 0, :].astype(np.float32)
    cw_r = c2(cw.reshape(16, 128, 4).transpose(1, 0, 2).reshape(128, 64))
    cb_r = c2(inp["L0_attn_conv_b"].astype(np.float32).reshape(16, 128).T)
    db_r = c2(inp["L0_attn_dt_proj_b"].astype(np.float32).reshape(16, 128).T)
    al_r = c2(inp["L0_attn_A_log"].astype(np.float32)
              .reshape(16, 128, 16).transpose(1, 0, 2).reshape(128, 256))
    dp_r = c2(inp["L0_attn_D_param"].astype(np.float32).reshape(16, 128).T)
    lg_r = c2(np.broadcast_to(inp["L0_attn_ln_gamma"].astype(np.float32), (128, 1024)))
    lb_r = c2(np.broadcast_to(inp["L0_attn_ln_beta"].astype(np.float32), (128, 1024)))
    maps = []
    for i in range(NCORES):
        lo = i * T_CORE
        halo = (np.zeros((HALO, D), np.float32) if i == 0
                else x_full[lo - HALO:lo])
        xs = np.ascontiguousarray(np.concatenate([halo, x_full[lo:lo + T_CORE]], 0))
        mask = np.zeros((128, HALO), np.float32) if i == 0 else np.ones((128, HALO), np.float32)
        maps.append({"x_s": xs, "halo_mask": mask, "inproj": ip_b, "dtw": dt_b,
                     "xpwB": xp_b, "convw_r": cw_r, "convb_r": cb_r, "dtb_r": db_r,
                     "alog_r": al_r, "dparam_r": dp_r, "lng_r": lg_r, "lnb_r": lb_r})
    return maps


def emit_mamba_b(prog, io):
    x_d = io["x"]
    y1_d = io["y1"]
    dc_d = io["dcum"]
    xt_d = io["xn2T"]
    ms_d = io["mstat"]
    rs_d = io["rstat"]
    sp_d = io["slotsP"]
    sh_d = io["slotsH"]
    zw_d = io["inprojZ"]
    ow_d = io["outw"]
    al_d = io["alog_r"]
    o_d = io["o"]
    xn2T = prog.sb("xn2T_sb", [128, 8 * XCOLS], BF16)
    outw = prog.sb("outw_sb", [128, 16 * 1024], BF16)
    ygb = prog.sb("ygb", [128, 16 * 512], BF16)
    A_sb = prog.sb("A_sb", [128, 256])
    h0 = prog.sb("h0", [128, 256])
    slP = prog.sb("slP", [128, 7 * 256])
    slH = prog.sb("slH", [128, 7 * 256])
    stat_m = prog.sb("stat_m", [128, 16])
    stat_r = prog.sb("stat_r", [128, 16])
    dcum_c = prog.sb("dcum_c", [128, 512])
    e_b = prog.sb("e_b", [128, 4 * 512])
    acc = prog.sb("acc", [128, 512])
    acc_g = prog.sb("acc_g", [128, 512])
    y1c = prog.sb("y1c", [128, 512])
    sz_t = prog.sb("sz_t", [128, 512])
    zrot = prog.sb("zrot", [128, 8 * 128], BF16)
    x1c = prog.sb("x1c", [128, 1024])
    xnp = prog.sb("xnp", [128, 1024])
    oc = prog.sb("oc", [128, 1024])
    ps_z = prog.ps("ps_z", [128, 512])
    ps_o = prog.ps("ps_o", [128, 1024])

    def pld(y):
        prog.dma(xn2T[:, :], xt_d.ap(), sem="dm")
        prog.dma(outw[:, :].rearrange("p (c n) -> p c n", n=1024),
                 ow_d.ap().rearrange("(c p) n -> p c n", p=128), sem="dm")
        prog.dma(stat_m[:, :], ms_d.ap(), sem="dm")
        prog.dma(stat_r[:, :], rs_d.ap(), sem="dm")
        prog.dma(A_sb[:, :], al_d.ap(), sem="dm")
        prog.dma(slP[:, :].rearrange("p (j c) -> p j c", c=256), sp_d.ap().transpose([1, 0, 2]), sem="dm")
        prog.dma(slH[:, :].rearrange("p (j c) -> p j c", c=256), sh_d.ap().transpose([1, 0, 2]), sem="dm")
    prog.section("y", pld)
    pld_tok = ("dm", prog.cnt["dm"])

    def sA(s):
        prog.op(lambda: s.activation(out=A_sb[:, :], in_=A_sb[:, :], func=AF.Exp),
                waits=[pld_tok])
        prog.op(lambda: s.activation(out=A_sb[:, :], in_=A_sb[:, :], func=AF.Copy,
                                     scale=-1.0))
    prog.section("s", sA)
    A_tok = prog.tok("s")

    def vh0(v):
        prog.op(lambda: v.memset(h0[:, :], 0.0), waits=[pld_tok])
        for j in range(7):
            prog.op(lambda j=j: v.tensor_tensor(out=h0[:, :], in0=h0[:, :],
                                                in1=slP[:, j * 256:(j + 1) * 256],
                                                op=ALU.mult))
            prog.op(lambda j=j: v.tensor_tensor(out=h0[:, :], in0=h0[:, :],
                                                in1=slH[:, j * 256:(j + 1) * 256],
                                                op=ALU.add))
    prog.section("v", vh0)
    h0_tok = prog.tok("v")

    o_store = [("dso", 0)]
    zw_free = ("t", 0)
    eb_war = [("v", 0)] * 4
    chain_prev = [("v", 0), ("v", 0)]
    for tt in range(NT):
        for cc in range(16):
            dld = [None]

            def ydc(y, cc=cc, tt=tt):
                dld[0] = prog.dma(dcum_c[:, :],
                                  dc_d.ap()[cc * 128:(cc + 1) * 128, tt * LT:(tt + 1) * LT],
                                  waits=(prog.tok("s"),), sem="dx")
                dld[0] = (dld[0][0], dld[0][1])
                dld.append(prog.dma(y1c[:, :],
                                    y1_d.ap()[cc * 128:(cc + 1) * 128, tt * LT:(tt + 1) * LT],
                                    waits=(prog.tok("v"),), sem="dx"))
            prog.section("y", ydc)

            for n in range(16):
                slot = n % 4

                def sE(s, cc=cc, n=n, slot=slot, ws=eb_war[slot]):
                    prog.op(lambda: s.activation(out=e_b[:, slot * LT:(slot + 1) * LT],
                                                 in_=dcum_c[:, :], func=AF.Exp,
                                                 scale=A_sb[:, cc * 16 + n:cc * 16 + n + 1]),
                            waits=[dld[0], A_tok, ws], self_wait=False)
                prog.section("s", sE)
                e_tok = prog.tok("s")

                chain = n % 2
                accx = acc if chain == 0 else acc_g

                def xacc(v, cc=cc, n=n, slot=slot, et=e_tok, accx=accx,
                         cp=chain_prev[chain]):
                    col = h0[:, cc * 16 + n:cc * 16 + n + 1]
                    src = e_b[:, slot * LT:(slot + 1) * LT]
                    if n < 2:
                        prog.op(lambda: v.tensor_scalar(out=accx[:, :], in0=src,
                                                        scalar1=col, scalar2=None,
                                                        op0=ALU.mult),
                                waits=[et, h0_tok, cp], self_wait=False)
                    else:
                        prog.op(lambda: v.scalar_tensor_tensor(out=accx[:, :], in0=src,
                                                               scalar=col, in1=accx[:, :],
                                                               op0=ALU.mult, op1=ALU.add),
                                waits=[et, cp], self_wait=False)
                prog.section("v", xacc)
                chain_prev[chain] = prog.tok("v")
                eb_war[slot] = chain_prev[chain]

            def vcomb(v):
                prog.op(lambda: v.tensor_tensor(out=acc[:, :], in0=acc[:, :],
                                                in1=acc_g[:, :], op=ALU.add))
            prog.section("v", vcomb)
            corr_tok = prog.tok("v")

            zld = [None]

            def yz(y, cc=cc, zf=zw_free):
                zld[0] = prog.dma(zrot[:, :].rearrange("p (kc n) -> p kc n", n=128),
                                  zw_d.ap()[:, cc * 128:(cc + 1) * 128]
                                      .rearrange("(kc p) n -> p kc n", p=128),
                                  waits=(zf,), sem="dwa")
            prog.section("y", yz)

            def tz(t, cc=cc, tt=tt):
                prog.wait(zld[0])
                prog.wait(pld_tok)
                prog.wait(prog.tok("s"))
                for kc in range(8):
                    prog.op(lambda kc=kc: t.matmul(
                        ps_z[:, :512], zrot[:, kc * 128:(kc + 1) * 128],
                        xn2T[:, kc * XCOLS + tt * LT + HALO: kc * XCOLS + tt * LT + HALO + 512],
                        start=(kc == 0), stop=(kc == 7)))
            prog.section("t", tz)
            z_tok = prog.tok("t")
            zw_free = z_tok

            def ssz(s, cc=cc):
                prog.op(lambda: s.activation(out=sz_t[:, :], in_=ps_z[:, :512],
                                             func=AF.Silu), waits=[z_tok, prog.tok("v")])
            prog.section("s", ssz)
            sz_tok = prog.tok("s")

            def vyg(v, cc=cc):
                prog.op(lambda: v.tensor_tensor(out=y1c[:, :], in0=y1c[:, :],
                                                in1=acc[:, :], op=ALU.add),
                        waits=[corr_tok, dld[1], sz_tok])
                prog.op(lambda: v.tensor_tensor(out=ygb[:, cc * 512:(cc + 1) * 512],
                                                in0=y1c[:, :], in1=sz_t[:, :],
                                                op=ALU.mult))
            prog.section("v", vyg)
        ygb_tok = prog.tok("v")

        for tq in range(4):
            def to(t, tt=tt, tq=tq):
                prog.wait(ygb_tok)
                prog.wait(prog.tok("v"))
                for half in range(2):
                    for cc in range(16):
                        prog.op(lambda cc=cc, half=half: t.matmul(
                            ps_o[:, half * 512:(half + 1) * 512],
                            ygb[:, cc * 512 + tq * 128: cc * 512 + tq * 128 + 128],
                            outw[:, cc * 1024 + half * 512: cc * 1024 + half * 512 + 512],
                            start=(cc == 0), stop=(cc == 15)))
            prog.section("t", to)
            o_tok = prog.tok("t")

            xld = [None]

            def yx(y, tt=tt, tq=tq):
                xld[0] = prog.dma(x1c[:, :],
                                  x_d.ap()[tt * 512 + tq * 128: tt * 512 + tq * 128 + 128, :],
                                  waits=(prog.tok("v"),), sem="dx")
            prog.section("y", yx)

            def vev(v, tt=tt, tq=tq, ost=o_store[0]):
                tc = tt * 4 + tq
                prog.wait(ost)
                prog.op(lambda: v.tensor_scalar(out=xnp[:, :], in0=x1c[:, :],
                                                scalar1=stat_m[:, tc:tc + 1],
                                                scalar2=stat_r[:, tc:tc + 1],
                                                op0=ALU.subtract, op1=ALU.mult),
                        waits=[xld[0], pld_tok])
                prog.op(lambda: v.tensor_tensor(out=xnp[:, :], in0=xnp[:, :],
                                                in1=x1c[:, :], op=ALU.add))
                prog.op(lambda: v.tensor_tensor(out=oc[:, :], in0=xnp[:, :],
                                                in1=ps_o[:, :1024], op=ALU.add),
                        waits=[o_tok])
            prog.section("v", vev)
            ev_tok = prog.tok("v")

            def yo(y, tt=tt, tq=tq):
                o_store[0] = prog.dma(
                    o_d.ap()[tt * 512 + tq * 128: tt * 512 + tq * 128 + 128, :],
                    oc[:, :], waits=(ev_tok,), sem="dso")
            prog.section("y", yo)

    return


def _mamba_b_inmaps(x1_full, resA, casts, inp):
    c2 = lambda a: np.ascontiguousarray(a)
    zw = c2(np.asarray(casts["L0_attn_in_proj_w"])[:, 2048:])
    ow = c2(casts["L0_attn_out_proj_w"])
    al_r = c2(inp["L0_attn_A_log"].astype(np.float32)
              .reshape(16, 128, 16).transpose(1, 0, 2).reshape(128, 256))
    ones = np.ones((128, 256), np.float32)
    zeross = np.zeros((128, 256), np.float32)
    maps = []
    for i in range(NCORES):
        sp = []
        sh = []
        for j in range(7):
            src = i - 7 + j
            if src < 0:
                sp.append(ones); sh.append(zeross)
            else:
                sp.append(np.asarray(resA[src]["o_pcore"], np.float32))
                sh.append(np.asarray(resA[src]["o_hend"], np.float32))
        maps.append({
            "x1_s": c2(x1_full[i * T_CORE:(i + 1) * T_CORE]),
            "y1": np.asarray(resA[i]["o_y1"], np.float32),
            "dcum": np.asarray(resA[i]["o_dcum"], np.float32),
            "xn2T": np.asarray(resA[i]["o_xn2T"]),
            "mstat": np.asarray(resA[i]["o_mstat"], np.float32),
            "rstat": np.asarray(resA[i]["o_rstat"], np.float32),
            "slotsP": c2(np.stack(sp)), "slotsH": c2(np.stack(sh)),
            "inprojZ": zw, "outw": ow, "alog_r": al_r,
        })
    return maps


def run_mamba(x1_full, casts, inp):
    ncA = build_mamba_a()
    mapsA = _mamba_a_inmaps(x1_full, casts, inp)
    resA = run_bass_kernel_spmd(ncA, mapsA, core_ids=list(range(NCORES))).results
    ncB = build_mamba_b()
    mapsB = _mamba_b_inmaps(x1_full, resA, casts, inp)
    resB = run_bass_kernel_spmd(ncB, mapsB, core_ids=list(range(NCORES))).results
    return np.concatenate([resB[i]["o_x"] for i in range(NCORES)], axis=0)


def emit_moe(prog, io):
    TCn = 16
    x_d = io["x"]
    wg_d = io["wgate"]
    wd_d = io["wdview"]
    wu_d = io["wuview"]
    bu_d = io["buw"]
    bd_d = io["bdrep"]
    o_d = io["o"]
    ident = io["_ident"]
    id_tok = io["_id_tok"]

    xlT = prog.sb("xlT", [128, 8 * TCn * 128], BF16)
    wg_sb = prog.sb("wg_sb", [128, 8 * 8], BF16)
    wd_sb = prog.sb("wd_sb", [128, 8 * 512], BF16)
    wu_sb = prog.sb("wu_sb", [128, 4 * 1024], BF16)
    bu_sb = prog.sb("bu_sb", [8, 1024], BF16)
    bd_sb = prog.sb("bd_sb", [128, 512])
    xc = prog.sb("xc", [128, 1024])
    xln = prog.sb("xln", [128, 1024])
    xq = prog.sb("xq", [128, 1024], BF16)
    stat = {k: prog.sb("stq_" + k, [128, 1]) for k in
            ("sum", "ssq", "mean", "ex2", "m2", "var", "rstd")}
    l_sb = prog.sb("l_sb", [128, 8])
    g_sb = prog.sb("g_sb", [128, 8])
    gb_sb = prog.sb("gb_sb", [128, 8], BF16)
    gT_sb = prog.sb("gT_sb", [8, 128], BF16)
    m1 = prog.sb("m1", [128, 1])
    m2 = prog.sb("m2", [128, 1])
    nm1 = prog.sb("nm1", [128, 1])
    dd = prog.sb("dd", [128, 1])
    msk2 = prog.sb("msk2", [128, 8])
    hgb = prog.sb("hgb", [128, 512], BF16)
    hgT = prog.sb("hgT", [128, 4 * 128], BF16)
    hsc = prog.sb("hsc", [128, 512])
    oc = prog.sb("oc", [128, 1024])
    ps_l = prog.ps("ps_l", [128, 8])
    ps_hd = prog.ps("ps_hd", [128, 512])
    ps_o = prog.ps("ps_o", [128, 1024])
    ps_tp = prog.ps("ps_tp", [128, 128], BF16)
    ps_gt = prog.ps("ps_gt", [8, 128], BF16)

    def pld(y):
        prog.dma(wg_sb[:, :].rearrange("p (kc n) -> p kc n", n=8),
                 wg_d.ap().rearrange("(kc p) n -> p kc n", p=128), sem="dwa")
        prog.dma(wd_sb[:, :].rearrange("p (kc n) -> p kc n", n=512),
                 wd_d.ap().rearrange("(kc p) n -> p kc n", p=128), sem="dwa")
        prog.dma(wu_sb[:, :].rearrange("p (kc n) -> p kc n", n=1024),
                 wu_d.ap().rearrange("(kc p) n -> p kc n", p=128), sem="dwa")
        prog.dma(bu_sb[:, :], bu_d.ap(), sem="dwa")
        prog.dma(bd_sb[:, :], bd_d.ap(), sem="dwa")
    prog.section("y", pld)
    pld_tok = ("dwa", prog.cnt["dwa"])

    xc_free = ("v", 0)
    prev_aff = ("t", 0)
    for tc in range(TCn):
        ld = [None]

        def yld(y, tc=tc, xf=xc_free):
            ld[0] = prog.dma(xc[:, :], x_d.ap()[tc * 128:(tc + 1) * 128, :],
                             waits=(xf,), sem="dx")
        prog.section("y", yld)

        def ssec(s, pa=prev_aff):
            prog.op(lambda: s.activation(out=xln[:, :], in_=xc[:, :], func=AF.Square,
                                         accum_out=stat["ssq"][:, :]),
                    waits=[ld[0], pa])
        prog.section("s", ssec)
        sq_tok = prog.tok("s")

        def vsec(v, tc=tc):
            prog.op(lambda: v.tensor_reduce(out=stat["sum"][:, :], in_=xc[:, :],
                                            axis=AX, op=ALU.add), waits=[ld[0]])
            prog.op(lambda: v.tensor_scalar(out=stat["mean"][:, :], in0=stat["sum"][:, :],
                                            scalar1=1.0 / D, scalar2=None, op0=ALU.mult))
            prog.op(lambda: v.tensor_scalar(out=stat["ex2"][:, :], in0=stat["ssq"][:, :],
                                            scalar1=1.0 / D, scalar2=None, op0=ALU.mult),
                    waits=[sq_tok])
            prog.op(lambda: v.tensor_tensor(out=stat["m2"][:, :], in0=stat["mean"][:, :],
                                            in1=stat["mean"][:, :], op=ALU.mult))
            prog.op(lambda: v.tensor_tensor(out=stat["var"][:, :], in0=stat["ex2"][:, :],
                                            in1=stat["m2"][:, :], op=ALU.subtract))
            prog.op(lambda: v.tensor_scalar(out=stat["var"][:, :], in0=stat["var"][:, :],
                                            scalar1=1e-5, scalar2=None, op0=ALU.add))
        prog.section("v", vsec)
        var_tok = prog.tok("v")

        def ssec2(s):
            prog.op(lambda: s.activation(out=stat["rstd"][:, :], in_=stat["var"][:, :],
                                         func=AF.Ln), waits=[var_tok])
            prog.op(lambda: s.activation(out=stat["rstd"][:, :], in_=stat["rstd"][:, :],
                                         func=AF.Exp, scale=-0.5))
        prog.section("s", ssec2)
        rstd_tok = prog.tok("s")

        def vsec2(v, tc=tc, qf=prev_aff):
            prog.op(lambda: v.tensor_scalar(out=xq[:, :], in0=xc[:, :],
                                            scalar1=stat["mean"][:, :],
                                            scalar2=stat["rstd"][:, :],
                                            op0=ALU.subtract, op1=ALU.mult),
                    waits=[rstd_tok, qf])
        prog.section("v", vsec2)
        aff_tok = prog.tok("v")
        xc_free = aff_tok

        for kc in range(8):
            def tsec(t, kc=kc):
                prog.op(lambda: t.transpose(ps_tp[:, :],
                                            xq[:, kc * 128:(kc + 1) * 128],
                                            ident[:, :]),
                        waits=[aff_tok, id_tok, prog.tok("v")])
            prog.section("t", tsec)
            tr_tok = prog.tok("t")

            def vsec3(v, kc=kc, tc=tc):
                prog.op(lambda: v.tensor_copy(
                    out=xlT[:, (kc * TCn + tc) * 128:(kc * TCn + tc) * 128 + 128],
                    in_=ps_tp[:, :]), waits=[tr_tok])
            prog.section("v", vsec3)
        prev_aff = prog.tok("t")
    xlT_tok = prog.tok("v")

    o_store = ("dso", 0)
    for tc in range(TCn):
        def tmm(t, tc=tc):
            prog.wait(xlT_tok)
            prog.wait(pld_tok)
            prog.wait(prog.tok("v"))
            for kc in range(8):
                prog.op(lambda kc=kc: t.matmul(
                    ps_l[:, :8],
                    xlT[:, (kc * TCn + tc) * 128:(kc * TCn + tc) * 128 + 128],
                    wg_sb[:, kc * 8:(kc + 1) * 8],
                    start=(kc == 0), stop=(kc == 7)))
            for kc in range(8):
                prog.op(lambda kc=kc: t.matmul(
                    ps_hd[:, :512],
                    xlT[:, (kc * TCn + tc) * 128:(kc * TCn + tc) * 128 + 128],
                    wd_sb[:, kc * 512:(kc + 1) * 512],
                    start=(kc == 0), stop=(kc == 7)))
        prog.section("t", tmm)
        mm_tok = prog.tok("t")

        def vg(v, tc=tc):
            prog.op(lambda: v.tensor_copy(out=l_sb[:, :], in_=ps_l[:, :8]),
                    waits=[mm_tok])
            prog.op(lambda: v.tensor_reduce(out=m1[:, :], in_=l_sb[:, :],
                                            axis=AX, op=ALU.max))
            prog.op(lambda: v.tensor_scalar(out=msk2[:, :], in0=l_sb[:, :],
                                            scalar1=m1[:, :], scalar2=-1e30,
                                            op0=ALU.is_ge, op1=ALU.mult))
            prog.op(lambda: v.tensor_tensor(out=msk2[:, :], in0=msk2[:, :],
                                            in1=l_sb[:, :], op=ALU.add))
            prog.op(lambda: v.tensor_reduce(out=m2[:, :], in_=msk2[:, :],
                                            axis=AX, op=ALU.max))
            prog.op(lambda: v.tensor_scalar(out=nm1[:, :], in0=m1[:, :],
                                            scalar1=-1.0, scalar2=None, op0=ALU.mult))
            prog.op(lambda: v.tensor_tensor(out=dd[:, :], in0=m2[:, :],
                                            in1=m1[:, :], op=ALU.subtract))
        prog.section("v", vg)
        vg_tok = prog.tok("v")

        def sg(s, tc=tc):
            prog.op(lambda: s.activation(out=g_sb[:, :], in_=l_sb[:, :], func=AF.Exp,
                                         bias=nm1[:, :]), waits=[vg_tok])
            prog.op(lambda: s.activation(out=dd[:, :], in_=dd[:, :], func=AF.Exp))
        prog.section("s", sg)
        sg_tok = prog.tok("s")

        def vg2(v, tc=tc):
            prog.op(lambda: v.tensor_scalar(out=dd[:, :], in0=dd[:, :],
                                            scalar1=1.0, scalar2=None, op0=ALU.add),
                    waits=[sg_tok])
            prog.op(lambda: v.reciprocal(out=dd[:, :], in_=dd[:, :]))
            prog.op(lambda: v.tensor_scalar(out=g_sb[:, :], in0=g_sb[:, :],
                                            scalar1=dd[:, :], scalar2=None,
                                            op0=ALU.mult))
            prog.op(lambda: v.tensor_scalar(out=msk2[:, :], in0=l_sb[:, :],
                                            scalar1=m2[:, :], scalar2=None,
                                            op0=ALU.is_ge))
            prog.op(lambda: v.tensor_tensor(out=g_sb[:, :], in0=g_sb[:, :],
                                            in1=msk2[:, :], op=ALU.mult))
            prog.op(lambda: v.tensor_copy(out=gb_sb[:, :], in_=g_sb[:, :]))
            prog.op(lambda: v.tensor_tensor(out=hsc[:, :], in0=ps_hd[:, :512],
                                            in1=bd_sb[:, :], op=ALU.add))
            prog.op(lambda: v.tensor_scalar(out=hsc[:, :], in0=hsc[:, :],
                                            scalar1=0.0, scalar2=None, op0=ALU.max))
            prog.op(lambda: v.tensor_tensor(out=hgb[:, :].rearrange("p (e b) -> p e b", b=64),
                                            in0=hsc[:, :].rearrange("p (e b) -> p e b", b=64),
                                            in1=g_sb[:, :].unsqueeze(2).broadcast_to([128, 8, 64]),
                                            op=ALU.mult))
        prog.section("v", vg2)
        hg_tok = prog.tok("v")

        for kc in range(4):
            def tsec2(t, kc=kc):
                prog.op(lambda: t.transpose(ps_tp[:, :],
                                            hgb[:, kc * 128:(kc + 1) * 128],
                                            ident[:, :]), waits=[hg_tok, prog.tok("v")])
            prog.section("t", tsec2)
            tr2 = prog.tok("t")

            def vsec4(v, kc=kc):
                prog.op(lambda: v.tensor_copy(out=hgT[:, kc * 128:(kc + 1) * 128],
                                              in_=ps_tp[:, :]), waits=[tr2])
            prog.section("v", vsec4)

        def tgt(t, tc=tc):
            prog.op(lambda: t.transpose(ps_gt[:8, :], gb_sb[:, 0:8], ident[:, :]),
                    waits=[hg_tok, prog.tok("v")])
        prog.section("t", tgt)
        gt_tok = prog.tok("t")

        def vgt(v):
            prog.op(lambda: v.tensor_copy(out=gT_sb[:8, :], in_=ps_gt[:8, :128]),
                    waits=[gt_tok])
        prog.section("v", vgt)
        hgT_tok = prog.tok("v")

        def tup(t, tc=tc):
            prog.wait(hgT_tok)
            prog.wait(prog.tok("v"))
            for half in range(2):
                for kc in range(4):
                    prog.op(lambda kc=kc, half=half: t.matmul(
                        ps_o[:, half * 512:(half + 1) * 512],
                        hgT[:, kc * 128:(kc + 1) * 128],
                        wu_sb[:, kc * 1024 + half * 512: kc * 1024 + half * 512 + 512],
                        start=(kc == 0), stop=False))
                prog.op(lambda half=half: t.matmul(
                    ps_o[:, half * 512:(half + 1) * 512],
                    gT_sb[:8, :], bu_sb[:8, half * 512:(half + 1) * 512],
                    start=False, stop=True))
        prog.section("t", tup)
        up_tok = prog.tok("t")

        ld2 = [None]

        def yld2(y, tc=tc):
            ld2[0] = prog.dma(xc[:, :], x_d.ap()[tc * 128:(tc + 1) * 128, :],
                              waits=(prog.tok("v"),), sem="dx")
        prog.section("y", yld2)

        def vev(v, tc=tc, ost=o_store):
            prog.wait(ost)
            prog.op(lambda: v.scalar_tensor_tensor(out=oc[:, :], in0=ps_o[:, :1024],
                                                   scalar=0.5, in1=xc[:, :],
                                                   op0=ALU.mult, op1=ALU.add),
                    waits=[up_tok, ld2[0]])
        prog.section("v", vev)
        ev_tok = prog.tok("v")
        xc_free = ev_tok

        def yst(y, tc=tc):
            prog.dma(o_d.ap()[tc * 128:(tc + 1) * 128, :], oc[:, :], waits=(ev_tok,),
                     sem="dso")
        prog.section("y", yst)
        o_store = ("dso", prog.cnt["dso"])

    return


def run_moe(nc, x_full, casts, inp):
    c2 = lambda a: np.ascontiguousarray(a)
    wd_view = c2(np.asarray(casts["L1_ffn_Wd"]).transpose(1, 0, 2).reshape(1024, 512))
    wu_view = c2(np.asarray(casts["L1_ffn_Wu"]).reshape(512, 1024))
    bd_rep = c2(np.broadcast_to(inp["L1_ffn_bd"].astype(np.float32).reshape(512), (128, 512)))
    bu_b = c2(np.asarray(casts["L1_ffn_bu"]))
    wg_b = c2(casts["L1_ffn_w_gate"])
    in_maps = []
    for i in range(NCORES):
        in_maps.append({
            "x_s": c2(x_full[i * T_CORE:(i + 1) * T_CORE]),
            "wgate": wg_b, "wdview": wd_view, "wuview": wu_view,
            "buw": bu_b, "bdrep": bd_rep,
        })
    res = run_bass_kernel_spmd(nc, in_maps, core_ids=list(range(NCORES)))
    return np.concatenate([res.results[i]["o_x"] for i in range(NCORES)], axis=0)


def emit_ln_xlT(prog, x_d, xlT, ident, id_tok, bufs):
    xc, xln, xq, stat = bufs["xc"], bufs["xln"], bufs["xq"], bufs["stat"]
    TCn = 16
    xc_free = ("v", 0)
    prev_aff = ("t", 0)
    for tc in range(TCn):
        ld = [None]

        def yld(y, tc=tc, xf=xc_free):
            ld[0] = prog.dma(xc[:, :], x_d.ap()[tc * 128:(tc + 1) * 128, :],
                             waits=(xf,), sem="dx")
        prog.section("y", yld)

        def ssec(s, pa=prev_aff):
            prog.op(lambda: s.activation(out=xln[:, :], in_=xc[:, :], func=AF.Square,
                                         accum_out=stat["ssq"][:, :]),
                    waits=[ld[0], pa])
        prog.section("s", ssec)
        sq_tok = prog.tok("s")

        def vsec(v, tc=tc):
            prog.op(lambda: v.tensor_reduce(out=stat["sum"][:, :], in_=xc[:, :],
                                            axis=AX, op=ALU.add), waits=[ld[0]])
            prog.op(lambda: v.tensor_scalar(out=stat["mean"][:, :], in0=stat["sum"][:, :],
                                            scalar1=1.0 / D, scalar2=None, op0=ALU.mult))
            prog.op(lambda: v.tensor_scalar(out=stat["ex2"][:, :], in0=stat["ssq"][:, :],
                                            scalar1=1.0 / D, scalar2=None, op0=ALU.mult),
                    waits=[sq_tok])
            prog.op(lambda: v.tensor_tensor(out=stat["m2"][:, :], in0=stat["mean"][:, :],
                                            in1=stat["mean"][:, :], op=ALU.mult))
            prog.op(lambda: v.tensor_tensor(out=stat["var"][:, :], in0=stat["ex2"][:, :],
                                            in1=stat["m2"][:, :], op=ALU.subtract))
            prog.op(lambda: v.tensor_scalar(out=stat["var"][:, :], in0=stat["var"][:, :],
                                            scalar1=1e-5, scalar2=None, op0=ALU.add))
        prog.section("v", vsec)
        var_tok = prog.tok("v")

        def ssec2(s):
            prog.op(lambda: s.activation(out=stat["rstd"][:, :], in_=stat["var"][:, :],
                                         func=AF.Ln), waits=[var_tok])
            prog.op(lambda: s.activation(out=stat["rstd"][:, :], in_=stat["rstd"][:, :],
                                         func=AF.Exp, scale=-0.5))
        prog.section("s", ssec2)
        rstd_tok = prog.tok("s")

        def vsec2(v, tc=tc, qf=prev_aff):
            prog.op(lambda: v.tensor_scalar(out=xq[:, :], in0=xc[:, :],
                                            scalar1=stat["mean"][:, :],
                                            scalar2=stat["rstd"][:, :],
                                            op0=ALU.subtract, op1=ALU.mult),
                    waits=[rstd_tok, qf])
        prog.section("v", vsec2)
        aff_tok = prog.tok("v")
        xc_free = aff_tok

        for kc in range(8):
            def tsec(t, kc=kc):
                prog.op(lambda: t.transpose(bufs["ps_tp"][:, :],
                                            xq[:, kc * 128:(kc + 1) * 128],
                                            ident[:, :]),
                        waits=[aff_tok, id_tok, prog.tok("v")])
            prog.section("t", tsec)
            tr_tok = prog.tok("t")

            def vsec3(v, kc=kc, tc=tc):
                prog.op(lambda: v.tensor_copy(
                    out=xlT[:, (kc * TCn + tc) * 128:(kc * TCn + tc) * 128 + 128],
                    in_=bufs["ps_tp"][:, :]), waits=[tr_tok])
            prog.section("v", vsec3)
        prev_aff = prog.tok("t")
    return prog.tok("v")


def emit_consts_att(prog):
    E2f = prog.sbuf("E2f", [128, 2])
    onesb = prog.sbuf("onesb", [128, 128], BF16)
    iop = prog.sbuf("iop", [128, 1], I32)

    def gsec(g):
        prog.op(lambda: g.iota(iop[:, :], [[0, 1]], channel_multiplier=1,
                               allow_small_or_imprecise_dtypes=True))
    prog.section("g", gsec)
    io_tok = prog.tok("g")

    def vsec(v):
        prog.op(lambda: v.tensor_scalar(out=E2f[:, 0:1], in0=iop[:, :],
                                        scalar1=64, scalar2=None, op0=ALU.is_lt),
                waits=[io_tok])
        prog.op(lambda: v.tensor_scalar(out=E2f[:, 1:2], in0=iop[:, :],
                                        scalar1=63, scalar2=None, op0=ALU.is_gt))
        prog.op(lambda: v.memset(onesb[:, :], 1.0))
    prog.section("v", vsec)
    return E2f, onesb, prog.tok("v")


def emit_att_a(prog, io):
    TCn = 16
    x_d = io["x"]
    wk_d = io["Wk_b"]
    wv_d = io["Wv_b"]
    bk_d = io["bk_r"]
    bv_d = io["bv_rep"]
    pj_d = io["projT_b"]
    kv_d = io["kv"]
    ident = io["_ident"]
    id_tok = io["_id_tok"]
    identf = io["_identf"]
    idf_tok = io["_idf_tok"]
    E2f = io["_E2f"]
    onesb = io["_onesb"]
    cst_tok = io["_cst_tok"]

    xlT = prog.sb("xlT", [128, 8 * TCn * 128], BF16)
    bufs = {"xc": prog.sb("xc", [128, 1024]),
            "xln": prog.sb("xln", [128, 1024]),
            "xq": prog.sb("xq", [128, 1024], BF16),
            "stat": {k: prog.sb("sta_" + k, [128, 1]) for k in
                     ("sum", "ssq", "mean", "ex2", "m2", "var", "rstd")},
            "ps_tp": prog.ps("ps_tp", [128, 128], BF16)}
    xlT_tok = emit_ln_xlT(prog, x_d, xlT, ident, id_tok, bufs)

    wrot = prog.sb("wrot", [128, 8 * 128], BF16)
    vrot = prog.sb("vrot", [128, 8 * 128], BF16)
    prot = prog.sb("prot", [128, 512], BF16)
    bk_sb = prog.sb("bk_sb", [128, 8])
    bv_sb = prog.sb("bv_sb", [128, 1024])
    Kf = prog.sb("Kf", [128, 2048])
    Kfb = prog.sb("Kfb", [128, 2048], BF16)
    sqs = prog.sb("sqs", [128, 2048])
    ss_sb = prog.sb("ss_sb", [2, 2048])
    rrT = prog.sb("rrT", [128, 32])
    Vf = prog.sb("Vf", [128, TCn * 128])
    RK = prog.sb("RK", [128, TCn * 256], BF16)
    Vs = prog.sb("Vs", [128, TCn * 65], BF16)
    Vu = prog.sb("Vu", [128, TCn * 65], BF16)
    kvacc = prog.sb("kvacc", [128, 2080])
    ps_f = prog.ps("ps_f", [128, 512])
    ps_ss = prog.ps("ps_ss", [2, 512])
    ps_rt = prog.ps("ps_rt", [128, 2])
    ps_kp = prog.ps("ps_kp", [128, 256])
    ps_kv = prog.ps("ps_kv", [128, 65])

    def pbias(y):
        prog.dma(bk_sb[:, :], bk_d.ap(), sem="dm")
        prog.dma(bv_sb[:, :], bv_d.ap(), sem="dm")
    prog.section("y", pbias)
    pb_tok = ("dm", prog.cnt["dm"])

    w_free = ("t", 0)
    ss_bounce_busy = ("dh", 0)
    for hp in range(8):
        wld = [None, None, None]

        def yw(y, hp=hp, wf=w_free):
            wld[0] = prog.dma(wrot[:, :].rearrange("p (kc n) -> p kc n", n=128),
                              wk_d.ap()[:, hp * 128:(hp + 1) * 128]
                                  .rearrange("(kc p) n -> p kc n", p=128),
                              waits=(wf,), sem="dwa")
            wld[1] = prog.dma(vrot[:, :].rearrange("p (kc n) -> p kc n", n=128),
                              wv_d.ap()[:, hp * 128:(hp + 1) * 128]
                                  .rearrange("(kc p) n -> p kc n", p=128), sem="dwa")
            wld[2] = prog.dma(prot[0:64, 0:256],
                              pj_d.ap()[hp * 2], sem="dwb")
            wld[2] = prog.dma(prot[64:128, 256:512],
                              pj_d.ap()[hp * 2 + 1], sem="dwb")
        prog.section("y", yw)

        for tt in range(4):
            def tkf(t, tt=tt, hp=hp):
                prog.wait(wld[0])
                prog.wait(xlT_tok)
                prog.wait(prog.tok("s"))
                for kc in range(8):
                    prog.op(lambda kc=kc: t.matmul(
                        ps_f[:, :512], wrot[:, kc * 128:(kc + 1) * 128],
                        xlT[:, kc * TCn * 128 + tt * 512: kc * TCn * 128 + tt * 512 + 512],
                        start=(kc == 0), stop=(kc == 7)))
            prog.section("t", tkf)
            kf_tok = prog.tok("t")

            def skf(s, tt=tt, hp=hp):
                prog.op(lambda: s.activation(out=Kf[:, tt * 512:(tt + 1) * 512],
                                             in_=ps_f[:, :512], func=AF.Identity,
                                             bias=bk_sb[:, hp:hp + 1]),
                        waits=[kf_tok, pb_tok, prog.tok("v")])
            prog.section("s", skf)
        kf_done = prog.tok("s")

        def skq(s, hp=hp):
            prog.op(lambda: s.activation(out=sqs[:, :], in_=Kf[:, :], func=AF.Square),
                    waits=[kf_done, prog.tok("t")])
        prog.section("s", skq)
        sq_tok = prog.tok("s")

        def vkb(v, hp=hp):
            prog.op(lambda: v.tensor_copy(out=Kfb[:, :], in_=Kf[:, :]),
                    waits=[kf_done, prog.tok("t")])
        prog.section("v", vkb)
        kfb_tok = prog.tok("v")

        for tt in range(4):
            def tss(t, tt=tt):
                prog.wait(sq_tok)
                prog.wait(cst_tok)
                prog.wait(prog.tok("v"))
                prog.op(lambda: t.matmul(ps_ss[:2, :512], E2f[:, :],
                                         sqs[:, tt * 512:(tt + 1) * 512],
                                         start=True, stop=True))
            prog.section("t", tss)
            ssm_tok = prog.tok("t")

            def vss(v, tt=tt):
                prog.op(lambda: v.tensor_copy(out=ss_sb[:2, tt * 512:(tt + 1) * 512],
                                              in_=ps_ss[:2, :512]), waits=[ssm_tok])
            prog.section("v", vss)
        ss_tok = prog.tok("v")

        def srr(s):
            prog.op(lambda: s.activation(out=ss_sb[:2, :], in_=ss_sb[:2, :], func=AF.Ln),
                    waits=[ss_tok])
            prog.op(lambda: s.activation(out=ss_sb[:2, :], in_=ss_sb[:2, :], func=AF.Exp,
                                         scale=0.5))
        prog.section("s", srr)
        nrm_tok = prog.tok("s")

        def vrr(v):
            prog.op(lambda: v.tensor_scalar(out=ss_sb[:2, :], in0=ss_sb[:2, :],
                                            scalar1=1e-8, scalar2=None, op0=ALU.add),
                    waits=[nrm_tok])
            prog.op(lambda: v.reciprocal(out=ss_sb[:2, :], in_=ss_sb[:2, :]))
        prog.section("v", vrr)
        rr_tok = prog.tok("v")

        for tc in range(TCn):
            def trt(t, tc=tc):
                prog.op(lambda: t.transpose(ps_rt[:, :2],
                                            ss_sb[0:2, tc * 128:(tc + 1) * 128],
                                            identf[0:2, 0:2]),
                        waits=[rr_tok, idf_tok, prog.tok("v")])
            prog.section("t", trt)
            rt_tok = prog.tok("t")

            def vrt(v, tc=tc):
                prog.op(lambda: v.tensor_copy(out=rrT[:, tc:tc + 1],
                                              in_=ps_rt[:, 0:1]), waits=[rt_tok])
                prog.op(lambda: v.tensor_copy(out=rrT[:, 16 + tc:16 + tc + 1],
                                              in_=ps_rt[:, 1:2]))
            prog.section("v", vrt)
        rrT_tok = prog.tok("v")

        for tc in range(TCn):
            def tv(t, tc=tc):
                prog.wait(wld[1])
                prog.wait(xlT_tok)
                prog.wait(prog.tok("v"))
                for kc in range(8):
                    prog.op(lambda kc=kc: t.matmul(
                        ps_f[:, :128],
                        xlT[:, (kc * TCn + tc) * 128:(kc * TCn + tc) * 128 + 128],
                        vrot[:, kc * 128:(kc + 1) * 128],
                        start=(kc == 0), stop=(kc == 7)))
            prog.section("t", tv)
            v_tok = prog.tok("t")

            def vv(v, tc=tc, hp=hp):
                prog.op(lambda: v.tensor_tensor(out=Vf[:, tc * 128:(tc + 1) * 128],
                                                in0=ps_f[:, :128],
                                                in1=bv_sb[:, hp * 128:(hp + 1) * 128],
                                                op=ALU.add), waits=[v_tok, pb_tok])
            prog.section("v", vv)
        vf_tok = prog.tok("v")

        for h2 in range(2):
            for tc in range(TCn):
                def vaug(v, h2=h2, tc=tc):
                    src = Vf[:, tc * 128 + h2 * 64: tc * 128 + h2 * 64 + 64]
                    rcol = rrT[:, h2 * 16 + tc: h2 * 16 + tc + 1]
                    prog.op(lambda: v.tensor_scalar(
                        out=Vs[:, tc * 65:tc * 65 + 64], in0=src,
                        scalar1=rcol, scalar2=None, op0=ALU.mult),
                        waits=[vf_tok, rrT_tok])
                    prog.op(lambda: v.tensor_copy(out=Vs[:, tc * 65 + 64:tc * 65 + 65],
                                                  in_=rcol))
                    prog.op(lambda: v.tensor_scalar(out=Vu[:, tc * 65:tc * 65 + 64],
                                                    in0=src, scalar1=0.1,
                                                    scalar2=None, op0=ALU.mult))
                    prog.op(lambda: v.memset(Vu[:, tc * 65 + 64:tc * 65 + 65], 0.1))
                prog.section("v", vaug)
            vaug_tok = prog.tok("v")

            for tc in range(TCn):
                def trk(t, h2=h2, tc=tc):
                    prog.wait(wld[2])
                    prog.wait(kfb_tok)
                    prog.wait(prog.tok("v"))
                    prog.op(lambda: t.matmul(
                        ps_kp[:, :256],
                        Kfb[h2 * 64:h2 * 64 + 64, tc * 128:(tc + 1) * 128],
                        prot[h2 * 64:h2 * 64 + 64, h2 * 256:(h2 + 1) * 256],
                        start=True, stop=True))
                prog.section("t", trk)
                rk_tok = prog.tok("t")

                def vrk(v, h2=h2, tc=tc):
                    prog.op(lambda: v.tensor_scalar(out=RK[:, tc * 256:(tc + 1) * 256],
                                                    in0=ps_kp[:, :256],
                                                    scalar1=0.0, scalar2=None,
                                                    op0=ALU.max), waits=[rk_tok])
                prog.section("v", vrk)
            rk_done = prog.tok("v")

            for fc in range(2):
                def tkv(t, h2=h2, fc=fc):
                    prog.wait(rk_done)
                    prog.wait(vaug_tok)
                    prog.wait(cst_tok)
                    prog.wait(prog.tok("v"))
                    for tc in range(TCn):
                        prog.op(lambda tc=tc: t.matmul(
                            ps_kv[:, :65],
                            RK[:, tc * 256 + fc * 128: tc * 256 + fc * 128 + 128],
                            Vs[:, tc * 65:(tc + 1) * 65],
                            start=(tc == 0), stop=False))
                    for tc in range(TCn):
                        prog.op(lambda tc=tc: t.matmul(
                            ps_kv[:, :65],
                            onesb[:, 0:128], Vu[:, tc * 65:(tc + 1) * 65],
                            start=False, stop=(tc == TCn - 1)))
                prog.section("t", tkv)
                kv_tok = prog.tok("t")

                def vkv(v, h2=h2, fc=fc, hp=hp):
                    h = hp * 2 + h2
                    col = (h * 2 + fc) * 65
                    prog.op(lambda: v.tensor_copy(out=kvacc[:, col:col + 65],
                                                  in_=ps_kv[:, :65]), waits=[kv_tok])
                prog.section("v", vkv)
        w_free = prog.tok("t")

    def yout(y):
        prog.dma(kv_d.ap(), kvacc[:, :], waits=(prog.tok("v"),), sem="dso")
    prog.section("y", yout)
    return


def emit_att_b(prog, io):
    TCn = 16
    x_d = io["x"]
    kva_d = io["kv_all"]
    wq_d = io["Wq_b"]
    bq_d = io["bq_r"]
    pj_d = io["projT_b"]
    wo_d = io["Wo_b"]
    bo_d = io["bo_rep"]
    o_d = io["o"]
    ident = io["_ident"]
    id_tok = io["_id_tok"]
    identf = io["_identf"]
    idf_tok = io["_idf_tok"]
    E2f = io["_E2f"]
    onesb = io["_onesb"]
    cst_tok = io["_cst_tok"]

    xlT = prog.sb("xlT", [128, 8 * TCn * 128], BF16)
    bufs = {"xc": prog.sb("xc", [128, 1024]),
            "xln": prog.sb("xln", [128, 1024]),
            "xq": prog.sb("xq", [128, 1024], BF16),
            "stat": {k: prog.sb("sta_" + k, [128, 1]) for k in
                     ("sum", "ssq", "mean", "ex2", "m2", "var", "rstd")},
            "ps_tp": prog.ps("ps_tp", [128, 128], BF16)}
    xlT_tok = emit_ln_xlT(prog, x_d, xlT, ident, id_tok, bufs)

    kvs = prog.sb("kvs", [128, 2080])
    kvb = prog.sb("kvb", [128, 2080], BF16)
    kvt = prog.sb("kvt", [128, 2080])
    wrot = prog.sb("wrot", [128, 8 * 128], BF16)
    prot = prog.sb("prot", [128, 512], BF16)
    bq_sb = prog.sb("bq_sb", [128, 8])
    bo_sb = prog.sb("bo_sb", [128, 1024])
    Qf = prog.sb("Qf", [128, 2048])
    Qfb = prog.sb("Qfb", [128, 2048], BF16)
    sqs = prog.sb("sqs", [128, 2048])
    ss_sb = prog.sb("ss_sb", [4, 2048])
    qq_sb = prog.sb("qq_sb", [2, 2048])
    rqT = prog.sb("rqT", [128, 64])
    Rf = prog.sb("Rf", [128, 2 * 2048], BF16)
    Srep = prog.sb("Srep", [128, 65])
    Srow = prog.sb("Srow", [1, 65], BF16)
    t1 = prog.sb("t1", [128, 65])
    outg = prog.sb("outg", [128, TCn * 1024], BF16)
    ogT = prog.sb("ogT", [128, 8 * TCn * 128], BF16)
    oc = prog.sb("oc", [128, 1024])
    xc2 = prog.sb("xc2", [128, 1024])
    ps_f = prog.ps("ps_f", [128, 512])
    ps_ss = prog.ps("ps_ss", [4, 512])
    ps_small = prog.ps("ps_small", [128, 512])
    ps_rt = ps_small[:, 0:4]
    ps_m = ps_small[:, 128:193]
    ps_s1 = ps_small[0:1, 256:321]
    ps_s2 = ps_small[:, 384:449]
    ps_o = prog.ps("ps_o", [128, 1024])

    def pbias(y):
        prog.dma(bq_sb[:, :], bq_d.ap(), sem="dm")
        prog.dma(bo_sb[:, :], bo_d.ap(), sem="dm")
    prog.section("y", pbias)

    def ykv0(y):
        prog.dma(kvs[:, :], kva_d.ap()[0], sem="dm")
    prog.section("y", ykv0)
    kv_tok = ("dm", prog.cnt["dm"])
    for j in range(1, 8):
        def ykvj(y, j=j):
            prog.dma(kvt[:, :], kva_d.ap()[j], sem="dm",
                     waits=(prog.tok("v"),))
        prog.section("y", ykvj)
        ld_tok = ("dm", prog.cnt["dm"])

        def vadd(v, j=j):
            prog.op(lambda: v.tensor_tensor(out=kvs[:, :], in0=kvs[:, :],
                                            in1=kvt[:, :], op=ALU.add),
                    waits=[ld_tok, kv_tok])
        prog.section("v", vadd)
    
    def vcast(v):
        prog.op(lambda: v.tensor_copy(out=kvb[:, :], in_=kvs[:, :]))
    prog.section("v", vcast)
    kvb_tok = prog.tok("v")
    pb_tok = ("dm", prog.cnt["dm"])

    o_store = ("dso", 0)
    w_free = ("t", 0)
    for hp in range(8):
        wld = [None, None]

        def yw(y, hp=hp, wf=w_free):
            wld[0] = prog.dma(wrot[:, :].rearrange("p (kc n) -> p kc n", n=128),
                              wq_d.ap()[:, hp * 128:(hp + 1) * 128]
                                  .rearrange("(kc p) n -> p kc n", p=128),
                              waits=(wf,), sem="dwa")
            prog.dma(prot[0:64, 0:256], pj_d.ap()[hp * 2], sem="dwb")
            wld[1] = prog.dma(prot[64:128, 256:512], pj_d.ap()[hp * 2 + 1], sem="dwb")
        prog.section("y", yw)

        for tt in range(4):
            def tqf(t, tt=tt, hp=hp):
                prog.wait(wld[0])
                prog.wait(xlT_tok)
                prog.wait(prog.tok("s"))
                for kc in range(8):
                    prog.op(lambda kc=kc: t.matmul(
                        ps_f[:, :512], wrot[:, kc * 128:(kc + 1) * 128],
                        xlT[:, kc * TCn * 128 + tt * 512: kc * TCn * 128 + tt * 512 + 512],
                        start=(kc == 0), stop=(kc == 7)))
            prog.section("t", tqf)
            qf_tok = prog.tok("t")

            def sqf(s, tt=tt, hp=hp):
                prog.op(lambda: s.activation(out=Qf[:, tt * 512:(tt + 1) * 512],
                                             in_=ps_f[:, :512], func=AF.Identity,
                                             bias=bq_sb[:, hp:hp + 1]),
                        waits=[qf_tok, pb_tok, prog.tok("v")])
            prog.section("s", sqf)
        qf_done = prog.tok("s")

        def sqq(s, hp=hp):
            prog.op(lambda: s.activation(out=sqs[:, :], in_=Qf[:, :], func=AF.Square),
                    waits=[qf_done, prog.tok("t")])
        prog.section("s", sqq)
        sq_tok = prog.tok("s")

        def vqb(v, hp=hp):
            prog.op(lambda: v.tensor_copy(out=Qfb[:, :], in_=Qf[:, :]),
                    waits=[qf_done, prog.tok("t")])
        prog.section("v", vqb)
        qfb_tok = prog.tok("v")

        for tt in range(4):
            def tss(t, tt=tt):
                prog.wait(sq_tok)
                prog.wait(cst_tok)
                prog.wait(prog.tok("v"))
                prog.op(lambda: t.matmul(ps_ss[:2, :512], E2f[:, :],
                                         sqs[:, tt * 512:(tt + 1) * 512],
                                         start=True, stop=True))
            prog.section("t", tss)
            ssm_tok = prog.tok("t")

            def vss(v, tt=tt):
                prog.op(lambda: v.tensor_copy(out=ss_sb[0:2, tt * 512:(tt + 1) * 512],
                                              in_=ps_ss[:2, :512]), waits=[ssm_tok])
            prog.section("v", vss)
        ss_tok = prog.tok("v")

        def srr(s):
            prog.op(lambda: s.activation(out=ss_sb[0:2, :], in_=ss_sb[0:2, :], func=AF.Ln),
                    waits=[ss_tok])
            prog.op(lambda: s.activation(out=ss_sb[0:2, :], in_=ss_sb[0:2, :], func=AF.Exp,
                                         scale=0.5))
        prog.section("s", srr)
        nrm_tok = prog.tok("s")

        def vrr(v):
            prog.op(lambda: v.tensor_scalar(out=ss_sb[0:2, :], in0=ss_sb[0:2, :],
                                            scalar1=1e-8, scalar2=None, op0=ALU.add),
                    waits=[nrm_tok])
            prog.op(lambda: v.tensor_scalar(out=qq_sb[0:2, :], in0=ss_sb[0:2, :],
                                            scalar1=0.1, scalar2=None, op0=ALU.mult))
            prog.op(lambda: v.reciprocal(out=ss_sb[0:2, :], in_=ss_sb[0:2, :]))
        prog.section("v", vrr)
        rr_tok = prog.tok("v")

        for tc in range(TCn):
            def trt(t, tc=tc):
                prog.op(lambda: t.transpose(ps_rt[:, 0:2],
                                            ss_sb[0:2, tc * 128:(tc + 1) * 128],
                                            identf[0:2, 0:2]),
                        waits=[rr_tok, idf_tok, prog.tok("v")])
                prog.op(lambda: t.transpose(ps_rt[:, 2:4],
                                            qq_sb[0:2, tc * 128:(tc + 1) * 128],
                                            identf[0:2, 0:2]))
            prog.section("t", trt)
            rt_tok = prog.tok("t")

            def vrt(v, tc=tc):
                prog.op(lambda: v.tensor_copy(out=rqT[:, tc:tc + 1],
                                              in_=ps_rt[:, 0:1]), waits=[rt_tok])
                prog.op(lambda: v.tensor_copy(out=rqT[:, 16 + tc:16 + tc + 1],
                                              in_=ps_rt[:, 1:2]))
                prog.op(lambda: v.tensor_copy(out=rqT[:, 32 + tc:32 + tc + 1],
                                              in_=ps_rt[:, 2:3]))
                prog.op(lambda: v.tensor_copy(out=rqT[:, 48 + tc:48 + tc + 1],
                                              in_=ps_rt[:, 3:4]))
            prog.section("v", vrt)
        rqT_tok = prog.tok("v")

        for h2 in range(2):
            h = hp * 2 + h2
            for fc in range(2):
                for tt in range(4):
                    def trf(t, h2=h2, fc=fc, tt=tt):
                        prog.wait(wld[1])
                        prog.wait(qfb_tok)
                        prog.wait(prog.tok("v"))
                        prog.op(lambda: t.matmul(
                            ps_f[:, :512],
                            prot[h2 * 64:h2 * 64 + 64,
                                 h2 * 256 + fc * 128: h2 * 256 + fc * 128 + 128],
                            Qfb[h2 * 64:h2 * 64 + 64, tt * 512:(tt + 1) * 512],
                            start=True, stop=True))
                    prog.section("t", trf)
                    rf_tok = prog.tok("t")

                    def vrf(v, fc=fc, tt=tt):
                        prog.op(lambda: v.tensor_scalar(
                            out=Rf[:, fc * 2048 + tt * 512: fc * 2048 + (tt + 1) * 512],
                            in0=ps_f[:, :512], scalar1=0.0, scalar2=None,
                            op0=ALU.max), waits=[rf_tok])
                    prog.section("v", vrf)
            rf_done = prog.tok("v")

            def tsp(t, h=h):
                prog.wait(kvb_tok)
                prog.wait(cst_tok)
                prog.wait(prog.tok("v"))
                for fc in range(2):
                    prog.op(lambda fc=fc: t.matmul(
                        ps_s1[0:1, 0:65], onesb[:, 0:1],
                        kvb[:, (h * 2 + fc) * 65:(h * 2 + fc) * 65 + 65],
                        start=(fc == 0), stop=(fc == 1)))
            prog.section("t", tsp)
            sp_tok = prog.tok("t")

            def vsp(v):
                prog.op(lambda: v.tensor_copy(out=Srow[:1, :], in_=ps_s1[0:1, 0:65]),
                        waits=[sp_tok])
            prog.section("v", vsp)
            srow_tok = prog.tok("v")

            def tsp2(t):
                prog.wait(srow_tok)
                prog.op(lambda: t.matmul(ps_s2[:, 0:65], onesb[0:1, 0:128],
                                         Srow[0:1, 0:65], start=True, stop=True))
            prog.section("t", tsp2)
            sp2_tok = prog.tok("t")

            def vsp2(v):
                prog.op(lambda: v.tensor_copy(out=Srep[:, :], in_=ps_s2[:, 0:65]),
                        waits=[sp2_tok])
            prog.section("v", vsp2)
            srep_tok = prog.tok("v")

            for tc in range(TCn):
                def tm(t, h=h, h2=h2, tc=tc):
                    prog.wait(rf_done)
                    prog.wait(kvb_tok)
                    prog.wait(prog.tok("v"))
                    for fc in range(2):
                        prog.op(lambda fc=fc: t.matmul(
                            ps_m[:, 0:65],
                            Rf[:, fc * 2048 + tc * 128: fc * 2048 + tc * 128 + 128],
                            kvb[:, (h * 2 + fc) * 65:(h * 2 + fc) * 65 + 65],
                            start=(fc == 0), stop=(fc == 1)))
                prog.section("t", tm)
                m_tok = prog.tok("t")

                def vm(v, h=h, h2=h2, tc=tc):
                    qcol = rqT[:, 32 + h2 * 16 + tc: 32 + h2 * 16 + tc + 1]
                    prog.op(lambda: v.tensor_scalar(out=t1[:, :], in0=Srep[:, :],
                                                    scalar1=qcol, scalar2=None,
                                                    op0=ALU.mult),
                            waits=[srep_tok, rqT_tok])
                    prog.op(lambda: v.tensor_tensor(out=t1[:, :], in0=t1[:, :],
                                                    in1=ps_m[:, 0:65], op=ALU.add),
                            waits=[m_tok])
                    prog.op(lambda: v.tensor_scalar(out=oc[:, 0:1], in0=qcol,
                                                    scalar1=1e-5, scalar2=None,
                                                    op0=ALU.mult))
                    prog.op(lambda: v.tensor_tensor(out=oc[:, 0:1], in0=t1[:, 64:65],
                                                    in1=oc[:, 0:1], op=ALU.max))
                    prog.op(lambda: v.reciprocal(out=oc[:, 0:1], in_=oc[:, 0:1]))
                    prog.op(lambda: v.tensor_scalar(
                        out=outg[:, tc * 1024 + h * 64: tc * 1024 + h * 64 + 64],
                        in0=t1[:, 0:64], scalar1=oc[:, 0:1], scalar2=None,
                        op0=ALU.mult))
                prog.section("v", vm)
        w_free = prog.tok("t")
    outg_tok = prog.tok("v")

    for tc in range(TCn):
        for kc in range(8):
            def tsec(t, tc=tc, kc=kc):
                prog.op(lambda: t.transpose(bufs["ps_tp"][:, :],
                                            outg[:, tc * 1024 + kc * 128: tc * 1024 + kc * 128 + 128],
                                            ident[:, :]),
                        waits=[outg_tok, prog.tok("v")])
            prog.section("t", tsec)
            tr_tok = prog.tok("t")

            def vsec3(v, tc=tc, kc=kc):
                prog.op(lambda: v.tensor_copy(
                    out=ogT[:, (kc * TCn + tc) * 128:(kc * TCn + tc) * 128 + 128],
                    in_=bufs["ps_tp"][:, :]), waits=[tr_tok])
            prog.section("v", vsec3)
    ogT_tok = prog.tok("v")

    woall = prog.sb("woall", [128, 8 * 1024], BF16)

    def ywoall(y):
        prog.dma(woall[:, :].rearrange("p (kc n) -> p kc n", n=1024),
                 wo_d.ap().rearrange("(kc p) n -> p kc n", p=128), sem="dwa")
    prog.section("y", ywoall)
    wo_tok = ("dwa", prog.cnt["dwa"])

    for tc in range(TCn):
        def tmo(t, tc=tc):
            prog.wait(ogT_tok)
            prog.wait(wo_tok)
            prog.wait(prog.tok("v"))
            for half in range(2):
                for kc in range(8):
                    prog.op(lambda kc=kc, half=half: t.matmul(
                        ps_o[:, half * 512:(half + 1) * 512],
                        ogT[:, (kc * TCn + tc) * 128:(kc * TCn + tc) * 128 + 128],
                        woall[:, kc * 1024 + half * 512: kc * 1024 + half * 512 + 512],
                        start=(kc == 0), stop=(kc == 7)))
        prog.section("t", tmo)
        mo_tok = prog.tok("t")

        ld2 = [None]

        def yx2(y, tc=tc):
            ld2[0] = prog.dma(xc2[:, :], x_d.ap()[tc * 128:(tc + 1) * 128, :],
                              waits=(prog.tok("v"),), sem="dx")
        prog.section("y", yx2)

        def vev(v, tc=tc, ost=o_store):
            prog.wait(ost)
            prog.op(lambda: v.tensor_tensor(out=oc[:, :], in0=ps_o[:, :1024],
                                            in1=bo_sb[:, :], op=ALU.add),
                    waits=[mo_tok, pb_tok])
            prog.op(lambda: v.tensor_tensor(out=oc[:, :], in0=oc[:, :],
                                            in1=xc2[:, :], op=ALU.add),
                    waits=[ld2[0]])
        prog.section("v", vev)
        ev_tok = prog.tok("v")

        def yst(y, tc=tc):
            prog.dma(o_d.ap()[tc * 128:(tc + 1) * 128, :], oc[:, :], waits=(ev_tok,),
                     sem="dso")
        prog.section("y", yst)
        o_store = ("dso", prog.cnt["dso"])

    return


def run_attention(x_full, casts, inp):
    c2 = lambda a: np.ascontiguousarray(a)
    projT = c2(np.asarray(casts["L1_attn_projection"]).transpose(0, 2, 1))
    bk_r = c2(inp["L1_attn_bk"].astype(np.float32).reshape(8, 128).T)
    bq_r = c2(inp["L1_attn_bq"].astype(np.float32).reshape(8, 128).T)
    bv_rep = c2(np.broadcast_to(inp["L1_attn_bv"].astype(np.float32), (128, 1024)))
    bo_rep = c2(np.broadcast_to(inp["L1_attn_bo"].astype(np.float32), (128, 1024)))
    ncA = build_att_a()
    mapsA = []
    for i in range(NCORES):
        mapsA.append({"x_s": c2(x_full[i * T_CORE:(i + 1) * T_CORE]),
                      "Wk_b": c2(casts["L1_attn_Wk"]), "Wv_b": c2(casts["L1_attn_Wv"]),
                      "bk_r": bk_r, "bv_rep": bv_rep, "projT_b": projT})
    resA = run_bass_kernel_spmd(ncA, mapsA, core_ids=list(range(NCORES))).results
    kv_all = np.stack([np.asarray(resA[i]["o_kv"], np.float32) for i in range(NCORES)])
    ncB = build_att_b()
    mapsB = []
    for i in range(NCORES):
        mapsB.append({"x_s": c2(x_full[i * T_CORE:(i + 1) * T_CORE]),
                      "kv_all": kv_all, "Wq_b": c2(casts["L1_attn_Wq"]),
                      "bq_r": bq_r, "projT_b": projT,
                      "Wo_b": c2(casts["L1_attn_Wo"]), "bo_rep": bo_rep})
    resB = run_bass_kernel_spmd(ncB, mapsB, core_ids=list(range(NCORES))).results
    return np.concatenate([resB[i]["o_x"] for i in range(NCORES)], axis=0)


def emit_prelude(prog):
    ident = prog.sbuf("P_ident", [128, 128], BF16)
    io1 = prog.sbuf("P_io1", [128, 128], I32)
    io2 = prog.sbuf("P_io2", [128, 128], I32)
    id_tok = emit_identity(prog, ident, io1, io2)
    E2f, onesb, cst_tok = emit_consts_att(prog)
    identf = prog.sbuf("P_identf", [128, 128])

    def vidf(v):
        prog.op(lambda: v.tensor_copy(out=identf[:, :], in_=ident[:, :]),
                waits=[id_tok])
    prog.section("v", vidf)
    idf_tok = prog.tok("v")
    return {"_ident": ident, "_id_tok": id_tok, "_identf": identf,
            "_idf_tok": idf_tok, "_E2f": E2f, "_onesb": onesb,
            "_cst_tok": cst_tok}


def _fin(prog):
    def fin(y):
        for sname in ("dh", "dx", "dwa", "dwb", "dso", "dm"):
            if prog.cnt[sname]:
                y.wait_ge(prog.sem[sname], prog.cnt[sname])
        if prog.cnt["dg"]:
            y.wait_ge(prog.sem["dg"], prog.cnt["dg"])
    prog.section("y", fin)


def build_L1():
    prog = Prog("L1")
    prog.init_arena(52400)
    pre = emit_prelude(prog)
    TH = T_CORE + HALO
    x_d = prog.dram("x_s", [TH, D], F32, kind="ExternalInput")
    wg_d = prog.dram("wgqT", [1024, 2816], BF16, kind="ExternalInput")
    wu_d = prog.dram("wuqT", [1024, 2816], BF16, kind="ExternalInput")
    wd_d = prog.dram("wdqT", [2816, 1024], BF16, kind="ExternalInput")
    x1_d = prog.dram("o_x1", [TH, D], F32, kind="ExternalOutput")
    io_m = {k: prog.dram(n, sh, dt, kind=kk) for k, n, sh, dt, kk in [
        ("y1", "o_y1", [2048, 2048], F32, "ExternalOutput"),
        ("dcum", "o_dcum", [2048, 2048], F32, "ExternalOutput"),
        ("xn2T", "o_xn2T", [128, 8 * XCOLS], BF16, "ExternalOutput"),
        ("hend", "o_hend", [128, 256], F32, "ExternalOutput"),
        ("pcore", "o_pcore", [128, 256], F32, "ExternalOutput"),
        ("mstat", "o_mstat", [128, 16], F32, "ExternalOutput"),
        ("rstat", "o_rstat", [128, 16], F32, "ExternalOutput"),
        ("halo_mask", "halo_mask", [128, HALO], F32, "ExternalInput"),
        ("inproj", "inproj", [1024, 4096], BF16, "ExternalInput"),
        ("dtw", "dtw", [2048, 2048], BF16, "ExternalInput"),
        ("xpwB", "xpwB", [2048, 16], BF16, "ExternalInput"),
        ("convw_r", "convw_r", [128, 64], F32, "ExternalInput"),
        ("convb_r", "convb_r", [128, 16], F32, "ExternalInput"),
        ("dtb_r", "dtb_r", [128, 16], F32, "ExternalInput"),
        ("alog_r", "alog_r", [128, 256], F32, "ExternalInput"),
        ("dparam_r", "dparam_r", [128, 16], F32, "ExternalInput"),
        ("lng_r", "lng_r", [128, 1024], F32, "ExternalInput"),
        ("lnb_r", "lnb_r", [128, 1024], F32, "ExternalInput"),
    ]}
    io_m["b_bounce"] = prog.dram("b_bounce", [16, LT], F32)
    io_m["x"] = x1_d
    prog.phase_reset()
    emit_ffn(prog, {"x": x_d, "o": x1_d, "wgqT": wg_d, "wuqT": wu_d,
                    "wdqT": wd_d, **pre}, halo=True)
    prog.phase_reset()
    emit_mamba_a(prog, {**io_m, **pre})
    _fin(prog)
    return prog.done()


def build_L2():
    prog = Prog("L2")
    prog.init_arena(52400)
    pre = emit_prelude(prog)
    x1_d = prog.dram("x1_s", [T_CORE, D], F32, kind="ExternalInput")
    x2_d = prog.dram("i_x2", [T_CORE, D], F32)
    x3_d = prog.dram("i_x3", [T_CORE, D], F32)
    x4_d = prog.dram("o_x4", [T_CORE, D], F32, kind="ExternalOutput")
    kv_d = prog.dram("o_kv", [128, 2080], F32, kind="ExternalOutput")
    dd = lambda n, sh, dt, kk=None: prog.dram(n, sh, dt, kind=kk)
    io_b = {
        "x": x1_d, "o": x2_d,
        "y1": dd("y1", [2048, 2048], F32, "ExternalInput"),
        "dcum": dd("dcum", [2048, 2048], F32, "ExternalInput"),
        "xn2T": dd("xn2T", [128, 8 * XCOLS], BF16, "ExternalInput"),
        "mstat": dd("mstat", [128, 16], F32, "ExternalInput"),
        "rstat": dd("rstat", [128, 16], F32, "ExternalInput"),
        "slotsP": dd("slotsP", [7, 128, 256], F32, "ExternalInput"),
        "slotsH": dd("slotsH", [7, 128, 256], F32, "ExternalInput"),
        "inprojZ": dd("inprojZ", [1024, 2048], BF16, "ExternalInput"),
        "outw": dd("outw", [2048, 1024], BF16, "ExternalInput"),
        "alog_r": dd("alog_r", [128, 256], F32, "ExternalInput"),
    }
    io_f = {"x": x2_d, "o": x3_d,
            "wgqT": dd("wgqT", [1024, 2816], BF16, "ExternalInput"),
            "wuqT": dd("wuqT", [1024, 2816], BF16, "ExternalInput"),
            "wdqT": dd("wdqT", [2816, 1024], BF16, "ExternalInput")}
    io_moe = {"x": x3_d, "o": x4_d,
              "wgate": dd("wgate", [1024, 8], BF16, "ExternalInput"),
              "wdview": dd("wdview", [1024, 512], BF16, "ExternalInput"),
              "wuview": dd("wuview", [512, 1024], BF16, "ExternalInput"),
              "buw": dd("buw", [8, 1024], BF16, "ExternalInput"),
              "bdrep": dd("bdrep", [128, 512], F32, "ExternalInput")}
    io_a = {"x": x4_d, "kv": kv_d,
            "Wk_b": dd("Wk_b", [1024, 1024], BF16, "ExternalInput"),
            "Wv_b": dd("Wv_b", [1024, 1024], BF16, "ExternalInput"),
            "bk_r": dd("bk_r", [128, 8], F32, "ExternalInput"),
            "bv_rep": dd("bv_rep", [128, 1024], F32, "ExternalInput"),
            "projT_b": dd("projT_b", [16, 64, 256], BF16, "ExternalInput")}
    prog.phase_reset()
    emit_mamba_b(prog, {**io_b, **pre})
    prog.phase_reset()
    emit_ffn(prog, {**io_f, **pre}, halo=False)
    prog.phase_reset()
    emit_moe(prog, {**io_moe, **pre})
    prog.phase_reset()
    emit_att_a(prog, {**io_a, **pre})
    _fin(prog)
    return prog.done()


def build_L3():
    prog = Prog("L3")
    prog.init_arena(52400)
    pre = emit_prelude(prog)
    x4_d = prog.dram("x4_s", [T_CORE, D], F32, kind="ExternalInput")
    x5_d = prog.dram("i_x5", [T_CORE, D], F32)
    o_d = prog.dram("o_x", [T_CORE, D], F32, kind="ExternalOutput")
    dd = lambda n, sh, dt, kk=None: prog.dram(n, sh, dt, kind=kk)
    io_ab = {"x": x4_d, "o": x5_d,
             "kv_all": dd("kv_all", [8, 128, 2080], F32, "ExternalInput"),
             "Wq_b": dd("Wq_b", [1024, 1024], BF16, "ExternalInput"),
             "bq_r": dd("bq_r", [128, 8], F32, "ExternalInput"),
             "projT_b": dd("projT_b", [16, 64, 256], BF16, "ExternalInput"),
             "Wo_b": dd("Wo_b", [1024, 1024], BF16, "ExternalInput"),
             "bo_rep": dd("bo_rep", [128, 1024], F32, "ExternalInput"),
             "ss_bounce": dd("ss_bounce", [4, 2048], F32)}
    io_moe = {"x": x5_d, "o": o_d,
              "wgate": dd("wgate", [1024, 8], BF16, "ExternalInput"),
              "wdview": dd("wdview", [1024, 512], BF16, "ExternalInput"),
              "wuview": dd("wuview", [512, 1024], BF16, "ExternalInput"),
              "buw": dd("buw", [8, 1024], BF16, "ExternalInput"),
              "bdrep": dd("bdrep", [128, 512], F32, "ExternalInput")}
    prog.phase_reset()
    emit_att_b(prog, {**io_ab, **pre})
    prog.phase_reset()
    emit_moe(prog, {**io_moe, **pre})
    _fin(prog)
    return prog.done()


def kernel_fused(**inp):
    wgqT, wuqT, wdqT, casts = run_wprep(inp)
    c2 = lambda a: np.ascontiguousarray(a)
    x = c2(inp["L0_attn_x"][0]).astype(np.float32, copy=False)

    nc1 = _get_nc("L1", build_L1)
    mapsA = _mamba_a_inmaps(x, casts, inp)
    maps1 = []
    for i in range(NCORES):
        m = dict(mapsA[i])
        m["wgqT"] = wgqT; m["wuqT"] = wuqT; m["wdqT"] = wdqT
        maps1.append(m)
    res1 = run_bass_kernel_spmd(nc1, maps1, core_ids=list(range(NCORES))).results

    x1_full = np.concatenate([np.asarray(res1[i]["o_x1"], np.float32)[HALO:]
                              for i in range(NCORES)], axis=0)

    nc2 = _get_nc("L2", build_L2)
    mapsB = _mamba_b_inmaps(x1_full, res1, casts, inp)
    wd_view = c2(np.asarray(casts["L1_ffn_Wd"]).transpose(1, 0, 2).reshape(1024, 512))
    wu_view = c2(np.asarray(casts["L1_ffn_Wu"]).reshape(512, 1024))
    bd_rep = c2(np.broadcast_to(inp["L1_ffn_bd"].astype(np.float32).reshape(512), (128, 512)))
    projT = c2(np.asarray(casts["L1_attn_projection"]).transpose(0, 2, 1))
    bk_r = c2(inp["L1_attn_bk"].astype(np.float32).reshape(8, 128).T)
    bq_r = c2(inp["L1_attn_bq"].astype(np.float32).reshape(8, 128).T)
    bv_rep = c2(np.broadcast_to(inp["L1_attn_bv"].astype(np.float32), (128, 1024)))
    bo_rep = c2(np.broadcast_to(inp["L1_attn_bo"].astype(np.float32), (128, 1024)))
    maps2 = []
    for i in range(NCORES):
        m = dict(mapsB[i])
        m.pop("x1_s", None)
        m["x1_s"] = c2(x1_full[i * T_CORE:(i + 1) * T_CORE])
        m["wgqT"] = wgqT; m["wuqT"] = wuqT; m["wdqT"] = wdqT
        m["wgate"] = c2(casts["L1_ffn_w_gate"]); m["wdview"] = wd_view
        m["wuview"] = wu_view; m["buw"] = c2(np.asarray(casts["L1_ffn_bu"]))
        m["bdrep"] = bd_rep
        m["Wk_b"] = c2(casts["L1_attn_Wk"]); m["Wv_b"] = c2(casts["L1_attn_Wv"])
        m["bk_r"] = bk_r; m["bv_rep"] = bv_rep; m["projT_b"] = projT
        maps2.append(m)
    res2 = run_bass_kernel_spmd(nc2, maps2, core_ids=list(range(NCORES))).results

    x4_full = np.concatenate([np.asarray(res2[i]["o_x4"], np.float32)
                              for i in range(NCORES)], axis=0)
    kv_all = np.stack([np.asarray(res2[i]["o_kv"], np.float32) for i in range(NCORES)])

    nc3 = _get_nc("L3", build_L3)
    maps3 = []
    for i in range(NCORES):
        maps3.append({"x4_s": c2(x4_full[i * T_CORE:(i + 1) * T_CORE]),
                      "kv_all": kv_all, "Wq_b": c2(casts["L1_attn_Wq"]),
                      "bq_r": bq_r, "projT_b": projT,
                      "Wo_b": c2(casts["L1_attn_Wo"]), "bo_rep": bo_rep,
                      "wgate": c2(casts["L1_ffn_w_gate"]), "wdview": wd_view,
                      "wuview": wu_view, "buw": c2(np.asarray(casts["L1_ffn_bu"])),
                      "bdrep": bd_rep})
    res3 = run_bass_kernel_spmd(nc3, maps3, core_ids=list(range(NCORES))).results
    out = np.concatenate([res3[i]["o_x"] for i in range(NCORES)], axis=0)
    return out[None].astype(np.float32)


# version 06 (final)
# speedup vs baseline: 1.2865x; 1.0286x over previous
import numpy as np
import ml_dtypes
from contextlib import ExitStack

import concourse.bass as bass
import concourse.mybir as mybir
from concourse.bass_utils import run_bass_kernel_spmd

F32 = mybir.dt.float32
BF16 = mybir.dt.bfloat16
I32 = mybir.dt.int32
AF = mybir.ActivationFunctionType
ALU = mybir.AluOpType
AX = mybir.AxisListType.X

NCORES = 8
T_CORE = 2048
D = 1024
NPBF16 = ml_dtypes.bfloat16

J_ONE = 0x7F0000
J_HALF = 0x7E0000
J_B025 = 0x3E800000 >> 7
J_B075 = 0x3F400000 >> 7
J_GMASK = 0x00FF8000
SIXTH = float(np.float32(1.0 / 6.0))


class Prog:

    def __init__(self, name="k"):
        self.nc = bass.Bass("TRN2", target_bir_lowering=False)
        self.ctx = ExitStack()
        self.sem = {}
        self.cnt = {}
        for s in ("t", "v", "s", "g", "y", "dh", "dg", "cc",
                  "dx", "dwa", "dwb", "dso", "dm"):
            self.sem[s] = self.ctx.enter_context(self.nc.semaphore("sem_" + s))
            self.cnt[s] = 0
        self.block = self.ctx.enter_context(self.nc.Block())
        self.watermark = {e: {} for e in ("t", "v", "s", "g", "y")}
        self.oplog = []
        Prog.last = self
        self._cur = None

    def sbuf(self, name, shape, dt=F32):
        return self.ctx.enter_context(self.nc.sbuf_tensor(name, shape, dt))

    def psum(self, name, shape, dt=F32):
        return self.ctx.enter_context(self.nc.psum_tensor(name, shape, dt))

    def dram(self, name, shape, dt=F32, kind=None):
        if kind:
            return self.nc.dram_tensor(name, shape, dt, kind=kind)
        return self.nc.dram_tensor(name, shape, dt)

    def section(self, engname, fn):
        attr = {"t": "tensor", "v": "vector", "s": "scalar", "g": "gpsimd",
                "y": "sync"}[engname]

        def run(e):
            self._cur = (engname, e)
            fn(e)
            self._cur = None
        getattr(self.block, attr)(run)

    def wait(self, tok):
        engname, e = self._cur
        s, v = tok
        if v <= 0:
            return
        wm = self.watermark[engname]
        if wm.get(s, 0) >= v:
            return
        e.wait_ge(self.sem[s], v)
        wm[s] = v

    def op(self, fn, waits=(), self_wait=True):
        engname, e = self._cur
        for w in waits:
            self.wait(w)
        if self_wait and engname != "t":
            self.wait((engname, self.cnt[engname]))
        inst = fn()
        self.cnt[engname] += 1
        inst.then_inc(self.sem[engname], 1)
        import sys as _sys
        self.oplog.append((engname, _sys._getframe(1).f_lineno))
        return (engname, self.cnt[engname])

    def dma(self, out, in_, waits=(), sem=None):
        engname, e = self._cur
        if sem is None:
            sem = "dg" if engname == "g" else "dh"
        for w in waits:
            self.wait(w)
        self.wait((sem, self.cnt[sem]))
        if engname != "t":
            self.wait((engname, self.cnt[engname]))
        self.cnt[sem] += 16
        e.dma_start(out=out, in_=in_).then_inc(self.sem[sem], 16)
        return (sem, self.cnt[sem])

    def tok(self, engname):
        return (engname, self.cnt[engname])

    def barrier(self):
        snap = {e: self.cnt[e] for e in ("t", "v", "s", "g")}
        dsnap = {d: self.cnt[d] for d in ("dh", "dx", "dwa", "dwb", "dso", "dm", "dg")}

        def mk(engname):
            def fn(e):
                for oe, c in snap.items():
                    self.wait((oe, c))
                for d, c in dsnap.items():
                    self.wait((d, c))
            return fn
        for e in ("t", "v", "s", "g", "y"):
            self.section(e, mk(e))

    def init_arena(self, fp32_elems=46000):
        self.ar_sb = self.sbuf("ARENA", [128, fp32_elems], F32)
        self.ar_ps = self.psum("PSARENA", [128, 4096], F32)
        self.ar_off = 0
        self.ar_ps_off = 0
        self.ar_elems = fp32_elems

    def phase_reset(self):
        self.ar_off = 0
        self.ar_ps_off = 0
        self.barrier()

    def sb(self, name, shape, dt=F32):
        p, n = shape[0], int(np.prod(shape[1:]))
        if dt == F32 or dt == I32:
            elems = n
        elif dt == BF16:
            elems = (n + 1) // 2
        else:
            raise ValueError(dt)
        off = self.ar_off
        self.ar_off += elems
        assert self.ar_off <= self.ar_elems, f"SBUF arena overflow at {name}: {self.ar_off}"
        view = self.ar_sb[0:p, off:off + elems]
        if dt == BF16:
            view = self.ar_sb[0:p, off:off + elems].bitcast(BF16)
            if view.shape[1] != n:
                view = view[:, 0:n]
        elif dt == I32:
            view = view.bitcast(I32)
        return view

    def ps(self, name, shape, dt=F32):
        p, n = shape[0], int(np.prod(shape[1:]))
        elems = n if dt in (F32, I32) else (n + 1) // 2
        banks = -(-elems // 512)
        off = self.ar_ps_off
        self.ar_ps_off += banks * 512
        assert self.ar_ps_off <= 4096, f"PSUM arena overflow at {name}"
        view = self.ar_ps[0:p, off:off + elems]
        if dt == BF16:
            view = self.ar_ps[0:p, off:off + (n + 1) // 2].bitcast(BF16)
            if view.shape[1] != n:
                view = view[:, 0:n]
        return view

    def done(self):
        self.ctx.close()
        return self.nc


def bcast16(scale_ap, p, nblk):
    return scale_ap.unsqueeze(2).broadcast_to([p, nblk, 16])


def blk16(ap):
    return ap.rearrange("p (b k) -> p b k", k=16)


def make_quant_scratch(prog, tag, p, Fmax):
    nblk = Fmax // 16
    alloc = prog.sb if hasattr(prog, "ar_sb") else prog.sbuf
    return {
        "smax": alloc(f"q_smax_{tag}", [p, nblk]),
        "rs6": alloc(f"q_rs6_{tag}", [p, nblk]),
        "scale": alloc(f"q_scale_{tag}", [p, nblk]),
        "t2": alloc(f"q_t2_{tag}", [p, Fmax]),
        "sgn": alloc(f"q_sgn_{tag}", [p, Fmax], BF16),
    }


def emit_quant(prog, src_ap, out_bf_ap, scr, p, F, in_tok, scr_free_tok=None, out_free_tok=None):
    nblk = F // 16
    smax, rs6, scale = scr["smax"], scr["rs6"], scr["scale"]
    t2, sgn = scr["t2"], scr["sgn"]
    xn = src_ap
    toks = {}

    def vsec(v):
        waits = [in_tok]
        if scr_free_tok is not None:
            waits.append(scr_free_tok)
        prog.op(lambda: v.tensor_reduce(out=smax[:p, :nblk], in_=blk16(src_ap),
                                        axis=AX, op=ALU.max,
                                        apply_absolute_value=True), waits=waits)
        prog.op(lambda: v.tensor_scalar(out=smax[:p, :nblk], in0=smax[:p, :nblk],
                                        scalar1=6e-12, scalar2=None, op0=ALU.max))
        prog.op(lambda: v.reciprocal(out=rs6[:p, :nblk], in_=smax[:p, :nblk]))
        prog.op(lambda: v.tensor_scalar(out=rs6[:p, :nblk], in0=rs6[:p, :nblk],
                                        scalar1=6.0, scalar2=None, op0=ALU.mult))
        prog.op(lambda: v.tensor_scalar(out=scale[:p, :nblk], in0=smax[:p, :nblk],
                                        scalar1=SIXTH, scalar2=None, op0=ALU.mult))
        toks["scales"] = prog.tok("v")
    prog.section("v", vsec)

    def gsec2(g):
        prog.op(lambda: g.tensor_tensor(out=blk16(src_ap), in0=blk16(src_ap),
                                        in1=bcast16(rs6[:p, :nblk], p, nblk),
                                        op=ALU.mult), waits=[toks["scales"]])
        toks["xn"] = prog.tok("g")
    prog.section("g", gsec2)

    def vsec2(v):
        prog.op(lambda: v.tensor_scalar(out=t2[:p, :F].bitcast(I32),
                                        in0=xn.bitcast(I32),
                                        scalar1=7, scalar2=0x00FFFFFF,
                                        op0=ALU.logical_shift_right, op1=ALU.bitwise_and),
                waits=[toks["xn"]])
        prog.op(lambda: v.tensor_scalar(out=sgn[:p, :F], in0=xn,
                                        scalar1=0.0, scalar2=None,
                                        op0=ALU.is_ge))
        prog.op(lambda: v.tensor_scalar(out=sgn[:p, :F], in0=sgn[:p, :F],
                                        scalar1=2.0, scalar2=-1.0,
                                        op0=ALU.mult, op1=ALU.add))
        prog.op(lambda: v.tensor_scalar(out=xn[:p, :F].bitcast(I32),
                                        in0=t2[:p, :F].bitcast(I32),
                                        scalar1=0x4000, scalar2=J_ONE,
                                        op0=ALU.add, op1=ALU.max))
        prog.op(lambda: v.tensor_scalar(out=xn[:p, :F].bitcast(I32),
                                        in0=xn[:p, :F].bitcast(I32),
                                        scalar1=J_GMASK, scalar2=None,
                                        op0=ALU.bitwise_and))
        prog.op(lambda: v.tensor_scalar(out=xn[:p, :F].bitcast(I32),
                                        in0=xn[:p, :F].bitcast(I32),
                                        scalar1=J_HALF, scalar2=None,
                                        op0=ALU.subtract))
        prog.op(lambda: v.scalar_tensor_tensor(out=xn[:p, :F].bitcast(I32),
                                               in0=t2[:p, :F].bitcast(I32),
                                               scalar=J_B075,
                                               in1=xn[:p, :F].bitcast(I32),
                                               op0=ALU.is_gt, op1=ALU.mult))
        prog.op(lambda: v.tensor_scalar(out=t2[:p, :F].bitcast(I32),
                                        in0=t2[:p, :F].bitcast(I32),
                                        scalar1=J_B025, scalar2=J_HALF,
                                        op0=ALU.is_gt, op1=ALU.mult))
        prog.op(lambda: v.tensor_tensor(out=t2[:p, :F].bitcast(I32),
                                        in0=t2[:p, :F].bitcast(I32),
                                        in1=xn[:p, :F].bitcast(I32), op=ALU.add))
        prog.op(lambda: v.tensor_scalar(out=t2[:p, :F].bitcast(I32),
                                        in0=t2[:p, :F].bitcast(I32),
                                        scalar1=7, scalar2=None,
                                        op0=ALU.logical_shift_left))
        prog.op(lambda: v.tensor_tensor(out=t2[:p, :F], in0=t2[:p, :F],
                                        in1=sgn[:p, :F], op=ALU.mult))
        toks["qb"] = prog.tok("v")
    prog.section("v", vsec2)

    def gsec3(g):
        w3 = [toks["qb"]]
        if out_free_tok is not None:
            w3.append(out_free_tok)
        prog.op(lambda: g.tensor_tensor(out=blk16(out_bf_ap),
                                        in0=blk16(t2[:p, :F]),
                                        in1=bcast16(scale[:p, :nblk], p, nblk),
                                        op=ALU.mult), waits=w3)
        toks["out"] = prog.tok("g")
    prog.section("g", gsec3)
    return toks["out"], toks["out"]


def emit_identity(prog, ident_bf, io1, io2):
    toks = {}

    def gsec(g):
        prog.op(lambda: g.iota(io1[:, :], [[1, 128]], channel_multiplier=0,
                               allow_small_or_imprecise_dtypes=True))
        prog.op(lambda: g.iota(io2[:, :], [[0, 128]], channel_multiplier=1,
                               allow_small_or_imprecise_dtypes=True))
        toks["io"] = prog.tok("g")
    prog.section("g", gsec)

    def vsec(v):
        prog.op(lambda: v.tensor_tensor(out=ident_bf[:, :], in0=io1[:, :],
                                        in1=io2[:, :], op=ALU.is_equal),
                waits=[toks["io"]])
        toks["id"] = prog.tok("v")
    prog.section("v", vsec)
    return toks["id"]


def build_wprep(rows_g, rows_d, ncast_cols):
    prog = Prog("wprep")
    wg = prog.dram("wg_s", [rows_g, 1024], F32, kind="ExternalInput")
    wu = prog.dram("wu_s", [rows_g, 1024], F32, kind="ExternalInput")
    wd = prog.dram("wd_s", [rows_d, 2816], F32, kind="ExternalInput")
    o_wg = prog.dram("o_wgqT", [1024, rows_g], BF16, kind="ExternalOutput")
    o_wu = prog.dram("o_wuqT", [1024, rows_g], BF16, kind="ExternalOutput")
    o_wd = prog.dram("o_wdqT", [2816, rows_d], BF16, kind="ExternalOutput")
    cin = prog.dram("cast_in", [128, ncast_cols], F32, kind="ExternalInput")
    cout = prog.dram("o_cast", [128, ncast_cols], BF16, kind="ExternalOutput")

    ident = prog.sbuf("ident", [128, 128], BF16)
    io1 = prog.sbuf("id_io1", [128, 128], I32)
    io2 = prog.sbuf("id_io2", [128, 128], I32)
    id_tok = emit_identity(prog, ident, io1, io2)

    scr = make_quant_scratch(prog, "w", 128, 2816)
    src = prog.sbuf("w_src", [128, 2816], F32)
    qbf = prog.sbuf("w_qbf", [128, 2816], BF16)
    tps = prog.psum("w_tp", [128, 128], BF16)
    tsb = prog.sbuf("w_tsb", [128, 128], BF16)

    jobs = [(wg, o_wg, rows_g, 1024), (wu, o_wu, rows_g, 1024),
            (wd, o_wd, rows_d, 2816)]

    prev_q = (None)
    src_free = ("v", 0)
    qbf_free = ("v", 0)
    store_done = ("dh", 0)
    for jn, (src_d, out_d, nrows, K) in enumerate(jobs):
        r0 = 0
        while r0 < nrows:
            p = min(128, nrows - r0)
            ld_tok = [None]

            def ysec(y, src_d=src_d, r0=r0, p=p, K=K, sf=src_free):
                ld_tok[0] = prog.dma(src[:p, :K], src_d.ap()[r0:r0 + p, :K],
                                     waits=(sf,))
            prog.section("y", ysec)

            q_tok, _ = emit_quant(prog, src[:p, :K], qbf[:p, :K], scr, p, K,
                                  ld_tok[0], scr_free_tok=qbf_free)
            src_free = q_tok

            nkc = K // 128
            last_cp = None
            for kc in range(nkc):
                def tsec(t, kc=kc, sd=store_done):
                    prog.op(lambda: t.transpose(tps[:, :],
                                                qbf[:, kc * 128:(kc + 1) * 128],
                                                ident[:, :]),
                            waits=[q_tok, id_tok, prog.tok("v")])
                prog.section("t", tsec)
                tr_tok = prog.tok("t")

                def vsec(v, sd=store_done):
                    prog.op(lambda: v.tensor_copy(out=tsb[:, :], in_=tps[:, :]),
                            waits=[tr_tok, sd])
                prog.section("v", vsec)
                cp_tok = prog.tok("v")
                last_cp = cp_tok

                def ysec2(y, kc=kc, out_d=out_d, r0=r0, p=p):
                    prog.dma(out_d.ap()[kc * 128:(kc + 1) * 128, r0:r0 + p],
                             tsb[:, 0:p], waits=(cp_tok,))
                prog.section("y", ysec2)
                store_done = ("dh", prog.cnt["dh"])
            qbf_free = last_cp
            qbf_free = tr_tok
            r0 += 128

    cast_sb = prog.sbuf("cast_sb", [128, 4096], F32)
    cast_ob = prog.sbuf("cast_ob", [128, 4096], BF16)
    c0 = 0
    cast_free = ("v", 0)
    cast_store = ("dh", 0)
    while c0 < ncast_cols:
        w = min(4096, ncast_cols - c0)
        ld = [None]

        def ysec3(y, c0=c0, w=w, cf=cast_free):
            ld[0] = prog.dma(cast_sb[:, :w], cin.ap()[:, c0:c0 + w], waits=(cf,))
        prog.section("y", ysec3)

        def vsec2(v, w=w, cs=cast_store):
            prog.op(lambda: v.tensor_copy(out=cast_ob[:, :w], in_=cast_sb[:, :w]),
                    waits=[ld[0], cs])
        prog.section("v", vsec2)
        cast_tok = prog.tok("v")
        cast_free = cast_tok

        def ysec4(y, c0=c0, w=w):
            prog.dma(cout.ap()[:, c0:c0 + w], cast_ob[:, :w], waits=(cast_tok,))
        prog.section("y", ysec4)
        cast_store = ("dh", prog.cnt["dh"])
        c0 += 4096

    def fin(y):
        y.wait_ge(prog.sem["dh"], prog.cnt["dh"])
        if prog.cnt["dg"]:
            y.wait_ge(prog.sem["dg"], prog.cnt["dg"])
    prog.section("y", fin)
    return prog.done()


def _collect_cast_weights(inp):
    names = ["L0_attn_in_proj_w", "L0_attn_dt_proj_w", "L0_attn_x_proj_w",
             "L0_attn_out_proj_w", "L1_attn_Wq", "L1_attn_Wk", "L1_attn_Wv",
             "L1_attn_Wo", "L1_ffn_w_gate", "L1_ffn_Wd", "L1_ffn_Wu",
             "L1_ffn_bu", "L1_attn_projection"]
    return [(n, np.ascontiguousarray(inp[n], dtype=np.float32)) for n in names]


def run_wprep(inp):
    arrs = _collect_cast_weights(inp)
    total = sum(a.size for _, a in arrs)
    per_core = -(-total // NCORES)
    per_core = -(-per_core // 128) * 128
    ncast_cols = per_core // 128
    rows_g = 2816 // NCORES
    rows_d = 1024 // NCORES
    nc0 = build_wprep(rows_g, rows_d, ncast_cols)

    flat = np.concatenate([a.reshape(-1) for _, a in arrs]).astype(np.float32)
    flat = np.concatenate([flat, np.zeros(per_core * NCORES - total, np.float32)])
    in_maps = []
    for i in range(NCORES):
        in_maps.append({
            "wg_s": np.ascontiguousarray(inp["L0_ffn_w_gate"][i * rows_g:(i + 1) * rows_g]),
            "wu_s": np.ascontiguousarray(inp["L0_ffn_w_up"][i * rows_g:(i + 1) * rows_g]),
            "wd_s": np.ascontiguousarray(inp["L0_ffn_w_down"][i * rows_d:(i + 1) * rows_d]),
            "cast_in": np.ascontiguousarray(
                flat[i * per_core:(i + 1) * per_core].reshape(128, ncast_cols)),
        })
    res = run_bass_kernel_spmd(nc0, in_maps, core_ids=list(range(NCORES)))
    wgqT = np.concatenate([res.results[i]["o_wgqT"] for i in range(NCORES)], axis=1)
    wuqT = np.concatenate([res.results[i]["o_wuqT"] for i in range(NCORES)], axis=1)
    wdqT = np.concatenate([res.results[i]["o_wdqT"] for i in range(NCORES)], axis=1)
    cflat = np.concatenate([res.results[i]["o_cast"].reshape(-1) for i in range(NCORES)])[:total]
    casts = {}
    off = 0
    for n, a in arrs:
        casts[n] = cflat[off:off + a.size].reshape(a.shape)
        off += a.size
    return wgqT, wuqT, wdqT, casts


_NC_CACHE = {}


def _get_nc(name, builder):
    if name not in _NC_CACHE:
        _NC_CACHE[name] = builder()
    return _NC_CACHE[name]


def kernel(**inp):
    return kernel_fused(**inp)


def emit_ffn(prog, io, halo=False):
    NBW = [512] * 5 + [256]
    x_d = io["x"]
    wg_d = io["wgqT"]
    wu_d = io["wuqT"]
    wd_d = io["wdqT"]
    o_d = io["o"]
    ident = io["_ident"]
    id_tok = io["_id_tok"]
    if halo:
        chunks = [(128, HALO + i * 128) for i in range(16)] + [(HALO, 0)]
    else:
        chunks = [(128, i * 128) for i in range(16)]
    TCn = len(chunks)

    xqT = prog.sb("xqT", [128, 8 * TCn * 128], BF16)
    hq = prog.sb("hq", [128, TCn * 2816], BF16)
    wd_sb = prog.sb("wd_sb", [128, 22 * 1024], BF16)
    xc = prog.sb("xc", [128, 1024])
    xln = prog.sb("xln", [128, 1024])
    sqs = xln
    xq = prog.sb("xq", [128, 1024], BF16)
    stat = {k: prog.sb("st_" + k, [128, 1]) for k in
            ("sum", "ssq", "mean", "ex2", "m2", "var", "rstd")}
    scr = make_quant_scratch(prog, "f", 128, 1024)
    scrh = scr
    wgc = prog.sb("wgc", [128, 8 * 512], BF16)
    wuc = prog.sb("wuc", [128, 8 * 512], BF16)
    hs = None
    hf = None
    hqT = None
    oc = xln
    ps_tp = prog.ps("ps_tp", [128, 128], BF16)
    ps_g = prog.ps("ps_g", [128, 512])
    ps_u = prog.ps("ps_u", [128, 512])
    ps_dn = prog.ps("ps_dn", [128, 1024])

    hs = xc[:, 0:512]
    hf = xc[:, 512:1024]
    hqT = wgc[:, 0:2816]

    def vini(v):
        prog.op(lambda: v.memset(xc[:, :], 0.0))
        prog.op(lambda: v.memset(xq[:, :], 0.0))
    prog.section("v", vini)

    def ldwd(y):
        prog.dma(wd_sb[:, :].rearrange("p (kc n) -> p kc n", n=1024),
                 wd_d.ap().rearrange("(kc p) n -> p kc n", p=128), sem="dwa")
    prog.section("y", ldwd)
    wd_tok = ("dwa", prog.cnt["dwa"])

    xc_free = ("v", 0)
    xq_free = ("t", 0)
    q_prev = ("g", 0)
    prev_aff2 = ("t", 0)
    for tc in range(TCn):
        p, r0 = chunks[tc]
        ld = [None]

        def yld(y, tc=tc, xf=xc_free, p=p, r0=r0):
            ld[0] = prog.dma(xc[:p, :], x_d.ap()[r0:r0 + p, :],
                             waits=(xf,), sem="dx")
        prog.section("y", yld)

        def ssec(s, tc=tc, pa=prev_aff2, p=p):
            prog.op(lambda: s.activation(out=sqs[:p, :], in_=xc[:p, :], func=AF.Square,
                                         accum_out=stat["ssq"][:p, :]),
                    waits=[ld[0], pa])
        prog.section("s", ssec)
        sq_tok = prog.tok("s")

        def vsec(v, tc=tc):
            prog.op(lambda: v.tensor_reduce(out=stat["sum"][:, :], in_=xc[:, :],
                                            axis=AX, op=ALU.add), waits=[ld[0]])
            prog.op(lambda: v.tensor_scalar(out=stat["mean"][:, :], in0=stat["sum"][:, :],
                                            scalar1=1.0 / D, scalar2=None, op0=ALU.mult))
            prog.op(lambda: v.tensor_scalar(out=stat["ex2"][:, :], in0=stat["ssq"][:, :],
                                            scalar1=1.0 / D, scalar2=None, op0=ALU.mult),
                    waits=[sq_tok])
            prog.op(lambda: v.tensor_tensor(out=stat["m2"][:, :], in0=stat["mean"][:, :],
                                            in1=stat["mean"][:, :], op=ALU.mult))
            prog.op(lambda: v.tensor_tensor(out=stat["var"][:, :], in0=stat["ex2"][:, :],
                                            in1=stat["m2"][:, :], op=ALU.subtract))
            prog.op(lambda: v.tensor_scalar(out=stat["var"][:, :], in0=stat["var"][:, :],
                                            scalar1=1e-5, scalar2=None, op0=ALU.add))
        prog.section("v", vsec)
        var_tok = prog.tok("v")

        def ssec2(s, tc=tc):
            prog.op(lambda: s.activation(out=stat["rstd"][:, :], in_=stat["var"][:, :],
                                         func=AF.Ln), waits=[var_tok])
            prog.op(lambda: s.activation(out=stat["rstd"][:, :], in_=stat["rstd"][:, :],
                                         func=AF.Exp, scale=-0.5))
        prog.section("s", ssec2)
        rstd_tok = prog.tok("s")

        def vsec2(v, tc=tc):
            prog.op(lambda: v.tensor_scalar(out=xln[:, :], in0=xc[:, :],
                                            scalar1=stat["mean"][:, :],
                                            scalar2=stat["rstd"][:, :],
                                            op0=ALU.subtract, op1=ALU.mult),
                    waits=[rstd_tok])
        prog.section("v", vsec2)
        xln_tok = prog.tok("v")
        xc_free = xln_tok

        q_tok, _ = emit_quant(prog, xln[:p, :], xq[:p, :], scr, p, 1024, xln_tok,
                              scr_free_tok=q_prev, out_free_tok=xq_free)
        q_prev = q_tok

        for kc in range(8):
            def tsec(t, kc=kc, tc=tc):
                prog.op(lambda: t.transpose(ps_tp[:, :],
                                            xq[:, kc * 128:(kc + 1) * 128],
                                            ident[:, :]),
                        waits=[q_tok, id_tok, prog.tok("v")])
            prog.section("t", tsec)
            tr_tok = prog.tok("t")

            def vsec3(v, kc=kc, tc=tc, p=p):
                prog.op(lambda: v.tensor_copy(
                    out=xqT[:, (kc * TCn + tc) * 128:(kc * TCn + tc) * 128 + p],
                    in_=ps_tp[:, 0:p]), waits=[tr_tok])
            prog.section("v", vsec3)
        xq_free = prog.tok("t")
        prev_aff2 = prog.tok("t")
    xqT_tok = prog.tok("v")

    w_free = ("t", 0)
    hq_toks = {}
    ps_g_free = ("v", 0)
    ps_u_free = ("v", 0)
    n0 = 0
    for nb, NW in enumerate(NBW):
        wld = [None, None]

        def yw(y, nb=nb, NW=NW, n0=n0, wf=w_free):
            wld[0] = prog.dma(wgc[:, :8 * NW].rearrange("p (kc n) -> p kc n", n=NW),
                              wg_d.ap()[:, n0:n0 + NW].rearrange("(kc p) n -> p kc n", p=128),
                              waits=(wf,), sem="dwb")
            wld[1] = prog.dma(wuc[:, :8 * NW].rearrange("p (kc n) -> p kc n", n=NW),
                              wu_d.ap()[:, n0:n0 + NW].rearrange("(kc p) n -> p kc n", p=128),
                              sem="dwb")
        prog.section("y", yw)

        for tc in range(TCn):
            p, r0 = chunks[tc]
            def tmm(t, tc=tc, NW=NW, pg=ps_g_free, pu=ps_u_free, p=p):
                prog.wait(wld[0]); prog.wait(wld[1])
                prog.wait(xqT_tok)
                prog.wait(pg); prog.wait(pu)
                for kc in range(8):
                    lhs = xqT[:, (kc * TCn + tc) * 128:(kc * TCn + tc) * 128 + p]
                    prog.op(lambda kc=kc, lhs=lhs: t.matmul(ps_g[:p, :NW], lhs,
                                                   wgc[:, kc * NW:(kc + 1) * NW],
                                                   start=(kc == 0), stop=(kc == 7)))
                for kc in range(8):
                    lhs = xqT[:, (kc * TCn + tc) * 128:(kc * TCn + tc) * 128 + p]
                    prog.op(lambda kc=kc, lhs=lhs: t.matmul(ps_u[:p, :NW], lhs,
                                                   wuc[:, kc * NW:(kc + 1) * NW],
                                                   start=(kc == 0), stop=(kc == 7)))
            prog.section("t", tmm)
            mm_tok = prog.tok("t")

            def shsec(s, NW=NW, p=p):
                prog.op(lambda: s.activation(out=hs[:p, :NW], in_=ps_g[:p, :NW],
                                             func=AF.Silu), waits=[mm_tok])
            prog.section("s", shsec)
            hs_tok = prog.tok("s")

            def vh(v, NW=NW, p=p):
                prog.op(lambda: v.tensor_tensor(out=hf[:p, :NW], in0=hs[:p, :NW],
                                                in1=ps_u[:p, :NW], op=ALU.mult),
                        waits=[hs_tok])
            prog.section("v", vh)
            hf_tok = prog.tok("v")
            ps_g_free = hs_tok
            ps_u_free = hf_tok

            hq_out = hq[:p, tc * 2816 + n0: tc * 2816 + n0 + NW]
            q2_tok, _ = emit_quant(prog, hf[:p, :NW], hq_out, scrh, p, NW,
                                   hf_tok, scr_free_tok=hq_toks.get("prev"))
            hq_toks["prev"] = q2_tok
        w_free = prog.tok("t")
        n0 += NW
    hq_done = hq_toks["prev"]

    o_store = [("dso", 0)]
    ps_dn_free = ("v", 0)
    hqT_free = ("t", 0)
    for tc in range(TCn):
        p, r0 = chunks[tc]
        cps = []
        for kc in range(22):
            def tsec2(t, kc=kc, tc=tc):
                prog.op(lambda: t.transpose(
                    ps_tp[:, :], hq[:, tc * 2816 + kc * 128: tc * 2816 + kc * 128 + 128],
                    ident[:, :]), waits=[hq_done, prog.tok("v")])
            prog.section("t", tsec2)
            tr2 = prog.tok("t")

            def vsec4(v, kc=kc):
                prog.op(lambda: v.tensor_copy(out=hqT[:, kc * 128:(kc + 1) * 128],
                                              in_=ps_tp[:, :]), waits=[tr2])
            prog.section("v", vsec4)
        hqT_tok = prog.tok("v")

        def tdn(t, tc=tc, pf=ps_dn_free, p=p):
            prog.wait(hqT_tok)
            prog.wait(wd_tok)
            prog.wait(pf)
            for half in range(2):
                for kc in range(22):
                    prog.op(lambda kc=kc, half=half: t.matmul(
                        ps_dn[:p, half * 512:(half + 1) * 512],
                        hqT[:, kc * 128:kc * 128 + p],
                        wd_sb[:, kc * 1024 + half * 512: kc * 1024 + half * 512 + 512],
                        start=(kc == 0), stop=(kc == 21)))
        prog.section("t", tdn)
        dn_tok = prog.tok("t")

        ld2 = [None]

        def yld2(y, tc=tc, p=p, r0=r0):
            ld2[0] = prog.dma(xc[:p, :], x_d.ap()[r0:r0 + p, :],
                              waits=(prog.tok("v"),), sem="dx")
        prog.section("y", yld2)

        def vev(v, tc=tc, p=p, ost=o_store[0]):
            prog.wait(ost)
            prog.op(lambda: v.scalar_tensor_tensor(out=oc[:p, :], in0=ps_dn[:p, :1024],
                                                   scalar=0.5, in1=xc[:p, :],
                                                   op0=ALU.mult, op1=ALU.add),
                    waits=[dn_tok, ld2[0]])
        prog.section("v", vev)
        ev_tok = prog.tok("v")
        ps_dn_free = ev_tok
        xc_free = ev_tok

        def yst(y, tc=tc, p=p, r0=r0):
            o_store[0] = prog.dma(o_d.ap()[r0:r0 + p, :], oc[:p, :], waits=(ev_tok,),
                                  sem="dso")
        prog.section("y", yst)

    return


def run_ffn(nc, x_full, wgqT, wuqT, wdqT):
    in_maps = []
    for i in range(NCORES):
        in_maps.append({
            "x_s": np.ascontiguousarray(x_full[i * T_CORE:(i + 1) * T_CORE]),
            "wgqT": wgqT, "wuqT": wuqT, "wdqT": wdqT,
        })
    res = run_bass_kernel_spmd(nc, in_maps, core_ids=list(range(NCORES)))
    return np.concatenate([res.results[i]["o_x"] for i in range(NCORES)], axis=0)


HALO = 3
XCOLS = 2176
LT = 512
NT = 4


def emit_mamba_a(prog, io):
    x_d = io["x"]
    msk_d = io["halo_mask"]
    ip_d = io["inproj"]
    dt_d = io["dtw"]
    xp_d = io["xpwB"]
    cw_d = io["convw_r"]
    cb_d = io["convb_r"]
    db_d = io["dtb_r"]
    al_d = io["alog_r"]
    dp_d = io["dparam_r"]
    lg_d = io["lng_r"]
    lb_d = io["lnb_r"]
    y1_d = io["y1"]
    dc_d = io["dcum"]
    xt_d = io["xn2T"]
    he_d = io["hend"]
    pc_d = io["pcore"]
    ms_d = io["mstat"]
    rs_d = io["rstat"]
    bbounce = io["b_bounce"]
    ident = io["_ident"]
    id_tok = io["_id_tok"]

    xn2T = prog.sb("xn2T", [128, 8 * XCOLS], BF16)
    u_f = prog.sb("u_f", [128, 16 * LT])
    u_bf = prog.sb("u_bf", [128, 16 * LT], BF16)
    Brep = prog.sb("Brep", [128, 16 * LT])
    A_sb = prog.sb("A_sb", [128, 256])
    st_sb = prog.sb("st_sb", [128, 256])
    Dst = prog.sb("Dst", [128, 16])
    gam = prog.sb("gam", [128, 1024])
    bet = prog.sb("bet", [128, 1024])
    cw_sb = prog.sb("cw_sb", [128, 16 * 4])
    cb_sb = prog.sb("cb_sb", [128, 16])
    db_sb = prog.sb("db_sb", [128, 16])
    dp_sb = prog.sb("dp_sb", [128, 16])
    msk = prog.sb("msk", [128, HALO])
    xc = prog.sb("xc", [128, 1024])
    xl2 = prog.sb("xl2", [128, 1024])
    xq = prog.sb("xq", [128, 1024], BF16)
    stat = {k: prog.sb("stm_" + k, [128, 1]) for k in
            ("sum", "ssq", "mean", "ex2", "m2", "var", "rstd")}
    stat_m = prog.sb("stat_m", [128, 16])
    stat_r = prog.sb("stat_r", [128, 16])
    uraw = prog.sb("uraw", [128, LT + 4])
    wrot = prog.sb("wrot", [128, 16 * 128], BF16)
    wrot2 = prog.sb("wrot2", [128, 16 * 128], BF16)
    delta = prog.sb("delta", [128, LT])
    delta2 = prog.sb("delta2", [128, LT])
    dscr2 = prog.sb("dscr2", [128, LT])
    udc = prog.sb("udc", [128, LT])
    dA_b = prog.sb("dA_b", [128, 7 * LT])
    b_b = prog.sb("b_b", [128, 7 * LT])
    h_b = prog.sb("h_b", [128, 16 * LT], BF16)
    dcum_t = prog.sb("dcum_t", [128, LT])
    zeros = prog.sb("zeros", [128, LT])
    y1c = prog.sb("y1c", [128, LT], BF16)
    bsb = prog.sb("bsb", [16, LT])

    ps_a = prog.ps("ps_a", [128, 512])
    ps_a2 = prog.ps("ps_a2", [128, 512])
    ps_h = prog.ps("ps_h", [128, 16])
    ps_y = prog.ps("ps_y", [128, 512])
    ps_x = prog.ps("ps_x", [16, 512])
    ps_tp = prog.ps("ps_tp", [128, 128], BF16)

    def pld(y):
        prog.dma(gam[:, :], lg_d.ap(), sem="dm")
        prog.dma(bet[:, :], lb_d.ap(), sem="dm")
        prog.dma(cw_sb[:, :], cw_d.ap(), sem="dm")
        prog.dma(cb_sb[:, :], cb_d.ap(), sem="dm")
        prog.dma(db_sb[:, :], db_d.ap(), sem="dm")
        prog.dma(dp_sb[:, :], dp_d.ap(), sem="dm")
        prog.dma(msk[:, :], msk_d.ap(), sem="dm")
        prog.dma(A_sb[:, :], al_d.ap(), sem="dm")
    prog.section("y", pld)
    pld_tok = ("dm", prog.cnt["dm"])

    def vinit(v):
        prog.op(lambda: v.memset(zeros[:, :], 0.0))
        prog.op(lambda: v.memset(st_sb[:, :], 0.0))
        prog.op(lambda: v.memset(Dst[:, :], 0.0))
        prog.op(lambda: v.memset(xn2T[:, :], 0.0))
    prog.section("v", vinit)

    def sA(s):
        prog.op(lambda: s.activation(out=A_sb[:, :], in_=A_sb[:, :], func=AF.Exp),
                waits=[pld_tok])
        prog.op(lambda: s.activation(out=A_sb[:, :], in_=A_sb[:, :], func=AF.Copy,
                                     scale=-1.0))
    prog.section("s", sA)
    A_tok = prog.tok("s")

    xc_free = ("g", 0)
    q_free = ("t", 0)
    prev_aff = ("g", 0)
    for tc in range(17):
        if tc < 16:
            p = 128
            rows = (HALO + tc * 128, HALO + tc * 128 + 128)
            colbase = 3 + tc * 128
        else:
            p = HALO
            rows = (0, HALO)
            colbase = 0
        ld = [None]

        def yld(y, rows=rows, p=p, xf=xc_free):
            ld[0] = prog.dma(xc[:p, :], x_d.ap()[rows[0]:rows[1], :], waits=(xf,),
                             sem="dx")
        prog.section("y", yld)

        def ssec(s, p=p, pa=prev_aff):
            prog.op(lambda: s.activation(out=xl2[:p, :], in_=xc[:p, :], func=AF.Square,
                                         accum_out=stat["ssq"][:p, :]),
                    waits=[ld[0], pa])
        prog.section("s", ssec)
        sq_tok = prog.tok("s")

        def vsec(v, p=p, tc=tc):
            prog.op(lambda: v.tensor_reduce(out=stat["sum"][:p, :], in_=xc[:p, :],
                                            axis=AX, op=ALU.add), waits=[ld[0]])
            prog.op(lambda: v.tensor_scalar(out=stat["mean"][:p, :], in0=stat["sum"][:p, :],
                                            scalar1=1.0 / D, scalar2=None, op0=ALU.mult))
            prog.op(lambda: v.tensor_scalar(out=stat["ex2"][:p, :], in0=stat["ssq"][:p, :],
                                            scalar1=1.0 / D, scalar2=None, op0=ALU.mult),
                    waits=[sq_tok])
            prog.op(lambda: v.tensor_tensor(out=stat["m2"][:p, :], in0=stat["mean"][:p, :],
                                            in1=stat["mean"][:p, :], op=ALU.mult))
            prog.op(lambda: v.tensor_tensor(out=stat["var"][:p, :], in0=stat["ex2"][:p, :],
                                            in1=stat["m2"][:p, :], op=ALU.subtract))
            prog.op(lambda: v.tensor_scalar(out=stat["var"][:p, :], in0=stat["var"][:p, :],
                                            scalar1=1e-5, scalar2=None, op0=ALU.add))
        prog.section("v", vsec)
        var_tok = prog.tok("v")

        def ssec2(s, p=p):
            prog.op(lambda: s.activation(out=stat["rstd"][:p, :], in_=stat["var"][:p, :],
                                         func=AF.Ln), waits=[var_tok])
            prog.op(lambda: s.activation(out=stat["rstd"][:p, :], in_=stat["rstd"][:p, :],
                                         func=AF.Exp, scale=-0.5))
        prog.section("s", ssec2)
        rstd_tok = prog.tok("s")

        def vsec2(v, p=p, tc=tc):
            prog.op(lambda: v.tensor_scalar(out=xl2[:p, :], in0=xc[:p, :],
                                            scalar1=stat["mean"][:p, :],
                                            scalar2=stat["rstd"][:p, :],
                                            op0=ALU.subtract, op1=ALU.mult),
                    waits=[rstd_tok])
            if tc < 16:
                prog.op(lambda: v.tensor_copy(out=stat_m[:, tc:tc + 1],
                                              in_=stat["mean"][:, :]))
                prog.op(lambda: v.tensor_copy(out=stat_r[:, tc:tc + 1],
                                              in_=stat["rstd"][:, :]))
        prog.section("v", vsec2)
        xl_tok = prog.tok("v")
        xc_free = xl_tok

        def ssec3(s, p=p, qf=q_free):
            prog.op(lambda: s.activation(out=xq[:p, :], in_=xl2[:p, :], func=AF.Square,
                                         accum_out=stat["ssq"][:p, :]),
                    waits=[xl_tok, qf])
        prog.section("s", ssec3)
        sq2_tok = prog.tok("s")

        def vsec2b(v, p=p):
            prog.op(lambda: v.tensor_reduce(out=stat["sum"][:p, :], in_=xl2[:p, :],
                                            axis=AX, op=ALU.add), waits=[xl_tok])
            prog.op(lambda: v.tensor_scalar(out=stat["mean"][:p, :], in0=stat["sum"][:p, :],
                                            scalar1=1.0 / D, scalar2=None, op0=ALU.mult))
            prog.op(lambda: v.tensor_scalar(out=stat["ex2"][:p, :], in0=stat["ssq"][:p, :],
                                            scalar1=1.0 / D, scalar2=None, op0=ALU.mult),
                    waits=[sq2_tok])
            prog.op(lambda: v.tensor_tensor(out=stat["m2"][:p, :], in0=stat["mean"][:p, :],
                                            in1=stat["mean"][:p, :], op=ALU.mult))
            prog.op(lambda: v.tensor_tensor(out=stat["var"][:p, :], in0=stat["ex2"][:p, :],
                                            in1=stat["m2"][:p, :], op=ALU.subtract))
            prog.op(lambda: v.tensor_scalar(out=stat["var"][:p, :], in0=stat["var"][:p, :],
                                            scalar1=1e-5, scalar2=None, op0=ALU.add))
        prog.section("v", vsec2b)
        var2_tok = prog.tok("v")

        def ssec4(s, p=p):
            prog.op(lambda: s.activation(out=stat["rstd"][:p, :], in_=stat["var"][:p, :],
                                         func=AF.Ln), waits=[var2_tok])
            prog.op(lambda: s.activation(out=stat["rstd"][:p, :], in_=stat["rstd"][:p, :],
                                         func=AF.Exp, scale=-0.5))
        prog.section("s", ssec4)
        rstd2_tok = prog.tok("s")

        def vsec2c(v, p=p):
            prog.op(lambda: v.tensor_scalar(out=xl2[:p, :], in0=xl2[:p, :],
                                            scalar1=stat["mean"][:p, :],
                                            scalar2=stat["rstd"][:p, :],
                                            op0=ALU.subtract, op1=ALU.mult),
                    waits=[rstd2_tok])
        prog.section("v", vsec2c)

        def gsec(g, p=p):
            prog.op(lambda: g.tensor_tensor(out=xl2[:p, :], in0=xl2[:p, :],
                                            in1=gam[:p, :], op=ALU.mult),
                    waits=[prog.tok("v"), pld_tok])
            prog.op(lambda: g.tensor_tensor(out=xq[:p, :], in0=xl2[:p, :],
                                            in1=bet[:p, :], op=ALU.add))
        prog.section("g", gsec)
        aff_tok = prog.tok("g")
        prev_aff = aff_tok

        for kc in range(8):
            def tsec(t, kc=kc, p=p):
                prog.op(lambda: t.transpose(ps_tp[:, :],
                                            xq[:, kc * 128:(kc + 1) * 128],
                                            ident[:, :]),
                        waits=[aff_tok, id_tok, prog.tok("v")])
            prog.section("t", tsec)
            tr_tok = prog.tok("t")

            def vsec3(v, kc=kc, colbase=colbase, p=p):
                prog.op(lambda: v.tensor_copy(
                    out=xn2T[:, kc * XCOLS + colbase: kc * XCOLS + colbase + p],
                    in_=ps_tp[:, 0:p]), waits=[tr_tok])
            prog.section("v", vsec3)
        q_free = prog.tok("t")
    xn2T_tok = prog.tok("v")

    def ydump(y):
        prog.dma(xt_d.ap(), xn2T[:, :], waits=(xn2T_tok,), sem="dso")
        prog.dma(ms_d.ap(), stat_m[:, :], sem="dso")
        prog.dma(rs_d.ap(), stat_r[:, :], sem="dso")
    prog.section("y", ydump)

    bb_store = [0]
    w_free = ("t", 0)
    uraw_free = ("s", 0)
    brep_free = ("v", 0)
    dc_store = [("dso", 0)]
    y1_store = [("dh", 0)]
    wrot_free = {0: ("t", 0), 1: ("t", 0)}
    ps_free = {0: [], 1: []}
    dscr2_box = [("v", 0)]
    delta_war = [("s", 0), ("s", 0)]
    prev_b_tok = ("g", 0)
    prev_b_box = [("g", 0)]
    scan_war_s = [("v", 0)] * 7
    mm_war = [("t", 0)] * 16
    st_war = [("g", 0)] * 16
    stc_war = [("g", 0)]
    vy_war = [("v", 0)]
    for tt in range(NT):
        for nc_ in range(16):
            wld = [None]
            wsel_f = wrot if (nc_ % 2 == 0) else wrot2

            def yw(y, nc_=nc_, wsel_f=wsel_f, wf=wrot_free[nc_ % 2]):
                wld[0] = prog.dma(
                    wsel_f[:, 0:8 * 128].rearrange("p (kc n) -> p kc n", n=128),
                    ip_d.ap()[:, nc_ * 128:(nc_ + 1) * 128]
                        .rearrange("(kc p) n -> p kc n", p=128),
                    waits=(wf,), sem="dwa")
            prog.section("y", yw)

            def tmm(t, nc_=nc_, tt=tt, wsel_f=wsel_f, uf=uraw_free):
                prog.wait(wld[0])
                prog.wait(xn2T_tok)
                prog.wait(uf)
                for kc in range(8):
                    prog.op(lambda kc=kc: t.matmul(
                        ps_a[:, :512],
                        wsel_f[:, kc * 128:(kc + 1) * 128],
                        xn2T[:, kc * XCOLS + tt * LT: kc * XCOLS + tt * LT + 512],
                        start=(kc == 0), stop=(kc == 7)))
                for kc in range(8):
                    prog.op(lambda kc=kc: t.matmul(
                        ps_h[:, 0:HALO],
                        wsel_f[:, kc * 128:(kc + 1) * 128],
                        xn2T[:, kc * XCOLS + tt * LT + 512: kc * XCOLS + tt * LT + 512 + HALO],
                        start=(kc == 0), stop=(kc == 7)))
            prog.section("t", tmm)
            mm_tok = prog.tok("t")
            wrot_free[nc_ % 2] = mm_tok
            w_free = mm_tok

            def vev(v, nc_=nc_, tt=tt):
                prog.op(lambda: v.tensor_copy(out=uraw[:, 0:512], in_=ps_a[:, :512]),
                        waits=[mm_tok])
                prog.op(lambda: v.tensor_copy(out=uraw[:, 512:512 + HALO],
                                              in_=ps_h[:, 0:HALO]))
                if tt == 0:
                    prog.op(lambda: v.tensor_tensor(out=uraw[:, 0:HALO],
                                                    in0=uraw[:, 0:HALO],
                                                    in1=msk[:, :], op=ALU.mult))
            prog.section("v", vev)
            ur_tok = prog.tok("v")

            def vconv(v, nc_=nc_, pb=prev_b_tok):
                prog.op(lambda: v.tensor_scalar(out=udc[:, 0:LT], in0=uraw[:, 0:LT],
                                                scalar1=cw_sb[:, nc_ * 4:nc_ * 4 + 1],
                                                scalar2=None, op0=ALU.mult),
                        waits=[ur_tok, pld_tok, pb])
                for k in range(1, 4):
                    prog.op(lambda k=k: v.scalar_tensor_tensor(
                        out=udc[:, 0:LT], in0=uraw[:, k:k + LT],
                        scalar=cw_sb[:, nc_ * 4 + k:nc_ * 4 + k + 1],
                        in1=udc[:, 0:LT], op0=ALU.mult, op1=ALU.add))
            prog.section("v", vconv)
            cv_tok = prog.tok("v")

            def ssil(s, nc_=nc_):
                prog.op(lambda: s.activation(out=u_f[:, nc_ * LT:(nc_ + 1) * LT],
                                             in_=udc[:, 0:LT], func=AF.Silu,
                                             bias=cb_sb[:, nc_:nc_ + 1]),
                        waits=[cv_tok])
            prog.section("s", ssil)
            sil_tok = prog.tok("s")
            uraw_free = sil_tok

            def vcast(v, nc_=nc_):
                prog.op(lambda: v.tensor_copy(out=u_bf[:, nc_ * LT:(nc_ + 1) * LT],
                                              in_=u_f[:, nc_ * LT:(nc_ + 1) * LT]),
                        waits=[sil_tok])
            prog.section("v", vcast)
        ubf_tok = prog.tok("v")

        xpld = [None]

        def yxp(y, wf=w_free):
            xpld[0] = prog.dma(wrot[:, 0:16 * 16].rearrange("p (kc n) -> p kc n", n=16),
                               xp_d.ap().rearrange("(kc p) n -> p kc n", p=128),
                               waits=(wf,), sem="dwa")
        prog.section("y", yxp)

        def txp(t, tt=tt):
            prog.wait(xpld[0])
            prog.wait(ubf_tok)
            for kc in range(16):
                prog.op(lambda kc=kc: t.matmul(
                    ps_x[:16, :512], wrot[:, kc * 16:(kc + 1) * 16],
                    u_bf[:, kc * LT:(kc + 1) * LT],
                    start=(kc == 0), stop=(kc == 15)))
        prog.section("t", txp)
        xp_tok = prog.tok("t")
        w_free = xp_tok
        wrot_free[0] = xp_tok

        def vxp(v, bf_=brep_free, bt=None):
            prog.op(lambda: v.tensor_copy(out=bsb[:16, :], in_=ps_x[:16, :512]),
                    waits=[xp_tok, bf_, ("dwb", bb_store[0])])
        prog.section("v", vxp)
        bsb_tok = prog.tok("v")

        def ybb(y, tt=tt):
            tok = prog.dma(bbounce.ap(), bsb[:16, :], waits=(bsb_tok,), sem="dwb")
            bb_store[0] = tok[1]
            for n in range(16):
                prog.dma(Brep[:, n * LT:(n + 1) * LT],
                         bbounce.ap()[n:n + 1, :].broadcast_to([128, LT]),
                         waits=(tok,), sem="dwb")
        prog.section("y", ybb)
        brep_tok = ("dwb", prog.cnt["dwb"])
        wrot_free[0] = w_free

        P1 = {}

        def phase1a(k):
            wsel = wrot if (k % 2 == 0) else wrot2
            pa = ps_a if (k % 2 == 0) else ps_a2
            dtl = [None]

            def ydt(y, k=k, wsel=wsel, wf=wrot_free[k % 2]):
                dtl[0] = prog.dma(
                    wsel[:, :].rearrange("p (kc n) -> p kc n", n=128),
                    dt_d.ap()[:, k * 128:(k + 1) * 128]
                        .rearrange("(kc p) n -> p kc n", p=128),
                    waits=(wf,), sem="dwa")
            prog.section("y", ydt)

            def tdt(t, k=k, wsel=wsel, pa=pa, paf=ps_free[k % 2]):
                prog.wait(dtl[0])
                prog.wait(ubf_tok)
                for w in paf:
                    prog.wait(w)
                for kc in range(16):
                    prog.op(lambda kc=kc: t.matmul(
                        pa[:, :512], wsel[:, kc * 128:(kc + 1) * 128],
                        u_bf[:, kc * LT:(kc + 1) * LT],
                        start=(kc == 0), stop=(kc == 15)))
            prog.section("t", tdt)
            P1[k] = {"dt_tok": prog.tok("t"), "pa": pa,
                     "delta": delta if (k % 2 == 0) else delta2}
            wrot_free[k % 2] = P1[k]["dt_tok"]

        def phase1b(k):
            st = P1[k]
            pa = st["pa"]
            dl = st["delta"]

            def sdelta(s, k=k, df=dscr2_box[0]):
                prog.op(lambda: s.activation(out=dscr2[:, :], in_=pa[:, :512],
                                             func=AF.Abs,
                                             bias=db_sb[:, k:k + 1]),
                        waits=[st["dt_tok"], pld_tok, df], self_wait=False)
                prog.op(lambda: s.activation(out=dscr2[:, :], in_=dscr2[:, :],
                                             func=AF.Exp, scale=-1.0))
                prog.op(lambda: s.activation(out=dscr2[:, :], in_=dscr2[:, :],
                                             func=AF.Ln, bias=1.0))
            prog.section("s", sdelta)
            del_tok = prog.tok("s")
            abs_tok = ("s", del_tok[1] - 2)

            def vd(v, k=k, dw=delta_war[k % 2]):
                prog.op(lambda: v.tensor_scalar(out=dl[:, :], in0=pa[:, :512],
                                                scalar1=db_sb[:, k:k + 1],
                                                scalar2=0.0, op0=ALU.add,
                                                op1=ALU.max),
                        waits=[st["dt_tok"], pld_tok, dw], self_wait=False)
                prog.op(lambda: v.tensor_tensor(out=dl[:, :], in0=dl[:, :],
                                                in1=dscr2[:, :], op=ALU.add),
                        waits=[del_tok])
            prog.section("v", vd)
            st["d2_tok"] = prog.tok("v")
            ps_free[k % 2] = [abs_tok, st["d2_tok"]]
            dscr2_box[0] = st["d2_tok"]
            return st["d2_tok"]

        def phase2(cc, tt):
            st = P1[cc]
            dl = st["delta"]

            def vsp2(v, cc=cc, dct=dc_store[0], pb=prev_b_tok):
                prog.op(lambda: v.tensor_tensor(out=udc[:, 0:LT],
                                                in0=u_f[:, cc * LT:(cc + 1) * LT],
                                                in1=dl[:, :], op=ALU.mult),
                        waits=[pb], self_wait=True)
                prog.op(lambda: v.tensor_tensor_scan(out=dcum_t[:, :], data0=dl[:, :],
                                                     data1=zeros[:, :],
                                                     initial=Dst[:, cc:cc + 1],
                                                     op0=ALU.add, op1=ALU.add),
                        waits=[dct], self_wait=False)
                prog.op(lambda: v.tensor_copy(out=Dst[:, cc:cc + 1],
                                              in_=dcum_t[:, LT - 1:LT]))
            prog.section("v", vsp2)
            ud_tok = prog.tok("v")
            dsc_tok = ud_tok

            def ydc(y, cc=cc, tt=tt):
                dc_store[0] = prog.dma(
                    dc_d.ap()[cc * 128:(cc + 1) * 128, tt * LT:(tt + 1) * LT],
                    dcum_t[:, :], waits=(dsc_tok,), sem="dso")
            prog.section("y", ydc)

            scan_toks = [None] * 16
            dA_last = None
            for n in range(16):
                slot = n % 7

                def sdA(s, cc=cc, n=n, slot=slot, ws=scan_war_s[slot],
                        dlt=st["d2_tok"]):
                    prog.op(lambda: s.activation(
                        out=dA_b[:, slot * LT:(slot + 1) * LT],
                        in_=dl[:, :], func=AF.Exp,
                        scale=A_sb[:, cc * 16 + n:cc * 16 + n + 1]),
                        waits=[A_tok, ws, dlt], self_wait=False)
                prog.section("s", sdA)
                dA_tok = prog.tok("s")
                dA_last = dA_tok

                def gb(g, n=n, slot=slot, ws=scan_war_s[slot], ut=ud_tok):
                    prog.op(lambda: g.tensor_tensor(
                        out=b_b[:, slot * LT:(slot + 1) * LT],
                        in0=udc[:, 0:LT], in1=Brep[:, n * LT:(n + 1) * LT],
                        op=ALU.mult),
                        waits=[ut, brep_tok, ws], self_wait=False)
                prog.section("g", gb)
                b_tok = prog.tok("g")

                def vscan(v, cc=cc, n=n, slot=slot, dAt=dA_tok, bt=b_tok,
                          wm=mm_war[n], wst=st_war[cc], wg=stc_war[0]):
                    prog.op(lambda: v.tensor_tensor_scan(
                        out=h_b[:, n * LT:(n + 1) * LT],
                        data0=dA_b[:, slot * LT:(slot + 1) * LT],
                        data1=b_b[:, slot * LT:(slot + 1) * LT],
                        initial=st_sb[:, cc * 16 + n:cc * 16 + n + 1],
                        op0=ALU.mult, op1=ALU.add),
                        waits=[dAt, bt, wm, wst, wg], self_wait=False)
                prog.section("v", vscan)
                scan_toks[n] = prog.tok("v")
                scan_war_s[slot] = scan_toks[n]

                def ty(t, n=n, st2=scan_toks[n], vyw=vy_war[0]):
                    prog.wait(st2)
                    if n == 0:
                        prog.wait(vyw)
                    prog.op(lambda: t.matmul(ps_y[:, :512], ident[:, :],
                                             h_b[:, n * LT:(n + 1) * LT],
                                             start=(n == 0), stop=(n == 15)))
                prog.section("t", ty)
                mm_war[n] = prog.tok("t")
            ysum_tok = prog.tok("t")
            prev_b_box[0] = b_tok
            delta_war[cc % 2] = dA_last

            def gstc(g, cc=cc, st15=scan_toks[15]):
                src = h_b.rearrange("p (n l) -> p n l", l=LT)[:, :, LT - 1:LT]
                prog.op(lambda: g.tensor_copy(
                    out=st_sb[:, cc * 16:(cc + 1) * 16].unsqueeze(2),
                    in_=src), waits=[st15], self_wait=False)
            prog.section("g", gstc)
            stc_war[0] = prog.tok("g")
            st_war[cc] = stc_war[0]

            def vy(v, cc=cc, tt=tt, yst=y1_store[0]):
                prog.wait(yst)
                prog.op(lambda: v.scalar_tensor_tensor(
                    out=y1c[:, :], in0=u_f[:, cc * LT:(cc + 1) * LT],
                    scalar=dp_sb[:, cc:cc + 1], in1=ps_y[:, :512],
                    op0=ALU.mult, op1=ALU.add), waits=[ysum_tok])
            prog.section("v", vy)
            y1_tok = prog.tok("v")
            vy_war[0] = y1_tok

            def yy1(y, cc=cc, tt=tt):
                y1_store[0] = prog.dma(
                    y1_d.ap()[cc * 128:(cc + 1) * 128, tt * LT:(tt + 1) * LT],
                    y1c[:, :], waits=(y1_tok,), sem="dh")
            prog.section("y", yy1)

        for k in range(17):
            if k < 16:
                phase1a(k)
            if k >= 1:
                phase2(k - 1, tt)
            if k < 16:
                phase1b(k)
            prev_b_tok = prev_b_box[0]
        w_free = wrot_free[0]
        brep_free = prog.tok("g")

    def sfin(s):
        for cc in range(16):
            prog.op(lambda cc=cc: s.activation(out=dA_b[:, cc * 16:cc * 16 + 16],
                                               in_=A_sb[:, cc * 16:cc * 16 + 16],
                                               func=AF.Exp,
                                               scale=Dst[:, cc:cc + 1]),
                    waits=[prog.tok("v"), A_tok])
    prog.section("s", sfin)
    p_tok = prog.tok("s")

    def yfin(y):
        prog.dma(he_d.ap(), st_sb[:, :], waits=(prog.tok("v"),), sem="dso")
        prog.dma(pc_d.ap(), dA_b[:, 0:256], waits=(p_tok,), sem="dso")
    prog.section("y", yfin)
    return


def _mamba_a_inmaps(x_full, casts, inp):
    c2 = lambda a: np.ascontiguousarray(a)
    ip_b = c2(casts["L0_attn_in_proj_w"])
    dt_b = c2(casts["L0_attn_dt_proj_w"])
    xp_b = c2(np.asarray(casts["L0_attn_x_proj_w"])[:, 16:32])
    cw = inp["L0_attn_conv_w"][:, 0, :].astype(np.float32)
    cw_r = c2(cw.reshape(16, 128, 4).transpose(1, 0, 2).reshape(128, 64))
    cb_r = c2(inp["L0_attn_conv_b"].astype(np.float32).reshape(16, 128).T)
    db_r = c2(inp["L0_attn_dt_proj_b"].astype(np.float32).reshape(16, 128).T)
    al_r = c2(inp["L0_attn_A_log"].astype(np.float32)
              .reshape(16, 128, 16).transpose(1, 0, 2).reshape(128, 256))
    dp_r = c2(inp["L0_attn_D_param"].astype(np.float32).reshape(16, 128).T)
    lg_r = c2(np.broadcast_to(inp["L0_attn_ln_gamma"].astype(np.float32), (128, 1024)))
    lb_r = c2(np.broadcast_to(inp["L0_attn_ln_beta"].astype(np.float32), (128, 1024)))
    maps = []
    for i in range(NCORES):
        lo = i * T_CORE
        halo = (np.zeros((HALO, D), np.float32) if i == 0
                else x_full[lo - HALO:lo])
        xs = np.ascontiguousarray(np.concatenate([halo, x_full[lo:lo + T_CORE]], 0))
        mask = np.zeros((128, HALO), np.float32) if i == 0 else np.ones((128, HALO), np.float32)
        maps.append({"x_s": xs, "halo_mask": mask, "inproj": ip_b, "dtw": dt_b,
                     "xpwB": xp_b, "convw_r": cw_r, "convb_r": cb_r, "dtb_r": db_r,
                     "alog_r": al_r, "dparam_r": dp_r, "lng_r": lg_r, "lnb_r": lb_r})
    return maps


def emit_mamba_b(prog, io):
    x_d = io["x"]
    y1_d = io["y1"]
    dc_d = io["dcum"]
    xt_d = io["xn2T"]
    ms_d = io["mstat"]
    rs_d = io["rstat"]
    sp_d = io["slotsP"]
    sh_d = io["slotsH"]
    zw_d = io["inprojZ"]
    ow_d = io["outw"]
    al_d = io["alog_r"]
    o_d = io["o"]
    xn2T = prog.sb("xn2T_sb", [128, 8 * XCOLS], BF16)
    outw = prog.sb("outw_sb", [128, 16 * 1024], BF16)
    ygb = prog.sb("ygb", [128, 16 * 512], BF16)
    A_sb = prog.sb("A_sb", [128, 256])
    h0 = prog.sb("h0", [128, 256])
    slP = prog.sb("slP", [128, 7 * 256])
    slH = prog.sb("slH", [128, 7 * 256])
    stat_m = prog.sb("stat_m", [128, 16])
    stat_r = prog.sb("stat_r", [128, 16])
    dcum_c = [prog.sb("dcum_c0", [128, 512]), prog.sb("dcum_c1", [128, 512])]
    e_b = prog.sb("e_b", [128, 4 * 512])
    acc = prog.sb("acc", [128, 512])
    acc_g = prog.sb("acc_g", [128, 512])
    y1c = [prog.sb("y1c0", [128, 512], BF16), prog.sb("y1c1", [128, 512], BF16)]
    sz_t = prog.sb("sz_t", [128, 512])
    zrot = [prog.sb("zrot0", [128, 8 * 128], BF16), prog.sb("zrot1", [128, 8 * 128], BF16)]
    x1c = prog.sb("x1c", [128, 1024])
    xnp = prog.sb("xnp", [128, 1024])
    oc = prog.sb("oc", [128, 1024])
    ps_z = prog.ps("ps_z", [128, 512])
    ps_o = prog.ps("ps_o", [128, 1024])

    def pld(y):
        prog.dma(xn2T[:, :], xt_d.ap(), sem="dm")
        prog.dma(outw[:, :].rearrange("p (c n) -> p c n", n=1024),
                 ow_d.ap().rearrange("(c p) n -> p c n", p=128), sem="dm")
        prog.dma(stat_m[:, :], ms_d.ap(), sem="dm")
        prog.dma(stat_r[:, :], rs_d.ap(), sem="dm")
        prog.dma(A_sb[:, :], al_d.ap(), sem="dm")
        prog.dma(slP[:, :].rearrange("p (j c) -> p j c", c=256), sp_d.ap().transpose([1, 0, 2]), sem="dm")
        prog.dma(slH[:, :].rearrange("p (j c) -> p j c", c=256), sh_d.ap().transpose([1, 0, 2]), sem="dm")
    prog.section("y", pld)
    pld_tok = ("dm", prog.cnt["dm"])

    def sA(s):
        prog.op(lambda: s.activation(out=A_sb[:, :], in_=A_sb[:, :], func=AF.Exp),
                waits=[pld_tok])
        prog.op(lambda: s.activation(out=A_sb[:, :], in_=A_sb[:, :], func=AF.Copy,
                                     scale=-1.0))
    prog.section("s", sA)
    A_tok = prog.tok("s")

    def vh0(v):
        prog.op(lambda: v.memset(h0[:, :], 0.0), waits=[pld_tok])
        for j in range(7):
            prog.op(lambda j=j: v.tensor_tensor(out=h0[:, :], in0=h0[:, :],
                                                in1=slP[:, j * 256:(j + 1) * 256],
                                                op=ALU.mult))
            prog.op(lambda j=j: v.tensor_tensor(out=h0[:, :], in0=h0[:, :],
                                                in1=slH[:, j * 256:(j + 1) * 256],
                                                op=ALU.add))
    prog.section("v", vh0)
    h0_tok = prog.tok("v")

    o_store = [("dso", 0)]
    zw_free = {0: ("t", 0), 1: ("t", 0)}
    eb_war = [("v", 0)] * 4
    chain_prev = [("v", 0), ("v", 0)]
    ld_free = {0: [], 1: []}
    sz_free = ("s", 0)
    ps_z_free = ("s", 0)
    for tt in range(NT):
        for cc in range(16):
            par = cc % 2
            dcc = dcum_c[par]
            y1b = y1c[par]
            zsel = zrot[par]
            dld = [None]

            def ydc(y, cc=cc, tt=tt, par=par, lf=ld_free[par]):
                dld[0] = prog.dma(dcc[:, :],
                                  dc_d.ap()[cc * 128:(cc + 1) * 128, tt * LT:(tt + 1) * LT],
                                  waits=tuple(lf), sem="dx")
                dld.append(prog.dma(y1b[:, :],
                                    y1_d.ap()[cc * 128:(cc + 1) * 128, tt * LT:(tt + 1) * LT],
                                    sem="dx"))
            prog.section("y", ydc)

            sE_last = None
            for n in range(16):
                slot = n % 4

                def sE(s, cc=cc, n=n, slot=slot, ws=eb_war[slot]):
                    prog.op(lambda: s.activation(out=e_b[:, slot * LT:(slot + 1) * LT],
                                                 in_=dcc[:, :], func=AF.Exp,
                                                 scale=A_sb[:, cc * 16 + n:cc * 16 + n + 1]),
                            waits=[dld[0], A_tok, ws], self_wait=False)
                prog.section("s", sE)
                e_tok = prog.tok("s")
                sE_last = e_tok

                chain = n % 2
                accx = acc if chain == 0 else acc_g

                def xacc(v, cc=cc, n=n, slot=slot, et=e_tok, accx=accx,
                         cp=chain_prev[chain]):
                    col = h0[:, cc * 16 + n:cc * 16 + n + 1]
                    src = e_b[:, slot * LT:(slot + 1) * LT]
                    if n < 2:
                        prog.op(lambda: v.tensor_scalar(out=accx[:, :], in0=src,
                                                        scalar1=col, scalar2=None,
                                                        op0=ALU.mult),
                                waits=[et, h0_tok, cp], self_wait=False)
                    else:
                        prog.op(lambda: v.scalar_tensor_tensor(out=accx[:, :], in0=src,
                                                               scalar=col, in1=accx[:, :],
                                                               op0=ALU.mult, op1=ALU.add),
                                waits=[et, cp], self_wait=False)
                prog.section("v", xacc)
                chain_prev[chain] = prog.tok("v")
                eb_war[slot] = chain_prev[chain]

            def vcomb(v):
                prog.op(lambda: v.tensor_tensor(out=acc[:, :], in0=acc[:, :],
                                                in1=acc_g[:, :], op=ALU.add))
            prog.section("v", vcomb)
            corr_tok = prog.tok("v")
            chain_prev[1] = corr_tok

            zld = [None]

            def yz(y, cc=cc, par=par, zf=zw_free[par]):
                zld[0] = prog.dma(zsel[:, :].rearrange("p (kc n) -> p kc n", n=128),
                                  zw_d.ap()[:, cc * 128:(cc + 1) * 128]
                                      .rearrange("(kc p) n -> p kc n", p=128),
                                  waits=(zf,), sem="dwa")
            prog.section("y", yz)

            def tz(t, cc=cc, tt=tt, pzf=ps_z_free):
                prog.wait(zld[0])
                prog.wait(pld_tok)
                prog.wait(pzf)
                for kc in range(8):
                    prog.op(lambda kc=kc: t.matmul(
                        ps_z[:, :512], zsel[:, kc * 128:(kc + 1) * 128],
                        xn2T[:, kc * XCOLS + tt * LT + HALO: kc * XCOLS + tt * LT + HALO + 512],
                        start=(kc == 0), stop=(kc == 7)))
            prog.section("t", tz)
            z_tok = prog.tok("t")
            zw_free[par] = z_tok

            def ssz(s, cc=cc, szf=sz_free):
                prog.op(lambda: s.activation(out=sz_t[:, :], in_=ps_z[:, :512],
                                             func=AF.Silu), waits=[z_tok, szf],
                        self_wait=False)
            prog.section("s", ssz)
            sz_tok = prog.tok("s")
            ps_z_free = sz_tok

            def vyg(v, cc=cc, dl1=dld[1], st=sz_tok):
                prog.op(lambda: v.tensor_tensor(out=acc[:, :], in0=acc[:, :],
                                                in1=y1b[:, :], op=ALU.add),
                        waits=[dl1])
                prog.op(lambda: v.tensor_tensor(out=ygb[:, cc * 512:(cc + 1) * 512],
                                                in0=acc[:, :], in1=sz_t[:, :],
                                                op=ALU.mult), waits=[st])
            prog.section("v", vyg)
            vyg_tok = prog.tok("v")
            chain_prev[0] = vyg_tok
            sz_free = vyg_tok
            ld_free[par] = [("s", sE_last[1]), ("v", vyg_tok[1])]
        ygb_tok = prog.tok("v")

        for tq in range(4):
            def to(t, tt=tt, tq=tq):
                prog.wait(ygb_tok)
                prog.wait(prog.tok("v"))
                for half in range(2):
                    for cc in range(16):
                        prog.op(lambda cc=cc, half=half: t.matmul(
                            ps_o[:, half * 512:(half + 1) * 512],
                            ygb[:, cc * 512 + tq * 128: cc * 512 + tq * 128 + 128],
                            outw[:, cc * 1024 + half * 512: cc * 1024 + half * 512 + 512],
                            start=(cc == 0), stop=(cc == 15)))
            prog.section("t", to)
            o_tok = prog.tok("t")

            xld = [None]

            def yx(y, tt=tt, tq=tq):
                xld[0] = prog.dma(x1c[:, :],
                                  x_d.ap()[tt * 512 + tq * 128: tt * 512 + tq * 128 + 128, :],
                                  waits=(prog.tok("v"),), sem="dx")
            prog.section("y", yx)

            def vev(v, tt=tt, tq=tq, ost=o_store[0]):
                tc = tt * 4 + tq
                prog.wait(ost)
                prog.op(lambda: v.tensor_scalar(out=xnp[:, :], in0=x1c[:, :],
                                                scalar1=stat_m[:, tc:tc + 1],
                                                scalar2=stat_r[:, tc:tc + 1],
                                                op0=ALU.subtract, op1=ALU.mult),
                        waits=[xld[0], pld_tok])
                prog.op(lambda: v.tensor_tensor(out=xnp[:, :], in0=xnp[:, :],
                                                in1=x1c[:, :], op=ALU.add))
                prog.op(lambda: v.tensor_tensor(out=oc[:, :], in0=xnp[:, :],
                                                in1=ps_o[:, :1024], op=ALU.add),
                        waits=[o_tok])
            prog.section("v", vev)
            ev_tok = prog.tok("v")

            def yo(y, tt=tt, tq=tq):
                o_store[0] = prog.dma(
                    o_d.ap()[tt * 512 + tq * 128: tt * 512 + tq * 128 + 128, :],
                    oc[:, :], waits=(ev_tok,), sem="dso")
            prog.section("y", yo)

    return


def _mamba_b_inmaps(x1_full, resA, casts, inp):
    c2 = lambda a: np.ascontiguousarray(a)
    zw = c2(np.asarray(casts["L0_attn_in_proj_w"])[:, 2048:])
    ow = c2(casts["L0_attn_out_proj_w"])
    al_r = c2(inp["L0_attn_A_log"].astype(np.float32)
              .reshape(16, 128, 16).transpose(1, 0, 2).reshape(128, 256))
    ones = np.ones((128, 256), np.float32)
    zeross = np.zeros((128, 256), np.float32)
    maps = []
    for i in range(NCORES):
        sp = []
        sh = []
        for j in range(7):
            src = i - 7 + j
            if src < 0:
                sp.append(ones); sh.append(zeross)
            else:
                sp.append(np.asarray(resA[src]["o_pcore"], np.float32))
                sh.append(np.asarray(resA[src]["o_hend"], np.float32))
        maps.append({
            "x1_s": c2(x1_full[i * T_CORE:(i + 1) * T_CORE]),
            "y1": np.ascontiguousarray(np.asarray(resA[i]["o_y1"])),
            "dcum": np.asarray(resA[i]["o_dcum"], np.float32),
            "xn2T": np.asarray(resA[i]["o_xn2T"]),
            "mstat": np.asarray(resA[i]["o_mstat"], np.float32),
            "rstat": np.asarray(resA[i]["o_rstat"], np.float32),
            "slotsP": c2(np.stack(sp)), "slotsH": c2(np.stack(sh)),
            "inprojZ": zw, "outw": ow, "alog_r": al_r,
        })
    return maps


def run_mamba(x1_full, casts, inp):
    ncA = build_mamba_a()
    mapsA = _mamba_a_inmaps(x1_full, casts, inp)
    resA = run_bass_kernel_spmd(ncA, mapsA, core_ids=list(range(NCORES))).results
    ncB = build_mamba_b()
    mapsB = _mamba_b_inmaps(x1_full, resA, casts, inp)
    resB = run_bass_kernel_spmd(ncB, mapsB, core_ids=list(range(NCORES))).results
    return np.concatenate([resB[i]["o_x"] for i in range(NCORES)], axis=0)


def emit_moe(prog, io):
    TCn = 16
    x_d = io["x"]
    wg_d = io["wgate"]
    wd_d = io["wdview"]
    wu_d = io["wuview"]
    bu_d = io["buw"]
    bd_d = io["bdrep"]
    o_d = io["o"]
    ident = io["_ident"]
    id_tok = io["_id_tok"]

    xlT = prog.sb("xlT", [128, 8 * TCn * 128], BF16)
    wg_sb = prog.sb("wg_sb", [128, 8 * 8], BF16)
    wd_sb = prog.sb("wd_sb", [128, 8 * 512], BF16)
    wu_sb = prog.sb("wu_sb", [128, 4 * 1024], BF16)
    bu_sb = prog.sb("bu_sb", [8, 1024], BF16)
    bd_sb = prog.sb("bd_sb", [128, 512])
    xc = prog.sb("xc", [128, 1024])
    xln = prog.sb("xln", [128, 1024])
    xq = prog.sb("xq", [128, 1024], BF16)
    stat = {k: prog.sb("stq_" + k, [128, 1]) for k in
            ("sum", "ssq", "mean", "ex2", "m2", "var", "rstd")}
    l_sb = prog.sb("l_sb", [128, 8])
    g_sb = prog.sb("g_sb", [128, 8])
    gb_sb = prog.sb("gb_sb", [128, 8], BF16)
    gT_sb = prog.sb("gT_sb", [8, 128], BF16)
    m1 = prog.sb("m1", [128, 1])
    m2 = prog.sb("m2", [128, 1])
    nm1 = prog.sb("nm1", [128, 1])
    dd = prog.sb("dd", [128, 1])
    msk2 = prog.sb("msk2", [128, 8])
    hgb = prog.sb("hgb", [128, 512], BF16)
    hgT = prog.sb("hgT", [128, 4 * 128], BF16)
    hsc = prog.sb("hsc", [128, 512])
    oc = prog.sb("oc", [128, 1024])
    ps_l = prog.ps("ps_l", [128, 8])
    ps_hd = prog.ps("ps_hd", [128, 512])
    ps_o = prog.ps("ps_o", [128, 1024])
    ps_tp = prog.ps("ps_tp", [128, 128], BF16)
    ps_gt = prog.ps("ps_gt", [8, 128], BF16)

    def pld(y):
        prog.dma(wg_sb[:, :].rearrange("p (kc n) -> p kc n", n=8),
                 wg_d.ap().rearrange("(kc p) n -> p kc n", p=128), sem="dwa")
        prog.dma(wd_sb[:, :].rearrange("p (kc n) -> p kc n", n=512),
                 wd_d.ap().rearrange("(kc p) n -> p kc n", p=128), sem="dwa")
        prog.dma(wu_sb[:, :].rearrange("p (kc n) -> p kc n", n=1024),
                 wu_d.ap().rearrange("(kc p) n -> p kc n", p=128), sem="dwa")
        prog.dma(bu_sb[:, :], bu_d.ap(), sem="dwa")
        prog.dma(bd_sb[:, :], bd_d.ap(), sem="dwa")
    prog.section("y", pld)
    pld_tok = ("dwa", prog.cnt["dwa"])

    xc_free = ("v", 0)
    prev_aff = ("t", 0)
    for tc in range(TCn):
        ld = [None]

        def yld(y, tc=tc, xf=xc_free):
            ld[0] = prog.dma(xc[:, :], x_d.ap()[tc * 128:(tc + 1) * 128, :],
                             waits=(xf,), sem="dx")
        prog.section("y", yld)

        def ssec(s, pa=prev_aff):
            prog.op(lambda: s.activation(out=xln[:, :], in_=xc[:, :], func=AF.Square,
                                         accum_out=stat["ssq"][:, :]),
                    waits=[ld[0], pa])
        prog.section("s", ssec)
        sq_tok = prog.tok("s")

        def vsec(v, tc=tc):
            prog.op(lambda: v.tensor_reduce(out=stat["sum"][:, :], in_=xc[:, :],
                                            axis=AX, op=ALU.add), waits=[ld[0]])
            prog.op(lambda: v.tensor_scalar(out=stat["mean"][:, :], in0=stat["sum"][:, :],
                                            scalar1=1.0 / D, scalar2=None, op0=ALU.mult))
            prog.op(lambda: v.tensor_scalar(out=stat["ex2"][:, :], in0=stat["ssq"][:, :],
                                            scalar1=1.0 / D, scalar2=None, op0=ALU.mult),
                    waits=[sq_tok])
            prog.op(lambda: v.tensor_tensor(out=stat["m2"][:, :], in0=stat["mean"][:, :],
                                            in1=stat["mean"][:, :], op=ALU.mult))
            prog.op(lambda: v.tensor_tensor(out=stat["var"][:, :], in0=stat["ex2"][:, :],
                                            in1=stat["m2"][:, :], op=ALU.subtract))
            prog.op(lambda: v.tensor_scalar(out=stat["var"][:, :], in0=stat["var"][:, :],
                                            scalar1=1e-5, scalar2=None, op0=ALU.add))
        prog.section("v", vsec)
        var_tok = prog.tok("v")

        def ssec2(s):
            prog.op(lambda: s.activation(out=stat["rstd"][:, :], in_=stat["var"][:, :],
                                         func=AF.Ln), waits=[var_tok])
            prog.op(lambda: s.activation(out=stat["rstd"][:, :], in_=stat["rstd"][:, :],
                                         func=AF.Exp, scale=-0.5))
        prog.section("s", ssec2)
        rstd_tok = prog.tok("s")

        def vsec2(v, tc=tc, qf=prev_aff):
            prog.op(lambda: v.tensor_scalar(out=xq[:, :], in0=xc[:, :],
                                            scalar1=stat["mean"][:, :],
                                            scalar2=stat["rstd"][:, :],
                                            op0=ALU.subtract, op1=ALU.mult),
                    waits=[rstd_tok, qf])
        prog.section("v", vsec2)
        aff_tok = prog.tok("v")
        xc_free = aff_tok

        for kc in range(8):
            def tsec(t, kc=kc):
                prog.op(lambda: t.transpose(ps_tp[:, :],
                                            xq[:, kc * 128:(kc + 1) * 128],
                                            ident[:, :]),
                        waits=[aff_tok, id_tok, prog.tok("v")])
            prog.section("t", tsec)
            tr_tok = prog.tok("t")

            def vsec3(v, kc=kc, tc=tc):
                prog.op(lambda: v.tensor_copy(
                    out=xlT[:, (kc * TCn + tc) * 128:(kc * TCn + tc) * 128 + 128],
                    in_=ps_tp[:, :]), waits=[tr_tok])
            prog.section("v", vsec3)
        prev_aff = prog.tok("t")
    xlT_tok = prog.tok("v")

    o_store = ("dso", 0)
    for tc in range(TCn):
        def tmm(t, tc=tc):
            prog.wait(xlT_tok)
            prog.wait(pld_tok)
            prog.wait(prog.tok("v"))
            for kc in range(8):
                prog.op(lambda kc=kc: t.matmul(
                    ps_l[:, :8],
                    xlT[:, (kc * TCn + tc) * 128:(kc * TCn + tc) * 128 + 128],
                    wg_sb[:, kc * 8:(kc + 1) * 8],
                    start=(kc == 0), stop=(kc == 7)))
            for kc in range(8):
                prog.op(lambda kc=kc: t.matmul(
                    ps_hd[:, :512],
                    xlT[:, (kc * TCn + tc) * 128:(kc * TCn + tc) * 128 + 128],
                    wd_sb[:, kc * 512:(kc + 1) * 512],
                    start=(kc == 0), stop=(kc == 7)))
        prog.section("t", tmm)
        mm_tok = prog.tok("t")

        def vg(v, tc=tc):
            prog.op(lambda: v.tensor_copy(out=l_sb[:, :], in_=ps_l[:, :8]),
                    waits=[mm_tok])
            prog.op(lambda: v.tensor_reduce(out=m1[:, :], in_=l_sb[:, :],
                                            axis=AX, op=ALU.max))
            prog.op(lambda: v.tensor_scalar(out=msk2[:, :], in0=l_sb[:, :],
                                            scalar1=m1[:, :], scalar2=-1e30,
                                            op0=ALU.is_ge, op1=ALU.mult))
            prog.op(lambda: v.tensor_tensor(out=msk2[:, :], in0=msk2[:, :],
                                            in1=l_sb[:, :], op=ALU.add))
            prog.op(lambda: v.tensor_reduce(out=m2[:, :], in_=msk2[:, :],
                                            axis=AX, op=ALU.max))
            prog.op(lambda: v.tensor_scalar(out=nm1[:, :], in0=m1[:, :],
                                            scalar1=-1.0, scalar2=None, op0=ALU.mult))
            prog.op(lambda: v.tensor_tensor(out=dd[:, :], in0=m2[:, :],
                                            in1=m1[:, :], op=ALU.subtract))
        prog.section("v", vg)
        vg_tok = prog.tok("v")

        def sg(s, tc=tc):
            prog.op(lambda: s.activation(out=g_sb[:, :], in_=l_sb[:, :], func=AF.Exp,
                                         bias=nm1[:, :]), waits=[vg_tok])
            prog.op(lambda: s.activation(out=dd[:, :], in_=dd[:, :], func=AF.Exp))
        prog.section("s", sg)
        sg_tok = prog.tok("s")

        def vg2(v, tc=tc):
            prog.op(lambda: v.tensor_scalar(out=dd[:, :], in0=dd[:, :],
                                            scalar1=1.0, scalar2=None, op0=ALU.add),
                    waits=[sg_tok])
            prog.op(lambda: v.reciprocal(out=dd[:, :], in_=dd[:, :]))
            prog.op(lambda: v.tensor_scalar(out=g_sb[:, :], in0=g_sb[:, :],
                                            scalar1=dd[:, :], scalar2=None,
                                            op0=ALU.mult))
            prog.op(lambda: v.tensor_scalar(out=msk2[:, :], in0=l_sb[:, :],
                                            scalar1=m2[:, :], scalar2=None,
                                            op0=ALU.is_ge))
            prog.op(lambda: v.tensor_tensor(out=g_sb[:, :], in0=g_sb[:, :],
                                            in1=msk2[:, :], op=ALU.mult))
            prog.op(lambda: v.tensor_copy(out=gb_sb[:, :], in_=g_sb[:, :]))
            prog.op(lambda: v.tensor_tensor(out=hsc[:, :], in0=ps_hd[:, :512],
                                            in1=bd_sb[:, :], op=ALU.add))
            prog.op(lambda: v.tensor_scalar(out=hsc[:, :], in0=hsc[:, :],
                                            scalar1=0.0, scalar2=None, op0=ALU.max))
            prog.op(lambda: v.tensor_tensor(out=hgb[:, :].rearrange("p (e b) -> p e b", b=64),
                                            in0=hsc[:, :].rearrange("p (e b) -> p e b", b=64),
                                            in1=g_sb[:, :].unsqueeze(2).broadcast_to([128, 8, 64]),
                                            op=ALU.mult))
        prog.section("v", vg2)
        hg_tok = prog.tok("v")

        for kc in range(4):
            def tsec2(t, kc=kc):
                prog.op(lambda: t.transpose(ps_tp[:, :],
                                            hgb[:, kc * 128:(kc + 1) * 128],
                                            ident[:, :]), waits=[hg_tok, prog.tok("v")])
            prog.section("t", tsec2)
            tr2 = prog.tok("t")

            def vsec4(v, kc=kc):
                prog.op(lambda: v.tensor_copy(out=hgT[:, kc * 128:(kc + 1) * 128],
                                              in_=ps_tp[:, :]), waits=[tr2])
            prog.section("v", vsec4)

        def tgt(t, tc=tc):
            prog.op(lambda: t.transpose(ps_gt[:8, :], gb_sb[:, 0:8], ident[:, :]),
                    waits=[hg_tok, prog.tok("v")])
        prog.section("t", tgt)
        gt_tok = prog.tok("t")

        def vgt(v):
            prog.op(lambda: v.tensor_copy(out=gT_sb[:8, :], in_=ps_gt[:8, :128]),
                    waits=[gt_tok])
        prog.section("v", vgt)
        hgT_tok = prog.tok("v")

        def tup(t, tc=tc):
            prog.wait(hgT_tok)
            prog.wait(prog.tok("v"))
            for half in range(2):
                for kc in range(4):
                    prog.op(lambda kc=kc, half=half: t.matmul(
                        ps_o[:, half * 512:(half + 1) * 512],
                        hgT[:, kc * 128:(kc + 1) * 128],
                        wu_sb[:, kc * 1024 + half * 512: kc * 1024 + half * 512 + 512],
                        start=(kc == 0), stop=False))
                prog.op(lambda half=half: t.matmul(
                    ps_o[:, half * 512:(half + 1) * 512],
                    gT_sb[:8, :], bu_sb[:8, half * 512:(half + 1) * 512],
                    start=False, stop=True))
        prog.section("t", tup)
        up_tok = prog.tok("t")

        ld2 = [None]

        def yld2(y, tc=tc):
            ld2[0] = prog.dma(xc[:, :], x_d.ap()[tc * 128:(tc + 1) * 128, :],
                              waits=(prog.tok("v"),), sem="dx")
        prog.section("y", yld2)

        def vev(v, tc=tc, ost=o_store):
            prog.wait(ost)
            prog.op(lambda: v.scalar_tensor_tensor(out=oc[:, :], in0=ps_o[:, :1024],
                                                   scalar=0.5, in1=xc[:, :],
                                                   op0=ALU.mult, op1=ALU.add),
                    waits=[up_tok, ld2[0]])
        prog.section("v", vev)
        ev_tok = prog.tok("v")
        xc_free = ev_tok

        def yst(y, tc=tc):
            prog.dma(o_d.ap()[tc * 128:(tc + 1) * 128, :], oc[:, :], waits=(ev_tok,),
                     sem="dso")
        prog.section("y", yst)
        o_store = ("dso", prog.cnt["dso"])

    return


def run_moe(nc, x_full, casts, inp):
    c2 = lambda a: np.ascontiguousarray(a)
    wd_view = c2(np.asarray(casts["L1_ffn_Wd"]).transpose(1, 0, 2).reshape(1024, 512))
    wu_view = c2(np.asarray(casts["L1_ffn_Wu"]).reshape(512, 1024))
    bd_rep = c2(np.broadcast_to(inp["L1_ffn_bd"].astype(np.float32).reshape(512), (128, 512)))
    bu_b = c2(np.asarray(casts["L1_ffn_bu"]))
    wg_b = c2(casts["L1_ffn_w_gate"])
    in_maps = []
    for i in range(NCORES):
        in_maps.append({
            "x_s": c2(x_full[i * T_CORE:(i + 1) * T_CORE]),
            "wgate": wg_b, "wdview": wd_view, "wuview": wu_view,
            "buw": bu_b, "bdrep": bd_rep,
        })
    res = run_bass_kernel_spmd(nc, in_maps, core_ids=list(range(NCORES)))
    return np.concatenate([res.results[i]["o_x"] for i in range(NCORES)], axis=0)


def emit_ln_xlT(prog, x_d, xlT, ident, id_tok, bufs):
    xc, xln, xq, stat = bufs["xc"], bufs["xln"], bufs["xq"], bufs["stat"]
    TCn = 16
    xc_free = ("v", 0)
    prev_aff = ("t", 0)
    for tc in range(TCn):
        ld = [None]

        def yld(y, tc=tc, xf=xc_free):
            ld[0] = prog.dma(xc[:, :], x_d.ap()[tc * 128:(tc + 1) * 128, :],
                             waits=(xf,), sem="dx")
        prog.section("y", yld)

        def ssec(s, pa=prev_aff):
            prog.op(lambda: s.activation(out=xln[:, :], in_=xc[:, :], func=AF.Square,
                                         accum_out=stat["ssq"][:, :]),
                    waits=[ld[0], pa])
        prog.section("s", ssec)
        sq_tok = prog.tok("s")

        def vsec(v, tc=tc):
            prog.op(lambda: v.tensor_reduce(out=stat["sum"][:, :], in_=xc[:, :],
                                            axis=AX, op=ALU.add), waits=[ld[0]])
            prog.op(lambda: v.tensor_scalar(out=stat["mean"][:, :], in0=stat["sum"][:, :],
                                            scalar1=1.0 / D, scalar2=None, op0=ALU.mult))
            prog.op(lambda: v.tensor_scalar(out=stat["ex2"][:, :], in0=stat["ssq"][:, :],
                                            scalar1=1.0 / D, scalar2=None, op0=ALU.mult),
                    waits=[sq_tok])
            prog.op(lambda: v.tensor_tensor(out=stat["m2"][:, :], in0=stat["mean"][:, :],
                                            in1=stat["mean"][:, :], op=ALU.mult))
            prog.op(lambda: v.tensor_tensor(out=stat["var"][:, :], in0=stat["ex2"][:, :],
                                            in1=stat["m2"][:, :], op=ALU.subtract))
            prog.op(lambda: v.tensor_scalar(out=stat["var"][:, :], in0=stat["var"][:, :],
                                            scalar1=1e-5, scalar2=None, op0=ALU.add))
        prog.section("v", vsec)
        var_tok = prog.tok("v")

        def ssec2(s):
            prog.op(lambda: s.activation(out=stat["rstd"][:, :], in_=stat["var"][:, :],
                                         func=AF.Ln), waits=[var_tok])
            prog.op(lambda: s.activation(out=stat["rstd"][:, :], in_=stat["rstd"][:, :],
                                         func=AF.Exp, scale=-0.5))
        prog.section("s", ssec2)
        rstd_tok = prog.tok("s")

        def vsec2(v, tc=tc, qf=prev_aff):
            prog.op(lambda: v.tensor_scalar(out=xq[:, :], in0=xc[:, :],
                                            scalar1=stat["mean"][:, :],
                                            scalar2=stat["rstd"][:, :],
                                            op0=ALU.subtract, op1=ALU.mult),
                    waits=[rstd_tok, qf])
        prog.section("v", vsec2)
        aff_tok = prog.tok("v")
        xc_free = aff_tok

        for kc in range(8):
            def tsec(t, kc=kc):
                prog.op(lambda: t.transpose(bufs["ps_tp"][:, :],
                                            xq[:, kc * 128:(kc + 1) * 128],
                                            ident[:, :]),
                        waits=[aff_tok, id_tok, prog.tok("v")])
            prog.section("t", tsec)
            tr_tok = prog.tok("t")

            def vsec3(v, kc=kc, tc=tc):
                prog.op(lambda: v.tensor_copy(
                    out=xlT[:, (kc * TCn + tc) * 128:(kc * TCn + tc) * 128 + 128],
                    in_=bufs["ps_tp"][:, :]), waits=[tr_tok])
            prog.section("v", vsec3)
        prev_aff = prog.tok("t")
    return prog.tok("v")


def emit_consts_att(prog):
    E2f = prog.sbuf("E2f", [128, 2])
    onesb = prog.sbuf("onesb", [128, 128], BF16)
    iop = prog.sbuf("iop", [128, 1], I32)

    def gsec(g):
        prog.op(lambda: g.iota(iop[:, :], [[0, 1]], channel_multiplier=1,
                               allow_small_or_imprecise_dtypes=True))
    prog.section("g", gsec)
    io_tok = prog.tok("g")

    def vsec(v):
        prog.op(lambda: v.tensor_scalar(out=E2f[:, 0:1], in0=iop[:, :],
                                        scalar1=64, scalar2=None, op0=ALU.is_lt),
                waits=[io_tok])
        prog.op(lambda: v.tensor_scalar(out=E2f[:, 1:2], in0=iop[:, :],
                                        scalar1=63, scalar2=None, op0=ALU.is_gt))
        prog.op(lambda: v.memset(onesb[:, :], 1.0))
    prog.section("v", vsec)
    return E2f, onesb, prog.tok("v")


def emit_att_a(prog, io):
    TCn = 16
    x_d = io["x"]
    wk_d = io["Wk_b"]
    wv_d = io["Wv_b"]
    bk_d = io["bk_r"]
    bv_d = io["bv_rep"]
    pj_d = io["projT_b"]
    kv_d = io["kv"]
    ident = io["_ident"]
    id_tok = io["_id_tok"]
    identf = io["_identf"]
    idf_tok = io["_idf_tok"]
    E2f = io["_E2f"]
    onesb = io["_onesb"]
    cst_tok = io["_cst_tok"]

    xlT = prog.sb("xlT", [128, 8 * TCn * 128], BF16)
    bufs = {"xc": prog.sb("xc", [128, 1024]),
            "xln": prog.sb("xln", [128, 1024]),
            "xq": prog.sb("xq", [128, 1024], BF16),
            "stat": {k: prog.sb("sta_" + k, [128, 1]) for k in
                     ("sum", "ssq", "mean", "ex2", "m2", "var", "rstd")},
            "ps_tp": prog.ps("ps_tp", [128, 128], BF16)}
    xlT_tok = emit_ln_xlT(prog, x_d, xlT, ident, id_tok, bufs)

    wrot = prog.sb("wrot", [128, 8 * 128], BF16)
    vrot = prog.sb("vrot", [128, 8 * 128], BF16)
    prot = prog.sb("prot", [128, 512], BF16)
    bk_sb = prog.sb("bk_sb", [128, 8])
    bv_sb = prog.sb("bv_sb", [128, 1024])
    Kf = prog.sb("Kf", [128, 2048])
    Kfb = prog.sb("Kfb", [128, 2048], BF16)
    sqs = prog.sb("sqs", [128, 2048])
    ss_sb = prog.sb("ss_sb", [2, 2048])
    rrT = prog.sb("rrT", [128, 32])
    Vf = prog.sb("Vf", [128, TCn * 128])
    RK = prog.sb("RK", [128, TCn * 256], BF16)
    Vs = prog.sb("Vs", [128, TCn * 65], BF16)
    Vu = prog.sb("Vu", [128, TCn * 65], BF16)
    kvacc = prog.sb("kvacc", [128, 2080])
    ps_f = prog.ps("ps_f", [128, 512])
    ps_ss = prog.ps("ps_ss", [2, 512])
    ps_rt = prog.ps("ps_rt", [128, 2])
    ps_kp = prog.ps("ps_kp", [128, 256])
    ps_kv = prog.ps("ps_kv", [128, 65])

    def pbias(y):
        prog.dma(bk_sb[:, :], bk_d.ap(), sem="dm")
        prog.dma(bv_sb[:, :], bv_d.ap(), sem="dm")
    prog.section("y", pbias)
    pb_tok = ("dm", prog.cnt["dm"])

    w_free = ("t", 0)
    ss_bounce_busy = ("dh", 0)
    for hp in range(8):
        wld = [None, None, None]

        def yw(y, hp=hp, wf=w_free):
            wld[0] = prog.dma(wrot[:, :].rearrange("p (kc n) -> p kc n", n=128),
                              wk_d.ap()[:, hp * 128:(hp + 1) * 128]
                                  .rearrange("(kc p) n -> p kc n", p=128),
                              waits=(wf,), sem="dwa")
            wld[1] = prog.dma(vrot[:, :].rearrange("p (kc n) -> p kc n", n=128),
                              wv_d.ap()[:, hp * 128:(hp + 1) * 128]
                                  .rearrange("(kc p) n -> p kc n", p=128), sem="dwa")
            wld[2] = prog.dma(prot[0:64, 0:256],
                              pj_d.ap()[hp * 2], sem="dwb")
            wld[2] = prog.dma(prot[64:128, 256:512],
                              pj_d.ap()[hp * 2 + 1], sem="dwb")
        prog.section("y", yw)

        for tt in range(4):
            def tkf(t, tt=tt, hp=hp):
                prog.wait(wld[0])
                prog.wait(xlT_tok)
                prog.wait(prog.tok("s"))
                for kc in range(8):
                    prog.op(lambda kc=kc: t.matmul(
                        ps_f[:, :512], wrot[:, kc * 128:(kc + 1) * 128],
                        xlT[:, kc * TCn * 128 + tt * 512: kc * TCn * 128 + tt * 512 + 512],
                        start=(kc == 0), stop=(kc == 7)))
            prog.section("t", tkf)
            kf_tok = prog.tok("t")

            def skf(s, tt=tt, hp=hp):
                prog.op(lambda: s.activation(out=Kf[:, tt * 512:(tt + 1) * 512],
                                             in_=ps_f[:, :512], func=AF.Identity,
                                             bias=bk_sb[:, hp:hp + 1]),
                        waits=[kf_tok, pb_tok, prog.tok("v")])
            prog.section("s", skf)
        kf_done = prog.tok("s")

        def skq(s, hp=hp):
            prog.op(lambda: s.activation(out=sqs[:, :], in_=Kf[:, :], func=AF.Square),
                    waits=[kf_done, prog.tok("t")])
        prog.section("s", skq)
        sq_tok = prog.tok("s")

        def vkb(v, hp=hp):
            prog.op(lambda: v.tensor_copy(out=Kfb[:, :], in_=Kf[:, :]),
                    waits=[kf_done, prog.tok("t")])
        prog.section("v", vkb)
        kfb_tok = prog.tok("v")

        for tt in range(4):
            def tss(t, tt=tt):
                prog.wait(sq_tok)
                prog.wait(cst_tok)
                prog.wait(prog.tok("v"))
                prog.op(lambda: t.matmul(ps_ss[:2, :512], E2f[:, :],
                                         sqs[:, tt * 512:(tt + 1) * 512],
                                         start=True, stop=True))
            prog.section("t", tss)
            ssm_tok = prog.tok("t")

            def vss(v, tt=tt):
                prog.op(lambda: v.tensor_copy(out=ss_sb[:2, tt * 512:(tt + 1) * 512],
                                              in_=ps_ss[:2, :512]), waits=[ssm_tok])
            prog.section("v", vss)
        ss_tok = prog.tok("v")

        def srr(s):
            prog.op(lambda: s.activation(out=ss_sb[:2, :], in_=ss_sb[:2, :], func=AF.Ln),
                    waits=[ss_tok])
            prog.op(lambda: s.activation(out=ss_sb[:2, :], in_=ss_sb[:2, :], func=AF.Exp,
                                         scale=0.5))
        prog.section("s", srr)
        nrm_tok = prog.tok("s")

        def vrr(v):
            prog.op(lambda: v.tensor_scalar(out=ss_sb[:2, :], in0=ss_sb[:2, :],
                                            scalar1=1e-8, scalar2=None, op0=ALU.add),
                    waits=[nrm_tok])
            prog.op(lambda: v.reciprocal(out=ss_sb[:2, :], in_=ss_sb[:2, :]))
        prog.section("v", vrr)
        rr_tok = prog.tok("v")

        for tc in range(TCn):
            def trt(t, tc=tc):
                prog.op(lambda: t.transpose(ps_rt[:, :2],
                                            ss_sb[0:2, tc * 128:(tc + 1) * 128],
                                            identf[0:2, 0:2]),
                        waits=[rr_tok, idf_tok, prog.tok("v")])
            prog.section("t", trt)
            rt_tok = prog.tok("t")

            def vrt(v, tc=tc):
                prog.op(lambda: v.tensor_copy(out=rrT[:, tc:tc + 1],
                                              in_=ps_rt[:, 0:1]), waits=[rt_tok])
                prog.op(lambda: v.tensor_copy(out=rrT[:, 16 + tc:16 + tc + 1],
                                              in_=ps_rt[:, 1:2]))
            prog.section("v", vrt)
        rrT_tok = prog.tok("v")

        for tc in range(TCn):
            def tv(t, tc=tc):
                prog.wait(wld[1])
                prog.wait(xlT_tok)
                prog.wait(prog.tok("v"))
                for kc in range(8):
                    prog.op(lambda kc=kc: t.matmul(
                        ps_f[:, :128],
                        xlT[:, (kc * TCn + tc) * 128:(kc * TCn + tc) * 128 + 128],
                        vrot[:, kc * 128:(kc + 1) * 128],
                        start=(kc == 0), stop=(kc == 7)))
            prog.section("t", tv)
            v_tok = prog.tok("t")

            def vv(v, tc=tc, hp=hp):
                prog.op(lambda: v.tensor_tensor(out=Vf[:, tc * 128:(tc + 1) * 128],
                                                in0=ps_f[:, :128],
                                                in1=bv_sb[:, hp * 128:(hp + 1) * 128],
                                                op=ALU.add), waits=[v_tok, pb_tok])
            prog.section("v", vv)
        vf_tok = prog.tok("v")

        for h2 in range(2):
            for tc in range(TCn):
                def vaug(v, h2=h2, tc=tc):
                    src = Vf[:, tc * 128 + h2 * 64: tc * 128 + h2 * 64 + 64]
                    rcol = rrT[:, h2 * 16 + tc: h2 * 16 + tc + 1]
                    prog.op(lambda: v.tensor_scalar(
                        out=Vs[:, tc * 65:tc * 65 + 64], in0=src,
                        scalar1=rcol, scalar2=None, op0=ALU.mult),
                        waits=[vf_tok, rrT_tok])
                    prog.op(lambda: v.tensor_copy(out=Vs[:, tc * 65 + 64:tc * 65 + 65],
                                                  in_=rcol))
                    prog.op(lambda: v.tensor_scalar(out=Vu[:, tc * 65:tc * 65 + 64],
                                                    in0=src, scalar1=0.1,
                                                    scalar2=None, op0=ALU.mult))
                    prog.op(lambda: v.memset(Vu[:, tc * 65 + 64:tc * 65 + 65], 0.1))
                prog.section("v", vaug)
            vaug_tok = prog.tok("v")

            for tc in range(TCn):
                def trk(t, h2=h2, tc=tc):
                    prog.wait(wld[2])
                    prog.wait(kfb_tok)
                    prog.wait(prog.tok("v"))
                    prog.op(lambda: t.matmul(
                        ps_kp[:, :256],
                        Kfb[h2 * 64:h2 * 64 + 64, tc * 128:(tc + 1) * 128],
                        prot[h2 * 64:h2 * 64 + 64, h2 * 256:(h2 + 1) * 256],
                        start=True, stop=True))
                prog.section("t", trk)
                rk_tok = prog.tok("t")

                def vrk(v, h2=h2, tc=tc):
                    prog.op(lambda: v.tensor_scalar(out=RK[:, tc * 256:(tc + 1) * 256],
                                                    in0=ps_kp[:, :256],
                                                    scalar1=0.0, scalar2=None,
                                                    op0=ALU.max), waits=[rk_tok])
                prog.section("v", vrk)
            rk_done = prog.tok("v")

            for fc in range(2):
                def tkv(t, h2=h2, fc=fc):
                    prog.wait(rk_done)
                    prog.wait(vaug_tok)
                    prog.wait(cst_tok)
                    prog.wait(prog.tok("v"))
                    for tc in range(TCn):
                        prog.op(lambda tc=tc: t.matmul(
                            ps_kv[:, :65],
                            RK[:, tc * 256 + fc * 128: tc * 256 + fc * 128 + 128],
                            Vs[:, tc * 65:(tc + 1) * 65],
                            start=(tc == 0), stop=False))
                    for tc in range(TCn):
                        prog.op(lambda tc=tc: t.matmul(
                            ps_kv[:, :65],
                            onesb[:, 0:128], Vu[:, tc * 65:(tc + 1) * 65],
                            start=False, stop=(tc == TCn - 1)))
                prog.section("t", tkv)
                kv_tok = prog.tok("t")

                def vkv(v, h2=h2, fc=fc, hp=hp):
                    h = hp * 2 + h2
                    col = (h * 2 + fc) * 65
                    prog.op(lambda: v.tensor_copy(out=kvacc[:, col:col + 65],
                                                  in_=ps_kv[:, :65]), waits=[kv_tok])
                prog.section("v", vkv)
        w_free = prog.tok("t")

    def yout(y):
        prog.dma(kv_d.ap(), kvacc[:, :], waits=(prog.tok("v"),), sem="dso")
    prog.section("y", yout)
    return


def emit_att_b(prog, io):
    TCn = 16
    x_d = io["x"]
    kva_d = io["kv_all"]
    wq_d = io["Wq_b"]
    bq_d = io["bq_r"]
    pj_d = io["projT_b"]
    wo_d = io["Wo_b"]
    bo_d = io["bo_rep"]
    o_d = io["o"]
    ident = io["_ident"]
    id_tok = io["_id_tok"]
    identf = io["_identf"]
    idf_tok = io["_idf_tok"]
    E2f = io["_E2f"]
    onesb = io["_onesb"]
    cst_tok = io["_cst_tok"]

    xlT = prog.sb("xlT", [128, 8 * TCn * 128], BF16)
    bufs = {"xc": prog.sb("xc", [128, 1024]),
            "xln": prog.sb("xln", [128, 1024]),
            "xq": prog.sb("xq", [128, 1024], BF16),
            "stat": {k: prog.sb("sta_" + k, [128, 1]) for k in
                     ("sum", "ssq", "mean", "ex2", "m2", "var", "rstd")},
            "ps_tp": prog.ps("ps_tp", [128, 128], BF16)}
    xlT_tok = emit_ln_xlT(prog, x_d, xlT, ident, id_tok, bufs)

    kvs = prog.sb("kvs", [128, 2080])
    kvb = prog.sb("kvb", [128, 2080], BF16)
    kvt = prog.sb("kvt", [128, 2080])
    wrot = prog.sb("wrot", [128, 8 * 128], BF16)
    prot = prog.sb("prot", [128, 512], BF16)
    bq_sb = prog.sb("bq_sb", [128, 8])
    bo_sb = prog.sb("bo_sb", [128, 1024])
    Qf = prog.sb("Qf", [128, 2048])
    Qfb = prog.sb("Qfb", [128, 2048], BF16)
    sqs = prog.sb("sqs", [128, 2048])
    ss_sb = prog.sb("ss_sb", [4, 2048])
    qq_sb = prog.sb("qq_sb", [2, 2048])
    rqT = prog.sb("rqT", [128, 64])
    Rf = prog.sb("Rf", [128, 2 * 2048], BF16)
    Srep = prog.sb("Srep", [128, 65])
    Srow = prog.sb("Srow", [1, 65], BF16)
    t1 = prog.sb("t1", [128, 65])
    outg = prog.sb("outg", [128, TCn * 1024], BF16)
    ogT = prog.sb("ogT", [128, 8 * TCn * 128], BF16)
    oc = prog.sb("oc", [128, 1024])
    xc2 = prog.sb("xc2", [128, 1024])
    ps_f = prog.ps("ps_f", [128, 512])
    ps_ss = prog.ps("ps_ss", [4, 512])
    ps_small = prog.ps("ps_small", [128, 512])
    ps_rt = ps_small[:, 0:4]
    ps_m = ps_small[:, 128:193]
    ps_s1 = ps_small[0:1, 256:321]
    ps_s2 = ps_small[:, 384:449]
    ps_o = prog.ps("ps_o", [128, 1024])

    def pbias(y):
        prog.dma(bq_sb[:, :], bq_d.ap(), sem="dm")
        prog.dma(bo_sb[:, :], bo_d.ap(), sem="dm")
    prog.section("y", pbias)

    def ykv0(y):
        prog.dma(kvs[:, :], kva_d.ap()[0], sem="dm")
    prog.section("y", ykv0)
    kv_tok = ("dm", prog.cnt["dm"])
    for j in range(1, 8):
        def ykvj(y, j=j):
            prog.dma(kvt[:, :], kva_d.ap()[j], sem="dm",
                     waits=(prog.tok("v"),))
        prog.section("y", ykvj)
        ld_tok = ("dm", prog.cnt["dm"])

        def vadd(v, j=j):
            prog.op(lambda: v.tensor_tensor(out=kvs[:, :], in0=kvs[:, :],
                                            in1=kvt[:, :], op=ALU.add),
                    waits=[ld_tok, kv_tok])
        prog.section("v", vadd)
    
    def vcast(v):
        prog.op(lambda: v.tensor_copy(out=kvb[:, :], in_=kvs[:, :]))
    prog.section("v", vcast)
    kvb_tok = prog.tok("v")
    pb_tok = ("dm", prog.cnt["dm"])

    o_store = ("dso", 0)
    w_free = ("t", 0)
    for hp in range(8):
        wld = [None, None]

        def yw(y, hp=hp, wf=w_free):
            wld[0] = prog.dma(wrot[:, :].rearrange("p (kc n) -> p kc n", n=128),
                              wq_d.ap()[:, hp * 128:(hp + 1) * 128]
                                  .rearrange("(kc p) n -> p kc n", p=128),
                              waits=(wf,), sem="dwa")
            prog.dma(prot[0:64, 0:256], pj_d.ap()[hp * 2], sem="dwb")
            wld[1] = prog.dma(prot[64:128, 256:512], pj_d.ap()[hp * 2 + 1], sem="dwb")
        prog.section("y", yw)

        for tt in range(4):
            def tqf(t, tt=tt, hp=hp):
                prog.wait(wld[0])
                prog.wait(xlT_tok)
                prog.wait(prog.tok("s"))
                for kc in range(8):
                    prog.op(lambda kc=kc: t.matmul(
                        ps_f[:, :512], wrot[:, kc * 128:(kc + 1) * 128],
                        xlT[:, kc * TCn * 128 + tt * 512: kc * TCn * 128 + tt * 512 + 512],
                        start=(kc == 0), stop=(kc == 7)))
            prog.section("t", tqf)
            qf_tok = prog.tok("t")

            def sqf(s, tt=tt, hp=hp):
                prog.op(lambda: s.activation(out=Qf[:, tt * 512:(tt + 1) * 512],
                                             in_=ps_f[:, :512], func=AF.Identity,
                                             bias=bq_sb[:, hp:hp + 1]),
                        waits=[qf_tok, pb_tok, prog.tok("v")])
            prog.section("s", sqf)
        qf_done = prog.tok("s")

        def sqq(s, hp=hp):
            prog.op(lambda: s.activation(out=sqs[:, :], in_=Qf[:, :], func=AF.Square),
                    waits=[qf_done, prog.tok("t")])
        prog.section("s", sqq)
        sq_tok = prog.tok("s")

        def vqb(v, hp=hp):
            prog.op(lambda: v.tensor_copy(out=Qfb[:, :], in_=Qf[:, :]),
                    waits=[qf_done, prog.tok("t")])
        prog.section("v", vqb)
        qfb_tok = prog.tok("v")

        for tt in range(4):
            def tss(t, tt=tt):
                prog.wait(sq_tok)
                prog.wait(cst_tok)
                prog.wait(prog.tok("v"))
                prog.op(lambda: t.matmul(ps_ss[:2, :512], E2f[:, :],
                                         sqs[:, tt * 512:(tt + 1) * 512],
                                         start=True, stop=True))
            prog.section("t", tss)
            ssm_tok = prog.tok("t")

            def vss(v, tt=tt):
                prog.op(lambda: v.tensor_copy(out=ss_sb[0:2, tt * 512:(tt + 1) * 512],
                                              in_=ps_ss[:2, :512]), waits=[ssm_tok])
            prog.section("v", vss)
        ss_tok = prog.tok("v")

        def srr(s):
            prog.op(lambda: s.activation(out=ss_sb[0:2, :], in_=ss_sb[0:2, :], func=AF.Ln),
                    waits=[ss_tok])
            prog.op(lambda: s.activation(out=ss_sb[0:2, :], in_=ss_sb[0:2, :], func=AF.Exp,
                                         scale=0.5))
        prog.section("s", srr)
        nrm_tok = prog.tok("s")

        def vrr(v):
            prog.op(lambda: v.tensor_scalar(out=ss_sb[0:2, :], in0=ss_sb[0:2, :],
                                            scalar1=1e-8, scalar2=None, op0=ALU.add),
                    waits=[nrm_tok])
            prog.op(lambda: v.tensor_scalar(out=qq_sb[0:2, :], in0=ss_sb[0:2, :],
                                            scalar1=0.1, scalar2=None, op0=ALU.mult))
            prog.op(lambda: v.reciprocal(out=ss_sb[0:2, :], in_=ss_sb[0:2, :]))
        prog.section("v", vrr)
        rr_tok = prog.tok("v")

        for tc in range(TCn):
            def trt(t, tc=tc):
                prog.op(lambda: t.transpose(ps_rt[:, 0:2],
                                            ss_sb[0:2, tc * 128:(tc + 1) * 128],
                                            identf[0:2, 0:2]),
                        waits=[rr_tok, idf_tok, prog.tok("v")])
                prog.op(lambda: t.transpose(ps_rt[:, 2:4],
                                            qq_sb[0:2, tc * 128:(tc + 1) * 128],
                                            identf[0:2, 0:2]))
            prog.section("t", trt)
            rt_tok = prog.tok("t")

            def vrt(v, tc=tc):
                prog.op(lambda: v.tensor_copy(out=rqT[:, tc:tc + 1],
                                              in_=ps_rt[:, 0:1]), waits=[rt_tok])
                prog.op(lambda: v.tensor_copy(out=rqT[:, 16 + tc:16 + tc + 1],
                                              in_=ps_rt[:, 1:2]))
                prog.op(lambda: v.tensor_copy(out=rqT[:, 32 + tc:32 + tc + 1],
                                              in_=ps_rt[:, 2:3]))
                prog.op(lambda: v.tensor_copy(out=rqT[:, 48 + tc:48 + tc + 1],
                                              in_=ps_rt[:, 3:4]))
            prog.section("v", vrt)
        rqT_tok = prog.tok("v")

        for h2 in range(2):
            h = hp * 2 + h2
            for fc in range(2):
                for tt in range(4):
                    def trf(t, h2=h2, fc=fc, tt=tt):
                        prog.wait(wld[1])
                        prog.wait(qfb_tok)
                        prog.wait(prog.tok("v"))
                        prog.op(lambda: t.matmul(
                            ps_f[:, :512],
                            prot[h2 * 64:h2 * 64 + 64,
                                 h2 * 256 + fc * 128: h2 * 256 + fc * 128 + 128],
                            Qfb[h2 * 64:h2 * 64 + 64, tt * 512:(tt + 1) * 512],
                            start=True, stop=True))
                    prog.section("t", trf)
                    rf_tok = prog.tok("t")

                    def vrf(v, fc=fc, tt=tt):
                        prog.op(lambda: v.tensor_scalar(
                            out=Rf[:, fc * 2048 + tt * 512: fc * 2048 + (tt + 1) * 512],
                            in0=ps_f[:, :512], scalar1=0.0, scalar2=None,
                            op0=ALU.max), waits=[rf_tok])
                    prog.section("v", vrf)
            rf_done = prog.tok("v")

            def tsp(t, h=h):
                prog.wait(kvb_tok)
                prog.wait(cst_tok)
                prog.wait(prog.tok("v"))
                for fc in range(2):
                    prog.op(lambda fc=fc: t.matmul(
                        ps_s1[0:1, 0:65], onesb[:, 0:1],
                        kvb[:, (h * 2 + fc) * 65:(h * 2 + fc) * 65 + 65],
                        start=(fc == 0), stop=(fc == 1)))
            prog.section("t", tsp)
            sp_tok = prog.tok("t")

            def vsp(v):
                prog.op(lambda: v.tensor_copy(out=Srow[:1, :], in_=ps_s1[0:1, 0:65]),
                        waits=[sp_tok])
            prog.section("v", vsp)
            srow_tok = prog.tok("v")

            def tsp2(t):
                prog.wait(srow_tok)
                prog.op(lambda: t.matmul(ps_s2[:, 0:65], onesb[0:1, 0:128],
                                         Srow[0:1, 0:65], start=True, stop=True))
            prog.section("t", tsp2)
            sp2_tok = prog.tok("t")

            def vsp2(v):
                prog.op(lambda: v.tensor_copy(out=Srep[:, :], in_=ps_s2[:, 0:65]),
                        waits=[sp2_tok])
            prog.section("v", vsp2)
            srep_tok = prog.tok("v")

            for tc in range(TCn):
                def tm(t, h=h, h2=h2, tc=tc):
                    prog.wait(rf_done)
                    prog.wait(kvb_tok)
                    prog.wait(prog.tok("v"))
                    for fc in range(2):
                        prog.op(lambda fc=fc: t.matmul(
                            ps_m[:, 0:65],
                            Rf[:, fc * 2048 + tc * 128: fc * 2048 + tc * 128 + 128],
                            kvb[:, (h * 2 + fc) * 65:(h * 2 + fc) * 65 + 65],
                            start=(fc == 0), stop=(fc == 1)))
                prog.section("t", tm)
                m_tok = prog.tok("t")

                def vm(v, h=h, h2=h2, tc=tc):
                    qcol = rqT[:, 32 + h2 * 16 + tc: 32 + h2 * 16 + tc + 1]
                    prog.op(lambda: v.tensor_scalar(out=t1[:, :], in0=Srep[:, :],
                                                    scalar1=qcol, scalar2=None,
                                                    op0=ALU.mult),
                            waits=[srep_tok, rqT_tok])
                    prog.op(lambda: v.tensor_tensor(out=t1[:, :], in0=t1[:, :],
                                                    in1=ps_m[:, 0:65], op=ALU.add),
                            waits=[m_tok])
                    prog.op(lambda: v.tensor_scalar(out=oc[:, 0:1], in0=qcol,
                                                    scalar1=1e-5, scalar2=None,
                                                    op0=ALU.mult))
                    prog.op(lambda: v.tensor_tensor(out=oc[:, 0:1], in0=t1[:, 64:65],
                                                    in1=oc[:, 0:1], op=ALU.max))
                    prog.op(lambda: v.reciprocal(out=oc[:, 0:1], in_=oc[:, 0:1]))
                    prog.op(lambda: v.tensor_scalar(
                        out=outg[:, tc * 1024 + h * 64: tc * 1024 + h * 64 + 64],
                        in0=t1[:, 0:64], scalar1=oc[:, 0:1], scalar2=None,
                        op0=ALU.mult))
                prog.section("v", vm)
        w_free = prog.tok("t")
    outg_tok = prog.tok("v")

    for tc in range(TCn):
        for kc in range(8):
            def tsec(t, tc=tc, kc=kc):
                prog.op(lambda: t.transpose(bufs["ps_tp"][:, :],
                                            outg[:, tc * 1024 + kc * 128: tc * 1024 + kc * 128 + 128],
                                            ident[:, :]),
                        waits=[outg_tok, prog.tok("v")])
            prog.section("t", tsec)
            tr_tok = prog.tok("t")

            def vsec3(v, tc=tc, kc=kc):
                prog.op(lambda: v.tensor_copy(
                    out=ogT[:, (kc * TCn + tc) * 128:(kc * TCn + tc) * 128 + 128],
                    in_=bufs["ps_tp"][:, :]), waits=[tr_tok])
            prog.section("v", vsec3)
    ogT_tok = prog.tok("v")

    woall = prog.sb("woall", [128, 8 * 1024], BF16)

    def ywoall(y):
        prog.dma(woall[:, :].rearrange("p (kc n) -> p kc n", n=1024),
                 wo_d.ap().rearrange("(kc p) n -> p kc n", p=128), sem="dwa")
    prog.section("y", ywoall)
    wo_tok = ("dwa", prog.cnt["dwa"])

    for tc in range(TCn):
        def tmo(t, tc=tc):
            prog.wait(ogT_tok)
            prog.wait(wo_tok)
            prog.wait(prog.tok("v"))
            for half in range(2):
                for kc in range(8):
                    prog.op(lambda kc=kc, half=half: t.matmul(
                        ps_o[:, half * 512:(half + 1) * 512],
                        ogT[:, (kc * TCn + tc) * 128:(kc * TCn + tc) * 128 + 128],
                        woall[:, kc * 1024 + half * 512: kc * 1024 + half * 512 + 512],
                        start=(kc == 0), stop=(kc == 7)))
        prog.section("t", tmo)
        mo_tok = prog.tok("t")

        ld2 = [None]

        def yx2(y, tc=tc):
            ld2[0] = prog.dma(xc2[:, :], x_d.ap()[tc * 128:(tc + 1) * 128, :],
                              waits=(prog.tok("v"),), sem="dx")
        prog.section("y", yx2)

        def vev(v, tc=tc, ost=o_store):
            prog.wait(ost)
            prog.op(lambda: v.tensor_tensor(out=oc[:, :], in0=ps_o[:, :1024],
                                            in1=bo_sb[:, :], op=ALU.add),
                    waits=[mo_tok, pb_tok])
            prog.op(lambda: v.tensor_tensor(out=oc[:, :], in0=oc[:, :],
                                            in1=xc2[:, :], op=ALU.add),
                    waits=[ld2[0]])
        prog.section("v", vev)
        ev_tok = prog.tok("v")

        def yst(y, tc=tc):
            prog.dma(o_d.ap()[tc * 128:(tc + 1) * 128, :], oc[:, :], waits=(ev_tok,),
                     sem="dso")
        prog.section("y", yst)
        o_store = ("dso", prog.cnt["dso"])

    return


def run_attention(x_full, casts, inp):
    c2 = lambda a: np.ascontiguousarray(a)
    projT = c2(np.asarray(casts["L1_attn_projection"]).transpose(0, 2, 1))
    bk_r = c2(inp["L1_attn_bk"].astype(np.float32).reshape(8, 128).T)
    bq_r = c2(inp["L1_attn_bq"].astype(np.float32).reshape(8, 128).T)
    bv_rep = c2(np.broadcast_to(inp["L1_attn_bv"].astype(np.float32), (128, 1024)))
    bo_rep = c2(np.broadcast_to(inp["L1_attn_bo"].astype(np.float32), (128, 1024)))
    ncA = build_att_a()
    mapsA = []
    for i in range(NCORES):
        mapsA.append({"x_s": c2(x_full[i * T_CORE:(i + 1) * T_CORE]),
                      "Wk_b": c2(casts["L1_attn_Wk"]), "Wv_b": c2(casts["L1_attn_Wv"]),
                      "bk_r": bk_r, "bv_rep": bv_rep, "projT_b": projT})
    resA = run_bass_kernel_spmd(ncA, mapsA, core_ids=list(range(NCORES))).results
    kv_all = np.stack([np.asarray(resA[i]["o_kv"], np.float32) for i in range(NCORES)])
    ncB = build_att_b()
    mapsB = []
    for i in range(NCORES):
        mapsB.append({"x_s": c2(x_full[i * T_CORE:(i + 1) * T_CORE]),
                      "kv_all": kv_all, "Wq_b": c2(casts["L1_attn_Wq"]),
                      "bq_r": bq_r, "projT_b": projT,
                      "Wo_b": c2(casts["L1_attn_Wo"]), "bo_rep": bo_rep})
    resB = run_bass_kernel_spmd(ncB, mapsB, core_ids=list(range(NCORES))).results
    return np.concatenate([resB[i]["o_x"] for i in range(NCORES)], axis=0)


def emit_prelude(prog):
    ident = prog.sbuf("P_ident", [128, 128], BF16)
    io1 = prog.sbuf("P_io1", [128, 128], I32)
    io2 = prog.sbuf("P_io2", [128, 128], I32)
    id_tok = emit_identity(prog, ident, io1, io2)
    E2f, onesb, cst_tok = emit_consts_att(prog)
    identf = prog.sbuf("P_identf", [128, 128])

    def vidf(v):
        prog.op(lambda: v.tensor_copy(out=identf[:, :], in_=ident[:, :]),
                waits=[id_tok])
    prog.section("v", vidf)
    idf_tok = prog.tok("v")
    return {"_ident": ident, "_id_tok": id_tok, "_identf": identf,
            "_idf_tok": idf_tok, "_E2f": E2f, "_onesb": onesb,
            "_cst_tok": cst_tok}


def _fin(prog):
    def fin(y):
        for sname in ("dh", "dx", "dwa", "dwb", "dso", "dm"):
            if prog.cnt[sname]:
                y.wait_ge(prog.sem[sname], prog.cnt[sname])
        if prog.cnt["dg"]:
            y.wait_ge(prog.sem["dg"], prog.cnt["dg"])
    prog.section("y", fin)


def build_L1():
    prog = Prog("L1")
    prog.init_arena(52400)
    pre = emit_prelude(prog)
    TH = T_CORE + HALO
    x_d = prog.dram("x_s", [TH, D], F32, kind="ExternalInput")
    wg_d = prog.dram("wgqT", [1024, 2816], BF16, kind="ExternalInput")
    wu_d = prog.dram("wuqT", [1024, 2816], BF16, kind="ExternalInput")
    wd_d = prog.dram("wdqT", [2816, 1024], BF16, kind="ExternalInput")
    x1_d = prog.dram("o_x1", [TH, D], F32, kind="ExternalOutput")
    io_m = {k: prog.dram(n, sh, dt, kind=kk) for k, n, sh, dt, kk in [
        ("y1", "o_y1", [2048, 2048], BF16, "ExternalOutput"),
        ("dcum", "o_dcum", [2048, 2048], F32, "ExternalOutput"),
        ("xn2T", "o_xn2T", [128, 8 * XCOLS], BF16, "ExternalOutput"),
        ("hend", "o_hend", [128, 256], F32, "ExternalOutput"),
        ("pcore", "o_pcore", [128, 256], F32, "ExternalOutput"),
        ("mstat", "o_mstat", [128, 16], F32, "ExternalOutput"),
        ("rstat", "o_rstat", [128, 16], F32, "ExternalOutput"),
        ("halo_mask", "halo_mask", [128, HALO], F32, "ExternalInput"),
        ("inproj", "inproj", [1024, 4096], BF16, "ExternalInput"),
        ("dtw", "dtw", [2048, 2048], BF16, "ExternalInput"),
        ("xpwB", "xpwB", [2048, 16], BF16, "ExternalInput"),
        ("convw_r", "convw_r", [128, 64], F32, "ExternalInput"),
        ("convb_r", "convb_r", [128, 16], F32, "ExternalInput"),
        ("dtb_r", "dtb_r", [128, 16], F32, "ExternalInput"),
        ("alog_r", "alog_r", [128, 256], F32, "ExternalInput"),
        ("dparam_r", "dparam_r", [128, 16], F32, "ExternalInput"),
        ("lng_r", "lng_r", [128, 1024], F32, "ExternalInput"),
        ("lnb_r", "lnb_r", [128, 1024], F32, "ExternalInput"),
    ]}
    io_m["b_bounce"] = prog.dram("b_bounce", [16, LT], F32)
    io_m["x"] = x1_d
    prog.phase_reset()
    emit_ffn(prog, {"x": x_d, "o": x1_d, "wgqT": wg_d, "wuqT": wu_d,
                    "wdqT": wd_d, **pre}, halo=True)
    prog.phase_reset()
    emit_mamba_a(prog, {**io_m, **pre})
    _fin(prog)
    return prog.done()


def build_L2():
    prog = Prog("L2")
    prog.init_arena(52400)
    pre = emit_prelude(prog)
    x1_d = prog.dram("x1_s", [T_CORE, D], F32, kind="ExternalInput")
    x2_d = prog.dram("i_x2", [T_CORE, D], F32)
    x3_d = prog.dram("i_x3", [T_CORE, D], F32)
    x4_d = prog.dram("o_x4", [T_CORE, D], F32, kind="ExternalOutput")
    kv_d = prog.dram("o_kv", [128, 2080], F32, kind="ExternalOutput")
    dd = lambda n, sh, dt, kk=None: prog.dram(n, sh, dt, kind=kk)
    io_b = {
        "x": x1_d, "o": x2_d,
        "y1": dd("y1", [2048, 2048], BF16, "ExternalInput"),
        "dcum": dd("dcum", [2048, 2048], F32, "ExternalInput"),
        "xn2T": dd("xn2T", [128, 8 * XCOLS], BF16, "ExternalInput"),
        "mstat": dd("mstat", [128, 16], F32, "ExternalInput"),
        "rstat": dd("rstat", [128, 16], F32, "ExternalInput"),
        "slotsP": dd("slotsP", [7, 128, 256], F32, "ExternalInput"),
        "slotsH": dd("slotsH", [7, 128, 256], F32, "ExternalInput"),
        "inprojZ": dd("inprojZ", [1024, 2048], BF16, "ExternalInput"),
        "outw": dd("outw", [2048, 1024], BF16, "ExternalInput"),
        "alog_r": dd("alog_r", [128, 256], F32, "ExternalInput"),
    }
    io_f = {"x": x2_d, "o": x3_d,
            "wgqT": dd("wgqT", [1024, 2816], BF16, "ExternalInput"),
            "wuqT": dd("wuqT", [1024, 2816], BF16, "ExternalInput"),
            "wdqT": dd("wdqT", [2816, 1024], BF16, "ExternalInput")}
    io_moe = {"x": x3_d, "o": x4_d,
              "wgate": dd("wgate", [1024, 8], BF16, "ExternalInput"),
              "wdview": dd("wdview", [1024, 512], BF16, "ExternalInput"),
              "wuview": dd("wuview", [512, 1024], BF16, "ExternalInput"),
              "buw": dd("buw", [8, 1024], BF16, "ExternalInput"),
              "bdrep": dd("bdrep", [128, 512], F32, "ExternalInput")}
    io_a = {"x": x4_d, "kv": kv_d,
            "Wk_b": dd("Wk_b", [1024, 1024], BF16, "ExternalInput"),
            "Wv_b": dd("Wv_b", [1024, 1024], BF16, "ExternalInput"),
            "bk_r": dd("bk_r", [128, 8], F32, "ExternalInput"),
            "bv_rep": dd("bv_rep", [128, 1024], F32, "ExternalInput"),
            "projT_b": dd("projT_b", [16, 64, 256], BF16, "ExternalInput")}
    prog.phase_reset()
    emit_mamba_b(prog, {**io_b, **pre})
    prog.phase_reset()
    emit_ffn(prog, {**io_f, **pre}, halo=False)
    prog.phase_reset()
    emit_moe(prog, {**io_moe, **pre})
    prog.phase_reset()
    emit_att_a(prog, {**io_a, **pre})
    _fin(prog)
    return prog.done()


def build_L3():
    prog = Prog("L3")
    prog.init_arena(52400)
    pre = emit_prelude(prog)
    x4_d = prog.dram("x4_s", [T_CORE, D], F32, kind="ExternalInput")
    x5_d = prog.dram("i_x5", [T_CORE, D], F32)
    o_d = prog.dram("o_x", [T_CORE, D], F32, kind="ExternalOutput")
    dd = lambda n, sh, dt, kk=None: prog.dram(n, sh, dt, kind=kk)
    io_ab = {"x": x4_d, "o": x5_d,
             "kv_all": dd("kv_all", [8, 128, 2080], F32, "ExternalInput"),
             "Wq_b": dd("Wq_b", [1024, 1024], BF16, "ExternalInput"),
             "bq_r": dd("bq_r", [128, 8], F32, "ExternalInput"),
             "projT_b": dd("projT_b", [16, 64, 256], BF16, "ExternalInput"),
             "Wo_b": dd("Wo_b", [1024, 1024], BF16, "ExternalInput"),
             "bo_rep": dd("bo_rep", [128, 1024], F32, "ExternalInput"),
             "ss_bounce": dd("ss_bounce", [4, 2048], F32)}
    io_moe = {"x": x5_d, "o": o_d,
              "wgate": dd("wgate", [1024, 8], BF16, "ExternalInput"),
              "wdview": dd("wdview", [1024, 512], BF16, "ExternalInput"),
              "wuview": dd("wuview", [512, 1024], BF16, "ExternalInput"),
              "buw": dd("buw", [8, 1024], BF16, "ExternalInput"),
              "bdrep": dd("bdrep", [128, 512], F32, "ExternalInput")}
    prog.phase_reset()
    emit_att_b(prog, {**io_ab, **pre})
    prog.phase_reset()
    emit_moe(prog, {**io_moe, **pre})
    _fin(prog)
    return prog.done()


def kernel_fused(**inp):
    wgqT, wuqT, wdqT, casts = run_wprep(inp)
    c2 = lambda a: np.ascontiguousarray(a)
    x = c2(inp["L0_attn_x"][0]).astype(np.float32, copy=False)

    nc1 = _get_nc("L1", build_L1)
    mapsA = _mamba_a_inmaps(x, casts, inp)
    maps1 = []
    for i in range(NCORES):
        m = dict(mapsA[i])
        m["wgqT"] = wgqT; m["wuqT"] = wuqT; m["wdqT"] = wdqT
        maps1.append(m)
    res1 = run_bass_kernel_spmd(nc1, maps1, core_ids=list(range(NCORES))).results

    x1_full = np.concatenate([np.asarray(res1[i]["o_x1"], np.float32)[HALO:]
                              for i in range(NCORES)], axis=0)

    nc2 = _get_nc("L2", build_L2)
    mapsB = _mamba_b_inmaps(x1_full, res1, casts, inp)
    wd_view = c2(np.asarray(casts["L1_ffn_Wd"]).transpose(1, 0, 2).reshape(1024, 512))
    wu_view = c2(np.asarray(casts["L1_ffn_Wu"]).reshape(512, 1024))
    bd_rep = c2(np.broadcast_to(inp["L1_ffn_bd"].astype(np.float32).reshape(512), (128, 512)))
    projT = c2(np.asarray(casts["L1_attn_projection"]).transpose(0, 2, 1))
    bk_r = c2(inp["L1_attn_bk"].astype(np.float32).reshape(8, 128).T)
    bq_r = c2(inp["L1_attn_bq"].astype(np.float32).reshape(8, 128).T)
    bv_rep = c2(np.broadcast_to(inp["L1_attn_bv"].astype(np.float32), (128, 1024)))
    bo_rep = c2(np.broadcast_to(inp["L1_attn_bo"].astype(np.float32), (128, 1024)))
    maps2 = []
    for i in range(NCORES):
        m = dict(mapsB[i])
        m.pop("x1_s", None)
        m["x1_s"] = c2(x1_full[i * T_CORE:(i + 1) * T_CORE])
        m["wgqT"] = wgqT; m["wuqT"] = wuqT; m["wdqT"] = wdqT
        m["wgate"] = c2(casts["L1_ffn_w_gate"]); m["wdview"] = wd_view
        m["wuview"] = wu_view; m["buw"] = c2(np.asarray(casts["L1_ffn_bu"]))
        m["bdrep"] = bd_rep
        m["Wk_b"] = c2(casts["L1_attn_Wk"]); m["Wv_b"] = c2(casts["L1_attn_Wv"])
        m["bk_r"] = bk_r; m["bv_rep"] = bv_rep; m["projT_b"] = projT
        maps2.append(m)
    res2 = run_bass_kernel_spmd(nc2, maps2, core_ids=list(range(NCORES))).results

    x4_full = np.concatenate([np.asarray(res2[i]["o_x4"], np.float32)
                              for i in range(NCORES)], axis=0)
    kv_all = np.stack([np.asarray(res2[i]["o_kv"], np.float32) for i in range(NCORES)])

    nc3 = _get_nc("L3", build_L3)
    maps3 = []
    for i in range(NCORES):
        maps3.append({"x4_s": c2(x4_full[i * T_CORE:(i + 1) * T_CORE]),
                      "kv_all": kv_all, "Wq_b": c2(casts["L1_attn_Wq"]),
                      "bq_r": bq_r, "projT_b": projT,
                      "Wo_b": c2(casts["L1_attn_Wo"]), "bo_rep": bo_rep,
                      "wgate": c2(casts["L1_ffn_w_gate"]), "wdview": wd_view,
                      "wuview": wu_view, "buw": c2(np.asarray(casts["L1_ffn_bu"])),
                      "bdrep": bd_rep})
    res3 = run_bass_kernel_spmd(nc3, maps3, core_ids=list(range(NCORES))).results
    out = np.concatenate([res3[i]["o_x"] for i in range(NCORES)], axis=0)
    return out[None].astype(np.float32)
